# version 01
# baseline (speedup 1.0000x reference)
import math
from contextlib import ExitStack

import numpy as np
import ml_dtypes

import concourse.bass as bass
import concourse.mybir as mybir
from concourse.bass_utils import run_bass_kernel_spmd

BF = ml_dtypes.bfloat16
F32 = mybir.dt.float32
BF16 = mybir.dt.bfloat16
I32 = mybir.dt.int32
U8 = mybir.dt.uint8
AF = mybir.ActivationFunctionType
ALU = mybir.AluOpType
AX = mybir.AxisListType

T = 4096
D = 1024
NKC = D // 128
TW = 512
NTT = T // TW

ENGS = ('t', 'v', 's', 'g', 'y')
SEM_OF = {'t': 'et', 'v': 'ev', 's': 'es'}
NLANE = 4


class Bld:

    def __init__(self, nc, stack):
        self.nc = nc
        self.q = {e: [] for e in ENGS}
        self.semh = {}
        self.cnt = {}
        for name in SEM_OF.values():
            self.semh[name] = stack.enter_context(nc.semaphore(name))
            self.cnt[name] = 0
        for qn in ('g', 'y'):
            for i in range(NLANE):
                nm = f"{qn}{i}"
                self.semh[nm] = stack.enter_context(nc.semaphore(nm))
                self.cnt[nm] = 0
        self.rot = {'g': 0, 'y': 0}
        self.lane_prev = {f"{qn}{i}": 0 for qn in ('g', 'y') for i in range(NLANE)}

    def op(self, eng, fn, n=1):
        assert eng in ('t', 'v', 's')
        sem = SEM_OF[eng]
        self.cnt[sem] += n
        h = self.semh[sem]
        def f(e, fn=fn, h=h, n=n):
            fn(e).then_inc(h, n)
        self.q[eng].append(f)
        return self.cnt[sem]

    def _lane(self, qn):
        i = self.rot[qn] % NLANE
        self.rot[qn] += 1
        return f"{qn}{i}"

    def dma(self, qn, dst, src, **kw):
        return self.dma_group(qn, [(dst, src, kw)])

    def dma_group(self, qn, items):
        assert qn in ('g', 'y')
        lane = self._lane(qn)
        if self.lane_prev[lane] > 0:
            self.wait(qn, lane, self.lane_prev[lane])
        h = self.semh[lane]
        for dst, src, kw in items:
            self.cnt[lane] += 16
            def f(e, dst=dst, src=src, kw=kw, h=h):
                e.dma_start(dst, src, **kw).then_inc(h, 16)
            self.q[qn].append(f)
        self.lane_prev[lane] = self.cnt[lane]
        return (lane, self.cnt[lane])

    def wait(self, eng, sem, v):
        assert v <= self.cnt[sem], f"wait {eng} {sem} {v} > {self.cnt[sem]}"
        if v <= 0:
            return
        h = self.semh[sem]
        self.q[eng].append(lambda e, h=h, v=v: e.wait_ge(h, v))

    def wait_t(self, eng, tok):
        if tok is not None:
            self.wait(eng, tok[0], tok[1])

    def selfwait(self, eng):
        sem = SEM_OF[eng]
        self.wait(eng, sem, self.cnt[sem])

    def barrier(self):
        snap = dict(self.cnt)
        own = {'t': 'et', 'v': 'ev', 's': 'es'}
        for eng in ENGS:
            for sem, v in snap.items():
                if own.get(eng) != sem:
                    self.wait(eng, sem, v)

    def finalize_into_block(self):
        nc = self.nc
        with nc.Block() as blk:
            @blk.tensor
            def _(e):
                for f in self.q['t']: f(e)
            @blk.vector
            def _(e):
                for f in self.q['v']: f(e)
            @blk.scalar
            def _(e):
                for f in self.q['s']: f(e)
            @blk.gpsimd
            def _(e):
                for f in self.q['g']: f(e)
            @blk.sync
            def _(e):
                for f in self.q['y']: f(e)


class MK:
    def __init__(self, nc, stack, layers, dbg=False):
        self.nc = nc
        self.b = Bld(nc, stack)
        self.layers = layers
        self.dbg = dbg
        self.dram = {}
        self._declare_dram()
        self._alloc_sbuf(stack)
        self.ps = [stack.enter_context(nc.psum_tensor(f"ps{i}", [128, 512], F32))
                   for i in range(7)]
        self.psb = stack.enter_context(nc.psum_tensor("psb", [128, 1024], BF16))

    def dt(self, name, shape, dtype, kind="ExternalInput"):
        t = self.nc.dram_tensor(name, list(shape), dtype, kind=kind)
        self.dram[name] = t.ap()
        return self.dram[name]

    def _declare_dram(self):
        dt = self.dt
        L = self.layers
        dt("x_in", [D, T], F32)
        dt("x_out", [D, T], F32, kind="ExternalOutput")
        dt("x_work", [D, T], F32, kind="Internal")
        spk = "ExternalOutput" if self.dbg else "Internal"
        dt("spill_a", [4096, T], BF16, kind=spk)
        dt("spill_b", [4096, T], BF16, kind=spk)
        if self.dbg:
            dt("dbg_nrm", [D, T], BF16, kind="ExternalOutput")
            dt("dbg_w", [128, 32768], BF16, kind="ExternalOutput")
        dt("ones_tab", [1, 128], F32)
        dt("ident_tab", [128, 128], BF16)
        if "l0_attn" in L:
            dt("L0_attn_W_qkv", [D, 4096], F32)
            dt("L0_attn_W_g", [D, 2048], F32)
            dt("L0_attn_W_proj", [2048, D], F32)
            dt("L0_attn_decay", [1, 8], F32)
            for nm in ("xpos_cq", "xpos_sq", "xpos_ck", "xpos_sk"):
                dt(nm, [D, T], F32)
            dt("rot_tab", [128, 128], BF16)
            dt("exp_tab", [128, 128], F32)
            dt("iota_col", [128, 1], F32)
        if "l0_ffn" in L:
            dt("L0_ffn_lng", [128, NKC], F32)
            dt("L0_ffn_lnb", [128, NKC], F32)
            dt("L0_ffn_fc1_q", [D, 4096], I32)
            dt("L0_ffn_fc1_scales", [8, 4096], F32)
            dt("L0_ffn_fc1_bias", [128, 32], F32)
            dt("L0_ffn_fc2_q", [4096, D], I32)
            dt("L0_ffn_fc2_scales", [32, D], F32)
            dt("L0_ffn_fc2_bias", [128, 8], F32)
        if "l1_attn" in L:
            dt("L1_attn_W1t", [D, D], F32)
            dt("L1_attn_b1", [128, 8], F32)
            dt("L1_attn_W2t", [D, D], F32)
            dt("L1_attn_b2", [128, 8], F32)
            dt("L1_attn_Wgt", [D, 3072], F32)
            dt("L1_attn_bg", [128, 24], F32)
            dt("L1_attn_Wot", [D, D], F32)
            dt("L1_attn_bo", [128, 8], F32)
            dt("l1_masks", [128, 5 * TW], BF16)
        if "l1_ffn" in L:
            dt("L1_ffn_w1t", [D, 4096], F32)
            dt("L1_ffn_w1_s", [128, 32], F32)
            dt("L1_ffn_w2t", [D, 4096], F32)
            dt("L1_ffn_w2_s", [128, 32], F32)
            dt("L1_ffn_h1t", [4096, 4096], F32)
            dt("L1_ffn_h1_s", [128, 32], F32)
            dt("L1_ffn_h2t", [4096, 4096], F32)
            dt("L1_ffn_h2_s", [128, 32], F32)
            dt("L1_ffn_w3t", [4096, D], F32)
            dt("L1_ffn_w3_s", [128, 8], F32)
        if "l2_attn" in L:
            dt("L2_attn_Wv", [D, 2048], F32)
            dt("L2_attn_bv", [128, 16], F32)
            dt("L2_attn_Wg", [D, 2048], F32)
            dt("L2_attn_bg", [128, 16], F32)
            dt("L2_attn_Win", [D, 128], F32)
            dt("L2_attn_binc", [128, 1], F32)
            dt("L2_attn_Wout", [2048, D], F32)
            dt("L2_attn_bout", [128, 8], F32)
            dt("L2_attn_affs", [128, 8], F32)
            dt("l2_masks", [128, 2 * 256], BF16)
        if "l2_ffn" in L:
            dt("L2_ffn_k0", [2048, 4096], U8)
            dt("L2_ffn_k1", [4096, 4096], U8)
            dt("L2_ffn_k2", [4096, D], U8)
            dt("L2_ffn_sc", [1, 4], F32)
        if "l3_attn" in L:
            dt("L3_attn_Wq", [D, D], F32)
            dt("L3_attn_Wk", [D, D], F32)
            dt("L3_attn_Wv", [D, D], F32)
            dt("L3_attn_Wo", [D, D], F32)
            dt("L3_attn_bo", [128, 8], F32)
            dt("l3_masks", [128, 4 * TW], BF16)
        if "l3_ffn" in L:
            dt("L3_ffn_W1t", [D, 4096], F32)
            dt("L3_ffn_b1", [128, 32], F32)
            dt("L3_ffn_W2t", [4096, D], F32)
            dt("L3_ffn_b2", [128, 8], F32)

    def _alloc_sbuf(self, stack):
        nc = self.nc
        def sb(name, shape, dtype):
            t = stack.enter_context(nc.sbuf_tensor(name, list(shape), dtype))
            setattr(self, name, t)
            return t
        sb("nrm", [128, NKC * T], BF16)
        sb("xtile", [128, NKC * TW], F32)
        sb("evac", [128, 6 * TW], F32)
        sb("rows", [1, 2048], F32)
        sb("rows2", [1, 4096], F32)
        sb("ab", [128, 2 * TW], F32)
        sb("cols", [128, 64], F32)
        sb("cols2", [128, 64], F32)
        sb("ones_f", [1, 128], F32)
        sb("ones_c", [128, 1], F32)
        sb("ones_b", [1, 128], BF16)
        sb("ident", [128, 128], BF16)
        sb("masks", [128, 3072], BF16)
        sb("scr", [128, 2 * TW], F32)
        sb("scr2", [128, 2 * TW], F32)
        self.ARENA = 72 * 1024
        sb("arena8", [128, self.ARENA], U8)
        self.ar_bf = self.arena8.bitcast(BF16)
        self.ar_f32 = self.arena8.bitcast(F32)
        rem = nc.sbuf_bytes_remaining
        assert rem >= 0, f"SBUF over budget by {-rem}"
        self.sbuf_remaining = rem

    def abf(self, off_b, width):
        assert off_b % 2 == 0 and off_b + 2 * width <= self.ARENA, (off_b, width)
        return self.ar_bf[:, off_b // 2: off_b // 2 + width]

    def af32(self, off_b, width):
        assert off_b % 4 == 0 and off_b + 4 * width <= self.ARENA, (off_b, width)
        return self.ar_f32[:, off_b // 4: off_b // 4 + width]

    def emit_init(self):
        b, d = self.b, self.dram
        b.dma('y', self.ones_f[:], d["ones_tab"])
        b.dma('y', self.ident[:], d["ident_tab"])
        b.op('v', lambda e: e.memset(self.ones_b[:], 1.0))
        b.op('v', lambda e: e.memset(self.ones_c[:], 1.0))
        b.dma('y', d["x_work"], d["x_in"])
        b.barrier()

    def emit_finish(self):
        b, d = self.b, self.dram
        b.barrier()
        done = b.dma('y', d["x_out"], d["x_work"])
        b.wait_t('y', done)
        b.barrier()

    def emit_ln(self, gamma_col=None, beta_col=None, eps=1e-5):
        b, d = self.b, self.dram
        x_src = d["x_work"]
        nrm, xtile, R, ab, scr = self.nrm, self.xtile, self.rows, self.ab, self.scr
        psS, psQ, psB = self.ps[4], self.ps[5], self.ps[6]
        for tt in range(NTT):
            t0 = tt * TW
            ldv = b.dma_group('y', [(xtile[:, kc*TW:(kc+1)*TW],
                                     x_src[kc*128:(kc+1)*128, t0:t0+TW], {})
                                    for kc in range(NKC)])
            b.wait_t('t', ldv)
            for kc in range(NKC):
                b.op('t', lambda e, kc=kc: e.matmul(
                    psS[0:1, 0:TW], self.ones_c[:, 0:1], xtile[:, kc*TW:(kc+1)*TW],
                    start=(kc == 0), stop=(kc == NKC - 1)))
            et_s1 = b.cnt['et']
            b.wait_t('s', ldv)
            sqv = []
            mm2 = []
            for kc in range(NKC):
                half = (kc % 2) * TW
                if kc >= 2:
                    b.wait('s', 'et', mm2[kc - 2])
                sqv.append(b.op('s', lambda e, kc=kc, half=half: e.activation(
                    scr[:, half:half+TW], xtile[:, kc*TW:(kc+1)*TW], AF.Square)))
                b.wait('t', 'es', sqv[kc])
                mm2.append(b.op('t', lambda e, kc=kc, half=half: e.matmul(
                    psQ[0:1, 0:TW], self.ones_c[:, 0:1], scr[:, half:half+TW],
                    start=(kc == 0), stop=(kc == NKC - 1))))
            mmv = b.cnt['et']
            b.wait('v', 'et', mmv)
            b.op('v', lambda e: e.tensor_scalar_mul(R[0:1, 0:TW], psS[0:1, 0:TW], 1.0 / D))
            b.op('v', lambda e: e.tensor_scalar_mul(R[0:1, TW:2*TW], psQ[0:1, 0:TW], 1.0 / D))
            b.selfwait('v')
            b.op('v', lambda e: e.tensor_mul(R[0:1, 2*TW:3*TW], R[0:1, 0:TW], R[0:1, 0:TW]))
            b.selfwait('v')
            b.op('v', lambda e: e.tensor_scalar(R[0:1, 3*TW:4*TW], R[0:1, 2*TW:3*TW],
                                                -1.0, eps, ALU.mult, ALU.add))
            b.selfwait('v')
            vv = b.op('v', lambda e: e.tensor_add(R[0:1, 3*TW:4*TW], R[0:1, 3*TW:4*TW],
                                                  R[0:1, TW:2*TW]))
            b.wait('s', 'ev', vv)
            sv = b.op('s', lambda e: e.activation(R[0:1, 2*TW:3*TW], R[0:1, 3*TW:4*TW], AF.Sqrt))
            b.wait('v', 'es', sv)
            b.op('v', lambda e: e.reciprocal(R[0:1, 3*TW:4*TW], R[0:1, 2*TW:3*TW]))
            vv = b.op('v', lambda e: e.tensor_scalar_mul(R[0:1, 2*TW:3*TW],
                                                         R[0:1, 0:TW], -1.0))
            b.wait('t', 'ev', vv)
            b.op('t', lambda e: e.matmul(psB[:, 0:TW], self.ones_f[0:1, :],
                                         R[0:1, 3*TW:4*TW], start=True, stop=True))
            mmv = b.op('t', lambda e: e.matmul(psS[:, 0:TW], self.ones_f[0:1, :],
                                               R[0:1, 2*TW:3*TW], start=True, stop=True))
            b.wait('v', 'et', mmv)
            b.op('v', lambda e: e.tensor_copy(ab[:, 0:TW], psB[:, 0:TW]))
            b.op('v', lambda e: e.tensor_copy(ab[:, TW:2*TW], psS[:, 0:TW]))
            b.selfwait('v')
            for kc in range(NKC):
                half = (kc % 2) * TW
                dst = nrm[:, kc*T + t0: kc*T + t0 + TW]
                b.op('v', lambda e, kc=kc, half=half: e.tensor_add(
                    scr[:, half:half+TW], xtile[:, kc*TW:(kc+1)*TW], ab[:, TW:2*TW]))
                b.selfwait('v')
                b.op('v', lambda e, half=half, dst=dst: e.tensor_mul(
                    dst, scr[:, half:half+TW], ab[:, 0:TW]))
                if gamma_col is not None:
                    gk = gamma_col(kc); bk = beta_col(kc)
                    b.selfwait('v')
                    b.op('v', lambda e, dst=dst, gk=gk, bk=bk: e.tensor_scalar(
                        dst, dst, gk, bk, ALU.mult, ALU.add))
            b.wait('y', 'ev', b.cnt['ev'])
            b.wait('y', 'es', b.cnt['es'])
            b.wait('y', 'et', b.cnt['et'])
        b.barrier()

    def gemm_sw(self, W_dram, Kdim, Ndim, consume, *, in_tile=None, in_stream=None,
                gw=512, toks=T, wbuf_off=0, ibuf='arena', ibuf_off=None, name="",
                w_xform=None):
        b = self.b
        nkc = Kdim // 128
        nog = (Ndim + gw - 1) // gw
        ntt = toks // TW
        wbuf = self.abf(wbuf_off, 2 * nkc * gw)
        if in_stream is not None:
            if ibuf == 'nrm':
                assert 2 * nkc * TW <= NKC * T
                ibuft = self.nrm[:, 0: 2 * nkc * TW]
            else:
                ibuft = self.abf(ibuf_off, 2 * nkc * TW)
        PS = self.ps[0:4]
        group_after = {}
        tile_after = []
        cons = []
        ngroups = 0
        git = 0
        wl = {}
        for og in range(nog):
            buf = og % 2
            ow = min(gw, Ndim - og * gw)
            noc = (ow + 127) // 128
            if og >= 2:
                b.wait('g', 'et', group_after[og - 2])
            lv = b.dma_group('g', [(wbuf[:, buf*nkc*gw + kc*gw: buf*nkc*gw + kc*gw + ow],
                                    W_dram[kc*128:(kc+1)*128, og*gw:og*gw + ow], {})
                                   for kc in range(nkc)])
            if w_xform is not None:
                wl[og] = w_xform(og, buf, lv, wbuf, nkc, gw, ow)
            else:
                wl[og] = lv
            for tt in range(ntt):
                tok0 = tt * TW
                if in_stream is not None:
                    half = git % 2
                    if git >= 2:
                        b.wait('y', 'et', tile_after[git - 2])
                    ilv = b.dma_group('y', [
                        (ibuft[:, half*nkc*TW + kc*TW: half*nkc*TW + (kc+1)*TW],
                         in_stream[kc*128:(kc+1)*128, tok0:tok0+TW], {})
                        for kc in range(nkc)])
                for oc in range(noc):
                    ocg = og * (gw // 128) + oc
                    ps = PS[ngroups % 4]
                    if ngroups >= 4:
                        sem, val = cons[ngroups - 4]
                        b.wait('t', sem, val)
                    if oc == 0 and tt == 0:
                        b.wait_t('t', wl[og])
                    if in_stream is not None and oc == 0:
                        b.wait_t('t', ilv)
                    for kc in range(nkc):
                        if in_stream is not None:
                            rhs = ibuft[:, half*nkc*TW + kc*TW: half*nkc*TW + (kc+1)*TW]
                        else:
                            rhs = in_tile(kc, tt)
                        b.op('t', lambda e, ps=ps, buf=buf, kc=kc, oc=oc, ow=ow, rhs=rhs: e.matmul(
                            ps[:, 0:TW],
                            wbuf[:, buf*nkc*gw + kc*gw + oc*128: buf*nkc*gw + kc*gw + min(oc*128+128, ow)],
                            rhs, start=(kc == 0), stop=(kc == nkc - 1)))
                    group_after[og] = b.cnt['et']
                    cons.append(consume(ocg, tt, ps, tok0))
                    ngroups += 1
                tile_after.append(b.cnt['et'])
                git += 1
        return cons

    def gemm_rw(self, w_sb_fn, Kdim, Ndim, consume, *, in_tile=None, in_stream=None,
                toks=T, ibuf='nrm', ibuf_off=None, name=""):
        b = self.b
        nkc = Kdim // 128
        noc = (Ndim + 127) // 128
        ntt = toks // TW
        if in_stream is not None:
            if ibuf == 'nrm':
                assert 2 * nkc * TW <= NKC * T
                ibuft = self.nrm[:, 0: 2 * nkc * TW]
            else:
                ibuft = self.abf(ibuf_off, 2 * nkc * TW)
        PS = self.ps[0:4]
        cons = []
        tile_after = []
        ngroups = 0
        for tt in range(ntt):
            tok0 = tt * TW
            if in_stream is not None:
                half = tt % 2
                if tt >= 2:
                    b.wait('y', 'et', tile_after[tt - 2])
                ilv = b.dma_group('y', [
                    (ibuft[:, half*nkc*TW + kc*TW: half*nkc*TW + (kc+1)*TW],
                     in_stream[kc*128:(kc+1)*128, tok0:tok0+TW], {})
                    for kc in range(nkc)])
            for oc in range(noc):
                ps = PS[ngroups % 4]
                if ngroups >= 4:
                    sem, val = cons[ngroups - 4]
                    b.wait('t', sem, val)
                if in_stream is not None and oc == 0:
                    b.wait_t('t', ilv)
                for kc in range(nkc):
                    if in_stream is not None:
                        rhs = ibuft[:, half*nkc*TW + kc*TW: half*nkc*TW + (kc+1)*TW]
                    else:
                        rhs = in_tile(kc, tt)
                    b.op('t', lambda e, ps=ps, kc=kc, oc=oc, rhs=rhs: e.matmul(
                        ps[:, 0:TW], w_sb_fn(kc, oc), rhs,
                        start=(kc == 0), stop=(kc == nkc - 1)))
                cons.append(consume(oc, tt, ps, tok0))
                ngroups += 1
            tile_after.append(b.cnt['et'])
        return cons

    def nrm_tile(self, kc, tt):
        return self.nrm[:, kc*T + tt*TW: kc*T + tt*TW + TW]

    def mk_delta(self, bias_at=None, scale=None, scale_at=None):
        b, d = self.b, self.dram
        evac = self.evac
        state = {"n": 0, "dma": []}
        def consume(oc, tt, ps, tok0):
            n = state["n"]; state["n"] += 1
            dst = evac[:, (n % 6) * TW:(n % 6) * TW + TW]
            if n >= 6:
                b.wait_t('v', state["dma"][n - 6])
            b.wait('v', 'et', b.cnt['et'])
            if scale_at is not None:
                v = b.op('v', lambda e, dst=dst, ps=ps, oc=oc: e.tensor_scalar(
                    dst, ps[:, 0:TW], scale_at(oc), None, ALU.mult))
            elif bias_at is not None and scale is not None:
                v = b.op('v', lambda e, dst=dst, ps=ps, oc=oc: e.tensor_scalar(
                    dst, ps[:, 0:TW], scale, bias_at(oc), ALU.mult, ALU.add))
            elif bias_at is not None:
                v = b.op('v', lambda e, dst=dst, ps=ps, oc=oc: e.tensor_scalar(
                    dst, ps[:, 0:TW], bias_at(oc), None, ALU.add))
            elif scale is not None:
                v = b.op('v', lambda e, dst=dst, ps=ps: e.tensor_scalar(
                    dst, ps[:, 0:TW], scale, None, ALU.mult))
            else:
                v = b.op('v', lambda e, dst=dst, ps=ps: e.tensor_copy(dst, ps[:, 0:TW]))
            b.wait('g', 'ev', v)
            dv = b.dma('g', d["x_work"][oc*128:(oc+1)*128, tok0:tok0+TW], dst,
                       accum_op=ALU.add)
            state["dma"].append(dv)
            return ('ev', v)
        return consume

    def mk_spill(self, spill, act, bias_at=None, scale_at=None, scale=None,
                 ebuf_off=64 * 1024):
        b = self.b
        state = {"n": 0, "dma": []}
        ebuf = self.abf(ebuf_off, 8 * TW)
        def consume(oc, tt, ps, tok0):
            n = state["n"]; state["n"] += 1
            dst = ebuf[:, (n % 8) * TW:(n % 8) * TW + TW]
            if n >= 8:
                b.wait_t('s', state["dma"][n - 8])
            b.wait('s', 'et', b.cnt['et'])
            kw = {}
            if bias_at is not None:
                kw['bias'] = bias_at(oc)
            if scale_at is not None:
                kw['scale'] = scale_at(oc)
            elif scale is not None:
                kw['scale'] = scale
            v = b.op('s', lambda e, dst=dst, ps=ps, kw=kw: e.activation(
                dst, ps[:, 0:TW], act, **kw))
            b.wait('y', 'es', v)
            dv = b.dma('y', spill[oc*128:(oc+1)*128, tok0:tok0+TW], dst)
            state["dma"].append(dv)
            return ('es', v)
        return consume

    def emit_l3_ffn(self):
        b, d = self.b, self.dram
        ldb = b.dma_group('y', [(self.cols[:, 0:32], d["L3_ffn_b1"], {}),
                                (self.cols2[:, 0:8], d["L3_ffn_b2"], {})])
        b.wait_t('s', ldb)
        b.wait_t('v', ldb)
        self.emit_ln()
        self._ternarize("L3_ffn_W1t", D, 4096, wdst_off=0)
        b.barrier()
        w1 = self.abf(0, NKC * 4096)
        cons1 = self.mk_spill(d["spill_a"], AF.Gelu,
                              bias_at=lambda oc: self.cols[:, oc:oc+1])
        self.gemm_rw(lambda kc, oc: w1[:, kc*4096 + oc*128: kc*4096 + oc*128 + 128],
                     D, 4096, cons1, in_tile=self.nrm_tile, name="l3f1")
        b.barrier()
        self._ternarize("L3_ffn_W2t", 4096, D, wdst_off=0)
        b.barrier()
        w2 = self.abf(0, 32 * 1024)
        consd = self.mk_delta(bias_at=lambda oc: self.cols2[:, oc:oc+1])
        self.gemm_rw(lambda kc, oc: w2[:, kc*1024 + oc*128: kc*1024 + oc*128 + 128],
                     4096, D, consd, in_stream=d["spill_a"], ibuf='nrm', name="l3f2")
        b.barrier()

    def _ternarize(self, wname, Kdim, Ndim, wdst_off):
        b, d = self.b, self.dram
        b.barrier()
        nkc = Kdim // 128
        W = d[wname]
        xt, scr, scr2 = self.xtile, self.scr, self.scr2
        for kc in range(nkc):
            lv = b.dma('y', xt[:, 0:Ndim], W[kc*128:(kc+1)*128, 0:Ndim])
            b.wait_t('v', lv)
            vv = b.op('v', lambda e, kc=kc: e.reduce_sum(
                scr2[:, kc:kc+1], xt[:, 0:Ndim], axis=AX.X, apply_absolute_value=True))
            b.wait('y', 'ev', vv)
        b.selfwait('v')
        vv = b.op('v', lambda e, nkc=nkc: e.reduce_sum(scr2[:, 40:41], scr2[:, 0:nkc],
                                                       axis=AX.X))
        b.wait('t', 'ev', vv)
        mv = b.op('t', lambda e: e.matmul(self.ps[6][0:1, 0:1], self.ones_c[:, 0:1],
                                          scr2[:, 40:41], start=True, stop=True))
        b.wait('v', 'et', mv)
        b.op('v', lambda e: e.tensor_scalar(self.rows2[0:1, 0:1], self.ps[6][0:1, 0:1],
                                            1.0 / (Kdim * Ndim), 1e-8, ALU.mult, ALU.max))
        b.selfwait('v')
        b.op('v', lambda e: e.reciprocal(self.rows2[0:1, 1:2], self.rows2[0:1, 0:1]))
        b.selfwait('v')
        vv = b.op('v', lambda e: e.tensor_scalar_mul(self.rows2[0:1, 1:2],
                                                     self.rows2[0:1, 1:2], 2.0))
        b.wait('t', 'ev', vv)
        mv = b.op('t', lambda e: e.matmul(self.ps[6][:, 4:5], self.ones_f[0:1, :],
                                          self.rows2[0:1, 1:2], start=True, stop=True))
        b.wait('v', 'et', mv)
        rsc = b.op('v', lambda e: e.tensor_copy(self.cols2[:, 32:33], self.ps[6][:, 4:5]))
        wdst = self.abf(wdst_off, nkc * Ndim)
        nblk = (Ndim + TW - 1) // TW
        for kc in range(nkc):
            lv = b.dma('y', xt[:, 0:Ndim], W[kc*128:(kc+1)*128, 0:Ndim])
            b.wait_t('s', lv)
            b.wait_t('v', lv)
            if kc == 0:
                b.wait('v', 'ev', rsc)
            last_s = last_v = None
            for nb in range(nblk):
                bw = min(TW, Ndim - nb * TW)
                sg = scr[:, (nb % 2)*TW: (nb % 2)*TW + bw]
                sq = scr2[:, (nb % 2)*TW: (nb % 2)*TW + bw]
                xs = xt[:, nb*TW: nb*TW + bw]
                wd = wdst[:, kc*Ndim + nb*TW: kc*Ndim + nb*TW + bw]
                if nb >= 2:
                    b.wait('s', 'ev', last_v)
                sv = b.op('s', lambda e, sg=sg, xs=xs: e.activation(sg, xs, AF.Sign))
                b.op('v', lambda e, sq=sq, xs=xs: e.tensor_scalar_mul(
                    sq, xs, self.cols2[:, 32:33]))
                b.selfwait('v')
                b.op('v', lambda e, sq=sq: e.tensor_mul(sq, sq, sq))
                b.selfwait('v')
                b.op('v', lambda e, sq=sq: e.tensor_scalar(sq, sq, 1.0, None, ALU.is_ge))
                b.selfwait('v')
                b.wait('v', 'es', sv)
                last_v = b.op('v', lambda e, wd=wd, sq=sq, sg=sg: e.tensor_mul(wd, sq, sg))
                last_s = sv
            b.wait('y', 'ev', last_v)

    def emit_l0_ffn(self):
        b, d = self.b, self.dram
        b2 = b.dma_group('y', [(self.cols[:, 0:NKC], d["L0_ffn_lng"], {}),
                               (self.cols[:, 8:16], d["L0_ffn_lnb"], {}),
                               (self.cols[:, 16:48], d["L0_ffn_fc1_bias"], {}),
                               (self.cols2[:, 0:8], d["L0_ffn_fc2_bias"], {})])
        b.wait_t('s', b2)
        b.wait_t('v', b2)
        self.emit_ln(gamma_col=lambda kc: self.cols[:, kc:kc+1],
                     beta_col=lambda kc: self.cols[:, 8+kc:9+kc])
        self._dequant("L0_ffn_fc1_q", "L0_ffn_fc1_scales", D, 4096, wdst_off=0)
        b.barrier()
        w1 = self.abf(0, NKC * 4096)
        cons1 = self.mk_spill(d["spill_a"], AF.Silu,
                              bias_at=lambda oc: self.cols[:, 16+oc:17+oc])
        self.gemm_rw(lambda kc, oc: w1[:, kc*4096 + oc*128: kc*4096 + oc*128 + 128],
                     D, 4096, cons1, in_tile=self.nrm_tile, name="l0f1")
        b.barrier()
        self._dequant("L0_ffn_fc2_q", "L0_ffn_fc2_scales", 4096, D, wdst_off=0)
        b.barrier()
        w2 = self.abf(0, 32 * 1024)
        b.op('v', lambda e: e.tensor_scalar_mul(self.cols2[:, 8:16], self.cols2[:, 0:8], 0.5))
        b.selfwait('v')
        consd = self.mk_delta(bias_at=lambda oc: self.cols2[:, 8+oc:9+oc], scale=0.5)
        self.gemm_rw(lambda kc, oc: w2[:, kc*1024 + oc*128: kc*1024 + oc*128 + 128],
                     4096, D, consd, in_stream=d["spill_a"], ibuf='nrm', name="l0f2")
        b.barrier()

    def _dequant(self, qname, sname, Kdim, Ndim, wdst_off):
        b, d = self.b, self.dram
        b.barrier()
        nkc = Kdim // 128
        Q, S = d[qname], d[sname]
        xt = self.xtile
        wdst = self.abf(wdst_off, nkc * Ndim)
        nblk = Ndim // TW
        sttv = []
        for kc in range(nkc):
            if kc > 0:
                b.wait('g', 'ev', sttv[-1])
            lv = b.dma('g', xt[:, 0:Ndim], Q[kc*128:(kc+1)*128, 0:Ndim])
            sv = b.dma('y', self.rows2[0:1, 0:Ndim], S[kc:kc+1, 0:Ndim])
            b.wait_t('t', sv)
            for nb in range(nblk):
                gi = kc * nblk + nb
                if gi >= 2:
                    b.wait('t', 'ev', sttv[gi - 2])
                mv = b.op('t', lambda e, nb=nb: e.matmul(
                    self.ps[4 + nb % 2][:, 0:TW], self.ones_f[0:1, :],
                    self.rows2[0:1, nb*TW:(nb+1)*TW], start=True, stop=True))
                b.wait('v', 'et', mv)
                if nb == 0:
                    b.wait_t('v', lv)
                sttv.append(b.op('v', lambda e, kc=kc, nb=nb: e.scalar_tensor_tensor(
                    wdst[:, kc*Ndim + nb*TW: kc*Ndim + (nb+1)*TW],
                    xt[:, nb*TW:(nb+1)*TW], -128.0, self.ps[4 + nb % 2][:, 0:TW],
                    ALU.add, ALU.mult)))
            b.wait('y', 'et', b.cnt['et'])


    def emit_l0_attn(self):
        b, d = self.b, self.dram
        self.emit_ln()
        ldd = b.dma_group('y', [(self.rows[0:1, 0:8], d["L0_attn_decay"], {}),
                                (self.cols[:, 60:61], d["iota_col"], {}),
                                (self.scr[:, 0:128], d["exp_tab"], {}),
                                (self.masks[:, 2048:2176], d["rot_tab"], {})])
        b.wait_t('s', ldd)
        b.op('s', lambda e: e.activation(self.rows[0:1, 8:16], self.rows[0:1, 0:8], AF.Ln))
        b.selfwait('s')
        sv = b.op('s', lambda e: e.activation(self.rows[0:1, 16:24], self.rows[0:1, 8:16],
                                              AF.Exp, scale=128.0))
        b.wait('t', 'es', sv)
        for h in range(8):
            b.op('t', lambda e, h=h: e.matmul(self.ps[6][:, h:h+1], self.ones_f[0:1, :],
                                              self.rows[0:1, 8+h:9+h], start=True, stop=True))
        for h in range(8):
            mv = b.op('t', lambda e, h=h: e.matmul(self.ps[6][:, 8+h:9+h], self.ones_f[0:1, :],
                                                   self.rows[0:1, 16+h:17+h], start=True, stop=True))
        b.wait('s', 'et', mv)
        for h in range(8):
            b.op('s', lambda e, h=h: e.activation(self.cols2[:, h:h+1],
                                                  self.ps[6][:, h:h+1], AF.Copy))
            b.op('s', lambda e, h=h: e.activation(self.cols2[:, 32+h:33+h],
                                                  self.ps[6][:, 8+h:9+h], AF.Copy))
        b.selfwait('s')
        for h in range(8):
            b.op('s', lambda e, h=h: e.activation(self.cols2[:, 8+h:9+h],
                                                  self.cols2[:, h:h+1], AF.Copy, scale=-1.0))
            b.op('s', lambda e, h=h: e.activation(self.cols2[:, 16+h:17+h],
                                                  self.cols2[:, h:h+1], AF.Copy, scale=128.0))
        b.selfwait('s')
        for h in range(8):
            b.op('s', lambda e, h=h: e.activation(self.masks[:, h*128:(h+1)*128],
                                                  self.scr[:, 0:128], AF.Exp,
                                                  scale=self.cols2[:, h:h+1]))
        for h in range(8):
            sv = b.op('s', lambda e, h=h: e.activation(self.cols2[:, 24+h:25+h],
                                                       self.cols[:, 60:61], AF.Exp,
                                                       scale=self.cols2[:, 8+h:9+h],
                                                       bias=self.cols2[:, 16+h:17+h]))
        b.wait('t', 'es', b.cnt['es'])
        for h in range(8):
            mv = b.op('t', lambda e, h=h: e.matmul(self.ps[6][:, 128:256], self.ones_b[0:1, :],
                                                   self.masks[0:1, h*128:(h+1)*128],
                                                   start=True, stop=True))
            b.wait('v', 'et', mv)
            vv = b.op('v', lambda e, h=h: e.tensor_copy(
                self.masks[:, 1024 + h*128: 1024 + (h+1)*128], self.ps[6][:, 128:256]))
            b.wait('t', 'ev', vv)
        b.barrier()
        wq = self.abf(0, 2048)
        wk = self.abf(4*1024, 2048)
        wv = self.abf(8*1024, 4096)
        wg = self.abf(16*1024, 4096)
        qs = self.abf(24*1024, 1024)
        ks = self.abf(26*1024, 1024)
        rawb = self.abf(28*1024, 512)
        vs = self.abf(29*1024, 2048)
        gs = self.abf(33*1024, 2048)
        tabs = self.af32(37*1024, 4096)
        sbf = self.abf(53*1024, 256)
        ATb = self.abf(53*1024 + 512, 128)
        ktb = self.abf(53*1024 + 768, 128)
        kdb = self.abf(54*1024, 128)
        qgb = self.abf(54*1024 + 256, 128)
        nrb = self.abf(54*1024 + 512, 256)
        pinb = self.abf(55*1024, 2048)
        st = self.af32(60*1024, 512)
        Wqkv, Wg, Wp = d["L0_attn_W_qkv"], d["L0_attn_W_g"], d["L0_attn_W_proj"]
        TABN = ("xpos_cq", "xpos_sq", "xpos_ck", "xpos_sk")
        prev_stok = None
        for pr in range(getattr(self, 'l0_pair_limit', None) or 4):
            b.wait('g', 'et', b.cnt['et'])
            wl1 = b.dma_group('g', [(wq[:, kc*256:(kc+1)*256],
                                     Wqkv[kc*128:(kc+1)*128, pr*256: (pr+1)*256], {})
                                    for kc in range(NKC)] +
                                   [(wk[:, kc*256:(kc+1)*256],
                                     Wqkv[kc*128:(kc+1)*128, 1024 + pr*256: 1024 + (pr+1)*256], {})
                                    for kc in range(NKC)])
            wl2 = b.dma_group('g', [(wv[:, kc*512:(kc+1)*512],
                                     Wqkv[kc*128:(kc+1)*128, 2048 + pr*512: 2048 + (pr+1)*512], {})
                                    for kc in range(NKC)])
            wl3 = b.dma_group('g', [(wg[:, kc*512:(kc+1)*512],
                                     Wg[kc*128:(kc+1)*128, pr*512:(pr+1)*512], {})
                                    for kc in range(NKC)])
            b.selfwait('v')
            b.wait('v', 'et', b.cnt['et'])
            b.op('v', lambda e, st=st: e.memset(st[:], 0.0))
            for tt in range(getattr(self, 'l0_tt_limit', None) or NTT):
                t0 = tt * TW
                b.wait('y', 'ev', b.cnt['ev'])
                tl = b.dma_group('y', [
                    (tabs[:, (ti*2+hc)*512:(ti*2+hc+1)*512],
                     d[TABN[ti]][pr*256 + hc*128: pr*256 + (hc+1)*128, t0:t0+TW], {})
                    for ti in range(4) for hc in range(2)])
                if tt == 0:
                    b.wait_t('t', wl1)
                for which in range(2):
                    wsb = wq if which == 0 else wk
                    dstq = qs if which == 0 else ks
                    ci = 0 if which == 0 else 2
                    for hc in range(2):
                        b.wait('t', 'ev', b.cnt['ev'])
                        b.wait('t', 'es', b.cnt['es'])
                        for kc in range(NKC):
                            b.op('t', lambda e, wsb=wsb, kc=kc, hc=hc, tt=tt: e.matmul(
                                self.ps[0][:, 0:TW],
                                wsb[:, kc*256 + hc*128: kc*256 + (hc+1)*128],
                                self.nrm_tile(kc, tt), start=(kc == 0), stop=(kc == NKC-1)))
                        mv = b.cnt['et']
                        b.wait('s', 'et', mv)
                        sv = b.op('s', lambda e: e.activation(rawb[:], self.ps[0][:, 0:TW],
                                                              AF.Copy))
                        b.wait('t', 'es', sv)
                        mr = b.op('t', lambda e: e.matmul(self.ps[1][:, 0:TW],
                                                          self.masks[:, 2048:2176], rawb[:],
                                                          start=True, stop=True))
                        b.wait('v', 'et', mr)
                        b.wait_t('v', tl)
                        b.op('v', lambda e, ci=ci, hc=hc: e.tensor_mul(
                            self.scr[:, 0:TW], rawb[:], tabs[:, (ci*2+hc)*512:(ci*2+hc+1)*512]))
                        b.op('v', lambda e, ci=ci, hc=hc: e.tensor_mul(
                            self.scr[:, TW:2*TW], self.ps[1][:, 0:TW],
                            tabs[:, ((ci+1)*2+hc)*512:((ci+1)*2+hc+1)*512]))
                        b.selfwait('v')
                        b.op('v', lambda e, dstq=dstq, hc=hc: e.tensor_add(
                            dstq[:, hc*TW:(hc+1)*TW], self.scr[:, 0:TW], self.scr[:, TW:2*TW]))
                if tt == 0:
                    b.wait_t('t', wl2)
                    b.wait_t('t', wl3)
                for which in range(2):
                    wsb = wv if which == 0 else wg
                    dstv = vs if which == 0 else gs
                    for tc in range(4):
                        psv = self.ps[2 + tc % 2]
                        b.wait('t', 'ev', b.cnt['ev'])
                        b.wait('t', 'es', b.cnt['es'])
                        for kc in range(NKC):
                            b.op('t', lambda e, psv=psv, wsb=wsb, kc=kc, tc=tc, t0=t0: e.matmul(
                                psv[:, 0:TW],
                                self.nrm[:, kc*T + t0 + tc*128: kc*T + t0 + (tc+1)*128],
                                wsb[:, kc*512:(kc+1)*512], start=(kc == 0), stop=(kc == NKC-1)))
                        mv = b.cnt['et']
                        if which == 0:
                            b.wait('v', 'et', mv)
                            b.op('v', lambda e, psv=psv, dstv=dstv, tc=tc: e.tensor_copy(
                                dstv[:, tc*512:(tc+1)*512], psv[:, 0:TW]))
                        else:
                            b.wait('s', 'et', mv)
                            b.op('s', lambda e, psv=psv, dstv=dstv, tc=tc: e.activation(
                                dstv[:, tc*512:(tc+1)*512], psv[:, 0:TW], AF.Silu))
                b.wait('v', 'es', b.cnt['es'])
                if prev_stok is not None:
                    b.wait_t('v', prev_stok)
                for hc in range(2):
                    h = pr * 2 + hc
                    for c in range(4):
                        b.selfwait('v')
                        b.wait('v', 'et', b.cnt['et'])
                        scp = b.op('v', lambda e, hc=hc, st=st: e.tensor_copy(
                            sbf[:], st[:, hc*256:(hc+1)*256]))
                        b.wait('t', 'ev', scp)
                        mv = b.op('t', lambda e, hc=hc, c=c: e.matmul(
                            self.ps[0][:, 0:128], ks[:, hc*TW + c*128: hc*TW + (c+1)*128],
                            qs[:, hc*TW + c*128: hc*TW + (c+1)*128], start=True, stop=True))
                        b.wait('v', 'et', mv)
                        b.op('v', lambda e, h=h: e.tensor_mul(
                            ATb[:], self.ps[0][:, 0:128], self.masks[:, h*128:(h+1)*128]))
                        qv = b.op('v', lambda e, h=h, hc=hc, c=c: e.tensor_mul(
                            qgb[:], qs[:, hc*TW + c*128: hc*TW + (c+1)*128],
                            self.masks[:, 1024 + h*128: 1024 + (h+1)*128]))
                        b.wait('t', 'ev', qv)
                        b.wait('t', 'es', b.cnt['es'])
                        b.op('t', lambda e, c=c, hc=hc: e.matmul(
                            self.ps[1][:, 0:256], ATb[:],
                            vs[:, c*512 + hc*256: c*512 + hc*256 + 256],
                            start=True, stop=False))
                        mo = b.op('t', lambda e: e.matmul(
                            self.ps[1][:, 0:256], qgb[:], sbf[:], start=False, stop=True))
                        tv = b.op('t', lambda e, hc=hc, c=c: e.transpose(
                            self.psb[:, 0:128], ks[:, hc*TW + c*128: hc*TW + (c+1)*128],
                            self.ident[:]))
                        b.wait('v', 'et', tv)
                        b.op('v', lambda e: e.tensor_copy(ktb[:], self.psb[:, 0:128]))
                        b.selfwait('v')
                        k2 = b.op('v', lambda e, h=h: e.tensor_scalar_mul(
                            kdb[:], ktb[:], self.cols2[:, 24+h:25+h]))
                        b.wait('t', 'ev', k2)
                        mi = b.op('t', lambda e, c=c, hc=hc: e.matmul(
                            self.ps[2][:, 0:256], kdb[:],
                            vs[:, c*512 + hc*256: c*512 + hc*256 + 256],
                            start=True, stop=True))
                        b.wait('v', 'et', mi)
                        b.op('v', lambda e, h=h, hc=hc, st=st: e.scalar_tensor_tensor(
                            st[:, hc*256:(hc+1)*256], st[:, hc*256:(hc+1)*256],
                            self.cols2[:, 32+h:33+h], self.ps[2][:, 0:256],
                            ALU.mult, ALU.add))
                        b.wait('s', 'et', mo)
                        sq = b.op('s', lambda e: e.activation(self.scr2[:, 0:256],
                                                              self.ps[1][:, 0:256], AF.Square))
                        b.wait('v', 'et', mo)
                        b.op('v', lambda e: e.reduce_sum(self.cols[:, 61:62],
                                                         self.ps[1][:, 0:256], axis=AX.X))
                        b.wait('v', 'es', sq)
                        b.op('v', lambda e: e.reduce_sum(self.cols[:, 62:63],
                                                         self.scr2[:, 0:256], axis=AX.X))
                        b.selfwait('v')
                        b.op('v', lambda e: e.tensor_scalar_mul(self.cols[:, 61:62],
                                                                self.cols[:, 61:62], 1.0/256))
                        b.op('v', lambda e: e.tensor_scalar_mul(self.cols[:, 62:63],
                                                                self.cols[:, 62:63], 1.0/256))
                        b.selfwait('v')
                        b.op('v', lambda e: e.tensor_mul(self.cols[:, 63:64],
                                                         self.cols[:, 61:62], self.cols[:, 61:62]))
                        b.selfwait('v')
                        b.op('v', lambda e: e.tensor_scalar(self.cols[:, 63:64],
                                                            self.cols[:, 63:64], -1.0, 1e-5,
                                                            ALU.mult, ALU.add))
                        b.selfwait('v')
                        vv = b.op('v', lambda e: e.tensor_add(self.cols[:, 62:63],
                                                              self.cols[:, 62:63],
                                                              self.cols[:, 63:64]))
                        b.wait('s', 'ev', vv)
                        sv = b.op('s', lambda e: e.activation(self.cols[:, 63:64],
                                                              self.cols[:, 62:63], AF.Sqrt))
                        b.wait('v', 'es', sv)
                        b.op('v', lambda e: e.reciprocal(self.cols[:, 62:63],
                                                         self.cols[:, 63:64]))
                        b.selfwait('v')
                        b.op('v', lambda e: e.tensor_mul(self.cols[:, 63:64],
                                                         self.cols[:, 61:62], self.cols[:, 62:63]))
                        b.selfwait('v')
                        b.op('v', lambda e: e.tensor_scalar_mul(self.cols[:, 63:64],
                                                                self.cols[:, 63:64], -1.0))
                        b.selfwait('v')
                        b.op('v', lambda e: e.tensor_scalar(
                            nrb[:], self.ps[1][:, 0:256], self.cols[:, 62:63],
                            self.cols[:, 63:64], ALU.mult, ALU.add))
                        b.selfwait('v')
                        pv2 = b.op('v', lambda e, c=c, hc=hc: e.tensor_mul(
                            nrb[:], nrb[:], gs[:, c*512 + hc*256: c*512 + hc*256 + 256]))
                        b.wait('t', 'ev', pv2)
                        for dh in range(2):
                            b.op('t', lambda e, dh=dh: e.transpose(
                                self.psb[:, 128 + dh*128: 256 + dh*128],
                                nrb[:, dh*128:(dh+1)*128], self.ident[:]))
                        mt = b.cnt['et']
                        b.wait('v', 'et', mt)
                        for dh in range(2):
                            vcl = hc * 2 + dh
                            b.op('v', lambda e, vcl=vcl, c=c, dh=dh: e.tensor_copy(
                                pinb[:, vcl*512 + c*128: vcl*512 + (c+1)*128],
                                self.psb[:, 128 + dh*128: 256 + dh*128]))
                spv = b.cnt['ev']
                b.wait('y', 'ev', spv)
                prev_stok = b.dma_group('y', [
                    (d["spill_a"][(pr*4+v)*128:(pr*4+v+1)*128, t0:t0+TW],
                     pinb[:, v*512:(v+1)*512], {}) for v in range(4)])
        b.barrier()
        if getattr(self, 'l0_no_proj', False):
            return
        wp = self.abf(0, 16*1024)
        for half in range(2):
            wlp = b.dma_group('g', [(wp[:, (half*8+k2)*1024:(half*8+k2+1)*1024],
                                     Wp[(half*8+k2)*128:(half*8+k2+1)*128, :], {})
                                    for k2 in range(8)])
            b.wait_t('t', wlp)
        consd = self.mk_delta()
        self.gemm_rw(lambda kc, oc: wp[:, kc*1024 + oc*128: kc*1024 + (oc+1)*128],
                     2048, D, consd, in_stream=d["spill_a"], ibuf='arena',
                     ibuf_off=32*1024, name="l0proj")
        b.barrier()

    def emit_l1_attn(self):
        b, d = self.b, self.dram
        ldc = b.dma_group('y', [(self.cols[:, 0:8], d["L1_attn_bo"], {}),
                                (self.cols[:, 8:16], d["L1_attn_b1"], {}),
                                (self.cols[:, 16:24], d["L1_attn_b2"], {}),
                                (self.cols[:, 24:48], d["L1_attn_bg"], {})])
        ldm = b.dma('g', self.masks[:, 0:2560], d["l1_masks"])
        self.emit_ln()
        b.wait_t('s', ldc)
        b.wait_t('v', ldc)
        cons1 = self.mk_spill(d["spill_a"][0:1024, :], AF.Silu,
                              bias_at=lambda oc: self.cols[:, 8+oc:9+oc])
        self.gemm_sw(d["L1_attn_W1t"], D, D, cons1, gw=512, wbuf_off=0,
                     in_tile=self.nrm_tile, name="l1aA")
        b.barrier()
        st2 = {"n": 0}
        def cons2(oc, tt, ps, tok0):
            n = st2["n"]; st2["n"] += 1
            b.wait('v', 'et', b.cnt['et'])
            v = b.op('v', lambda e, oc=oc, tt=tt, ps=ps: e.tensor_scalar(
                self.nrm[:, oc*T + tt*TW: oc*T + (tt+1)*TW], ps[:, 0:TW],
                self.cols[:, 16+oc:17+oc], None, ALU.add))
            return ('ev', v)
        self.gemm_sw(d["L1_attn_W2t"], D, D, cons2, gw=512, wbuf_off=0,
                     in_stream=d["spill_a"][0:1024, :], ibuf='arena',
                     ibuf_off=16 * 1024, name="l1aB")
        b.barrier()
        qp = self.abf(0, 2 * T)
        kp = self.abf(16 * 1024, 2 * T)
        vfm = self.abf(32 * 1024, T)
        vp = self.abf(40 * 1024, 32 * 130)
        aop = self.abf(49 * 1024, T)
        wqkv = self.abf(57 * 1024, 3 * 1024)
        wo = self.abf(63 * 1024, 1024)
        pbuf = self.abf(65 * 1024, 2 * TW)
        b.op('v', lambda e, vp=vp: e.memset(
            vp[:].rearrange("p (a c) -> p a c", c=65)[:, :, 64:65], 1.0))
        beta = float(64.0 ** -0.5)
        consd = self.mk_delta()
        prev_flash_et = 0
        for p in range(getattr(self, 'l1_pair_limit', None) or 8):
            b.wait('g', 'et', b.cnt['et'])
            wl = b.dma_group('g', [
                (wqkv[:, pi*1024 + kc*128: pi*1024 + (kc+1)*128],
                 d["L1_attn_Wgt"][kc*128:(kc+1)*128,
                                  pi*1024 + p*128: pi*1024 + (p+1)*128], {})
                for pi in range(3) for kc in range(NKC)])
            wol = b.dma_group('g', [
                (wo[:, 0:1024], d["L1_attn_Wot"][p*128:(p+1)*128, :], {})])
            b.wait_t('t', wl)
            pv = {}
            for pi in range(3):
                dstt = (qp, kp, vfm)[pi]
                bgc = 24 + pi * 8 + p
                for tt in range(NTT):
                    psq = self.ps[(pi * NTT + tt) % 2]
                    if pi * NTT + tt >= 2:
                        b.wait('t', 'ev', pv[(pi * NTT + tt - 2)])
                    for kc in range(NKC):
                        b.op('t', lambda e, psq=psq, pi=pi, kc=kc, tt=tt: e.matmul(
                            psq[:, 0:TW],
                            wqkv[:, pi*1024 + kc*128: pi*1024 + (kc+1)*128],
                            self.nrm_tile(kc, tt), start=(kc == 0), stop=(kc == NKC - 1)))
                    mv = b.cnt['et']
                    b.wait('v', 'et', mv)
                    sc = beta if pi == 0 else 1.0
                    if pi == 2:
                        pv[pi * NTT + tt] = b.op('v', lambda e, psq=psq, dstt=dstt, tt=tt, bgc=bgc: e.tensor_scalar(
                            dstt[:, tt*TW:(tt+1)*TW], psq[:, 0:TW],
                            self.cols[:, bgc:bgc+1], None, ALU.add))
                    else:
                        b.op('v', lambda e, psq=psq, dstt=dstt, tt=tt, bgc=bgc, sc=sc: e.tensor_scalar(
                            dstt[0:64, tt*TW:(tt+1)*TW], psq[0:64, 0:TW],
                            self.cols[0:64, bgc:bgc+1], sc, ALU.add, ALU.mult))
                        pv[pi * NTT + tt] = b.op('v', lambda e, psq=psq, dstt=dstt, tt=tt, bgc=bgc, sc=sc: e.tensor_scalar(
                            dstt[0:64, T + tt*TW: T + (tt+1)*TW], psq[64:128, 0:TW],
                            self.cols[64:128, bgc:bgc+1], sc, ALU.add, ALU.mult))
            cpv = []
            b.wait('t', 'ev', pv[3 * NTT - 1])
            for jc in range(32):
                if cpv:
                    b.wait('t', 'ev', cpv[-1])
                b.op('t', lambda e, jc=jc, vfm=vfm: e.transpose(
                    self.psb[:, 0:128], vfm[:, jc*128:(jc+1)*128], self.ident[:]))
                mv = b.cnt['et']
                b.wait('v', 'et', mv)
                cpv.append(b.op('v', lambda e, jc=jc, vp=vp: e.tensor_copy(
                    vp[:, jc*130: jc*130 + 64], self.psb[:, 0:64])))
                cpv.append(b.op('v', lambda e, jc=jc, vp=vp: e.tensor_copy(
                    vp[:, jc*130 + 65: jc*130 + 129], self.psb[:, 64:128])))
            b.wait('t', 'ev', b.cnt['ev'])
            b.wait('s', 'ev', b.cnt['ev'])
            negb = {}
            for h2 in range(2):
                for side, src in (("q", qp), ("k", kp)):
                    for tt in range(NTT):
                        sv = b.op('s', lambda e, src=src, tt=tt, h2=h2: e.activation(
                            self.scr[0:64, 0:TW], src[0:64, h2*T + tt*TW: h2*T + (tt+1)*TW],
                            AF.Square))
                        b.wait('t', 'es', sv)
                        mv = b.op('t', lambda e: e.matmul(
                            self.ps[4][0:1, 0:TW], self.ones_c[0:64, 0:1],
                            self.scr[0:64, 0:TW], start=True, stop=True))
                        b.wait('v', 'et', mv)
                        vv = b.op('v', lambda e, tt=tt: e.tensor_copy(
                            self.rows2[0:1, tt*TW:(tt+1)*TW], self.ps[4][0:1, 0:TW]))
                        b.wait('s', 'ev', vv)
                    b.selfwait('v')
                    vv = b.op('v', lambda e: e.reduce_max(
                        self.rows[0:1, 0:1], self.rows2[0:1, 0:T], axis=AX.X))
                    b.selfwait('v')
                    idx = 1 if side == "q" else 2
                    vv = b.op('v', lambda e, idx=idx: e.tensor_copy(
                        self.rows[0:1, idx:idx+1], self.rows[0:1, 0:1]))
                    b.selfwait('v')
                vv = b.op('v', lambda e: e.tensor_mul(
                    self.rows[0:1, 3:4], self.rows[0:1, 1:2], self.rows[0:1, 2:3]))
                b.wait('s', 'ev', vv)
                sv = b.op('s', lambda e: e.activation(
                    self.rows[0:1, 4:5], self.rows[0:1, 3:4], AF.Sqrt))
                b.wait('v', 'es', sv)
                vv = b.op('v', lambda e: e.tensor_scalar_mul(
                    self.rows[0:1, 5:6], self.rows[0:1, 4:5], -1.0))
                b.wait('t', 'ev', vv)
                mv = b.op('t', lambda e: e.matmul(
                    self.ps[4][:, 8:9], self.ones_f[0:1, :], self.rows[0:1, 5:6],
                    start=True, stop=True))
                b.wait('s', 'et', mv)
                negb[h2] = b.op('s', lambda e, h2=h2: e.activation(
                    self.cols2[:, 44 + h2:45 + h2], self.ps[4][:, 8:9], AF.Copy))
                b.selfwait('s')
            b.wait('v', 'es', negb[1])
            for h2 in range(2):
                for qt in range(NTT):
                    njc = min(4 * qt + 5, 32)
                    pvv = {}
                    for jc in range(njc):
                        psS = self.ps[jc % 2]
                        if jc >= 2:
                            b.wait('t', 'es', pvv[jc - 2])
                        b.op('t', lambda e, psS=psS, h2=h2, jc=jc, qt=qt, qp=qp, kp=kp: e.matmul(
                            psS[:, 0:TW], kp[0:64, h2*T + jc*128: h2*T + (jc+1)*128],
                            qp[0:64, h2*T + qt*TW: h2*T + (qt+1)*TW], start=True, stop=True))
                        mv = b.cnt['et']
                        b.wait('s', 'et', mv)
                        if jc >= 2:
                            b.wait('s', 'et', b.cnt['et'])
                        sv = b.op('s', lambda e, psS=psS, jc=jc, h2=h2, pbuf=pbuf: e.activation(
                            pbuf[:, (jc % 2)*TW:(jc % 2)*TW + TW], psS[:, 0:TW], AF.Exp,
                            bias=self.cols2[:, 44 + h2:45 + h2]))
                        pvv[jc] = sv
                        rel = jc - 4 * qt
                        if rel >= 0:
                            b.wait('v', 'es', sv)
                            mvv = b.op('v', lambda e, jc=jc, rel=rel, pbuf=pbuf: e.tensor_mul(
                                pbuf[:, (jc % 2)*TW:(jc % 2)*TW + TW],
                                pbuf[:, (jc % 2)*TW:(jc % 2)*TW + TW],
                                self.masks[:, rel*TW:(rel+1)*TW]))
                            b.wait('t', 'ev', mvv)
                        else:
                            b.wait('t', 'es', sv)
                        b.op('t', lambda e, jc=jc, h2=h2, vp=vp, pbuf=pbuf, njc=njc: e.matmul(
                            self.ps[2][0:65, 0:TW], vp[:, jc*130 + 65*h2: jc*130 + 65*h2 + 65],
                            pbuf[:, (jc % 2)*TW:(jc % 2)*TW + TW],
                            start=(jc == 0), stop=(jc == njc - 1)))
                    mv = b.cnt['et']
                    b.wait('v', 'et', mv)
                    vv = b.op('v', lambda e: e.reciprocal(self.rows[0:1, 8:520],
                                                          self.ps[2][64:65, 0:TW]))
                    b.wait('t', 'ev', vv)
                    mv = b.op('t', lambda e: e.matmul(
                        self.ps[3][0:64, 0:TW], self.ones_f[0:1, 0:64],
                        self.rows[0:1, 8:520], start=True, stop=True))
                    b.wait('s', 'et', mv)
                    sv = b.op('s', lambda e: e.activation(self.scr[0:64, 0:TW],
                                                          self.ps[2][0:64, 0:TW], AF.Copy))
                    b.wait('v', 'et', mv)
                    b.wait('v', 'es', sv)
                    ov = b.op('v', lambda e, h2=h2, qt=qt, aop=aop: e.tensor_mul(
                        aop[64*h2:64*h2 + 64, qt*TW:(qt+1)*TW], self.scr[0:64, 0:TW],
                        self.ps[3][0:64, 0:TW]))
                    b.wait('t', 'ev', ov)
                    b.wait('t', 'es', b.cnt['es'])
            prev_flash_et = b.cnt['et']
            b.wait_t('t', wol)
            pgroups = {}
            gi = 0
            for oc in range(8):
                for tt in range(NTT):
                    psx = self.ps[4 + gi % 2]
                    if gi >= 2:
                        sem, val = pgroups[gi - 2]
                        b.wait('t', sem, val)
                    b.op('t', lambda e, psx=psx, oc=oc, tt=tt, aop=aop, wo=wo: e.matmul(
                        psx[:, 0:TW], wo[:, oc*128:(oc+1)*128],
                        aop[:, tt*TW:(tt+1)*TW], start=True, stop=True))
                    pgroups[gi] = consd(oc, tt, psx, tt * TW)
                    gi += 1
            b.selfwait('v')
        b.barrier()
        bcol = self.cols[:, 0:8]
        for oc in range(8):
            vv = b.op('v', lambda e, oc=oc, bcol=bcol: e.tensor_scalar(
                self.evac[:, (oc % 2)*TW:(oc % 2)*TW + TW], self.ab[:, 0:TW],
                0.0, bcol[:, oc:oc+1], ALU.mult, ALU.add))
            b.wait('g', 'ev', vv)
            dv = None
            for tt in range(NTT):
                dv = b.dma('g', d["x_work"][oc*128:(oc+1)*128, tt*TW:(tt+1)*TW],
                           self.evac[:, (oc % 2)*TW:(oc % 2)*TW + TW], accum_op=ALU.add)
            b.wait_t('v', dv)
        b.barrier()

    def emit_l2_attn(self):
        b, d = self.b, self.dram
        G = 256
        ldc = b.dma_group('y', [
            (self.cols[:, 0:16], d["L2_attn_bg"], {}),
            (self.cols[:, 16:17], d["L2_attn_binc"], {}),
            (self.cols[:, 17:25], d["L2_attn_bout"], {}),
            (self.cols2[:, 0:8], d["L2_attn_affs"], {}),
            (self.cols[:, 25:41], d["L2_attn_bv"], {}),
        ])
        ldm = b.dma('g', self.masks[:, 0:512], d["l2_masks"])
        ldw = b.dma_group('g', [(self.masks[:, 512 + kc*128: 512 + (kc+1)*128],
                                 d["L2_attn_Win"][kc*128:(kc+1)*128, :], {})
                                for kc in range(NKC)])
        b.wait_t('v', ldc)
        b.op('v', lambda e: e.tensor_scalar_mul(self.cols2[:, 8:10], self.cols2[:, 0:2],
                                                float(2048.0 ** -0.5)))
        st = self.af32(54 * 1024, 2048)
        b.op('v', lambda e: e.memset(st[:], 0.0))
        b.selfwait('v')
        self.emit_ln()
        consV = self.mk_spill(d["spill_a"][0:2048, :], AF.Silu,
                              bias_at=lambda oc: self.cols[:, 25+oc:26+oc])
        self.gemm_sw(d["L2_attn_Wv"], D, 2048, consV, gw=512, wbuf_off=0,
                     in_tile=self.nrm_tile, name="l2av")
        b.barrier()
        consG = self.mk_spill(d["spill_b"][0:2048, :], AF.Silu,
                              bias_at=lambda oc: self.cols[:, oc:oc+1])
        self.gemm_sw(d["L2_attn_Wg"], D, 2048, consG, gw=512, wbuf_off=0,
                     in_tile=self.nrm_tile, name="l2ag")
        b.barrier()
        AR_VF, AR_GT = 0, 16 * 1024
        AR_VT, AR_XQ = 32 * 1024, 40 * 1024
        AR_EB = 48 * 1024
        vf = self.abf(AR_VF, 16 * TW)
        gt = self.abf(AR_GT, 16 * TW)
        vt = self.nrm[:, 0:4096] if getattr(self, 'l2_vt_normal', False) else self.abf(AR_VT, 4096)
        x_ = self.abf(AR_XQ, TW)
        qq = self.abf(AR_XQ + 1024, TW)
        qk = self.abf(AR_XQ + 2048, TW)
        lq = self.abf(AR_XQ + 3072, TW)
        lk = self.abf(AR_XQ + 4096, TW)
        Pb = self.abf(AR_XQ + 5120, TW)
        P2 = self.abf(AR_XQ + 6144, TW)
        sbf = self.abf(AR_XQ + 7168, 256)
        lkT = self.abf(AR_EB + 3 * 1024, 256)
        oe = self.abf(AR_EB, 6 * 256)
        Win = self.masks.ap() if False else None
        win_sb = self.masks[:, 512:1536]
        spill_o = d["spill_a"][2048:4096, :]
        spill_v = d["spill_a"][0:2048, :]
        spill_g = d["spill_b"][0:2048, :]
        b.wait_t('t', ldw)
        ost = {"n": 0, "dma": [], "ev": []}
        self._l2_last_rl = None
        ntt_run = getattr(self, 'l2_tt_limit', None) or NTT
        for tt in range(ntt_run):
            t0 = tt * TW
            if tt > 0:
                b.wait('y', 'et', b.cnt['et'])
                b.wait('y', 'ev', b.cnt['ev'])
            lvf = b.dma_group('y', [(vf[:, ec*TW:(ec+1)*TW],
                                     spill_v[ec*128:(ec+1)*128, t0:t0+TW], {})
                                    for ec in range(16)])
            lgt = b.dma_group('y', [(gt[:, ec*TW:(ec+1)*TW],
                                     spill_g[ec*128:(ec+1)*128, t0:t0+TW], {})
                                    for ec in range(16)])
            for kc in range(NKC):
                b.op('t', lambda e, kc=kc, tt=tt: e.matmul(
                    self.ps[4][:, 0:TW], win_sb[:, kc*128:(kc+1)*128],
                    self.nrm_tile(kc, tt), start=(kc == 0), stop=(kc == NKC - 1)))
            mv = b.cnt['et']
            b.wait('s', 'et', mv)
            sv = b.op('s', lambda e: e.activation(x_[:], self.ps[4][:, 0:TW], AF.Silu,
                                                  bias=self.cols[:, 16:17]))
            b.wait('v', 'es', sv)
            b.op('v', lambda e: e.tensor_scalar(qq[:], x_[:], self.cols2[:, 8:9],
                                                self.cols2[:, 9:10], ALU.mult, ALU.add))
            b.op('v', lambda e: e.tensor_scalar(qk[:], x_[:], self.cols2[:, 2:3],
                                                self.cols2[:, 3:4], ALU.mult, ALU.add))
            b.op('v', lambda e: e.tensor_scalar(lq[:], x_[:], self.cols2[:, 4:5],
                                                self.cols2[:, 5:6], ALU.mult, ALU.add))
            aff = b.op('v', lambda e: e.tensor_scalar(lk[:], x_[:], self.cols2[:, 6:7],
                                                      self.cols2[:, 7:8], ALU.mult, ALU.add))
            b.wait_t('t', lvf)
            for g in range(getattr(self, 'l2_g_limit', None) or 2):
                n0 = g * G
                cpv = []
                for tc in range(2):
                    for ec8 in range(2):
                        if cpv:
                            b.wait('t', 'ev', cpv[-1])
                        for e8 in range(8):
                            ec = ec8 * 8 + e8
                            b.op('t', lambda e, ec=ec, tc=tc, g=g, e8=e8: e.transpose(
                                self.psb[:, e8*128:(e8+1)*128],
                                vf[:, ec*TW + g*G + tc*128: ec*TW + g*G + tc*128 + 128],
                                self.ident[:]))
                        mv = b.cnt['et']
                        b.wait('v', 'et', mv)
                        cpv.append(b.op('v', lambda e, tc=tc, ec8=ec8: e.tensor_copy(
                            vt[:, tc*2048 + ec8*1024: tc*2048 + (ec8+1)*1024],
                            self.psb[:, 0:1024])))
                b.wait('t', 'ev', aff)
                if self._l2_last_rl is not None:
                    b.wait('t', 'es', self._l2_last_rl)
                sep = getattr(self, 'l2_sep_st', False)
                for tc in range(2):
                    bank = self.ps[5 + tc] if sep else self.ps[5]
                    col = 0 if sep else tc * G
                    b.op('t', lambda e, tc=tc, g=g, bank=bank, col=col, n0=n0: e.matmul(
                        bank[:, col:col+G], qk[:, n0+tc*128: n0+tc*128+128],
                        qq[:, n0:n0+G], start=True, stop=True))
                mv = b.cnt['et']
                b.wait('s', 'et', mv)
                rl = []
                for tc in range(2):
                    bank = self.ps[5 + tc] if sep else self.ps[5]
                    col = 0 if sep else tc * G
                    rl.append(b.op('s', lambda e, tc=tc, bank=bank, col=col: e.activation(
                        Pb[:, tc*G:(tc+1)*G], bank[:, col:col+G], AF.Relu)))
                self._l2_last_rl = rl[-1]
                b.wait('v', 'es', rl[-1])
                for tc in range(2):
                    b.op('v', lambda e, tc=tc: e.tensor_mul(
                        P2[:, tc*G:(tc+1)*G], Pb[:, tc*G:(tc+1)*G], Pb[:, tc*G:(tc+1)*G]))
                b.selfwait('v')
                for tc in range(2):
                    pm = b.op('v', lambda e, tc=tc: e.tensor_mul(
                        Pb[:, tc*G:(tc+1)*G], P2[:, tc*G:(tc+1)*G],
                        self.masks[:, tc*G:(tc+1)*G]))
                if getattr(self, 'l2_pb_normal', False):
                    b.selfwait('v')
                    pm = b.op('v', lambda e: e.tensor_copy(self.masks[:, 1536:2048], Pb[:, 0:512]))
                b.wait('t', 'ev', pm)
                b.wait_t('t', lgt)
                esmm = []
                for es in range(getattr(self, 'l2_es_limit', None) or 16):
                    nb_rot = getattr(self, 'l2_nbank', 4)
                    psx = self.ps[es % nb_rot]
                    if es >= 2:
                        b.wait('v', 'et', esmm[es - 2])
                    scp = b.op('v', lambda e, es=es: e.tensor_copy(
                        sbf[:, (es % 2)*128:(es % 2)*128 + 128],
                        st[:, es*128:(es+1)*128]))
                    if len(ost["esv"]) >= 4 if "esv" in ost else False:
                        pass
                    if len(ost.get("esv", [])) >= 4:
                        b.wait('t', 'es', ost["esv"][-4])
                    b.wait('t', 'ev', scp)
                    novl = getattr(self, 'l2_no_vl', False)
                    pbsrc = self.masks[:, 1536:2048] if getattr(self, 'l2_pb_normal', False) else Pb
                    for tc in range(2):
                        b.op('t', lambda e, es=es, tc=tc, novl=novl, pbsrc=pbsrc, psx=psx: e.matmul(
                            psx[:, 0:G], vt[:, tc*2048 + es*128:(tc*2048) + (es+1)*128],
                            pbsrc[:, tc*G:(tc+1)*G], start=(tc == 0),
                            stop=(novl and tc == 1)))
                    if getattr(self, 'l2_no_vl', False):
                        pass
                    else:
                        b.op('t', lambda e, es=es, g=g, psx=psx, n0=n0: e.matmul(
                            psx[:, 0:G], sbf[:, (es % 2)*128:(es % 2)*128 + 128],
                            lq[:, n0:n0+G], start=False, stop=True))
                    if getattr(self, 'l2_drain', False):
                        mv = b.op('t', lambda e: e.drain())
                    else:
                        mv = b.cnt['et']
                    esmm.append(mv)
                    n = ost["n"]; ost["n"] += 1
                    if n >= 6:
                        b.wait_t('v', ost["dma"][n - 6])
                    b.wait('s', 'et', mv)
                    if len(ost["ev"]) >= 2:
                        b.wait('s', 'ev', ost["ev"][-2])
                    sv = b.op('s', lambda e, n=n, psx=psx: e.activation(
                        self.scr[:, (n % 2)*TW:(n % 2)*TW + G], psx[:, 0:G], AF.Copy))
                    b.wait('v', 'es', sv)
                    ov = b.op('v', lambda e, es=es, n=n, n0=n0: e.tensor_mul(
                        oe[:, (n % 6)*G:(n % 6)*G + G],
                        self.scr[:, (n % 2)*TW:(n % 2)*TW + G],
                        gt[:, es*TW + n0: es*TW + n0 + G]))
                    ost["ev"].append(max(ov, sv) if False else ov)
                    ost.setdefault("esv", []).append(sv)
                    b.wait('y', 'ev', ov)
                    ost["dma"].append(b.dma('y', spill_o[es*128:(es+1)*128, t0+n0:t0+n0+G],
                                            oe[:, (n % 6)*G:(n % 6)*G + G]))
                tv = []
                for tc in range(2):
                    b.op('t', lambda e, tc=tc, g=g, n0=n0: e.transpose(
                        self.psb[:, tc*128:(tc+1)*128],
                        lk[:, n0+tc*128: n0+tc*128+128], self.ident[:]))
                mv = b.cnt['et']
                b.wait('v', 'et', mv)
                lkc = b.op('v', lambda e: e.tensor_copy(lkT[:, 0:256], self.psb[:, 0:256]))
                b.wait('t', 'ev', lkc)
                upd = []
                for et4 in range(4):
                    for tc in range(2):
                        b.op('t', lambda e, et4=et4, tc=tc: e.matmul(
                            self.ps[4][:, 0:TW], lkT[:, tc*128:(tc+1)*128],
                            vt[:, tc*2048 + et4*TW: tc*2048 + (et4+1)*TW],
                            start=(tc == 0), stop=(tc == 1)))
                    mv = b.cnt['et']
                    b.wait('v', 'et', mv)
                    uv = b.op('v', lambda e, et4=et4: e.tensor_add(
                        st[:, et4*TW:(et4+1)*TW], st[:, et4*TW:(et4+1)*TW],
                        self.ps[4][:, 0:TW]))
                    upd.append(uv)
                    if et4 < 3:
                        b.wait('t', 'ev', uv)
                b.wait('t', 'ev', upd[-1])
                b.selfwait('v')
        b.barrier()
        if getattr(self, 'l2_no_pass4', False):
            return
        consd = self.mk_delta(bias_at=lambda oc: self.cols[:, 17+oc:18+oc])
        self.gemm_sw(d["L2_attn_Wout"], 2048, D, consd, gw=256, wbuf_off=0,
                     in_stream=spill_o, ibuf='nrm', name="l2ao")
        b.barrier()

    def emit_l3_attn(self):
        b, d = self.b, self.dram
        ldc = b.dma_group('y', [(self.cols[:, 0:8], d["L3_attn_bo"], {})])
        ldm = b.dma('g', self.masks[:, 0:2048], d["l3_masks"])
        self.emit_ln()
        qp = self.abf(0, 2 * T)
        kp = self.abf(16 * 1024, 2 * T)
        vfm = self.abf(32 * 1024, T)
        vp = self.abf(40 * 1024, 32 * 130)
        aop = self.abf(49 * 1024, T)
        wqkv = self.abf(57 * 1024, 3 * 1024)
        wo = self.abf(63 * 1024, 1024)
        pbuf = self.abf(65 * 1024, 2 * TW)
        b.op('v', lambda e, vp=vp: e.memset(
            vp[:].rearrange("p (a c) -> p a c", c=65)[:, :, 64:65], 1.0))
        beta = float(64.0 ** -0.5)
        consd = self.mk_delta()
        WN = {"q": "L3_attn_Wq", "k": "L3_attn_Wk", "v": "L3_attn_Wv"}
        prev_flash_et = 0
        for p in range(getattr(self, 'l3_pair_limit', None) or 8):
            buf = 0
            b.wait('g', 'et', b.cnt['et'])
            wl = b.dma_group('g', [
                (wqkv[:, pi*1024 + kc*128: pi*1024 + (kc+1)*128],
                 d[WN[pn]][kc*128:(kc+1)*128, p*128:(p+1)*128], {})
                for pi, pn in enumerate(("q", "k", "v")) for kc in range(NKC)])
            wol = b.dma_group('g', [
                (wo[:, 0:1024], d["L3_attn_Wo"][p*128:(p+1)*128, :], {})])
            b.wait_t('t', wl)
            pv = {}
            for pi, pn in enumerate(("q", "k", "v")):
                dstt = (qp, kp, vfm)[pi]
                for tt in range(NTT):
                    psq = self.ps[(pi * NTT + tt) % 2]
                    if pi * NTT + tt >= 2:
                        b.wait('t', 'es', pv[(pi * NTT + tt - 2)])
                    for kc in range(NKC):
                        b.op('t', lambda e, psq=psq, pi=pi, kc=kc, tt=tt: e.matmul(
                            psq[:, 0:TW],
                            wqkv[:, pi*1024 + kc*128: pi*1024 + (kc+1)*128],
                            self.nrm_tile(kc, tt), start=(kc == 0), stop=(kc == NKC - 1)))
                    mv = b.cnt['et']
                    b.wait('s', 'et', mv)
                    sc = {}
                    if pn == "q":
                        sc['scale'] = beta
                    if pn == "v":
                        pv[pi * NTT + tt] = b.op('s', lambda e, psq=psq, dstt=dstt, tt=tt, sc=sc: e.activation(
                            dstt[:, tt*TW:(tt+1)*TW], psq[:, 0:TW], AF.Copy))
                    else:
                        b.op('s', lambda e, psq=psq, dstt=dstt, tt=tt, sc=sc: e.activation(
                            dstt[0:64, tt*TW:(tt+1)*TW], psq[0:64, 0:TW], AF.Copy, **sc))
                        pv[pi * NTT + tt] = b.op('s', lambda e, psq=psq, dstt=dstt, tt=tt, sc=sc: e.activation(
                            dstt[0:64, T + tt*TW: T + (tt+1)*TW], psq[64:128, 0:TW], AF.Copy, **sc))
            if getattr(self, 'l3_stage', 9) < 2:
                continue
            cpv = []
            b.wait('t', 'es', pv[3 * NTT - 1])
            for jc in range(32):
                if cpv:
                    b.wait('t', 'ev', cpv[-1])
                b.op('t', lambda e, jc=jc, vfm=vfm: e.transpose(
                    self.psb[:, 0:128], vfm[:, jc*128:(jc+1)*128], self.ident[:]))
                mv = b.cnt['et']
                b.wait('v', 'et', mv)
                cpv.append(b.op('v', lambda e, jc=jc, vp=vp: e.tensor_copy(
                    vp[:, jc*130: jc*130 + 64], self.psb[:, 0:64])))
                cpv.append(b.op('v', lambda e, jc=jc, vp=vp: e.tensor_copy(
                    vp[:, jc*130 + 65: jc*130 + 129], self.psb[:, 64:128])))
            if getattr(self, 'l3_stage', 9) < 3:
                continue
            b.wait('t', 'ev', b.cnt['ev'])
            b.wait('s', 'ev', b.cnt['ev'])
            negb = {}
            for h2 in range(2):
                for side, src in (("q", qp), ("k", kp)):
                    for tt in range(NTT):
                        sv = b.op('s', lambda e, src=src, tt=tt, h2=h2: e.activation(
                            self.scr[0:64, 0:TW], src[0:64, h2*T + tt*TW: h2*T + (tt+1)*TW],
                            AF.Square))
                        b.wait('t', 'es', sv)
                        mv = b.op('t', lambda e: e.matmul(
                            self.ps[4][0:1, 0:TW], self.ones_c[0:64, 0:1],
                            self.scr[0:64, 0:TW], start=True, stop=True))
                        b.wait('v', 'et', mv)
                        vv = b.op('v', lambda e, tt=tt: e.tensor_copy(
                            self.rows2[0:1, tt*TW:(tt+1)*TW], self.ps[4][0:1, 0:TW]))
                        b.wait('s', 'ev', vv)
                    b.selfwait('v')
                    vv = b.op('v', lambda e: e.reduce_max(
                        self.rows[0:1, 0:1], self.rows2[0:1, 0:T], axis=AX.X))
                    b.selfwait('v')
                    idx = 1 if side == "q" else 2
                    vv = b.op('v', lambda e, idx=idx: e.tensor_copy(
                        self.rows[0:1, idx:idx+1], self.rows[0:1, 0:1]))
                    b.selfwait('v')
                vv = b.op('v', lambda e: e.tensor_mul(
                    self.rows[0:1, 3:4], self.rows[0:1, 1:2], self.rows[0:1, 2:3]))
                b.wait('s', 'ev', vv)
                sv = b.op('s', lambda e: e.activation(
                    self.rows[0:1, 4:5], self.rows[0:1, 3:4], AF.Sqrt))
                b.wait('v', 'es', sv)
                vv = b.op('v', lambda e: e.tensor_scalar_mul(
                    self.rows[0:1, 5:6], self.rows[0:1, 4:5], -1.0))
                b.wait('t', 'ev', vv)
                mv = b.op('t', lambda e: e.matmul(
                    self.ps[4][:, 8:9], self.ones_f[0:1, :], self.rows[0:1, 5:6],
                    start=True, stop=True))
                b.wait('s', 'et', mv)
                negb[h2] = b.op('s', lambda e, h2=h2: e.activation(
                    self.cols2[:, 44 + h2:45 + h2], self.ps[4][:, 8:9], AF.Copy))
                b.selfwait('s')
            b.wait('v', 'es', negb[1])
            if getattr(self, 'l3_stage', 9) < 4:
                continue
            for h2 in range(2):
                for qt in range(NTT):
                    njc = 4 * qt + 4
                    pvv = {}
                    for jc in range(njc):
                        psS = self.ps[jc % 2]
                        if jc >= 2:
                            b.wait('t', 'es', pvv[jc - 2])
                        b.op('t', lambda e, psS=psS, h2=h2, jc=jc, qt=qt, qp=qp, kp=kp: e.matmul(
                            psS[:, 0:TW], kp[0:64, h2*T + jc*128: h2*T + (jc+1)*128],
                            qp[0:64, h2*T + qt*TW: h2*T + (qt+1)*TW], start=True, stop=True))
                        mv = b.cnt['et']
                        b.wait('s', 'et', mv)
                        if jc >= 2:
                            b.wait('s', 'et', b.cnt['et'])
                        sv = b.op('s', lambda e, psS=psS, jc=jc, h2=h2, pbuf=pbuf: e.activation(
                            pbuf[:, (jc % 2)*TW:(jc % 2)*TW + TW], psS[:, 0:TW], AF.Exp,
                            bias=self.cols2[:, 44 + h2:45 + h2]))
                        pvv[jc] = sv
                        rel = jc - 4 * qt
                        if rel >= 0:
                            b.wait('v', 'es', sv)
                            mvv = b.op('v', lambda e, jc=jc, rel=rel, pbuf=pbuf: e.tensor_mul(
                                pbuf[:, (jc % 2)*TW:(jc % 2)*TW + TW],
                                pbuf[:, (jc % 2)*TW:(jc % 2)*TW + TW],
                                self.masks[:, rel*TW:(rel+1)*TW]))
                            b.wait('t', 'ev', mvv)
                        else:
                            b.wait('t', 'es', sv)
                        b.op('t', lambda e, jc=jc, h2=h2, vp=vp, pbuf=pbuf, njc=njc: e.matmul(
                            self.ps[2][0:65, 0:TW], vp[:, jc*130 + 65*h2: jc*130 + 65*h2 + 65],
                            pbuf[:, (jc % 2)*TW:(jc % 2)*TW + TW],
                            start=(jc == 0), stop=(jc == njc - 1)))
                    mv = b.cnt['et']
                    b.wait('v', 'et', mv)
                    vv = b.op('v', lambda e: e.reciprocal(self.rows[0:1, 8:520],
                                                          self.ps[2][64:65, 0:TW]))
                    b.wait('t', 'ev', vv)
                    mv = b.op('t', lambda e: e.matmul(
                        self.ps[3][0:64, 0:TW], self.ones_f[0:1, 0:64],
                        self.rows[0:1, 8:520], start=True, stop=True))
                    b.wait('s', 'et', mv)
                    sv = b.op('s', lambda e: e.activation(self.scr[0:64, 0:TW],
                                                          self.ps[2][0:64, 0:TW], AF.Copy))
                    b.wait('v', 'et', mv)
                    b.wait('v', 'es', sv)
                    ov = b.op('v', lambda e, h2=h2, qt=qt, aop=aop: e.tensor_mul(
                        aop[64*h2:64*h2 + 64, qt*TW:(qt+1)*TW], self.scr[0:64, 0:TW],
                        self.ps[3][0:64, 0:TW]))
                    b.wait('t', 'ev', ov)
                    b.wait('t', 'es', b.cnt['es'])
            prev_flash_et = b.cnt['et']
            if getattr(self, 'l3_stage', 9) < 5:
                continue
            b.wait_t('t', wol)
            pgroups = {}
            gi = 0
            for oc in range(8):
                for tt in range(NTT):
                    psx = self.ps[4 + gi % 2]
                    if gi >= 2:
                        sem, val = pgroups[gi - 2]
                        b.wait('t', sem, val)
                    b.op('t', lambda e, psx=psx, buf=buf, oc=oc, tt=tt, aop=aop, wo=wo: e.matmul(
                        psx[:, 0:TW], wo[:, buf*1024 + oc*128: buf*1024 + (oc+1)*128],
                        aop[:, tt*TW:(tt+1)*TW], start=True, stop=True))
                    pgroups[gi] = consd(oc, tt, psx, tt * TW)
                    gi += 1
            b.selfwait('v')
        b.barrier()
        bcol = self.cols[:, 0:8]
        for oc in range(8):
            vv = b.op('v', lambda e, oc=oc, bcol=bcol: e.tensor_scalar(
                self.evac[:, (oc % 2)*TW:(oc % 2)*TW + TW], self.ab[:, 0:TW],
                0.0, bcol[:, oc:oc+1], ALU.mult, ALU.add))
            b.wait('g', 'ev', vv)
            dv = None
            for tt in range(NTT):
                dv = b.dma('g', d["x_work"][oc*128:(oc+1)*128, tt*TW:(tt+1)*TW],
                           self.evac[:, (oc % 2)*TW:(oc % 2)*TW + TW], accum_op=ALU.add)
            b.wait_t('v', dv)
        b.barrier()

    def emit_l2_ffn(self):
        b, d = self.b, self.dram
        self.emit_ln()
        lv = b.dma('y', self.rows2[0:1, 8:12], d["L2_ffn_sc"])
        b.wait_t('v', lv)
        b.op('v', lambda e: e.tensor_scalar_mul(self.rows2[0:1, 12:13], self.rows2[0:1, 8:9], 4.0))
        b.op('v', lambda e: e.tensor_scalar_mul(self.rows2[0:1, 13:14], self.rows2[0:1, 9:10], 2.0))
        vv = b.op('v', lambda e: e.tensor_copy(self.rows2[0:1, 14:15], self.rows2[0:1, 10:11]))
        b.wait('t', 'ev', vv)
        for i in range(3):
            mv = b.op('t', lambda e, i=i: e.matmul(self.ps[6][:, 8+i:9+i], self.ones_f[0:1, :],
                                                   self.rows2[0:1, 12+i:13+i], start=True, stop=True))
        b.wait('v', 'et', mv)
        cv = None
        for i in range(3):
            cv = b.op('v', lambda e, i=i: e.tensor_copy(self.cols2[:, 40+i:41+i],
                                                        self.ps[6][:, 8+i:9+i]))
        b.wait('s', 'ev', cv)
        b.wait('t', 'ev', cv)
        b.barrier()
        wdst = self.abf(0, NKC * 4096)
        K0 = d["L2_ffn_k0"]
        subv = []
        for kc in range(NKC):
            for nb in range(8):
                half = (nb % 2) * TW
                gi = kc * 8 + nb
                if gi >= 2:
                    b.wait('g', 'ev', subv[gi - 2])
                l2 = b.dma_group('g', [
                    (self.scr[:, half:half+TW],
                     K0[kc*128:(kc+1)*128, nb*TW:(nb+1)*TW], {}),
                    (self.scr2[:, half:half+TW],
                     K0[1024 + kc*128: 1024 + (kc+1)*128, nb*TW:(nb+1)*TW], {})])
                b.wait_t('v', l2)
                subv.append(b.op('v', lambda e, kc=kc, nb=nb, half=half: e.tensor_sub(
                    wdst[:, kc*4096 + nb*TW: kc*4096 + (nb+1)*TW],
                    self.scr[:, half:half+TW], self.scr2[:, half:half+TW])))
        b.barrier()
        cons0 = self.mk_spill(d["spill_a"], AF.Relu,
                              scale_at=lambda oc: self.cols2[:, 40:41])
        self.gemm_rw(lambda kc, oc: wdst[:, kc*4096 + oc*128: kc*4096 + oc*128 + 128],
                     D, 4096, cons0, in_tile=self.nrm_tile, name="l2f0")
        b.barrier()
        def sgn_xform(og, buf, lv, wbuf, nkc, gw, ow):
            b.wait_t('v', lv)
            last = None
            for kc in range(nkc):
                w_ap = wbuf[:, buf*nkc*gw + kc*gw: buf*nkc*gw + kc*gw + ow]
                last = b.op('v', lambda e, w_ap=w_ap: e.tensor_scalar(
                    w_ap, w_ap, 2.0, -1.0, ALU.mult, ALU.add))
            return ('ev', last)
        cons1 = self.mk_spill(d["spill_b"], AF.Relu,
                              scale_at=lambda oc: self.cols2[:, 41:42])
        self.gemm_sw(d["L2_ffn_k1"], 4096, 4096, cons1, gw=256, wbuf_off=0,
                     in_stream=d["spill_a"], ibuf='nrm', w_xform=sgn_xform, name="l2f1")
        b.barrier()
        consd = self.mk_delta(scale_at=lambda oc: self.cols2[:, 42:43])
        self.gemm_sw(d["L2_ffn_k2"], 4096, D, consd, gw=256, wbuf_off=0,
                     in_stream=d["spill_b"], ibuf='nrm', w_xform=sgn_xform, name="l2f2")
        b.barrier()

    def emit_l1_ffn(self):
        b, d = self.b, self.dram
        lv = b.dma_group('y', [(self.cols[:, 0:32], d["L1_ffn_w1_s"], {}),
                               (self.cols[:, 32:64], d["L1_ffn_w2_s"], {}),
                               (self.cols2[:, 0:32], d["L1_ffn_h1_s"], {}),
                               (self.cols2[:, 32:64], d["L1_ffn_h2_s"], {})])
        b.wait_t('s', lv)
        b.wait_t('v', lv)
        self.emit_ln()
        consA = self.mk_spill(d["spill_b"], AF.Silu,
                              scale_at=lambda oc: self.cols[:, oc:oc+1])
        self.gemm_sw(d["L1_ffn_w1t"], D, 4096, consA, gw=512, wbuf_off=0,
                     in_tile=self.nrm_tile, name="l1fA")
        b.barrier()
        tbuf = self.abf(32 * 1024, 4 * TW)
        ebuf = self.abf(40 * 1024, 6 * TW)
        stateB = {"n": 0, "dma": [], "tl": []}
        def consB(oc, tt, ps, tok0):
            n = stateB["n"]; stateB["n"] += 1
            tb = tbuf[:, (n % 4)*TW:(n % 4)*TW + TW]
            if n >= 4:
                b.wait('y', 'ev', stateB["tl"][n - 4])
            tl = b.dma('y', tb, d["spill_b"][oc*128:(oc+1)*128, tok0:tok0+TW])
            dst = ebuf[:, (n % 6)*TW:(n % 6)*TW + TW]
            if n >= 6:
                b.wait_t('v', stateB["dma"][n - 6])
            b.wait('v', 'et', b.cnt['et'])
            b.wait_t('v', tl)
            v = b.op('v', lambda e, dst=dst, ps=ps, oc=oc, tb=tb: e.scalar_tensor_tensor(
                dst, ps[:, 0:TW], self.cols[:, 32+oc:33+oc], tb, ALU.mult, ALU.mult))
            stateB["tl"].append(v)
            b.wait('y', 'ev', v)
            dv = b.dma('y', d["spill_a"][oc*128:(oc+1)*128, tok0:tok0+TW], dst)
            stateB["dma"].append(dv)
            return ('ev', v)
        self.gemm_sw(d["L1_ffn_w2t"], D, 4096, consB, gw=512, wbuf_off=0,
                     in_tile=self.nrm_tile, name="l1fB")
        b.barrier()
        consC = self.mk_spill(d["spill_b"], AF.Gelu,
                              scale_at=lambda oc: self.cols2[:, oc:oc+1])
        self.gemm_sw(d["L1_ffn_h1t"], 4096, 4096, consC, gw=256, wbuf_off=0,
                     in_stream=d["spill_a"], ibuf='nrm', name="l1fC")
        b.barrier()
        consD = self.mk_spill(d["spill_a"], AF.Gelu,
                              scale_at=lambda oc: self.cols2[:, 32+oc:33+oc])
        self.gemm_sw(d["L1_ffn_h2t"], 4096, 4096, consD, gw=256, wbuf_off=0,
                     in_stream=d["spill_b"], ibuf='nrm', name="l1fD")
        b.barrier()
        lv = b.dma('y', self.cols[:, 0:8], d["L1_ffn_w3_s"])
        b.wait_t('v', lv)
        consE = self.mk_delta(scale_at=lambda oc: self.cols[:, oc:oc+1])
        self.gemm_sw(d["L1_ffn_w3t"], 4096, D, consE, gw=256, wbuf_off=0,
                     in_stream=d["spill_a"], ibuf='nrm', name="l1fE")
        b.barrier()


def host_prep(d, core, layers, x_start):
    ins = {}
    ins["x_in"] = np.ascontiguousarray(x_start[core].T).astype(np.float32)
    ins["ones_tab"] = np.ones((1, 128), np.float32)
    ins["ident_tab"] = np.eye(128, dtype=np.float32).astype(BF)
    if "l0_attn" in layers:
        ins["L0_attn_W_qkv"] = d["L0_attn_W_qkv"].astype(np.float32)
        ins["L0_attn_W_g"] = d["L0_attn_W_g"].astype(np.float32)
        ins["L0_attn_W_proj"] = d["L0_attn_W_proj"].astype(np.float32)
        ins["L0_attn_decay"] = d["L0_attn_decay"].reshape(1, 8).astype(np.float32)
        QK = 1024
        pos = np.arange(T, dtype=np.float32)
        base_scale = ((np.arange(0, QK, 2, dtype=np.float32) + 0.4 * QK) / (1.4 * QK))
        scale = base_scale[None, :] ** (pos[:, None] / np.float32(512.0))
        inv_freq = (1.0 / 10000.0 ** (np.arange(QK // 2, dtype=np.float32) / (QK // 2))
                    ).astype(np.float32)
        sinusoid = pos[:, None] * inv_freq[None, :]
        sn, cs = np.sin(sinusoid), np.cos(sinusoid)
        beta = np.float32(128.0 ** -0.5)
        rep = lambda a: np.ascontiguousarray(np.repeat(a, 2, axis=1).T.astype(np.float32))
        ins["xpos_cq"] = rep(cs * scale) * beta
        ins["xpos_sq"] = rep(sn * scale) * beta
        ins["xpos_ck"] = rep(cs / scale)
        ins["xpos_sk"] = rep(sn / scale)
        R = np.zeros((128, 128), np.float32)
        ii = np.arange(0, 128, 2)
        R[ii + 1, ii] = -1.0
        R[ii, ii + 1] = 1.0
        ins["rot_tab"] = R.astype(BF)
        jj = np.arange(128, dtype=np.float32)
        E = jj[None, :] - jj[:, None]
        ins["exp_tab"] = np.where(E >= 0, E, 1e9).astype(np.float32)
        ins["iota_col"] = jj.reshape(128, 1).copy()
    if "l0_ffn" in layers:
        ins["L0_ffn_lng"] = np.ascontiguousarray(d["L0_ffn_ln_gamma"].reshape(NKC, 128).T)
        ins["L0_ffn_lnb"] = np.ascontiguousarray(d["L0_ffn_ln_beta"].reshape(NKC, 128).T)
        ins["L0_ffn_fc1_q"] = d["L0_ffn_fc1_q"]
        ins["L0_ffn_fc1_scales"] = d["L0_ffn_fc1_scales"]
        ins["L0_ffn_fc1_bias"] = np.ascontiguousarray(d["L0_ffn_fc1_bias"].reshape(32, 128).T)
        ins["L0_ffn_fc2_q"] = d["L0_ffn_fc2_q"]
        ins["L0_ffn_fc2_scales"] = d["L0_ffn_fc2_scales"]
        ins["L0_ffn_fc2_bias"] = np.ascontiguousarray(d["L0_ffn_fc2_bias"].reshape(8, 128).T)
    if "l1_attn" in layers:
        ins["L1_attn_W1t"] = np.ascontiguousarray(d["L1_attn_W1"].T)
        ins["L1_attn_b1"] = d["L1_attn_b1"].reshape(8, 128).T.copy()
        ins["L1_attn_W2t"] = np.ascontiguousarray(d["L1_attn_W2"].T)
        ins["L1_attn_b2"] = d["L1_attn_b2"].reshape(8, 128).T.copy()
        ins["L1_attn_Wgt"] = np.ascontiguousarray(d["L1_attn_Wg"].T)
        ins["L1_attn_bg"] = d["L1_attn_bg"].reshape(24, 128).T.copy()
        ins["L1_attn_Wot"] = np.ascontiguousarray(d["L1_attn_Wo"].T)
        ins["L1_attn_bo"] = d["L1_attn_bo"].reshape(8, 128).T.copy()
        TWl = 512
        mm = np.zeros((128, 5 * TWl), np.float32)
        for k in range(5):
            r = np.arange(128)[:, None] + 128 * k
            c = np.arange(TWl)[None, :]
            mm[:, k*TWl:(k+1)*TWl] = (r <= c + 63)
        ins["l1_masks"] = mm.astype(BF)
    if "l2_attn" in layers:
        ins["L2_attn_Wv"] = d["L2_attn_Wv"]
        ins["L2_attn_bv"] = np.ascontiguousarray(d["L2_attn_bv"].reshape(16, 128).T)
        ins["L2_attn_Wg"] = d["L2_attn_Wg"]
        ins["L2_attn_bg"] = np.ascontiguousarray(d["L2_attn_bg"].reshape(16, 128).T)
        ins["L2_attn_Win"] = d["L2_attn_Win"]
        ins["L2_attn_binc"] = d["L2_attn_bin_"].reshape(1, 128).T.copy()
        ins["L2_attn_Wout"] = d["L2_attn_Wout"]
        ins["L2_attn_bout"] = np.ascontiguousarray(d["L2_attn_bout"].reshape(8, 128).T)
        ins["L2_attn_affs"] = np.stack([d["L2_attn_" + nm] for nm in
                                        ("g_qq", "b_qq", "g_qk", "b_qk",
                                         "g_lq", "b_lq", "g_lk", "b_lk")], axis=1).astype(np.float32)
        G = 256
        m = np.zeros((128, 512), np.float32)
        n = np.arange(G)
        m[:, 0:256] = (np.arange(128)[:, None] <= n[None, :])
        m[:, 256:512] = ((128 + np.arange(128))[:, None] <= n[None, :])
        ins["l2_masks"] = m.astype(BF)
    if "l2_ffn" in layers:
        ins["L2_ffn_k0"] = d["L2_ffn_k0"].astype(np.uint8)
        ins["L2_ffn_k1"] = d["L2_ffn_k1"].astype(np.uint8)
        ins["L2_ffn_k2"] = d["L2_ffn_k2"].astype(np.uint8)
        ins["L2_ffn_sc"] = np.array([[d["L2_ffn_s0"], d["L2_ffn_s1"], d["L2_ffn_s2"], 0.0]],
                                    np.float32)
    if "l1_ffn" in layers:
        for nm, tr in (("w1", "L1_ffn_w1_w"), ("w2", "L1_ffn_w2_w"),
                       ("h1", "L1_ffn_h1_w"), ("h2", "L1_ffn_h2_w"),
                       ("w3", "L1_ffn_w3_w")):
            ins[f"L1_ffn_{nm}t"] = np.ascontiguousarray(d[tr].T)
        for nm in ("w1_s", "w2_s", "h1_s", "h2_s", "w3_s"):
            v = d[f"L1_ffn_{nm}"]
            ins[f"L1_ffn_{nm}"] = np.ascontiguousarray(v.reshape(-1, 128).T)
    if "l3_attn" in layers:
        for nm in ("Wq", "Wk", "Wv", "Wo"):
            ins[f"L3_attn_{nm}"] = d[f"L3_attn_{nm}"]
        ins["L3_attn_bo"] = d["L3_attn_bo"].reshape(8, 128).T.copy()
        TWl = 512
        mm = np.zeros((128, 4 * TWl), np.float32)
        for k in range(4):
            r = np.arange(128)[:, None] + 128 * k
            c = np.arange(TWl)[None, :]
            mm[:, k*TWl:(k+1)*TWl] = (r <= c)
        ins["l3_masks"] = mm.astype(BF)
    if "l3_ffn" in layers:
        ins["L3_ffn_W1t"] = np.ascontiguousarray(d["L3_ffn_W1"].T)
        ins["L3_ffn_b1"] = np.ascontiguousarray(d["L3_ffn_b1"].reshape(32, 128).T)
        ins["L3_ffn_W2t"] = np.ascontiguousarray(d["L3_ffn_W2"].T)
        ins["L3_ffn_b2"] = np.ascontiguousarray(d["L3_ffn_b2"].reshape(8, 128).T)
    return ins


def build(layers, dbg=False, dbg_hook=None):
    nc = bass.Bass("TRN2", target_bir_lowering=False)
    with ExitStack() as stack:
        mk = MK(nc, stack, layers, dbg=dbg)
        mk.emit_init()
        for lay in layers:
            getattr(mk, "emit_" + lay)()
            if dbg_hook is not None:
                dbg_hook(mk, lay)
        mk.emit_finish()
        mk.b.finalize_into_block()
    return nc, mk


def dump_nrm(mk, lay=None):
    b, d = mk.b, mk.dram
    b.barrier()
    tok = b.dma_group('y', [(d["dbg_nrm"][kc*128:(kc+1)*128, :],
                             mk.nrm[:, kc*T:(kc+1)*T], {}) for kc in range(NKC)])
    b.wait_t('y', tok)
    b.barrier()


def dump_arena(mk):
    b, d = mk.b, mk.dram
    b.barrier()
    tok = b.dma_group('y', [(d["dbg_w"], mk.ar_bf[:, 0:32768], {})])
    b.wait_t('y', tok)
    b.barrier()


LAYERS = ["l0_attn", "l0_ffn", "l1_attn", "l1_ffn",
          "l2_attn", "l2_ffn", "l3_attn", "l3_ffn"]
_CACHE = {}


def _get_program():
    if "nc" not in _CACHE:
        nc, mkk = build(LAYERS)
        _CACHE["nc"] = nc
    return _CACHE["nc"]


def kernel(**inputs):
    nc = _get_program()
    x = np.asarray(inputs["L0_attn_hidden_states"], np.float32)
    ins_list = [host_prep(inputs, core, LAYERS, x) for core in range(8)]
    res = run_bass_kernel_spmd(nc, ins_list, core_ids=list(range(8)))
    out = np.empty((8, T, D), np.float32)
    for core in range(8):
        out[core] = np.asarray(res.results[core]["x_out"], np.float32).T
    return out


# version 02
# speedup vs baseline: 1.2854x; 1.2854x over previous
import math
from contextlib import ExitStack

import numpy as np
import ml_dtypes

import concourse.bass as bass
import concourse.mybir as mybir
from concourse.bass_utils import run_bass_kernel_spmd

BF = ml_dtypes.bfloat16
F32 = mybir.dt.float32
BF16 = mybir.dt.bfloat16
I32 = mybir.dt.int32
U8 = mybir.dt.uint8
AF = mybir.ActivationFunctionType
ALU = mybir.AluOpType
AX = mybir.AxisListType

T = 4096
D = 1024
NKC = D // 128
TW = 512
NTT = T // TW

ENGS = ('t', 'v', 's', 'g', 'y')
SEM_OF = {'t': 'et', 'v': 'ev', 's': 'es'}
NLANE = 4


class Bld:

    def __init__(self, nc, stack):
        self.nc = nc
        self.q = {e: [] for e in ENGS}
        self.semh = {}
        self.cnt = {}
        for name in SEM_OF.values():
            self.semh[name] = stack.enter_context(nc.semaphore(name))
            self.cnt[name] = 0
        for qn in ('g', 'y'):
            for i in range(NLANE):
                nm = f"{qn}{i}"
                self.semh[nm] = stack.enter_context(nc.semaphore(nm))
                self.cnt[nm] = 0
        self.rot = {'g': 0, 'y': 0}
        self.lane_prev = {f"{qn}{i}": 0 for qn in ('g', 'y') for i in range(NLANE)}

    def op(self, eng, fn, n=1):
        assert eng in ('t', 'v', 's')
        sem = SEM_OF[eng]
        self.cnt[sem] += n
        h = self.semh[sem]
        def f(e, fn=fn, h=h, n=n):
            fn(e).then_inc(h, n)
        self.q[eng].append(f)
        return self.cnt[sem]

    def _lane(self, qn):
        i = self.rot[qn] % NLANE
        self.rot[qn] += 1
        return f"{qn}{i}"

    def dma(self, qn, dst, src, **kw):
        return self.dma_group(qn, [(dst, src, kw)])

    def dma_group(self, qn, items):
        assert qn in ('g', 'y')
        lane = self._lane(qn)
        if self.lane_prev[lane] > 0:
            self.wait(qn, lane, self.lane_prev[lane])
        h = self.semh[lane]
        for dst, src, kw in items:
            self.cnt[lane] += 16
            def f(e, dst=dst, src=src, kw=kw, h=h):
                e.dma_start(dst, src, **kw).then_inc(h, 16)
            self.q[qn].append(f)
        self.lane_prev[lane] = self.cnt[lane]
        return (lane, self.cnt[lane])

    def wait(self, eng, sem, v):
        assert v <= self.cnt[sem], f"wait {eng} {sem} {v} > {self.cnt[sem]}"
        if v <= 0:
            return
        h = self.semh[sem]
        self.q[eng].append(lambda e, h=h, v=v: e.wait_ge(h, v))

    def wait_t(self, eng, tok):
        if tok is not None:
            self.wait(eng, tok[0], tok[1])

    def selfwait(self, eng):
        sem = SEM_OF[eng]
        self.wait(eng, sem, self.cnt[sem])

    def barrier(self):
        snap = dict(self.cnt)
        own = {'t': 'et', 'v': 'ev', 's': 'es'}
        for eng in ENGS:
            for sem, v in snap.items():
                if own.get(eng) != sem:
                    self.wait(eng, sem, v)

    def finalize_into_block(self):
        nc = self.nc
        with nc.Block() as blk:
            @blk.tensor
            def _(e):
                for f in self.q['t']: f(e)
            @blk.vector
            def _(e):
                for f in self.q['v']: f(e)
            @blk.scalar
            def _(e):
                for f in self.q['s']: f(e)
            @blk.gpsimd
            def _(e):
                for f in self.q['g']: f(e)
            @blk.sync
            def _(e):
                for f in self.q['y']: f(e)


class MK:
    def __init__(self, nc, stack, layers, dbg=False):
        self.nc = nc
        self.b = Bld(nc, stack)
        self.layers = layers
        self.dbg = dbg
        self.dram = {}
        self._declare_dram()
        self._alloc_sbuf(stack)
        self.ps = [stack.enter_context(nc.psum_tensor(f"ps{i}", [128, 512], F32))
                   for i in range(7)]
        self.psb = stack.enter_context(nc.psum_tensor("psb", [128, 1024], BF16))

    def dt(self, name, shape, dtype, kind="ExternalInput"):
        t = self.nc.dram_tensor(name, list(shape), dtype, kind=kind)
        self.dram[name] = t.ap()
        return self.dram[name]

    def _declare_dram(self):
        dt = self.dt
        L = self.layers
        dt("x_in", [D, T], F32)
        dt("x_out", [D, T], F32, kind="ExternalOutput")
        dt("x_work", [D, T], F32, kind="Internal")
        spk = "ExternalOutput" if self.dbg else "Internal"
        dt("spill_a", [4096, T], BF16, kind=spk)
        dt("spill_b", [4096, T], BF16, kind=spk)
        if self.dbg:
            dt("dbg_nrm", [D, T], BF16, kind="ExternalOutput")
            dt("dbg_w", [128, 32768], BF16, kind="ExternalOutput")
        dt("ones_tab", [1, 128], F32)
        dt("ident_tab", [128, 128], BF16)
        if "l0_attn" in L:
            dt("L0_attn_W_qkv", [D, 4096], F32)
            dt("L0_attn_W_g", [D, 2048], F32)
            dt("L0_attn_W_proj", [2048, D], F32)
            dt("L0_attn_decay", [1, 8], F32)
            for nm in ("xpos_cq", "xpos_sq", "xpos_ck", "xpos_sk"):
                dt(nm, [D, T], F32)
            dt("rot_tab", [128, 128], BF16)
            dt("exp_tab", [128, 128], F32)
            dt("iota_col", [128, 1], F32)
        if "l0_ffn" in L:
            dt("L0_ffn_lng", [128, NKC], F32)
            dt("L0_ffn_lnb", [128, NKC], F32)
            dt("L0_ffn_fc1_q", [D, 4096], I32)
            dt("L0_ffn_fc1_scales", [8, 4096], F32)
            dt("L0_ffn_fc1_bias", [128, 32], F32)
            dt("L0_ffn_fc2_q", [4096, D], I32)
            dt("L0_ffn_fc2_scales", [32, D], F32)
            dt("L0_ffn_fc2_bias", [128, 8], F32)
        if "l1_attn" in L:
            dt("L1_attn_W1t", [D, D], F32)
            dt("L1_attn_b1", [128, 8], F32)
            dt("L1_attn_W2t", [D, D], F32)
            dt("L1_attn_b2", [128, 8], F32)
            dt("L1_attn_Wgt", [D, 3072], F32)
            dt("L1_attn_bg", [128, 24], F32)
            dt("L1_attn_Wot", [D, D], F32)
            dt("L1_attn_bo", [128, 8], F32)
            dt("l1_masks", [128, 5 * TW], BF16)
        if "l1_ffn" in L:
            dt("L1_ffn_w1t", [D, 4096], F32)
            dt("L1_ffn_w1_s", [128, 32], F32)
            dt("L1_ffn_w2t", [D, 4096], F32)
            dt("L1_ffn_w2_s", [128, 32], F32)
            dt("L1_ffn_h1t", [4096, 4096], F32)
            dt("L1_ffn_h1_s", [128, 32], F32)
            dt("L1_ffn_h2t", [4096, 4096], F32)
            dt("L1_ffn_h2_s", [128, 32], F32)
            dt("L1_ffn_w3t", [4096, D], F32)
            dt("L1_ffn_w3_s", [128, 8], F32)
        if "l2_attn" in L:
            dt("L2_attn_Wv", [D, 2048], F32)
            dt("L2_attn_bv", [128, 16], F32)
            dt("L2_attn_Wg", [D, 2048], F32)
            dt("L2_attn_bg", [128, 16], F32)
            dt("L2_attn_Win", [D, 128], F32)
            dt("L2_attn_binc", [128, 1], F32)
            dt("L2_attn_Wout", [2048, D], F32)
            dt("L2_attn_bout", [128, 8], F32)
            dt("L2_attn_affs", [128, 8], F32)
            dt("l2_masks", [128, 2 * 256], BF16)
        if "l2_ffn" in L:
            dt("L2_ffn_k0", [2048, 4096], U8)
            dt("L2_ffn_k1", [4096, 4096], U8)
            dt("L2_ffn_k2", [4096, D], U8)
            dt("L2_ffn_sc", [1, 4], F32)
        if "l3_attn" in L:
            dt("L3_attn_Wq", [D, D], F32)
            dt("L3_attn_Wk", [D, D], F32)
            dt("L3_attn_Wv", [D, D], F32)
            dt("L3_attn_Wo", [D, D], F32)
            dt("L3_attn_bo", [128, 8], F32)
            dt("l3_masks", [128, 4 * TW], BF16)
        if "l3_ffn" in L:
            dt("L3_ffn_W1t", [D, 4096], F32)
            dt("L3_ffn_b1", [128, 32], F32)
            dt("L3_ffn_W2t", [4096, D], F32)
            dt("L3_ffn_b2", [128, 8], F32)

    def _alloc_sbuf(self, stack):
        nc = self.nc
        def sb(name, shape, dtype):
            t = stack.enter_context(nc.sbuf_tensor(name, list(shape), dtype))
            setattr(self, name, t)
            return t
        sb("nrm", [128, NKC * T], BF16)
        sb("xtile", [128, NKC * TW], F32)
        sb("evac", [128, 6 * TW], F32)
        sb("rows", [1, 2048], F32)
        sb("rows2", [1, 4096], F32)
        sb("ab", [128, 2 * TW], F32)
        sb("cols", [128, 64], F32)
        sb("cols2", [128, 64], F32)
        sb("ones_f", [1, 128], F32)
        sb("ones_c", [128, 1], F32)
        sb("ones_b", [1, 128], BF16)
        sb("ident", [128, 128], BF16)
        sb("bcrf", [128, 128], F32)
        sb("masks", [128, 2688], BF16)
        sb("scr", [128, 2 * TW], F32)
        sb("scr2", [128, 2 * TW], F32)
        self.ARENA = 72 * 1024
        sb("arena8", [128, self.ARENA], U8)
        self.ar_bf = self.arena8.bitcast(BF16)
        self.ar_f32 = self.arena8.bitcast(F32)
        rem = nc.sbuf_bytes_remaining
        assert rem >= 0, f"SBUF over budget by {-rem}"
        self.sbuf_remaining = rem

    def abf(self, off_b, width):
        assert off_b % 2 == 0 and off_b + 2 * width <= self.ARENA, (off_b, width)
        return self.ar_bf[:, off_b // 2: off_b // 2 + width]

    def af32(self, off_b, width):
        assert off_b % 4 == 0 and off_b + 4 * width <= self.ARENA, (off_b, width)
        return self.ar_f32[:, off_b // 4: off_b // 4 + width]

    def emit_init(self):
        b, d = self.b, self.dram
        b.dma('y', self.ones_f[:], d["ones_tab"])
        b.dma('y', self.ident[:], d["ident_tab"])
        b.op('v', lambda e: e.memset(self.ones_b[:], 1.0))
        b.op('v', lambda e: e.memset(self.bcrf[:], 0.0))
        b.op('v', lambda e: e.memset(self.bcrf[0:1, :], 1.0))
        b.op('v', lambda e: e.memset(self.ones_c[:], 1.0))
        b.dma('y', d["x_work"], d["x_in"])
        b.barrier()

    def emit_finish(self):
        b, d = self.b, self.dram
        b.barrier()
        done = b.dma('y', d["x_out"], d["x_work"])
        b.wait_t('y', done)
        b.barrier()

    def emit_ln(self, gamma_col=None, beta_col=None, eps=1e-5):
        b, d = self.b, self.dram
        x_src = d["x_work"]
        nrm, xtile, R, ab, scr = self.nrm, self.xtile, self.rows, self.ab, self.scr
        psS, psQ, psB = self.ps[4], self.ps[5], self.ps[6]
        for tt in range(NTT):
            t0 = tt * TW
            ldv = b.dma_group('y', [(xtile[:, kc*TW:(kc+1)*TW],
                                     x_src[kc*128:(kc+1)*128, t0:t0+TW], {})
                                    for kc in range(NKC)])
            b.wait_t('t', ldv)
            for kc in range(NKC):
                b.op('t', lambda e, kc=kc: e.matmul(
                    psS[0:1, 0:TW], self.ones_c[:, 0:1], xtile[:, kc*TW:(kc+1)*TW],
                    start=(kc == 0), stop=(kc == NKC - 1)))
            et_s1 = b.cnt['et']
            b.wait_t('s', ldv)
            sqv = []
            mm2 = []
            for kc in range(NKC):
                half = (kc % 2) * TW
                if kc >= 2:
                    b.wait('s', 'et', mm2[kc - 2])
                sqv.append(b.op('s', lambda e, kc=kc, half=half: e.activation(
                    scr[:, half:half+TW], xtile[:, kc*TW:(kc+1)*TW], AF.Square)))
                b.wait('t', 'es', sqv[kc])
                mm2.append(b.op('t', lambda e, kc=kc, half=half: e.matmul(
                    psQ[0:1, 0:TW], self.ones_c[:, 0:1], scr[:, half:half+TW],
                    start=(kc == 0), stop=(kc == NKC - 1))))
            mmv = b.cnt['et']
            b.wait('v', 'et', mmv)
            b.op('v', lambda e: e.tensor_scalar_mul(R[0:1, 0:TW], psS[0:1, 0:TW], 1.0 / D))
            b.op('v', lambda e: e.tensor_scalar_mul(R[0:1, TW:2*TW], psQ[0:1, 0:TW], 1.0 / D))
            b.selfwait('v')
            b.op('v', lambda e: e.tensor_mul(R[0:1, 2*TW:3*TW], R[0:1, 0:TW], R[0:1, 0:TW]))
            b.selfwait('v')
            b.op('v', lambda e: e.tensor_scalar(R[0:1, 3*TW:4*TW], R[0:1, 2*TW:3*TW],
                                                -1.0, eps, ALU.mult, ALU.add))
            b.selfwait('v')
            vv = b.op('v', lambda e: e.tensor_add(R[0:1, 3*TW:4*TW], R[0:1, 3*TW:4*TW],
                                                  R[0:1, TW:2*TW]))
            b.wait('s', 'ev', vv)
            sv = b.op('s', lambda e: e.activation(R[0:1, 2*TW:3*TW], R[0:1, 3*TW:4*TW], AF.Sqrt))
            b.wait('v', 'es', sv)
            b.op('v', lambda e: e.reciprocal(R[0:1, 3*TW:4*TW], R[0:1, 2*TW:3*TW]))
            vv = b.op('v', lambda e: e.tensor_scalar_mul(R[0:1, 2*TW:3*TW],
                                                         R[0:1, 0:TW], -1.0))
            b.wait('t', 'ev', vv)
            b.op('t', lambda e: e.matmul(psB[:, 0:TW], self.ones_f[0:1, :],
                                         R[0:1, 3*TW:4*TW], start=True, stop=True))
            mmv = b.op('t', lambda e: e.matmul(psS[:, 0:TW], self.ones_f[0:1, :],
                                               R[0:1, 2*TW:3*TW], start=True, stop=True))
            b.wait('v', 'et', mmv)
            b.op('v', lambda e: e.tensor_copy(ab[:, 0:TW], psB[:, 0:TW]))
            b.op('v', lambda e: e.tensor_copy(ab[:, TW:2*TW], psS[:, 0:TW]))
            b.selfwait('v')
            for kc in range(NKC):
                half = (kc % 2) * TW
                dst = nrm[:, kc*T + t0: kc*T + t0 + TW]
                b.op('v', lambda e, kc=kc, half=half: e.tensor_add(
                    scr[:, half:half+TW], xtile[:, kc*TW:(kc+1)*TW], ab[:, TW:2*TW]))
                b.selfwait('v')
                b.op('v', lambda e, half=half, dst=dst: e.tensor_mul(
                    dst, scr[:, half:half+TW], ab[:, 0:TW]))
                if gamma_col is not None:
                    gk = gamma_col(kc); bk = beta_col(kc)
                    b.selfwait('v')
                    b.op('v', lambda e, dst=dst, gk=gk, bk=bk: e.tensor_scalar(
                        dst, dst, gk, bk, ALU.mult, ALU.add))
            b.wait('y', 'ev', b.cnt['ev'])
            b.wait('y', 'es', b.cnt['es'])
            b.wait('y', 'et', b.cnt['et'])
        b.barrier()

    def gemm_sw(self, W_dram, Kdim, Ndim, consume, *, in_tile=None, in_stream=None,
                gw=512, toks=T, wbuf_off=0, ibuf='arena', ibuf_off=None, name="",
                w_xform=None):
        b = self.b
        nkc = Kdim // 128
        nog = (Ndim + gw - 1) // gw
        ntt = toks // TW
        wbuf = self.abf(wbuf_off, 2 * nkc * gw)
        if in_stream is not None:
            if ibuf == 'nrm':
                assert 2 * nkc * TW <= NKC * T
                ibuft = self.nrm[:, 0: 2 * nkc * TW]
            else:
                ibuft = self.abf(ibuf_off, 2 * nkc * TW)
        PS = self.ps[0:4]
        group_after = {}
        tile_after = []
        cons = []
        ngroups = 0
        git = 0
        wl = {}
        for og in range(nog):
            buf = og % 2
            ow = min(gw, Ndim - og * gw)
            noc = (ow + 127) // 128
            if og >= 2:
                b.wait('g', 'et', group_after[og - 2])
            lv = b.dma_group('g', [(wbuf[:, buf*nkc*gw + kc*gw: buf*nkc*gw + kc*gw + ow],
                                    W_dram[kc*128:(kc+1)*128, og*gw:og*gw + ow], {})
                                   for kc in range(nkc)])
            if w_xform is not None:
                wl[og] = w_xform(og, buf, lv, wbuf, nkc, gw, ow)
            else:
                wl[og] = lv
            for tt in range(ntt):
                tok0 = tt * TW
                if in_stream is not None:
                    half = git % 2
                    if git >= 2:
                        b.wait('y', 'et', tile_after[git - 2])
                    ilv = b.dma_group('y', [
                        (ibuft[:, half*nkc*TW + kc*TW: half*nkc*TW + (kc+1)*TW],
                         in_stream[kc*128:(kc+1)*128, tok0:tok0+TW], {})
                        for kc in range(nkc)])
                for oc in range(noc):
                    ocg = og * (gw // 128) + oc
                    ps = PS[ngroups % 4]
                    if ngroups >= 4:
                        sem, val = cons[ngroups - 4]
                        b.wait('t', sem, val)
                    if oc == 0 and tt == 0:
                        b.wait_t('t', wl[og])
                    if in_stream is not None and oc == 0:
                        b.wait_t('t', ilv)
                    for kc in range(nkc):
                        if in_stream is not None:
                            rhs = ibuft[:, half*nkc*TW + kc*TW: half*nkc*TW + (kc+1)*TW]
                        else:
                            rhs = in_tile(kc, tt)
                        b.op('t', lambda e, ps=ps, buf=buf, kc=kc, oc=oc, ow=ow, rhs=rhs: e.matmul(
                            ps[:, 0:TW],
                            wbuf[:, buf*nkc*gw + kc*gw + oc*128: buf*nkc*gw + kc*gw + min(oc*128+128, ow)],
                            rhs, start=(kc == 0), stop=(kc == nkc - 1)))
                    group_after[og] = b.cnt['et']
                    cons.append(consume(ocg, tt, ps, tok0))
                    ngroups += 1
                tile_after.append(b.cnt['et'])
                git += 1
        return cons

    def gemm_rw(self, w_sb_fn, Kdim, Ndim, consume, *, in_tile=None, in_stream=None,
                toks=T, ibuf='nrm', ibuf_off=None, name=""):
        b = self.b
        nkc = Kdim // 128
        noc = (Ndim + 127) // 128
        ntt = toks // TW
        if in_stream is not None:
            if ibuf == 'nrm':
                assert 2 * nkc * TW <= NKC * T
                ibuft = self.nrm[:, 0: 2 * nkc * TW]
            else:
                ibuft = self.abf(ibuf_off, 2 * nkc * TW)
        PS = self.ps[0:4]
        cons = []
        tile_after = []
        ngroups = 0
        for tt in range(ntt):
            tok0 = tt * TW
            if in_stream is not None:
                half = tt % 2
                if tt >= 2:
                    b.wait('y', 'et', tile_after[tt - 2])
                ilv = b.dma_group('y', [
                    (ibuft[:, half*nkc*TW + kc*TW: half*nkc*TW + (kc+1)*TW],
                     in_stream[kc*128:(kc+1)*128, tok0:tok0+TW], {})
                    for kc in range(nkc)])
            for oc in range(noc):
                ps = PS[ngroups % 4]
                if ngroups >= 4:
                    sem, val = cons[ngroups - 4]
                    b.wait('t', sem, val)
                if in_stream is not None and oc == 0:
                    b.wait_t('t', ilv)
                for kc in range(nkc):
                    if in_stream is not None:
                        rhs = ibuft[:, half*nkc*TW + kc*TW: half*nkc*TW + (kc+1)*TW]
                    else:
                        rhs = in_tile(kc, tt)
                    b.op('t', lambda e, ps=ps, kc=kc, oc=oc, rhs=rhs: e.matmul(
                        ps[:, 0:TW], w_sb_fn(kc, oc), rhs,
                        start=(kc == 0), stop=(kc == nkc - 1)))
                cons.append(consume(oc, tt, ps, tok0))
                ngroups += 1
            tile_after.append(b.cnt['et'])
        return cons

    def nrm_tile(self, kc, tt):
        return self.nrm[:, kc*T + tt*TW: kc*T + tt*TW + TW]

    def mk_delta(self, bias_at=None, scale=None, scale_at=None):
        b, d = self.b, self.dram
        evac = self.evac
        state = {"n": 0, "dma": []}
        def consume(oc, tt, ps, tok0):
            n = state["n"]; state["n"] += 1
            dst = evac[:, (n % 6) * TW:(n % 6) * TW + TW]
            if n >= 6:
                b.wait_t('v', state["dma"][n - 6])
            b.wait('v', 'et', b.cnt['et'])
            if scale_at is not None:
                v = b.op('v', lambda e, dst=dst, ps=ps, oc=oc: e.tensor_scalar(
                    dst, ps[:, 0:TW], scale_at(oc), None, ALU.mult))
            elif bias_at is not None and scale is not None:
                v = b.op('v', lambda e, dst=dst, ps=ps, oc=oc: e.tensor_scalar(
                    dst, ps[:, 0:TW], scale, bias_at(oc), ALU.mult, ALU.add))
            elif bias_at is not None:
                v = b.op('v', lambda e, dst=dst, ps=ps, oc=oc: e.tensor_scalar(
                    dst, ps[:, 0:TW], bias_at(oc), None, ALU.add))
            elif scale is not None:
                v = b.op('v', lambda e, dst=dst, ps=ps: e.tensor_scalar(
                    dst, ps[:, 0:TW], scale, None, ALU.mult))
            else:
                v = b.op('v', lambda e, dst=dst, ps=ps: e.tensor_copy(dst, ps[:, 0:TW]))
            b.wait('g', 'ev', v)
            dv = b.dma('g', d["x_work"][oc*128:(oc+1)*128, tok0:tok0+TW], dst,
                       accum_op=ALU.add)
            state["dma"].append(dv)
            return ('ev', v)
        return consume

    def mk_spill(self, spill, act, bias_at=None, scale_at=None, scale=None,
                 ebuf_off=64 * 1024):
        b = self.b
        state = {"n": 0, "dma": []}
        ebuf = self.abf(ebuf_off, 8 * TW)
        def consume(oc, tt, ps, tok0):
            n = state["n"]; state["n"] += 1
            dst = ebuf[:, (n % 8) * TW:(n % 8) * TW + TW]
            if n >= 8:
                b.wait_t('s', state["dma"][n - 8])
            b.wait('s', 'et', b.cnt['et'])
            kw = {}
            if bias_at is not None:
                kw['bias'] = bias_at(oc)
            if scale_at is not None:
                kw['scale'] = scale_at(oc)
            elif scale is not None:
                kw['scale'] = scale
            v = b.op('s', lambda e, dst=dst, ps=ps, kw=kw: e.activation(
                dst, ps[:, 0:TW], act, **kw))
            b.wait('y', 'es', v)
            dv = b.dma('y', spill[oc*128:(oc+1)*128, tok0:tok0+TW], dst)
            state["dma"].append(dv)
            return ('es', v)
        return consume

    def emit_l3_ffn(self):
        b, d = self.b, self.dram
        ldb = b.dma_group('y', [(self.cols[:, 0:32], d["L3_ffn_b1"], {}),
                                (self.cols2[:, 0:8], d["L3_ffn_b2"], {})])
        b.wait_t('s', ldb)
        b.wait_t('v', ldb)
        self.emit_ln()
        self._ternarize("L3_ffn_W1t", D, 4096, wdst_off=0)
        b.barrier()
        w1 = self.abf(0, NKC * 4096)
        cons1 = self.mk_spill(d["spill_a"], AF.Gelu,
                              bias_at=lambda oc: self.cols[:, oc:oc+1])
        self.gemm_rw(lambda kc, oc: w1[:, kc*4096 + oc*128: kc*4096 + oc*128 + 128],
                     D, 4096, cons1, in_tile=self.nrm_tile, name="l3f1")
        b.barrier()
        self._ternarize("L3_ffn_W2t", 4096, D, wdst_off=0)
        b.barrier()
        w2 = self.abf(0, 32 * 1024)
        consd = self.mk_delta(bias_at=lambda oc: self.cols2[:, oc:oc+1])
        self.gemm_rw(lambda kc, oc: w2[:, kc*1024 + oc*128: kc*1024 + oc*128 + 128],
                     4096, D, consd, in_stream=d["spill_a"], ibuf='nrm', name="l3f2")
        b.barrier()

    def _ternarize(self, wname, Kdim, Ndim, wdst_off):
        b, d = self.b, self.dram
        b.barrier()
        nkc = Kdim // 128
        W = d[wname]
        xt, scr, scr2 = self.xtile, self.scr, self.scr2
        for kc in range(nkc):
            lv = b.dma('y', xt[:, 0:Ndim], W[kc*128:(kc+1)*128, 0:Ndim])
            b.wait_t('v', lv)
            vv = b.op('v', lambda e, kc=kc: e.reduce_sum(
                scr2[:, kc:kc+1], xt[:, 0:Ndim], axis=AX.X, apply_absolute_value=True))
            b.wait('y', 'ev', vv)
        b.selfwait('v')
        vv = b.op('v', lambda e, nkc=nkc: e.reduce_sum(scr2[:, 40:41], scr2[:, 0:nkc],
                                                       axis=AX.X))
        b.wait('t', 'ev', vv)
        mv = b.op('t', lambda e: e.matmul(self.ps[6][0:1, 0:1], self.ones_c[:, 0:1],
                                          scr2[:, 40:41], start=True, stop=True))
        b.wait('v', 'et', mv)
        b.op('v', lambda e: e.tensor_scalar(self.rows2[0:1, 0:1], self.ps[6][0:1, 0:1],
                                            1.0 / (Kdim * Ndim), 1e-8, ALU.mult, ALU.max))
        b.selfwait('v')
        b.op('v', lambda e: e.reciprocal(self.rows2[0:1, 1:2], self.rows2[0:1, 0:1]))
        b.selfwait('v')
        vv = b.op('v', lambda e: e.tensor_scalar_mul(self.rows2[0:1, 1:2],
                                                     self.rows2[0:1, 1:2], 2.0))
        b.wait('t', 'ev', vv)
        mv = b.op('t', lambda e: e.matmul(self.ps[6][:, 4:5], self.ones_f[0:1, :],
                                          self.rows2[0:1, 1:2], start=True, stop=True))
        b.wait('v', 'et', mv)
        rsc = b.op('v', lambda e: e.tensor_copy(self.cols2[:, 32:33], self.ps[6][:, 4:5]))
        wdst = self.abf(wdst_off, nkc * Ndim)
        nblk = (Ndim + TW - 1) // TW
        for kc in range(nkc):
            lv = b.dma('y', xt[:, 0:Ndim], W[kc*128:(kc+1)*128, 0:Ndim])
            b.wait_t('s', lv)
            b.wait_t('v', lv)
            if kc == 0:
                b.wait('v', 'ev', rsc)
            last_s = last_v = None
            for nb in range(nblk):
                bw = min(TW, Ndim - nb * TW)
                sg = scr[:, (nb % 2)*TW: (nb % 2)*TW + bw]
                sq = scr2[:, (nb % 2)*TW: (nb % 2)*TW + bw]
                xs = xt[:, nb*TW: nb*TW + bw]
                wd = wdst[:, kc*Ndim + nb*TW: kc*Ndim + nb*TW + bw]
                if nb >= 2:
                    b.wait('s', 'ev', last_v)
                sv = b.op('s', lambda e, sg=sg, xs=xs: e.activation(sg, xs, AF.Sign))
                b.op('v', lambda e, sq=sq, xs=xs: e.tensor_scalar_mul(
                    sq, xs, self.cols2[:, 32:33]))
                b.selfwait('v')
                b.op('v', lambda e, sq=sq: e.tensor_mul(sq, sq, sq))
                b.selfwait('v')
                b.op('v', lambda e, sq=sq: e.tensor_scalar(sq, sq, 1.0, None, ALU.is_ge))
                b.selfwait('v')
                b.wait('v', 'es', sv)
                last_v = b.op('v', lambda e, wd=wd, sq=sq, sg=sg: e.tensor_mul(wd, sq, sg))
                last_s = sv
            b.wait('y', 'ev', last_v)

    def emit_l0_ffn(self):
        b, d = self.b, self.dram
        b2 = b.dma_group('y', [(self.cols[:, 0:NKC], d["L0_ffn_lng"], {}),
                               (self.cols[:, 8:16], d["L0_ffn_lnb"], {}),
                               (self.cols[:, 16:48], d["L0_ffn_fc1_bias"], {}),
                               (self.cols2[:, 0:8], d["L0_ffn_fc2_bias"], {})])
        b.wait_t('s', b2)
        b.wait_t('v', b2)
        self.emit_ln(gamma_col=lambda kc: self.cols[:, kc:kc+1],
                     beta_col=lambda kc: self.cols[:, 8+kc:9+kc])
        self._dequant("L0_ffn_fc1_q", "L0_ffn_fc1_scales", D, 4096, wdst_off=0)
        b.barrier()
        w1 = self.abf(0, NKC * 4096)
        cons1 = self.mk_spill(d["spill_a"], AF.Silu,
                              bias_at=lambda oc: self.cols[:, 16+oc:17+oc])
        self.gemm_rw(lambda kc, oc: w1[:, kc*4096 + oc*128: kc*4096 + oc*128 + 128],
                     D, 4096, cons1, in_tile=self.nrm_tile, name="l0f1")
        b.barrier()
        self._dequant("L0_ffn_fc2_q", "L0_ffn_fc2_scales", 4096, D, wdst_off=0)
        b.barrier()
        w2 = self.abf(0, 32 * 1024)
        b.op('v', lambda e: e.tensor_scalar_mul(self.cols2[:, 8:16], self.cols2[:, 0:8], 0.5))
        b.selfwait('v')
        consd = self.mk_delta(bias_at=lambda oc: self.cols2[:, 8+oc:9+oc], scale=0.5)
        self.gemm_rw(lambda kc, oc: w2[:, kc*1024 + oc*128: kc*1024 + oc*128 + 128],
                     4096, D, consd, in_stream=d["spill_a"], ibuf='nrm', name="l0f2")
        b.barrier()

    def _dequant(self, qname, sname, Kdim, Ndim, wdst_off):
        b, d = self.b, self.dram
        b.barrier()
        nkc = Kdim // 128
        Q, S = d[qname], d[sname]
        xt = self.xtile
        wdst = self.abf(wdst_off, nkc * Ndim)
        nblk = Ndim // TW
        sttv = []
        for kc in range(nkc):
            if kc > 0:
                b.wait('g', 'ev', sttv[-1])
            lv = b.dma('g', xt[:, 0:Ndim], Q[kc*128:(kc+1)*128, 0:Ndim])
            sv = b.dma('y', self.rows2[0:1, 0:Ndim], S[kc:kc+1, 0:Ndim])
            b.wait_t('t', sv)
            for nb in range(nblk):
                gi = kc * nblk + nb
                if gi >= 2:
                    b.wait('t', 'ev', sttv[gi - 2])
                mv = b.op('t', lambda e, nb=nb: e.matmul(
                    self.ps[4 + nb % 2][:, 0:TW], self.ones_f[0:1, :],
                    self.rows2[0:1, nb*TW:(nb+1)*TW], start=True, stop=True))
                b.wait('v', 'et', mv)
                if nb == 0:
                    b.wait_t('v', lv)
                sttv.append(b.op('v', lambda e, kc=kc, nb=nb: e.scalar_tensor_tensor(
                    wdst[:, kc*Ndim + nb*TW: kc*Ndim + (nb+1)*TW],
                    xt[:, nb*TW:(nb+1)*TW], -128.0, self.ps[4 + nb % 2][:, 0:TW],
                    ALU.add, ALU.mult)))
            b.wait('y', 'et', b.cnt['et'])


    def emit_l0_attn(self):
        b, d = self.b, self.dram
        self.emit_ln()
        ldd = b.dma_group('y', [(self.rows[0:1, 0:8], d["L0_attn_decay"], {}),
                                (self.cols[:, 60:61], d["iota_col"], {}),
                                (self.scr[:, 0:128], d["exp_tab"], {}),
                                (self.masks[:, 2048:2176], d["rot_tab"], {})])
        b.wait_t('s', ldd)
        b.op('s', lambda e: e.activation(self.rows[0:1, 8:16], self.rows[0:1, 0:8], AF.Ln))
        b.selfwait('s')
        sv = b.op('s', lambda e: e.activation(self.rows[0:1, 16:24], self.rows[0:1, 8:16],
                                              AF.Exp, scale=128.0))
        b.wait('t', 'es', sv)
        for h in range(8):
            b.op('t', lambda e, h=h: e.matmul(self.ps[6][:, h:h+1], self.ones_f[0:1, :],
                                              self.rows[0:1, 8+h:9+h], start=True, stop=True))
        for h in range(8):
            mv = b.op('t', lambda e, h=h: e.matmul(self.ps[6][:, 8+h:9+h], self.ones_f[0:1, :],
                                                   self.rows[0:1, 16+h:17+h], start=True, stop=True))
        b.wait('s', 'et', mv)
        for h in range(8):
            b.op('s', lambda e, h=h: e.activation(self.cols2[:, h:h+1],
                                                  self.ps[6][:, h:h+1], AF.Copy))
            b.op('s', lambda e, h=h: e.activation(self.cols2[:, 32+h:33+h],
                                                  self.ps[6][:, 8+h:9+h], AF.Copy))
        b.selfwait('s')
        for h in range(8):
            b.op('s', lambda e, h=h: e.activation(self.cols2[:, 8+h:9+h],
                                                  self.cols2[:, h:h+1], AF.Copy, scale=-1.0))
            b.op('s', lambda e, h=h: e.activation(self.cols2[:, 16+h:17+h],
                                                  self.cols2[:, h:h+1], AF.Copy, scale=128.0))
        b.selfwait('s')
        for h in range(8):
            b.op('s', lambda e, h=h: e.activation(self.masks[:, h*128:(h+1)*128],
                                                  self.scr[:, 0:128], AF.Exp,
                                                  scale=self.cols2[:, h:h+1]))
        for h in range(8):
            sv = b.op('s', lambda e, h=h: e.activation(self.cols2[:, 24+h:25+h],
                                                       self.cols[:, 60:61], AF.Exp,
                                                       scale=self.cols2[:, 8+h:9+h],
                                                       bias=self.cols2[:, 16+h:17+h]))
        b.wait('t', 'es', b.cnt['es'])
        for h in range(8):
            mv = b.op('t', lambda e, h=h: e.matmul(self.ps[6][:, 128:256], self.ones_b[0:1, :],
                                                   self.masks[0:1, h*128:(h+1)*128],
                                                   start=True, stop=True))
            b.wait('v', 'et', mv)
            vv = b.op('v', lambda e, h=h: e.tensor_copy(
                self.masks[:, 1024 + h*128: 1024 + (h+1)*128], self.ps[6][:, 128:256]))
            b.wait('t', 'ev', vv)
        b.barrier()
        wq = self.abf(0, 2048)
        wk = self.abf(4*1024, 2048)
        wv = self.abf(8*1024, 4096)
        wg = self.abf(16*1024, 4096)
        qs = self.abf(24*1024, 1024)
        ks = self.abf(26*1024, 1024)
        rawb = self.abf(28*1024, 512)
        vs = self.abf(29*1024, 2048)
        gs = self.abf(33*1024, 2048)
        tabs = self.af32(37*1024, 4096)
        sbf = self.abf(53*1024, 256)
        ATb = self.abf(53*1024 + 512, 128)
        ktb = self.abf(53*1024 + 768, 128)
        kdb = self.abf(54*1024, 128)
        qgb = self.abf(54*1024 + 256, 128)
        nrb = self.abf(54*1024 + 512, 256)
        pinb = self.abf(55*1024, 2048)
        st = self.af32(60*1024, 512)
        Wqkv, Wg, Wp = d["L0_attn_W_qkv"], d["L0_attn_W_g"], d["L0_attn_W_proj"]
        TABN = ("xpos_cq", "xpos_sq", "xpos_ck", "xpos_sk")
        prev_stok = None
        for pr in range(getattr(self, 'l0_pair_limit', None) or 4):
            b.wait('g', 'et', b.cnt['et'])
            wl1 = b.dma_group('g', [(wq[:, kc*256:(kc+1)*256],
                                     Wqkv[kc*128:(kc+1)*128, pr*256: (pr+1)*256], {})
                                    for kc in range(NKC)] +
                                   [(wk[:, kc*256:(kc+1)*256],
                                     Wqkv[kc*128:(kc+1)*128, 1024 + pr*256: 1024 + (pr+1)*256], {})
                                    for kc in range(NKC)])
            wl2 = b.dma_group('g', [(wv[:, kc*512:(kc+1)*512],
                                     Wqkv[kc*128:(kc+1)*128, 2048 + pr*512: 2048 + (pr+1)*512], {})
                                    for kc in range(NKC)])
            wl3 = b.dma_group('g', [(wg[:, kc*512:(kc+1)*512],
                                     Wg[kc*128:(kc+1)*128, pr*512:(pr+1)*512], {})
                                    for kc in range(NKC)])
            b.selfwait('v')
            b.wait('v', 'et', b.cnt['et'])
            b.op('v', lambda e, st=st: e.memset(st[:], 0.0))
            for tt in range(getattr(self, 'l0_tt_limit', None) or NTT):
                t0 = tt * TW
                b.wait('y', 'ev', b.cnt['ev'])
                tl = b.dma_group('y', [
                    (tabs[:, (ti*2+hc)*512:(ti*2+hc+1)*512],
                     d[TABN[ti]][pr*256 + hc*128: pr*256 + (hc+1)*128, t0:t0+TW], {})
                    for ti in range(4) for hc in range(2)])
                if tt == 0:
                    b.wait_t('t', wl1)
                for which in range(2):
                    wsb = wq if which == 0 else wk
                    dstq = qs if which == 0 else ks
                    ci = 0 if which == 0 else 2
                    for hc in range(2):
                        b.wait('t', 'ev', b.cnt['ev'])
                        b.wait('t', 'es', b.cnt['es'])
                        for kc in range(NKC):
                            b.op('t', lambda e, wsb=wsb, kc=kc, hc=hc, tt=tt: e.matmul(
                                self.ps[0][:, 0:TW],
                                wsb[:, kc*256 + hc*128: kc*256 + (hc+1)*128],
                                self.nrm_tile(kc, tt), start=(kc == 0), stop=(kc == NKC-1)))
                        mv = b.cnt['et']
                        b.wait('s', 'et', mv)
                        sv = b.op('s', lambda e: e.activation(rawb[:], self.ps[0][:, 0:TW],
                                                              AF.Copy))
                        b.wait('t', 'es', sv)
                        mr = b.op('t', lambda e: e.matmul(self.ps[1][:, 0:TW],
                                                          self.masks[:, 2048:2176], rawb[:],
                                                          start=True, stop=True))
                        b.wait('v', 'et', mr)
                        b.wait_t('v', tl)
                        b.op('v', lambda e, ci=ci, hc=hc: e.tensor_mul(
                            self.scr[:, 0:TW], rawb[:], tabs[:, (ci*2+hc)*512:(ci*2+hc+1)*512]))
                        b.op('v', lambda e, ci=ci, hc=hc: e.tensor_mul(
                            self.scr[:, TW:2*TW], self.ps[1][:, 0:TW],
                            tabs[:, ((ci+1)*2+hc)*512:((ci+1)*2+hc+1)*512]))
                        b.selfwait('v')
                        b.op('v', lambda e, dstq=dstq, hc=hc: e.tensor_add(
                            dstq[:, hc*TW:(hc+1)*TW], self.scr[:, 0:TW], self.scr[:, TW:2*TW]))
                if tt == 0:
                    b.wait_t('t', wl2)
                    b.wait_t('t', wl3)
                for which in range(2):
                    wsb = wv if which == 0 else wg
                    dstv = vs if which == 0 else gs
                    for tc in range(4):
                        psv = self.ps[2 + tc % 2]
                        b.wait('t', 'ev', b.cnt['ev'])
                        b.wait('t', 'es', b.cnt['es'])
                        for kc in range(NKC):
                            b.op('t', lambda e, psv=psv, wsb=wsb, kc=kc, tc=tc, t0=t0: e.matmul(
                                psv[:, 0:TW],
                                self.nrm[:, kc*T + t0 + tc*128: kc*T + t0 + (tc+1)*128],
                                wsb[:, kc*512:(kc+1)*512], start=(kc == 0), stop=(kc == NKC-1)))
                        mv = b.cnt['et']
                        if which == 0:
                            b.wait('v', 'et', mv)
                            b.op('v', lambda e, psv=psv, dstv=dstv, tc=tc: e.tensor_copy(
                                dstv[:, tc*512:(tc+1)*512], psv[:, 0:TW]))
                        else:
                            b.wait('s', 'et', mv)
                            b.op('s', lambda e, psv=psv, dstv=dstv, tc=tc: e.activation(
                                dstv[:, tc*512:(tc+1)*512], psv[:, 0:TW], AF.Silu))
                b.wait('v', 'es', b.cnt['es'])
                if prev_stok is not None:
                    b.wait_t('v', prev_stok)
                for hc in range(2):
                    h = pr * 2 + hc
                    for c in range(4):
                        b.selfwait('v')
                        b.wait('v', 'et', b.cnt['et'])
                        scp = b.op('v', lambda e, hc=hc, st=st: e.tensor_copy(
                            sbf[:], st[:, hc*256:(hc+1)*256]))
                        b.wait('t', 'ev', scp)
                        mv = b.op('t', lambda e, hc=hc, c=c: e.matmul(
                            self.ps[0][:, 0:128], ks[:, hc*TW + c*128: hc*TW + (c+1)*128],
                            qs[:, hc*TW + c*128: hc*TW + (c+1)*128], start=True, stop=True))
                        b.wait('v', 'et', mv)
                        b.op('v', lambda e, h=h: e.tensor_mul(
                            ATb[:], self.ps[0][:, 0:128], self.masks[:, h*128:(h+1)*128]))
                        qv = b.op('v', lambda e, h=h, hc=hc, c=c: e.tensor_mul(
                            qgb[:], qs[:, hc*TW + c*128: hc*TW + (c+1)*128],
                            self.masks[:, 1024 + h*128: 1024 + (h+1)*128]))
                        b.wait('t', 'ev', qv)
                        b.wait('t', 'es', b.cnt['es'])
                        b.op('t', lambda e, c=c, hc=hc: e.matmul(
                            self.ps[1][:, 0:256], ATb[:],
                            vs[:, c*512 + hc*256: c*512 + hc*256 + 256],
                            start=True, stop=False))
                        mo = b.op('t', lambda e: e.matmul(
                            self.ps[1][:, 0:256], qgb[:], sbf[:], start=False, stop=True))
                        tv = b.op('t', lambda e, hc=hc, c=c: e.transpose(
                            self.psb[:, 0:128], ks[:, hc*TW + c*128: hc*TW + (c+1)*128],
                            self.ident[:]))
                        b.wait('v', 'et', tv)
                        b.op('v', lambda e: e.tensor_copy(ktb[:], self.psb[:, 0:128]))
                        b.selfwait('v')
                        k2 = b.op('v', lambda e, h=h: e.tensor_scalar_mul(
                            kdb[:], ktb[:], self.cols2[:, 24+h:25+h]))
                        b.wait('t', 'ev', k2)
                        mi = b.op('t', lambda e, c=c, hc=hc: e.matmul(
                            self.ps[2][:, 0:256], kdb[:],
                            vs[:, c*512 + hc*256: c*512 + hc*256 + 256],
                            start=True, stop=True))
                        b.wait('v', 'et', mi)
                        b.op('v', lambda e, h=h, hc=hc, st=st: e.scalar_tensor_tensor(
                            st[:, hc*256:(hc+1)*256], st[:, hc*256:(hc+1)*256],
                            self.cols2[:, 32+h:33+h], self.ps[2][:, 0:256],
                            ALU.mult, ALU.add))
                        b.wait('s', 'et', mo)
                        sq = b.op('s', lambda e: e.activation(self.scr2[:, 0:256],
                                                              self.ps[1][:, 0:256], AF.Square))
                        b.wait('v', 'et', mo)
                        b.op('v', lambda e: e.reduce_sum(self.cols[:, 61:62],
                                                         self.ps[1][:, 0:256], axis=AX.X))
                        b.wait('v', 'es', sq)
                        b.op('v', lambda e: e.reduce_sum(self.cols[:, 62:63],
                                                         self.scr2[:, 0:256], axis=AX.X))
                        b.selfwait('v')
                        b.op('v', lambda e: e.tensor_scalar_mul(self.cols[:, 61:62],
                                                                self.cols[:, 61:62], 1.0/256))
                        b.op('v', lambda e: e.tensor_scalar_mul(self.cols[:, 62:63],
                                                                self.cols[:, 62:63], 1.0/256))
                        b.selfwait('v')
                        b.op('v', lambda e: e.tensor_mul(self.cols[:, 63:64],
                                                         self.cols[:, 61:62], self.cols[:, 61:62]))
                        b.selfwait('v')
                        b.op('v', lambda e: e.tensor_scalar(self.cols[:, 63:64],
                                                            self.cols[:, 63:64], -1.0, 1e-5,
                                                            ALU.mult, ALU.add))
                        b.selfwait('v')
                        vv = b.op('v', lambda e: e.tensor_add(self.cols[:, 62:63],
                                                              self.cols[:, 62:63],
                                                              self.cols[:, 63:64]))
                        b.wait('s', 'ev', vv)
                        sv = b.op('s', lambda e: e.activation(self.cols[:, 63:64],
                                                              self.cols[:, 62:63], AF.Sqrt))
                        b.wait('v', 'es', sv)
                        b.op('v', lambda e: e.reciprocal(self.cols[:, 62:63],
                                                         self.cols[:, 63:64]))
                        b.selfwait('v')
                        b.op('v', lambda e: e.tensor_mul(self.cols[:, 63:64],
                                                         self.cols[:, 61:62], self.cols[:, 62:63]))
                        b.selfwait('v')
                        b.op('v', lambda e: e.tensor_scalar_mul(self.cols[:, 63:64],
                                                                self.cols[:, 63:64], -1.0))
                        b.selfwait('v')
                        b.op('v', lambda e: e.tensor_scalar(
                            nrb[:], self.ps[1][:, 0:256], self.cols[:, 62:63],
                            self.cols[:, 63:64], ALU.mult, ALU.add))
                        b.selfwait('v')
                        pv2 = b.op('v', lambda e, c=c, hc=hc: e.tensor_mul(
                            nrb[:], nrb[:], gs[:, c*512 + hc*256: c*512 + hc*256 + 256]))
                        b.wait('t', 'ev', pv2)
                        for dh in range(2):
                            b.op('t', lambda e, dh=dh: e.transpose(
                                self.psb[:, 128 + dh*128: 256 + dh*128],
                                nrb[:, dh*128:(dh+1)*128], self.ident[:]))
                        mt = b.cnt['et']
                        b.wait('v', 'et', mt)
                        for dh in range(2):
                            vcl = hc * 2 + dh
                            b.op('v', lambda e, vcl=vcl, c=c, dh=dh: e.tensor_copy(
                                pinb[:, vcl*512 + c*128: vcl*512 + (c+1)*128],
                                self.psb[:, 128 + dh*128: 256 + dh*128]))
                spv = b.cnt['ev']
                b.wait('y', 'ev', spv)
                prev_stok = b.dma_group('y', [
                    (d["spill_a"][(pr*4+v)*128:(pr*4+v+1)*128, t0:t0+TW],
                     pinb[:, v*512:(v+1)*512], {}) for v in range(4)])
        b.barrier()
        if getattr(self, 'l0_no_proj', False):
            return
        wp = self.abf(0, 16*1024)
        for half in range(2):
            wlp = b.dma_group('g', [(wp[:, (half*8+k2)*1024:(half*8+k2+1)*1024],
                                     Wp[(half*8+k2)*128:(half*8+k2+1)*128, :], {})
                                    for k2 in range(8)])
            b.wait_t('t', wlp)
        consd = self.mk_delta()
        self.gemm_rw(lambda kc, oc: wp[:, kc*1024 + oc*128: kc*1024 + (oc+1)*128],
                     2048, D, consd, in_stream=d["spill_a"], ibuf='arena',
                     ibuf_off=32*1024, name="l0proj")
        b.barrier()

    def _attn_bound(self, qp, kp):
        b = self.b
        b.wait('t', 'ev', b.cnt['ev'])
        b.wait('s', 'ev', b.cnt['ev'])
        units = [(si, tt) for si in range(2) for tt in range(NTT)]
        mmv = {}
        rmv = {}
        for i, (si, tt) in enumerate(units):
            src = (qp, kp)[si]
            slot = (i % 2) * TW
            bb = 4 if i % 2 == 0 else 0
            if i >= 2:
                b.wait('s', 'et', mmv[i - 2])
            sv = b.op('s', lambda e, src=src, tt=tt, slot=slot: e.activation(
                self.scr[:, slot:slot+TW], src[:, tt*TW:(tt+1)*TW], AF.Square))
            b.wait('t', 'es', sv)
            if i >= 2:
                b.wait('t', 'ev', rmv[i - 2])
            b.op('t', lambda e, slot=slot, bb=bb: e.matmul(
                self.ps[bb][0:1, 0:TW], self.ones_c[0:64, 0:1],
                self.scr[0:64, slot:slot+TW], start=True, stop=True,
                tile_position=(0, 0)))
            mmv[i] = b.op('t', lambda e, slot=slot, bb=bb: e.matmul(
                self.ps[bb + 1][0:1, 0:TW], self.ones_c[64:128, 0:1],
                self.scr[64:128, slot:slot+TW], start=True, stop=True,
                tile_position=(64, 0)))
            b.wait('v', 'et', mmv[i])
            for h2 in range(2):
                dst = 32 + (h2 * 2 + si) * 8 + tt
                rmv[i] = b.op('v', lambda e, h2=h2, dst=dst, bb=bb: e.reduce_max(
                    self.rows[0:1, dst:dst+1], self.ps[bb + h2][0:1, 0:TW], axis=AX.X))
        b.selfwait('v')
        negb = {}
        for h2 in range(2):
            b.op('v', lambda e, h2=h2: e.reduce_max(
                self.rows[0:1, 0:1], self.rows[0:1, 32 + h2*16: 40 + h2*16], axis=AX.X))
            b.selfwait('v')
            b.op('v', lambda e, h2=h2: e.reduce_max(
                self.rows[0:1, 1:2], self.rows[0:1, 40 + h2*16: 48 + h2*16], axis=AX.X))
            b.selfwait('v')
            vv = b.op('v', lambda e: e.tensor_mul(self.rows[0:1, 2:3],
                                                  self.rows[0:1, 0:1], self.rows[0:1, 1:2]))
            b.wait('s', 'ev', vv)
            sv = b.op('s', lambda e: e.activation(self.rows[0:1, 3:4],
                                                  self.rows[0:1, 2:3], AF.Sqrt))
            b.wait('v', 'es', sv)
            vv = b.op('v', lambda e, h2=h2: e.tensor_scalar_mul(
                self.scr2[0:1, h2:h2+1], self.rows[0:1, 3:4], -1.0))
            b.wait('t', 'ev', vv)
            mv = b.op('t', lambda e, h2=h2: e.matmul(
                self.ps[4][:, 8:9], self.bcrf[0:64, 0:128], self.scr2[0:64, h2:h2+1],
                start=True, stop=True, tile_position=(0, 0)))
            b.wait('s', 'et', mv)
            negb[h2] = b.op('s', lambda e, h2=h2: e.activation(
                self.cols2[:, 44 + h2:45 + h2], self.ps[4][:, 8:9], AF.Copy))
            b.selfwait('s')
        b.wait('v', 'es', negb[1])
        b.wait('t', 'es', negb[1])
        b.wait('t', 'ev', b.cnt['ev'])
        b.wait('t', 'et', b.cnt['et'])

    def _attn_flash(self, qp, kp, vp, aop, pbuf, njc_of):
        b = self.b
        PIPE = 3
        SB = (self.ps[0], self.ps[1], self.ps[4], self.ps[5])
        st = self._flash_state
        for qt in range(NTT):
            njc = njc_of(qt)
            smm = {}
            expv = {}
            mskv = {}
            pvet = {}

            def S_(jc, qt=qt):
                if jc >= 2:
                    b.wait('t', 'es', expv[(jc - 2, 1)])
                else:
                    b.wait('t', 'es', st['last_exp'])
                for h2 in range(2):
                    b.op('t', lambda e, jc=jc, h2=h2, qt=qt: e.matmul(
                        SB[(2*jc + h2) % 4][:, 0:TW],
                        kp[64*h2: 64*h2 + 64, jc*128:(jc+1)*128],
                        qp[64*h2: 64*h2 + 64, qt*TW:(qt+1)*TW],
                        start=True, stop=True, tile_position=(64*h2, 0)))
                    smm[(jc, h2)] = b.cnt['et']

            def E_(jc, qt=qt):
                for h2 in range(2):
                    b.wait('s', 'et', smm[(jc, h2)])
                    b.wait('s', 'et', pvet[(jc - 4, h2)] if jc >= 4 else st['last_pv'])
                    expv[(jc, h2)] = b.op('s', lambda e, jc=jc, h2=h2: e.activation(
                        pbuf[:, ((2*jc + h2) % 8)*TW:((2*jc + h2) % 8 + 1)*TW],
                        SB[(2*jc + h2) % 4][:, 0:TW],
                        AF.Exp, bias=self.cols2[:, 44 + h2:45 + h2]))
                    rel = jc - 4 * qt
                    if rel >= 0:
                        b.wait('v', 'es', expv[(jc, h2)])
                        mskv[(jc, h2)] = b.op('v', lambda e, jc=jc, h2=h2, rel=rel: e.tensor_mul(
                            pbuf[:, ((2*jc + h2) % 8)*TW:((2*jc + h2) % 8 + 1)*TW],
                            pbuf[:, ((2*jc + h2) % 8)*TW:((2*jc + h2) % 8 + 1)*TW],
                            self.masks[:, rel*TW:(rel+1)*TW]))

            def PV_(jc, qt=qt, njc=njc):
                for h2 in range(2):
                    psA = self.ps[2 + h2]
                    if (jc, h2) in mskv:
                        b.wait('t', 'ev', mskv[(jc, h2)])
                    else:
                        b.wait('t', 'es', expv[(jc, h2)])
                    if jc == 0:
                        b.wait('t', 'ev', st['tail_c'][h2])
                    b.op('t', lambda e, jc=jc, h2=h2, njc=njc, psA=psA: e.matmul(
                        psA[0:65, 0:TW], vp[:, jc*130 + 65*h2: jc*130 + 65*h2 + 65],
                        pbuf[:, ((2*jc + h2) % 8)*TW:((2*jc + h2) % 8 + 1)*TW],
                        start=(jc == 0), stop=(jc == njc - 1)))
                    pvet[(jc, h2)] = b.cnt['et']

            for jc in range(min(PIPE, njc)):
                S_(jc)
                E_(jc)
            for jc in range(njc):
                if jc + PIPE < njc:
                    S_(jc + PIPE)
                    E_(jc + PIPE)
                PV_(jc)
            st['last_exp'] = expv[(njc - 1, 1)]
            st['last_pv'] = pvet[(njc - 1, 1)]
            prev_mb = 0
            for h2 in range(2):
                psA = self.ps[2 + h2]
                mv = max(pvet[(njc - 1, h2)], prev_mb)
                b.wait('v', 'et', mv)
                rv = b.op('v', lambda e, psA=psA: e.reciprocal(
                    self.scr2[0:1, 8:520], psA[64:65, 0:TW]))
                b.wait('t', 'ev', rv)
                mb = b.op('t', lambda e: e.matmul(
                    self.ps[6][0:64, 0:TW], self.bcrf[0:64, 0:64],
                    self.scr2[0:64, 8:520], start=True, stop=True,
                    tile_position=(0, 0)))
                prev_mb = mb
                slot = h2 * TW
                cv = b.op('v', lambda e, psA=psA, slot=slot: e.tensor_copy(
                    self.scr[0:64, slot:slot+TW], psA[0:64, 0:TW]))
                st['tail_c'][h2] = cv
                b.wait('v', 'et', mb)
                ov = b.op('v', lambda e, h2=h2, qt=qt, slot=slot: e.tensor_mul(
                    aop[64*h2:64*h2 + 64, qt*TW:(qt+1)*TW],
                    self.scr[0:64, slot:slot+TW], self.ps[6][0:64, 0:TW]))
                st['ov'].append(ov)
        b.wait('t', 'et', b.cnt['et'])

    def emit_l1_attn(self):
        b, d = self.b, self.dram
        ldc = b.dma_group('y', [(self.cols[:, 0:8], d["L1_attn_bo"], {}),
                                (self.cols[:, 8:16], d["L1_attn_b1"], {}),
                                (self.cols[:, 16:24], d["L1_attn_b2"], {}),
                                (self.cols[:, 24:48], d["L1_attn_bg"], {})])
        ldm = b.dma('g', self.masks[:, 0:2560], d["l1_masks"])
        self.emit_ln()
        b.wait_t('s', ldc)
        b.wait_t('v', ldc)
        cons1 = self.mk_spill(d["spill_a"][0:1024, :], AF.Silu,
                              bias_at=lambda oc: self.cols[:, 8+oc:9+oc])
        self.gemm_sw(d["L1_attn_W1t"], D, D, cons1, gw=512, wbuf_off=0,
                     in_tile=self.nrm_tile, name="l1aA")
        b.barrier()
        st2 = {"n": 0}
        def cons2(oc, tt, ps, tok0):
            n = st2["n"]; st2["n"] += 1
            b.wait('v', 'et', b.cnt['et'])
            v = b.op('v', lambda e, oc=oc, tt=tt, ps=ps: e.tensor_scalar(
                self.nrm[:, oc*T + tt*TW: oc*T + (tt+1)*TW], ps[:, 0:TW],
                self.cols[:, 16+oc:17+oc], None, ALU.add))
            return ('ev', v)
        self.gemm_sw(d["L1_attn_W2t"], D, D, cons2, gw=512, wbuf_off=0,
                     in_stream=d["spill_a"][0:1024, :], ibuf='arena',
                     ibuf_off=16 * 1024, name="l1aB")
        b.barrier()
        qp = self.abf(0, T)
        kp = self.abf(8 * 1024, T)
        vfm = self.abf(16 * 1024, T)
        vp = self.abf(24 * 1024, 32 * 130)
        aop = self.abf(33 * 1024, T)
        wqkv = self.abf(41 * 1024, 3 * 1024)
        pbuf = self.abf(47 * 1024, 8 * TW)
        b.op('v', lambda e, vp=vp: e.memset(
            vp[:].rearrange("p (a c) -> p a c", c=65)[:, :, 64:65], 1.0))
        b.op('v', lambda e: e.memset(self.scr2[:], 0.0))
        beta = float(64.0 ** -0.5)
        self._flash_state = {'last_exp': 0, 'last_pv': 0,
                             'tail_r': {0: 0, 1: 0}, 'tail_c': {0: 0, 1: 0},
                             'ov': []}
        proj_done = 0
        stok = None
        for p in range(getattr(self, 'l1_pair_limit', None) or 8):
            b.wait('g', 'et', proj_done)
            wl = b.dma_group('g', [
                (wqkv[:, pi*1024 + kc*128: pi*1024 + (kc+1)*128],
                 d["L1_attn_Wgt"][kc*128:(kc+1)*128,
                                  pi*1024 + p*128: pi*1024 + (p+1)*128], {})
                for pi in range(3) for kc in range(NKC)])
            b.wait_t('t', wl)
            pv = {}
            for pi in range(3):
                dstt = (qp, kp, vfm)[pi]
                bgc = 24 + pi * 8 + p
                for tt in range(NTT):
                    psq = self.ps[(pi * NTT + tt) % 2]
                    if pi * NTT + tt >= 2:
                        b.wait('t', 'ev', pv[(pi * NTT + tt - 2)])
                    for kc in range(NKC):
                        b.op('t', lambda e, psq=psq, pi=pi, kc=kc, tt=tt: e.matmul(
                            psq[:, 0:TW],
                            wqkv[:, pi*1024 + kc*128: pi*1024 + (kc+1)*128],
                            self.nrm_tile(kc, tt), start=(kc == 0), stop=(kc == NKC - 1)))
                    mv = b.cnt['et']
                    b.wait('v', 'et', mv)
                    sc = beta if pi == 0 else 1.0
                    if pi == 2:
                        pv[pi * NTT + tt] = b.op('v', lambda e, psq=psq, dstt=dstt, tt=tt, bgc=bgc: e.tensor_scalar(
                            dstt[:, tt*TW:(tt+1)*TW], psq[:, 0:TW],
                            self.cols[:, bgc:bgc+1], None, ALU.add))
                    else:
                        pv[pi * NTT + tt] = b.op('v', lambda e, psq=psq, dstt=dstt, tt=tt, bgc=bgc, sc=sc: e.tensor_scalar(
                            dstt[:, tt*TW:(tt+1)*TW], psq[:, 0:TW],
                            self.cols[:, bgc:bgc+1], sc, ALU.add, ALU.mult))
            proj_done = b.cnt['et']
            cpv = []
            b.wait('t', 'ev', pv[3 * NTT - 1])
            for jc in range(32):
                if cpv:
                    b.wait('t', 'ev', cpv[-1])
                b.op('t', lambda e, jc=jc, vfm=vfm: e.transpose(
                    self.psb[:, 0:128], vfm[:, jc*128:(jc+1)*128], self.ident[:]))
                mv = b.cnt['et']
                b.wait('v', 'et', mv)
                cpv.append(b.op('v', lambda e, jc=jc, vp=vp: e.tensor_copy(
                    vp[:, jc*130: jc*130 + 64], self.psb[:, 0:64])))
                cpv.append(b.op('v', lambda e, jc=jc, vp=vp: e.tensor_copy(
                    vp[:, jc*130 + 65: jc*130 + 129], self.psb[:, 64:128])))
            self._attn_bound(qp, kp)
            if stok is not None:
                b.wait_t('v', stok)
            self._attn_flash(qp, kp, vp, aop, pbuf, lambda qt: min(4 * qt + 5, 32))
            b.wait('y', 'ev', b.cnt['ev'])
            stok = b.dma('y', d["spill_b"][p*128:(p+1)*128, :], aop[:, 0:T])
        b.barrier()
        consd = self.mk_delta()
        self.gemm_sw(d["L1_attn_Wot"], D, D, consd, gw=512, wbuf_off=0,
                     in_stream=d["spill_b"][0:1024, :], ibuf='arena',
                     ibuf_off=16 * 1024, name="l1o")
        b.barrier()
        bcol = self.cols[:, 0:8]
        for oc in range(8):
            vv = b.op('v', lambda e, oc=oc, bcol=bcol: e.tensor_scalar(
                self.evac[:, (oc % 2)*TW:(oc % 2)*TW + TW], self.ab[:, 0:TW],
                0.0, bcol[:, oc:oc+1], ALU.mult, ALU.add))
            b.wait('g', 'ev', vv)
            dv = None
            for tt in range(NTT):
                dv = b.dma('g', d["x_work"][oc*128:(oc+1)*128, tt*TW:(tt+1)*TW],
                           self.evac[:, (oc % 2)*TW:(oc % 2)*TW + TW], accum_op=ALU.add)
            b.wait_t('v', dv)
        b.barrier()

    def emit_l2_attn(self):
        b, d = self.b, self.dram
        G = 256
        ldc = b.dma_group('y', [
            (self.cols[:, 0:16], d["L2_attn_bg"], {}),
            (self.cols[:, 16:17], d["L2_attn_binc"], {}),
            (self.cols[:, 17:25], d["L2_attn_bout"], {}),
            (self.cols2[:, 0:8], d["L2_attn_affs"], {}),
            (self.cols[:, 25:41], d["L2_attn_bv"], {}),
        ])
        ldm = b.dma('g', self.masks[:, 0:512], d["l2_masks"])
        ldw = b.dma_group('g', [(self.masks[:, 512 + kc*128: 512 + (kc+1)*128],
                                 d["L2_attn_Win"][kc*128:(kc+1)*128, :], {})
                                for kc in range(NKC)])
        b.wait_t('v', ldc)
        b.op('v', lambda e: e.tensor_scalar_mul(self.cols2[:, 8:10], self.cols2[:, 0:2],
                                                float(2048.0 ** -0.5)))
        st = self.af32(54 * 1024, 2048)
        b.op('v', lambda e: e.memset(st[:], 0.0))
        b.selfwait('v')
        self.emit_ln()
        consV = self.mk_spill(d["spill_a"][0:2048, :], AF.Silu,
                              bias_at=lambda oc: self.cols[:, 25+oc:26+oc])
        self.gemm_sw(d["L2_attn_Wv"], D, 2048, consV, gw=512, wbuf_off=0,
                     in_tile=self.nrm_tile, name="l2av")
        b.barrier()
        consG = self.mk_spill(d["spill_b"][0:2048, :], AF.Silu,
                              bias_at=lambda oc: self.cols[:, oc:oc+1])
        self.gemm_sw(d["L2_attn_Wg"], D, 2048, consG, gw=512, wbuf_off=0,
                     in_tile=self.nrm_tile, name="l2ag")
        b.barrier()
        AR_VF, AR_GT = 0, 16 * 1024
        AR_VT, AR_XQ = 32 * 1024, 40 * 1024
        AR_EB = 48 * 1024
        vf = self.abf(AR_VF, 16 * TW)
        gt = self.abf(AR_GT, 16 * TW)
        vt = self.nrm[:, 0:4096] if getattr(self, 'l2_vt_normal', False) else self.abf(AR_VT, 4096)
        x_ = self.abf(AR_XQ, TW)
        qq = self.abf(AR_XQ + 1024, TW)
        qk = self.abf(AR_XQ + 2048, TW)
        lq = self.abf(AR_XQ + 3072, TW)
        lk = self.abf(AR_XQ + 4096, TW)
        Pb = self.abf(AR_XQ + 5120, TW)
        P2 = self.abf(AR_XQ + 6144, TW)
        sbf = self.abf(AR_XQ + 7168, 256)
        lkT = self.abf(AR_EB + 3 * 1024, 256)
        oe = self.abf(AR_EB, 6 * 256)
        Win = self.masks.ap() if False else None
        win_sb = self.masks[:, 512:1536]
        spill_o = d["spill_a"][2048:4096, :]
        spill_v = d["spill_a"][0:2048, :]
        spill_g = d["spill_b"][0:2048, :]
        b.wait_t('t', ldw)
        ost = {"n": 0, "dma": [], "ev": []}
        self._l2_last_rl = None
        ntt_run = getattr(self, 'l2_tt_limit', None) or NTT
        for tt in range(ntt_run):
            t0 = tt * TW
            if tt > 0:
                b.wait('y', 'et', b.cnt['et'])
                b.wait('y', 'ev', b.cnt['ev'])
            lvf = b.dma_group('y', [(vf[:, ec*TW:(ec+1)*TW],
                                     spill_v[ec*128:(ec+1)*128, t0:t0+TW], {})
                                    for ec in range(16)])
            lgt = b.dma_group('y', [(gt[:, ec*TW:(ec+1)*TW],
                                     spill_g[ec*128:(ec+1)*128, t0:t0+TW], {})
                                    for ec in range(16)])
            for kc in range(NKC):
                b.op('t', lambda e, kc=kc, tt=tt: e.matmul(
                    self.ps[4][:, 0:TW], win_sb[:, kc*128:(kc+1)*128],
                    self.nrm_tile(kc, tt), start=(kc == 0), stop=(kc == NKC - 1)))
            mv = b.cnt['et']
            b.wait('s', 'et', mv)
            sv = b.op('s', lambda e: e.activation(x_[:], self.ps[4][:, 0:TW], AF.Silu,
                                                  bias=self.cols[:, 16:17]))
            b.wait('v', 'es', sv)
            b.op('v', lambda e: e.tensor_scalar(qq[:], x_[:], self.cols2[:, 8:9],
                                                self.cols2[:, 9:10], ALU.mult, ALU.add))
            b.op('v', lambda e: e.tensor_scalar(qk[:], x_[:], self.cols2[:, 2:3],
                                                self.cols2[:, 3:4], ALU.mult, ALU.add))
            b.op('v', lambda e: e.tensor_scalar(lq[:], x_[:], self.cols2[:, 4:5],
                                                self.cols2[:, 5:6], ALU.mult, ALU.add))
            aff = b.op('v', lambda e: e.tensor_scalar(lk[:], x_[:], self.cols2[:, 6:7],
                                                      self.cols2[:, 7:8], ALU.mult, ALU.add))
            b.wait_t('t', lvf)
            for g in range(getattr(self, 'l2_g_limit', None) or 2):
                n0 = g * G
                cpv = []
                for tc in range(2):
                    for ec8 in range(2):
                        if cpv:
                            b.wait('t', 'ev', cpv[-1])
                        for e8 in range(8):
                            ec = ec8 * 8 + e8
                            b.op('t', lambda e, ec=ec, tc=tc, g=g, e8=e8: e.transpose(
                                self.psb[:, e8*128:(e8+1)*128],
                                vf[:, ec*TW + g*G + tc*128: ec*TW + g*G + tc*128 + 128],
                                self.ident[:]))
                        mv = b.cnt['et']
                        b.wait('v', 'et', mv)
                        cpv.append(b.op('v', lambda e, tc=tc, ec8=ec8: e.tensor_copy(
                            vt[:, tc*2048 + ec8*1024: tc*2048 + (ec8+1)*1024],
                            self.psb[:, 0:1024])))
                b.wait('t', 'ev', aff)
                if self._l2_last_rl is not None:
                    b.wait('t', 'es', self._l2_last_rl)
                sep = getattr(self, 'l2_sep_st', False)
                for tc in range(2):
                    bank = self.ps[5 + tc] if sep else self.ps[5]
                    col = 0 if sep else tc * G
                    b.op('t', lambda e, tc=tc, g=g, bank=bank, col=col, n0=n0: e.matmul(
                        bank[:, col:col+G], qk[:, n0+tc*128: n0+tc*128+128],
                        qq[:, n0:n0+G], start=True, stop=True))
                mv = b.cnt['et']
                b.wait('s', 'et', mv)
                rl = []
                for tc in range(2):
                    bank = self.ps[5 + tc] if sep else self.ps[5]
                    col = 0 if sep else tc * G
                    rl.append(b.op('s', lambda e, tc=tc, bank=bank, col=col: e.activation(
                        Pb[:, tc*G:(tc+1)*G], bank[:, col:col+G], AF.Relu)))
                self._l2_last_rl = rl[-1]
                b.wait('v', 'es', rl[-1])
                for tc in range(2):
                    b.op('v', lambda e, tc=tc: e.tensor_mul(
                        P2[:, tc*G:(tc+1)*G], Pb[:, tc*G:(tc+1)*G], Pb[:, tc*G:(tc+1)*G]))
                b.selfwait('v')
                for tc in range(2):
                    pm = b.op('v', lambda e, tc=tc: e.tensor_mul(
                        Pb[:, tc*G:(tc+1)*G], P2[:, tc*G:(tc+1)*G],
                        self.masks[:, tc*G:(tc+1)*G]))
                if getattr(self, 'l2_pb_normal', False):
                    b.selfwait('v')
                    pm = b.op('v', lambda e: e.tensor_copy(self.masks[:, 1536:2048], Pb[:, 0:512]))
                b.wait('t', 'ev', pm)
                b.wait_t('t', lgt)
                esmm = []
                for es in range(getattr(self, 'l2_es_limit', None) or 16):
                    nb_rot = getattr(self, 'l2_nbank', 4)
                    psx = self.ps[es % nb_rot]
                    if es >= 2:
                        b.wait('v', 'et', esmm[es - 2])
                    scp = b.op('v', lambda e, es=es: e.tensor_copy(
                        sbf[:, (es % 2)*128:(es % 2)*128 + 128],
                        st[:, es*128:(es+1)*128]))
                    if len(ost["esv"]) >= 4 if "esv" in ost else False:
                        pass
                    if len(ost.get("esv", [])) >= 4:
                        b.wait('t', 'es', ost["esv"][-4])
                    b.wait('t', 'ev', scp)
                    novl = getattr(self, 'l2_no_vl', False)
                    pbsrc = self.masks[:, 1536:2048] if getattr(self, 'l2_pb_normal', False) else Pb
                    for tc in range(2):
                        b.op('t', lambda e, es=es, tc=tc, novl=novl, pbsrc=pbsrc, psx=psx: e.matmul(
                            psx[:, 0:G], vt[:, tc*2048 + es*128:(tc*2048) + (es+1)*128],
                            pbsrc[:, tc*G:(tc+1)*G], start=(tc == 0),
                            stop=(novl and tc == 1)))
                    if getattr(self, 'l2_no_vl', False):
                        pass
                    else:
                        b.op('t', lambda e, es=es, g=g, psx=psx, n0=n0: e.matmul(
                            psx[:, 0:G], sbf[:, (es % 2)*128:(es % 2)*128 + 128],
                            lq[:, n0:n0+G], start=False, stop=True))
                    if getattr(self, 'l2_drain', False):
                        mv = b.op('t', lambda e: e.drain())
                    else:
                        mv = b.cnt['et']
                    esmm.append(mv)
                    n = ost["n"]; ost["n"] += 1
                    if n >= 6:
                        b.wait_t('v', ost["dma"][n - 6])
                    b.wait('s', 'et', mv)
                    if len(ost["ev"]) >= 2:
                        b.wait('s', 'ev', ost["ev"][-2])
                    sv = b.op('s', lambda e, n=n, psx=psx: e.activation(
                        self.scr[:, (n % 2)*TW:(n % 2)*TW + G], psx[:, 0:G], AF.Copy))
                    b.wait('v', 'es', sv)
                    ov = b.op('v', lambda e, es=es, n=n, n0=n0: e.tensor_mul(
                        oe[:, (n % 6)*G:(n % 6)*G + G],
                        self.scr[:, (n % 2)*TW:(n % 2)*TW + G],
                        gt[:, es*TW + n0: es*TW + n0 + G]))
                    ost["ev"].append(max(ov, sv) if False else ov)
                    ost.setdefault("esv", []).append(sv)
                    b.wait('y', 'ev', ov)
                    ost["dma"].append(b.dma('y', spill_o[es*128:(es+1)*128, t0+n0:t0+n0+G],
                                            oe[:, (n % 6)*G:(n % 6)*G + G]))
                tv = []
                for tc in range(2):
                    b.op('t', lambda e, tc=tc, g=g, n0=n0: e.transpose(
                        self.psb[:, tc*128:(tc+1)*128],
                        lk[:, n0+tc*128: n0+tc*128+128], self.ident[:]))
                mv = b.cnt['et']
                b.wait('v', 'et', mv)
                lkc = b.op('v', lambda e: e.tensor_copy(lkT[:, 0:256], self.psb[:, 0:256]))
                b.wait('t', 'ev', lkc)
                upd = []
                for et4 in range(4):
                    for tc in range(2):
                        b.op('t', lambda e, et4=et4, tc=tc: e.matmul(
                            self.ps[4][:, 0:TW], lkT[:, tc*128:(tc+1)*128],
                            vt[:, tc*2048 + et4*TW: tc*2048 + (et4+1)*TW],
                            start=(tc == 0), stop=(tc == 1)))
                    mv = b.cnt['et']
                    b.wait('v', 'et', mv)
                    uv = b.op('v', lambda e, et4=et4: e.tensor_add(
                        st[:, et4*TW:(et4+1)*TW], st[:, et4*TW:(et4+1)*TW],
                        self.ps[4][:, 0:TW]))
                    upd.append(uv)
                    if et4 < 3:
                        b.wait('t', 'ev', uv)
                b.wait('t', 'ev', upd[-1])
                b.selfwait('v')
        b.barrier()
        if getattr(self, 'l2_no_pass4', False):
            return
        consd = self.mk_delta(bias_at=lambda oc: self.cols[:, 17+oc:18+oc])
        self.gemm_sw(d["L2_attn_Wout"], 2048, D, consd, gw=256, wbuf_off=0,
                     in_stream=spill_o, ibuf='nrm', name="l2ao")
        b.barrier()

    def emit_l3_attn(self):
        b, d = self.b, self.dram
        ldc = b.dma_group('y', [(self.cols[:, 0:8], d["L3_attn_bo"], {})])
        ldm = b.dma('g', self.masks[:, 0:2048], d["l3_masks"])
        self.emit_ln()
        qp = self.abf(0, T)
        kp = self.abf(8 * 1024, T)
        vfm = self.abf(16 * 1024, T)
        vp = self.abf(24 * 1024, 32 * 130)
        aop = self.abf(33 * 1024, T)
        wqkv = self.abf(41 * 1024, 3 * 1024)
        pbuf = self.abf(47 * 1024, 8 * TW)
        b.op('v', lambda e, vp=vp: e.memset(
            vp[:].rearrange("p (a c) -> p a c", c=65)[:, :, 64:65], 1.0))
        b.op('v', lambda e: e.memset(self.scr2[:], 0.0))
        beta = float(64.0 ** -0.5)
        self._flash_state = {'last_exp': 0, 'last_pv': 0,
                             'tail_r': {0: 0, 1: 0}, 'tail_c': {0: 0, 1: 0},
                             'ov': []}
        WN = {"q": "L3_attn_Wq", "k": "L3_attn_Wk", "v": "L3_attn_Wv"}
        proj_done = 0
        stok = None
        for p in range(getattr(self, 'l3_pair_limit', None) or 8):
            b.wait('g', 'et', proj_done)
            wl = b.dma_group('g', [
                (wqkv[:, pi*1024 + kc*128: pi*1024 + (kc+1)*128],
                 d[WN[pn]][kc*128:(kc+1)*128, p*128:(p+1)*128], {})
                for pi, pn in enumerate(("q", "k", "v")) for kc in range(NKC)])
            b.wait_t('t', wl)
            pv = {}
            for pi, pn in enumerate(("q", "k", "v")):
                dstt = (qp, kp, vfm)[pi]
                for tt in range(NTT):
                    psq = self.ps[(pi * NTT + tt) % 2]
                    if pi * NTT + tt >= 2:
                        b.wait('t', 'es', pv[(pi * NTT + tt - 2)])
                    for kc in range(NKC):
                        b.op('t', lambda e, psq=psq, pi=pi, kc=kc, tt=tt: e.matmul(
                            psq[:, 0:TW],
                            wqkv[:, pi*1024 + kc*128: pi*1024 + (kc+1)*128],
                            self.nrm_tile(kc, tt), start=(kc == 0), stop=(kc == NKC - 1)))
                    mv = b.cnt['et']
                    b.wait('s', 'et', mv)
                    sc = {}
                    if pn == "q":
                        sc['scale'] = beta
                    pv[pi * NTT + tt] = b.op('s', lambda e, psq=psq, dstt=dstt, tt=tt, sc=sc: e.activation(
                        dstt[:, tt*TW:(tt+1)*TW], psq[:, 0:TW], AF.Copy, **sc))
            proj_done = b.cnt['et']
            if getattr(self, 'l3_stage', 9) < 2:
                continue
            cpv = []
            b.wait('t', 'es', pv[3 * NTT - 1])
            for jc in range(32):
                if cpv:
                    b.wait('t', 'ev', cpv[-1])
                b.op('t', lambda e, jc=jc, vfm=vfm: e.transpose(
                    self.psb[:, 0:128], vfm[:, jc*128:(jc+1)*128], self.ident[:]))
                mv = b.cnt['et']
                b.wait('v', 'et', mv)
                cpv.append(b.op('v', lambda e, jc=jc, vp=vp: e.tensor_copy(
                    vp[:, jc*130: jc*130 + 64], self.psb[:, 0:64])))
                cpv.append(b.op('v', lambda e, jc=jc, vp=vp: e.tensor_copy(
                    vp[:, jc*130 + 65: jc*130 + 129], self.psb[:, 64:128])))
            if getattr(self, 'l3_stage', 9) < 3:
                continue
            self._attn_bound(qp, kp)
            if getattr(self, 'l3_stage', 9) < 4:
                continue
            if stok is not None:
                b.wait_t('v', stok)
            self._attn_flash(qp, kp, vp, aop, pbuf, lambda qt: 4 * qt + 4)
            b.wait('y', 'ev', b.cnt['ev'])
            stok = b.dma('y', d["spill_b"][p*128:(p+1)*128, :], aop[:, 0:T])
        b.barrier()
        if getattr(self, 'l3_stage', 9) < 5:
            return
        consd = self.mk_delta()
        self.gemm_sw(d["L3_attn_Wo"], D, D, consd, gw=512, wbuf_off=0,
                     in_stream=d["spill_b"][0:1024, :], ibuf='arena',
                     ibuf_off=16 * 1024, name="l3o")
        b.barrier()
        bcol = self.cols[:, 0:8]
        for oc in range(8):
            vv = b.op('v', lambda e, oc=oc, bcol=bcol: e.tensor_scalar(
                self.evac[:, (oc % 2)*TW:(oc % 2)*TW + TW], self.ab[:, 0:TW],
                0.0, bcol[:, oc:oc+1], ALU.mult, ALU.add))
            b.wait('g', 'ev', vv)
            dv = None
            for tt in range(NTT):
                dv = b.dma('g', d["x_work"][oc*128:(oc+1)*128, tt*TW:(tt+1)*TW],
                           self.evac[:, (oc % 2)*TW:(oc % 2)*TW + TW], accum_op=ALU.add)
            b.wait_t('v', dv)
        b.barrier()

    def emit_l2_ffn(self):
        b, d = self.b, self.dram
        self.emit_ln()
        lv = b.dma('y', self.rows2[0:1, 8:12], d["L2_ffn_sc"])
        b.wait_t('v', lv)
        b.op('v', lambda e: e.tensor_scalar_mul(self.rows2[0:1, 12:13], self.rows2[0:1, 8:9], 4.0))
        b.op('v', lambda e: e.tensor_scalar_mul(self.rows2[0:1, 13:14], self.rows2[0:1, 9:10], 2.0))
        vv = b.op('v', lambda e: e.tensor_copy(self.rows2[0:1, 14:15], self.rows2[0:1, 10:11]))
        b.wait('t', 'ev', vv)
        for i in range(3):
            mv = b.op('t', lambda e, i=i: e.matmul(self.ps[6][:, 8+i:9+i], self.ones_f[0:1, :],
                                                   self.rows2[0:1, 12+i:13+i], start=True, stop=True))
        b.wait('v', 'et', mv)
        cv = None
        for i in range(3):
            cv = b.op('v', lambda e, i=i: e.tensor_copy(self.cols2[:, 40+i:41+i],
                                                        self.ps[6][:, 8+i:9+i]))
        b.wait('s', 'ev', cv)
        b.wait('t', 'ev', cv)
        b.barrier()
        wdst = self.abf(0, NKC * 4096)
        K0 = d["L2_ffn_k0"]
        subv = []
        for kc in range(NKC):
            for nb in range(8):
                half = (nb % 2) * TW
                gi = kc * 8 + nb
                if gi >= 2:
                    b.wait('g', 'ev', subv[gi - 2])
                l2 = b.dma_group('g', [
                    (self.scr[:, half:half+TW],
                     K0[kc*128:(kc+1)*128, nb*TW:(nb+1)*TW], {}),
                    (self.scr2[:, half:half+TW],
                     K0[1024 + kc*128: 1024 + (kc+1)*128, nb*TW:(nb+1)*TW], {})])
                b.wait_t('v', l2)
                subv.append(b.op('v', lambda e, kc=kc, nb=nb, half=half: e.tensor_sub(
                    wdst[:, kc*4096 + nb*TW: kc*4096 + (nb+1)*TW],
                    self.scr[:, half:half+TW], self.scr2[:, half:half+TW])))
        b.barrier()
        cons0 = self.mk_spill(d["spill_a"], AF.Relu,
                              scale_at=lambda oc: self.cols2[:, 40:41])
        self.gemm_rw(lambda kc, oc: wdst[:, kc*4096 + oc*128: kc*4096 + oc*128 + 128],
                     D, 4096, cons0, in_tile=self.nrm_tile, name="l2f0")
        b.barrier()
        def sgn_xform(og, buf, lv, wbuf, nkc, gw, ow):
            b.wait_t('v', lv)
            last = None
            for kc in range(nkc):
                w_ap = wbuf[:, buf*nkc*gw + kc*gw: buf*nkc*gw + kc*gw + ow]
                last = b.op('v', lambda e, w_ap=w_ap: e.tensor_scalar(
                    w_ap, w_ap, 2.0, -1.0, ALU.mult, ALU.add))
            return ('ev', last)
        cons1 = self.mk_spill(d["spill_b"], AF.Relu,
                              scale_at=lambda oc: self.cols2[:, 41:42])
        self.gemm_sw(d["L2_ffn_k1"], 4096, 4096, cons1, gw=256, wbuf_off=0,
                     in_stream=d["spill_a"], ibuf='nrm', w_xform=sgn_xform, name="l2f1")
        b.barrier()
        consd = self.mk_delta(scale_at=lambda oc: self.cols2[:, 42:43])
        self.gemm_sw(d["L2_ffn_k2"], 4096, D, consd, gw=256, wbuf_off=0,
                     in_stream=d["spill_b"], ibuf='nrm', w_xform=sgn_xform, name="l2f2")
        b.barrier()

    def emit_l1_ffn(self):
        b, d = self.b, self.dram
        lv = b.dma_group('y', [(self.cols[:, 0:32], d["L1_ffn_w1_s"], {}),
                               (self.cols[:, 32:64], d["L1_ffn_w2_s"], {}),
                               (self.cols2[:, 0:32], d["L1_ffn_h1_s"], {}),
                               (self.cols2[:, 32:64], d["L1_ffn_h2_s"], {})])
        b.wait_t('s', lv)
        b.wait_t('v', lv)
        self.emit_ln()
        consA = self.mk_spill(d["spill_b"], AF.Silu,
                              scale_at=lambda oc: self.cols[:, oc:oc+1])
        self.gemm_sw(d["L1_ffn_w1t"], D, 4096, consA, gw=512, wbuf_off=0,
                     in_tile=self.nrm_tile, name="l1fA")
        b.barrier()
        tbuf = self.abf(32 * 1024, 4 * TW)
        ebuf = self.abf(40 * 1024, 6 * TW)
        stateB = {"n": 0, "dma": [], "tl": []}
        def consB(oc, tt, ps, tok0):
            n = stateB["n"]; stateB["n"] += 1
            tb = tbuf[:, (n % 4)*TW:(n % 4)*TW + TW]
            if n >= 4:
                b.wait('y', 'ev', stateB["tl"][n - 4])
            tl = b.dma('y', tb, d["spill_b"][oc*128:(oc+1)*128, tok0:tok0+TW])
            dst = ebuf[:, (n % 6)*TW:(n % 6)*TW + TW]
            if n >= 6:
                b.wait_t('v', stateB["dma"][n - 6])
            b.wait('v', 'et', b.cnt['et'])
            b.wait_t('v', tl)
            v = b.op('v', lambda e, dst=dst, ps=ps, oc=oc, tb=tb: e.scalar_tensor_tensor(
                dst, ps[:, 0:TW], self.cols[:, 32+oc:33+oc], tb, ALU.mult, ALU.mult))
            stateB["tl"].append(v)
            b.wait('y', 'ev', v)
            dv = b.dma('y', d["spill_a"][oc*128:(oc+1)*128, tok0:tok0+TW], dst)
            stateB["dma"].append(dv)
            return ('ev', v)
        self.gemm_sw(d["L1_ffn_w2t"], D, 4096, consB, gw=512, wbuf_off=0,
                     in_tile=self.nrm_tile, name="l1fB")
        b.barrier()
        consC = self.mk_spill(d["spill_b"], AF.Gelu,
                              scale_at=lambda oc: self.cols2[:, oc:oc+1])
        self.gemm_sw(d["L1_ffn_h1t"], 4096, 4096, consC, gw=256, wbuf_off=0,
                     in_stream=d["spill_a"], ibuf='nrm', name="l1fC")
        b.barrier()
        consD = self.mk_spill(d["spill_a"], AF.Gelu,
                              scale_at=lambda oc: self.cols2[:, 32+oc:33+oc])
        self.gemm_sw(d["L1_ffn_h2t"], 4096, 4096, consD, gw=256, wbuf_off=0,
                     in_stream=d["spill_b"], ibuf='nrm', name="l1fD")
        b.barrier()
        lv = b.dma('y', self.cols[:, 0:8], d["L1_ffn_w3_s"])
        b.wait_t('v', lv)
        consE = self.mk_delta(scale_at=lambda oc: self.cols[:, oc:oc+1])
        self.gemm_sw(d["L1_ffn_w3t"], 4096, D, consE, gw=256, wbuf_off=0,
                     in_stream=d["spill_a"], ibuf='nrm', name="l1fE")
        b.barrier()


def host_prep(d, core, layers, x_start):
    ins = {}
    ins["x_in"] = np.ascontiguousarray(x_start[core].T).astype(np.float32)
    ins["ones_tab"] = np.ones((1, 128), np.float32)
    ins["ident_tab"] = np.eye(128, dtype=np.float32).astype(BF)
    if "l0_attn" in layers:
        ins["L0_attn_W_qkv"] = d["L0_attn_W_qkv"].astype(np.float32)
        ins["L0_attn_W_g"] = d["L0_attn_W_g"].astype(np.float32)
        ins["L0_attn_W_proj"] = d["L0_attn_W_proj"].astype(np.float32)
        ins["L0_attn_decay"] = d["L0_attn_decay"].reshape(1, 8).astype(np.float32)
        QK = 1024
        pos = np.arange(T, dtype=np.float32)
        base_scale = ((np.arange(0, QK, 2, dtype=np.float32) + 0.4 * QK) / (1.4 * QK))
        scale = base_scale[None, :] ** (pos[:, None] / np.float32(512.0))
        inv_freq = (1.0 / 10000.0 ** (np.arange(QK // 2, dtype=np.float32) / (QK // 2))
                    ).astype(np.float32)
        sinusoid = pos[:, None] * inv_freq[None, :]
        sn, cs = np.sin(sinusoid), np.cos(sinusoid)
        beta = np.float32(128.0 ** -0.5)
        rep = lambda a: np.ascontiguousarray(np.repeat(a, 2, axis=1).T.astype(np.float32))
        ins["xpos_cq"] = rep(cs * scale) * beta
        ins["xpos_sq"] = rep(sn * scale) * beta
        ins["xpos_ck"] = rep(cs / scale)
        ins["xpos_sk"] = rep(sn / scale)
        R = np.zeros((128, 128), np.float32)
        ii = np.arange(0, 128, 2)
        R[ii + 1, ii] = -1.0
        R[ii, ii + 1] = 1.0
        ins["rot_tab"] = R.astype(BF)
        jj = np.arange(128, dtype=np.float32)
        E = jj[None, :] - jj[:, None]
        ins["exp_tab"] = np.where(E >= 0, E, 1e9).astype(np.float32)
        ins["iota_col"] = jj.reshape(128, 1).copy()
    if "l0_ffn" in layers:
        ins["L0_ffn_lng"] = np.ascontiguousarray(d["L0_ffn_ln_gamma"].reshape(NKC, 128).T)
        ins["L0_ffn_lnb"] = np.ascontiguousarray(d["L0_ffn_ln_beta"].reshape(NKC, 128).T)
        ins["L0_ffn_fc1_q"] = d["L0_ffn_fc1_q"]
        ins["L0_ffn_fc1_scales"] = d["L0_ffn_fc1_scales"]
        ins["L0_ffn_fc1_bias"] = np.ascontiguousarray(d["L0_ffn_fc1_bias"].reshape(32, 128).T)
        ins["L0_ffn_fc2_q"] = d["L0_ffn_fc2_q"]
        ins["L0_ffn_fc2_scales"] = d["L0_ffn_fc2_scales"]
        ins["L0_ffn_fc2_bias"] = np.ascontiguousarray(d["L0_ffn_fc2_bias"].reshape(8, 128).T)
    if "l1_attn" in layers:
        ins["L1_attn_W1t"] = np.ascontiguousarray(d["L1_attn_W1"].T)
        ins["L1_attn_b1"] = d["L1_attn_b1"].reshape(8, 128).T.copy()
        ins["L1_attn_W2t"] = np.ascontiguousarray(d["L1_attn_W2"].T)
        ins["L1_attn_b2"] = d["L1_attn_b2"].reshape(8, 128).T.copy()
        ins["L1_attn_Wgt"] = np.ascontiguousarray(d["L1_attn_Wg"].T)
        ins["L1_attn_bg"] = d["L1_attn_bg"].reshape(24, 128).T.copy()
        ins["L1_attn_Wot"] = np.ascontiguousarray(d["L1_attn_Wo"].T)
        ins["L1_attn_bo"] = d["L1_attn_bo"].reshape(8, 128).T.copy()
        TWl = 512
        mm = np.zeros((128, 5 * TWl), np.float32)
        for k in range(5):
            r = np.arange(128)[:, None] + 128 * k
            c = np.arange(TWl)[None, :]
            mm[:, k*TWl:(k+1)*TWl] = (r <= c + 63)
        ins["l1_masks"] = mm.astype(BF)
    if "l2_attn" in layers:
        ins["L2_attn_Wv"] = d["L2_attn_Wv"]
        ins["L2_attn_bv"] = np.ascontiguousarray(d["L2_attn_bv"].reshape(16, 128).T)
        ins["L2_attn_Wg"] = d["L2_attn_Wg"]
        ins["L2_attn_bg"] = np.ascontiguousarray(d["L2_attn_bg"].reshape(16, 128).T)
        ins["L2_attn_Win"] = d["L2_attn_Win"]
        ins["L2_attn_binc"] = d["L2_attn_bin_"].reshape(1, 128).T.copy()
        ins["L2_attn_Wout"] = d["L2_attn_Wout"]
        ins["L2_attn_bout"] = np.ascontiguousarray(d["L2_attn_bout"].reshape(8, 128).T)
        ins["L2_attn_affs"] = np.stack([d["L2_attn_" + nm] for nm in
                                        ("g_qq", "b_qq", "g_qk", "b_qk",
                                         "g_lq", "b_lq", "g_lk", "b_lk")], axis=1).astype(np.float32)
        G = 256
        m = np.zeros((128, 512), np.float32)
        n = np.arange(G)
        m[:, 0:256] = (np.arange(128)[:, None] <= n[None, :])
        m[:, 256:512] = ((128 + np.arange(128))[:, None] <= n[None, :])
        ins["l2_masks"] = m.astype(BF)
    if "l2_ffn" in layers:
        ins["L2_ffn_k0"] = d["L2_ffn_k0"].astype(np.uint8)
        ins["L2_ffn_k1"] = d["L2_ffn_k1"].astype(np.uint8)
        ins["L2_ffn_k2"] = d["L2_ffn_k2"].astype(np.uint8)
        ins["L2_ffn_sc"] = np.array([[d["L2_ffn_s0"], d["L2_ffn_s1"], d["L2_ffn_s2"], 0.0]],
                                    np.float32)
    if "l1_ffn" in layers:
        for nm, tr in (("w1", "L1_ffn_w1_w"), ("w2", "L1_ffn_w2_w"),
                       ("h1", "L1_ffn_h1_w"), ("h2", "L1_ffn_h2_w"),
                       ("w3", "L1_ffn_w3_w")):
            ins[f"L1_ffn_{nm}t"] = np.ascontiguousarray(d[tr].T)
        for nm in ("w1_s", "w2_s", "h1_s", "h2_s", "w3_s"):
            v = d[f"L1_ffn_{nm}"]
            ins[f"L1_ffn_{nm}"] = np.ascontiguousarray(v.reshape(-1, 128).T)
    if "l3_attn" in layers:
        for nm in ("Wq", "Wk", "Wv", "Wo"):
            ins[f"L3_attn_{nm}"] = d[f"L3_attn_{nm}"]
        ins["L3_attn_bo"] = d["L3_attn_bo"].reshape(8, 128).T.copy()
        TWl = 512
        mm = np.zeros((128, 4 * TWl), np.float32)
        for k in range(4):
            r = np.arange(128)[:, None] + 128 * k
            c = np.arange(TWl)[None, :]
            mm[:, k*TWl:(k+1)*TWl] = (r <= c)
        ins["l3_masks"] = mm.astype(BF)
    if "l3_ffn" in layers:
        ins["L3_ffn_W1t"] = np.ascontiguousarray(d["L3_ffn_W1"].T)
        ins["L3_ffn_b1"] = np.ascontiguousarray(d["L3_ffn_b1"].reshape(32, 128).T)
        ins["L3_ffn_W2t"] = np.ascontiguousarray(d["L3_ffn_W2"].T)
        ins["L3_ffn_b2"] = np.ascontiguousarray(d["L3_ffn_b2"].reshape(8, 128).T)
    return ins


def build(layers, dbg=False, dbg_hook=None):
    nc = bass.Bass("TRN2", target_bir_lowering=False)
    with ExitStack() as stack:
        mk = MK(nc, stack, layers, dbg=dbg)
        mk.emit_init()
        for lay in layers:
            getattr(mk, "emit_" + lay)()
            if dbg_hook is not None:
                dbg_hook(mk, lay)
        mk.emit_finish()
        mk.b.finalize_into_block()
    return nc, mk


def dump_nrm(mk, lay=None):
    b, d = mk.b, mk.dram
    b.barrier()
    tok = b.dma_group('y', [(d["dbg_nrm"][kc*128:(kc+1)*128, :],
                             mk.nrm[:, kc*T:(kc+1)*T], {}) for kc in range(NKC)])
    b.wait_t('y', tok)
    b.barrier()


def dump_arena(mk):
    b, d = mk.b, mk.dram
    b.barrier()
    tok = b.dma_group('y', [(d["dbg_w"], mk.ar_bf[:, 0:32768], {})])
    b.wait_t('y', tok)
    b.barrier()


LAYERS = ["l0_attn", "l0_ffn", "l1_attn", "l1_ffn",
          "l2_attn", "l2_ffn", "l3_attn", "l3_ffn"]
_CACHE = {}


def _get_program():
    if "nc" not in _CACHE:
        nc, mkk = build(LAYERS)
        _CACHE["nc"] = nc
    return _CACHE["nc"]


def kernel(**inputs):
    nc = _get_program()
    x = np.asarray(inputs["L0_attn_hidden_states"], np.float32)
    ins_list = [host_prep(inputs, core, LAYERS, x) for core in range(8)]
    res = run_bass_kernel_spmd(nc, ins_list, core_ids=list(range(8)))
    out = np.empty((8, T, D), np.float32)
    for core in range(8):
        out[core] = np.asarray(res.results[core]["x_out"], np.float32).T
    return out


# version 03
# speedup vs baseline: 1.7237x; 1.3410x over previous
import math
from contextlib import ExitStack

import numpy as np
import ml_dtypes

import concourse.bass as bass
import concourse.mybir as mybir
from concourse.bass_utils import run_bass_kernel_spmd

BF = ml_dtypes.bfloat16
F32 = mybir.dt.float32
BF16 = mybir.dt.bfloat16
I32 = mybir.dt.int32
U8 = mybir.dt.uint8
AF = mybir.ActivationFunctionType
ALU = mybir.AluOpType
AX = mybir.AxisListType

T = 4096
D = 1024
NKC = D // 128
TW = 512
NTT = T // TW

ENGS = ('t', 'v', 's', 'g', 'y')
SEM_OF = {'t': 'et', 'v': 'ev', 's': 'es'}
NLANE = 4


class Bld:

    def __init__(self, nc, stack):
        self.nc = nc
        self.q = {e: [] for e in ENGS}
        self.semh = {}
        self.cnt = {}
        for name in SEM_OF.values():
            self.semh[name] = stack.enter_context(nc.semaphore(name))
            self.cnt[name] = 0
        for qn in ('g', 'y'):
            for i in range(NLANE):
                nm = f"{qn}{i}"
                self.semh[nm] = stack.enter_context(nc.semaphore(nm))
                self.cnt[nm] = 0
        self.rot = {'g': 0, 'y': 0}
        self.lane_prev = {f"{qn}{i}": 0 for qn in ('g', 'y') for i in range(NLANE)}

    def op(self, eng, fn, n=1):
        assert eng in ('t', 'v', 's')
        sem = SEM_OF[eng]
        self.cnt[sem] += n
        h = self.semh[sem]
        def f(e, fn=fn, h=h, n=n):
            fn(e).then_inc(h, n)
        self.q[eng].append(f)
        return self.cnt[sem]

    def _lane(self, qn):
        i = self.rot[qn] % NLANE
        self.rot[qn] += 1
        return f"{qn}{i}"

    def dma(self, qn, dst, src, **kw):
        return self.dma_group(qn, [(dst, src, kw)])

    def dma_group(self, qn, items):
        assert qn in ('g', 'y')
        lane = self._lane(qn)
        if self.lane_prev[lane] > 0:
            self.wait(qn, lane, self.lane_prev[lane])
        h = self.semh[lane]
        for dst, src, kw in items:
            self.cnt[lane] += 16
            def f(e, dst=dst, src=src, kw=kw, h=h):
                e.dma_start(dst, src, **kw).then_inc(h, 16)
            self.q[qn].append(f)
        self.lane_prev[lane] = self.cnt[lane]
        return (lane, self.cnt[lane])

    def wait(self, eng, sem, v):
        assert v <= self.cnt[sem], f"wait {eng} {sem} {v} > {self.cnt[sem]}"
        if v <= 0:
            return
        h = self.semh[sem]
        self.q[eng].append(lambda e, h=h, v=v: e.wait_ge(h, v))

    def wait_t(self, eng, tok):
        if tok is not None:
            self.wait(eng, tok[0], tok[1])

    def selfwait(self, eng):
        sem = SEM_OF[eng]
        self.wait(eng, sem, self.cnt[sem])

    def barrier(self):
        snap = dict(self.cnt)
        own = {'t': 'et', 'v': 'ev', 's': 'es'}
        for eng in ENGS:
            for sem, v in snap.items():
                if own.get(eng) != sem:
                    self.wait(eng, sem, v)

    def finalize_into_block(self):
        nc = self.nc
        with nc.Block() as blk:
            @blk.tensor
            def _(e):
                for f in self.q['t']: f(e)
            @blk.vector
            def _(e):
                for f in self.q['v']: f(e)
            @blk.scalar
            def _(e):
                for f in self.q['s']: f(e)
            @blk.gpsimd
            def _(e):
                for f in self.q['g']: f(e)
            @blk.sync
            def _(e):
                for f in self.q['y']: f(e)


class MK:
    def __init__(self, nc, stack, layers, dbg=False):
        self.nc = nc
        self.b = Bld(nc, stack)
        self.layers = layers
        self.dbg = dbg
        self.dram = {}
        self._declare_dram()
        self._alloc_sbuf(stack)
        self.ps = [stack.enter_context(nc.psum_tensor(f"ps{i}", [128, 512], F32))
                   for i in range(7)]
        self.psb = stack.enter_context(nc.psum_tensor("psb", [128, 1024], BF16))

    def dt(self, name, shape, dtype, kind="ExternalInput"):
        t = self.nc.dram_tensor(name, list(shape), dtype, kind=kind)
        self.dram[name] = t.ap()
        return self.dram[name]

    def _declare_dram(self):
        dt = self.dt
        L = self.layers
        dt("x_in", [D, T], F32)
        dt("x_out", [D, T], F32, kind="ExternalOutput")
        dt("x_work", [D, T], F32, kind="Internal")
        spk = "ExternalOutput" if self.dbg else "Internal"
        dt("spill_a", [4096, T], BF16, kind=spk)
        dt("spill_b", [4096, T], BF16, kind=spk)
        if self.dbg:
            dt("dbg_nrm", [D, T], BF16, kind="ExternalOutput")
            dt("dbg_w", [128, 32768], BF16, kind="ExternalOutput")
        dt("ones_tab", [1, 128], F32)
        dt("ident_tab", [128, 128], BF16)
        if "l0_attn" in L:
            dt("L0_attn_W_qkv", [D, 4096], F32)
            dt("L0_attn_W_g", [D, 2048], F32)
            dt("L0_attn_W_proj", [2048, D], F32)
            dt("L0_attn_decay", [1, 8], F32)
            for nm in ("xpos_cq", "xpos_sq", "xpos_ck", "xpos_sk"):
                dt(nm, [D, T], F32)
            dt("rot_tab", [128, 128], BF16)
            dt("exp_tab", [128, 128], F32)
            dt("iota_col", [128, 1], F32)
        if "l0_ffn" in L:
            dt("L0_ffn_lng", [128, NKC], F32)
            dt("L0_ffn_lnb", [128, NKC], F32)
            dt("L0_ffn_fc1_q", [D, 4096], I32)
            dt("L0_ffn_fc1_scales", [8, 4096], F32)
            dt("L0_ffn_fc1_bias", [128, 32], F32)
            dt("L0_ffn_fc2_q", [4096, D], I32)
            dt("L0_ffn_fc2_scales", [32, D], F32)
            dt("L0_ffn_fc2_bias", [128, 8], F32)
        if "l1_attn" in L:
            dt("L1_attn_W1t", [D, D], F32)
            dt("L1_attn_b1", [128, 8], F32)
            dt("L1_attn_W2t", [D, D], F32)
            dt("L1_attn_b2", [128, 8], F32)
            dt("L1_attn_Wgt", [D, 3072], F32)
            dt("L1_attn_bg", [128, 24], F32)
            dt("L1_attn_Wot", [D, D], F32)
            dt("L1_attn_bo", [128, 8], F32)
            dt("l1_masks", [128, 5 * TW], BF16)
        if "l1_ffn" in L:
            dt("L1_ffn_w1t", [D, 4096], F32)
            dt("L1_ffn_w1_s", [128, 32], F32)
            dt("L1_ffn_w2t", [D, 4096], F32)
            dt("L1_ffn_w2_s", [128, 32], F32)
            dt("L1_ffn_h1t", [4096, 4096], F32)
            dt("L1_ffn_h1_s", [128, 32], F32)
            dt("L1_ffn_h2t", [4096, 4096], F32)
            dt("L1_ffn_h2_s", [128, 32], F32)
            dt("L1_ffn_w3t", [4096, D], F32)
            dt("L1_ffn_w3_s", [128, 8], F32)
        if "l2_attn" in L:
            dt("L2_attn_Wv", [D, 2048], F32)
            dt("L2_attn_bv", [128, 16], F32)
            dt("L2_attn_Wg", [D, 2048], F32)
            dt("L2_attn_bg", [128, 16], F32)
            dt("L2_attn_Win", [D, 128], F32)
            dt("L2_attn_binc", [128, 1], F32)
            dt("L2_attn_Wout", [2048, D], F32)
            dt("L2_attn_bout", [128, 8], F32)
            dt("L2_attn_affs", [128, 8], F32)
            dt("l2_masks", [128, 2 * 256], BF16)
        if "l2_ffn" in L:
            dt("L2_ffn_k0", [2048, 4096], U8)
            dt("L2_ffn_k1", [4096, 4096], U8)
            dt("L2_ffn_k2", [4096, D], U8)
            dt("L2_ffn_sc", [1, 4], F32)
        if "l3_attn" in L:
            dt("L3_attn_Wq", [D, D], F32)
            dt("L3_attn_Wk", [D, D], F32)
            dt("L3_attn_Wv", [D, D], F32)
            dt("L3_attn_Wo", [D, D], F32)
            dt("L3_attn_bo", [128, 8], F32)
            dt("l3_masks", [128, 4 * TW], BF16)
        if "l3_ffn" in L:
            dt("L3_ffn_W1t", [D, 4096], F32)
            dt("L3_ffn_b1", [128, 32], F32)
            dt("L3_ffn_W2t", [4096, D], F32)
            dt("L3_ffn_b2", [128, 8], F32)

    def _alloc_sbuf(self, stack):
        nc = self.nc
        def sb(name, shape, dtype):
            t = stack.enter_context(nc.sbuf_tensor(name, list(shape), dtype))
            setattr(self, name, t)
            return t
        sb("nrm", [128, NKC * T], BF16)
        sb("xtile", [128, NKC * TW], F32)
        sb("evac", [128, 6 * TW], F32)
        sb("rows", [1, 2048], F32)
        sb("rows2", [1, 4096], F32)
        sb("ab", [128, 2 * TW], F32)
        sb("cols", [128, 64], F32)
        sb("cols2", [128, 64], F32)
        sb("ones_f", [1, 128], F32)
        sb("ones_c", [128, 1], F32)
        sb("ones_b", [1, 128], BF16)
        sb("ident", [128, 128], BF16)
        sb("bcrf", [128, 128], F32)
        sb("masks", [128, 2688], BF16)
        sb("scr", [128, 2 * TW], F32)
        sb("scr2", [128, 2 * TW], F32)
        self.ARENA = 72 * 1024
        sb("arena8", [128, self.ARENA], U8)
        self.ar_bf = self.arena8.bitcast(BF16)
        self.ar_f32 = self.arena8.bitcast(F32)
        rem = nc.sbuf_bytes_remaining
        assert rem >= 0, f"SBUF over budget by {-rem}"
        self.sbuf_remaining = rem

    def abf(self, off_b, width):
        assert off_b % 2 == 0 and off_b + 2 * width <= self.ARENA, (off_b, width)
        return self.ar_bf[:, off_b // 2: off_b // 2 + width]

    def af32(self, off_b, width):
        assert off_b % 4 == 0 and off_b + 4 * width <= self.ARENA, (off_b, width)
        return self.ar_f32[:, off_b // 4: off_b // 4 + width]

    def emit_init(self):
        b, d = self.b, self.dram
        b.dma('y', self.ones_f[:], d["ones_tab"])
        b.dma('y', self.ident[:], d["ident_tab"])
        b.op('v', lambda e: e.memset(self.ones_b[:], 1.0))
        b.op('v', lambda e: e.memset(self.bcrf[:], 0.0))
        b.op('v', lambda e: e.memset(self.bcrf[0:1, :], 1.0))
        b.op('v', lambda e: e.memset(self.ones_c[:], 1.0))
        b.dma('y', d["x_work"], d["x_in"])
        b.barrier()

    def emit_finish(self):
        b, d = self.b, self.dram
        b.barrier()
        done = b.dma('y', d["x_out"], d["x_work"])
        b.wait_t('y', done)
        b.barrier()

    def emit_ln(self, gamma_col=None, beta_col=None, eps=1e-5):
        b, d = self.b, self.dram
        x_src = d["x_work"]
        nrm, xtile, R, ab, scr = self.nrm, self.xtile, self.rows, self.ab, self.scr
        psS, psQ, psB = self.ps[4], self.ps[5], self.ps[6]
        for tt in range(NTT):
            t0 = tt * TW
            ldv = b.dma_group('y', [(xtile[:, kc*TW:(kc+1)*TW],
                                     x_src[kc*128:(kc+1)*128, t0:t0+TW], {})
                                    for kc in range(NKC)])
            b.wait_t('t', ldv)
            for kc in range(NKC):
                b.op('t', lambda e, kc=kc: e.matmul(
                    psS[0:1, 0:TW], self.ones_c[:, 0:1], xtile[:, kc*TW:(kc+1)*TW],
                    start=(kc == 0), stop=(kc == NKC - 1)))
            et_s1 = b.cnt['et']
            b.wait_t('s', ldv)
            sqv = []
            mm2 = []
            for kc in range(NKC):
                half = (kc % 2) * TW
                if kc >= 2:
                    b.wait('s', 'et', mm2[kc - 2])
                sqv.append(b.op('s', lambda e, kc=kc, half=half: e.activation(
                    scr[:, half:half+TW], xtile[:, kc*TW:(kc+1)*TW], AF.Square)))
                b.wait('t', 'es', sqv[kc])
                mm2.append(b.op('t', lambda e, kc=kc, half=half: e.matmul(
                    psQ[0:1, 0:TW], self.ones_c[:, 0:1], scr[:, half:half+TW],
                    start=(kc == 0), stop=(kc == NKC - 1))))
            mmv = b.cnt['et']
            b.wait('v', 'et', mmv)
            b.op('v', lambda e: e.tensor_scalar_mul(R[0:1, 0:TW], psS[0:1, 0:TW], 1.0 / D))
            b.op('v', lambda e: e.tensor_scalar_mul(R[0:1, TW:2*TW], psQ[0:1, 0:TW], 1.0 / D))
            b.selfwait('v')
            b.op('v', lambda e: e.tensor_mul(R[0:1, 2*TW:3*TW], R[0:1, 0:TW], R[0:1, 0:TW]))
            b.selfwait('v')
            b.op('v', lambda e: e.tensor_scalar(R[0:1, 3*TW:4*TW], R[0:1, 2*TW:3*TW],
                                                -1.0, eps, ALU.mult, ALU.add))
            b.selfwait('v')
            vv = b.op('v', lambda e: e.tensor_add(R[0:1, 3*TW:4*TW], R[0:1, 3*TW:4*TW],
                                                  R[0:1, TW:2*TW]))
            b.wait('s', 'ev', vv)
            sv = b.op('s', lambda e: e.activation(R[0:1, 2*TW:3*TW], R[0:1, 3*TW:4*TW], AF.Sqrt))
            b.wait('v', 'es', sv)
            b.op('v', lambda e: e.reciprocal(R[0:1, 3*TW:4*TW], R[0:1, 2*TW:3*TW]))
            vv = b.op('v', lambda e: e.tensor_scalar_mul(R[0:1, 2*TW:3*TW],
                                                         R[0:1, 0:TW], -1.0))
            b.wait('t', 'ev', vv)
            b.op('t', lambda e: e.matmul(psB[:, 0:TW], self.ones_f[0:1, :],
                                         R[0:1, 3*TW:4*TW], start=True, stop=True))
            mmv = b.op('t', lambda e: e.matmul(psS[:, 0:TW], self.ones_f[0:1, :],
                                               R[0:1, 2*TW:3*TW], start=True, stop=True))
            b.wait('v', 'et', mmv)
            b.op('v', lambda e: e.tensor_copy(ab[:, 0:TW], psB[:, 0:TW]))
            b.op('v', lambda e: e.tensor_copy(ab[:, TW:2*TW], psS[:, 0:TW]))
            b.selfwait('v')
            for kc in range(NKC):
                half = (kc % 2) * TW
                dst = nrm[:, kc*T + t0: kc*T + t0 + TW]
                b.op('v', lambda e, kc=kc, half=half: e.tensor_add(
                    scr[:, half:half+TW], xtile[:, kc*TW:(kc+1)*TW], ab[:, TW:2*TW]))
                b.selfwait('v')
                b.op('v', lambda e, half=half, dst=dst: e.tensor_mul(
                    dst, scr[:, half:half+TW], ab[:, 0:TW]))
                if gamma_col is not None:
                    gk = gamma_col(kc); bk = beta_col(kc)
                    b.selfwait('v')
                    b.op('v', lambda e, dst=dst, gk=gk, bk=bk: e.tensor_scalar(
                        dst, dst, gk, bk, ALU.mult, ALU.add))
            b.wait('y', 'ev', b.cnt['ev'])
            b.wait('y', 'es', b.cnt['es'])
            b.wait('y', 'et', b.cnt['et'])
        b.barrier()

    def gemm_sw(self, W_dram, Kdim, Ndim, consume, *, in_tile=None, in_stream=None,
                gw=512, toks=T, wbuf_off=0, ibuf='arena', ibuf_off=None, name="",
                w_xform=None):
        b = self.b
        nkc = Kdim // 128
        nog = (Ndim + gw - 1) // gw
        ntt = toks // TW
        wbuf = self.abf(wbuf_off, 2 * nkc * gw)
        if in_stream is not None:
            if ibuf == 'nrm':
                assert 2 * nkc * TW <= NKC * T
                ibuft = self.nrm[:, 0: 2 * nkc * TW]
            else:
                ibuft = self.abf(ibuf_off, 2 * nkc * TW)
        PS = self.ps[0:4]
        group_after = {}
        tile_after = []
        cons = []
        ngroups = 0
        git = 0
        wl = {}
        ilvs = {}

        def issue_tile(g):
            half = g % 2
            if g >= 2:
                b.wait('y', 'et', tile_after[g - 2])
            assert nkc % 4 == 0
            tok0 = (g % ntt) * TW
            ilvs[g] = b.dma_group('y', [
                (ibuft[:, half*nkc*TW + g4*4*TW: half*nkc*TW + (g4+1)*4*TW]
                 .rearrange("p (k c) -> p k c", k=4),
                 in_stream[g4*512:(g4+1)*512, tok0:tok0+TW]
                 .rearrange("(k p) c -> p k c", p=128), {})
                for g4 in range(nkc // 4)])
        for og in range(nog):
            buf = og % 2
            ow = min(gw, Ndim - og * gw)
            noc = (ow + 127) // 128
            if og >= 2:
                b.wait('g', 'et', group_after[og - 2])
            lv = b.dma_group('g', [(wbuf[:, buf*nkc*gw + kc*gw: buf*nkc*gw + kc*gw + ow],
                                    W_dram[kc*128:(kc+1)*128, og*gw:og*gw + ow], {})
                                   for kc in range(nkc)])
            if w_xform is not None:
                wl[og] = w_xform(og, buf, lv, wbuf, nkc, gw, ow)
            else:
                wl[og] = lv
            for tt in range(ntt):
                tok0 = tt * TW
                if in_stream is not None:
                    half = git % 2
                    if git == 0:
                        issue_tile(0)
                    if git + 1 < nog * ntt:
                        issue_tile(git + 1)
                for oc in range(noc):
                    ocg = og * (gw // 128) + oc
                    ps = PS[ngroups % 4]
                    if ngroups >= 4:
                        sem, val = cons[ngroups - 4]
                        b.wait('t', sem, val)
                    if oc == 0 and tt == 0:
                        b.wait_t('t', wl[og])
                    if in_stream is not None and oc == 0:
                        b.wait_t('t', ilvs[git])
                    for kc in range(nkc):
                        if in_stream is not None:
                            rhs = ibuft[:, half*nkc*TW + kc*TW: half*nkc*TW + (kc+1)*TW]
                        else:
                            rhs = in_tile(kc, tt)
                        b.op('t', lambda e, ps=ps, buf=buf, kc=kc, oc=oc, ow=ow, rhs=rhs: e.matmul(
                            ps[:, 0:TW],
                            wbuf[:, buf*nkc*gw + kc*gw + oc*128: buf*nkc*gw + kc*gw + min(oc*128+128, ow)],
                            rhs, start=(kc == 0), stop=(kc == nkc - 1)))
                    group_after[og] = b.cnt['et']
                    cons.append(consume(ocg, tt, ps, tok0))
                    ngroups += 1
                tile_after.append(b.cnt['et'])
                git += 1
        return cons

    def gemm_rw(self, w_sb_fn, Kdim, Ndim, consume, *, in_tile=None, in_stream=None,
                toks=T, ibuf='nrm', ibuf_off=None, name=""):
        b = self.b
        nkc = Kdim // 128
        noc = (Ndim + 127) // 128
        ntt = toks // TW
        if in_stream is not None:
            if ibuf == 'nrm':
                assert 2 * nkc * TW <= NKC * T
                ibuft = self.nrm[:, 0: 2 * nkc * TW]
            else:
                ibuft = self.abf(ibuf_off, 2 * nkc * TW)
        PS = self.ps[0:4]
        cons = []
        tile_after = []
        ngroups = 0
        ilvs = {}

        def issue_tile(g):
            half = g % 2
            if g >= 2:
                b.wait('y', 'et', tile_after[g - 2])
            assert nkc % 4 == 0
            tok0 = g * TW
            ilvs[g] = b.dma_group('y', [
                (ibuft[:, half*nkc*TW + g4*4*TW: half*nkc*TW + (g4+1)*4*TW]
                 .rearrange("p (k c) -> p k c", k=4),
                 in_stream[g4*512:(g4+1)*512, tok0:tok0+TW]
                 .rearrange("(k p) c -> p k c", p=128), {})
                for g4 in range(nkc // 4)])
        for tt in range(ntt):
            tok0 = tt * TW
            if in_stream is not None:
                half = tt % 2
                if tt == 0:
                    issue_tile(0)
                if tt + 1 < ntt:
                    issue_tile(tt + 1)
            for oc in range(noc):
                ps = PS[ngroups % 4]
                if ngroups >= 4:
                    sem, val = cons[ngroups - 4]
                    b.wait('t', sem, val)
                if in_stream is not None and oc == 0:
                    b.wait_t('t', ilvs[tt])
                for kc in range(nkc):
                    if in_stream is not None:
                        rhs = ibuft[:, half*nkc*TW + kc*TW: half*nkc*TW + (kc+1)*TW]
                    else:
                        rhs = in_tile(kc, tt)
                    b.op('t', lambda e, ps=ps, kc=kc, oc=oc, rhs=rhs: e.matmul(
                        ps[:, 0:TW], w_sb_fn(kc, oc), rhs,
                        start=(kc == 0), stop=(kc == nkc - 1)))
                cons.append(consume(oc, tt, ps, tok0))
                ngroups += 1
            tile_after.append(b.cnt['et'])
        return cons

    def nrm_tile(self, kc, tt):
        return self.nrm[:, kc*T + tt*TW: kc*T + tt*TW + TW]

    def mk_delta(self, bias_at=None, scale=None, scale_at=None):
        b, d = self.b, self.dram
        evac = self.evac
        state = {"n": 0, "dma": []}
        def consume(oc, tt, ps, tok0):
            n = state["n"]; state["n"] += 1
            dst = evac[:, (n % 6) * TW:(n % 6) * TW + TW]
            if n >= 6:
                b.wait_t('v', state["dma"][n - 6])
            b.wait('v', 'et', b.cnt['et'])
            if scale_at is not None:
                v = b.op('v', lambda e, dst=dst, ps=ps, oc=oc: e.tensor_scalar(
                    dst, ps[:, 0:TW], scale_at(oc), None, ALU.mult))
            elif bias_at is not None and scale is not None:
                v = b.op('v', lambda e, dst=dst, ps=ps, oc=oc: e.tensor_scalar(
                    dst, ps[:, 0:TW], scale, bias_at(oc), ALU.mult, ALU.add))
            elif bias_at is not None:
                v = b.op('v', lambda e, dst=dst, ps=ps, oc=oc: e.tensor_scalar(
                    dst, ps[:, 0:TW], bias_at(oc), None, ALU.add))
            elif scale is not None:
                v = b.op('v', lambda e, dst=dst, ps=ps: e.tensor_scalar(
                    dst, ps[:, 0:TW], scale, None, ALU.mult))
            else:
                v = b.op('v', lambda e, dst=dst, ps=ps: e.tensor_copy(dst, ps[:, 0:TW]))
            b.wait('g', 'ev', v)
            dv = b.dma('g', d["x_work"][oc*128:(oc+1)*128, tok0:tok0+TW], dst,
                       accum_op=ALU.add)
            state["dma"].append(dv)
            return ('ev', v)
        return consume

    def mk_spill(self, spill, act, bias_at=None, scale_at=None, scale=None,
                 ebuf_off=64 * 1024):
        b = self.b
        state = {"n": 0, "dma": []}
        ebuf = self.abf(ebuf_off, 8 * TW)
        def consume(oc, tt, ps, tok0):
            n = state["n"]; state["n"] += 1
            dst = ebuf[:, (n % 8) * TW:(n % 8) * TW + TW]
            if n >= 8:
                b.wait_t('s', state["dma"][n - 8])
            b.wait('s', 'et', b.cnt['et'])
            kw = {}
            if bias_at is not None:
                kw['bias'] = bias_at(oc)
            if scale_at is not None:
                kw['scale'] = scale_at(oc)
            elif scale is not None:
                kw['scale'] = scale
            v = b.op('s', lambda e, dst=dst, ps=ps, kw=kw: e.activation(
                dst, ps[:, 0:TW], act, **kw))
            b.wait('y', 'es', v)
            dv = b.dma('y', spill[oc*128:(oc+1)*128, tok0:tok0+TW], dst)
            state["dma"].append(dv)
            return ('es', v)
        return consume

    def emit_l3_ffn(self):
        b, d = self.b, self.dram
        ldb = b.dma_group('y', [(self.cols[:, 0:32], d["L3_ffn_b1"], {}),
                                (self.cols2[:, 0:8], d["L3_ffn_b2"], {})])
        b.wait_t('s', ldb)
        b.wait_t('v', ldb)
        self.emit_ln()
        self._ternarize("L3_ffn_W1t", D, 4096, wdst_off=0)
        b.barrier()
        w1 = self.abf(0, NKC * 4096)
        cons1 = self.mk_spill(d["spill_a"], AF.Gelu,
                              bias_at=lambda oc: self.cols[:, oc:oc+1])
        self.gemm_rw(lambda kc, oc: w1[:, kc*4096 + oc*128: kc*4096 + oc*128 + 128],
                     D, 4096, cons1, in_tile=self.nrm_tile, name="l3f1")
        b.barrier()
        self._ternarize("L3_ffn_W2t", 4096, D, wdst_off=0)
        b.barrier()
        w2 = self.abf(0, 32 * 1024)
        consd = self.mk_delta(bias_at=lambda oc: self.cols2[:, oc:oc+1])
        self.gemm_rw(lambda kc, oc: w2[:, kc*1024 + oc*128: kc*1024 + oc*128 + 128],
                     4096, D, consd, in_stream=d["spill_a"], ibuf='nrm', name="l3f2")
        b.barrier()

    def _ternarize(self, wname, Kdim, Ndim, wdst_off):
        b, d = self.b, self.dram
        b.barrier()
        nkc = Kdim // 128
        W = d[wname]
        xt, scr, scr2 = self.xtile, self.scr, self.scr2
        for kc in range(nkc):
            lv = b.dma('y', xt[:, 0:Ndim], W[kc*128:(kc+1)*128, 0:Ndim])
            b.wait_t('v', lv)
            vv = b.op('v', lambda e, kc=kc: e.reduce_sum(
                scr2[:, kc:kc+1], xt[:, 0:Ndim], axis=AX.X, apply_absolute_value=True))
            b.wait('y', 'ev', vv)
        b.selfwait('v')
        vv = b.op('v', lambda e, nkc=nkc: e.reduce_sum(scr2[:, 40:41], scr2[:, 0:nkc],
                                                       axis=AX.X))
        b.wait('t', 'ev', vv)
        mv = b.op('t', lambda e: e.matmul(self.ps[6][0:1, 0:1], self.ones_c[:, 0:1],
                                          scr2[:, 40:41], start=True, stop=True))
        b.wait('v', 'et', mv)
        b.op('v', lambda e: e.tensor_scalar(self.rows2[0:1, 0:1], self.ps[6][0:1, 0:1],
                                            1.0 / (Kdim * Ndim), 1e-8, ALU.mult, ALU.max))
        b.selfwait('v')
        b.op('v', lambda e: e.reciprocal(self.rows2[0:1, 1:2], self.rows2[0:1, 0:1]))
        b.selfwait('v')
        vv = b.op('v', lambda e: e.tensor_scalar_mul(self.rows2[0:1, 1:2],
                                                     self.rows2[0:1, 1:2], 2.0))
        b.wait('t', 'ev', vv)
        mv = b.op('t', lambda e: e.matmul(self.ps[6][:, 4:5], self.ones_f[0:1, :],
                                          self.rows2[0:1, 1:2], start=True, stop=True))
        b.wait('v', 'et', mv)
        rsc = b.op('v', lambda e: e.tensor_copy(self.cols2[:, 32:33], self.ps[6][:, 4:5]))
        wdst = self.abf(wdst_off, nkc * Ndim)
        nblk = (Ndim + TW - 1) // TW
        for kc in range(nkc):
            lv = b.dma('y', xt[:, 0:Ndim], W[kc*128:(kc+1)*128, 0:Ndim])
            b.wait_t('s', lv)
            b.wait_t('v', lv)
            if kc == 0:
                b.wait('v', 'ev', rsc)
            last_s = last_v = None
            for nb in range(nblk):
                bw = min(TW, Ndim - nb * TW)
                sg = scr[:, (nb % 2)*TW: (nb % 2)*TW + bw]
                sq = scr2[:, (nb % 2)*TW: (nb % 2)*TW + bw]
                xs = xt[:, nb*TW: nb*TW + bw]
                wd = wdst[:, kc*Ndim + nb*TW: kc*Ndim + nb*TW + bw]
                if nb >= 2:
                    b.wait('s', 'ev', last_v)
                sv = b.op('s', lambda e, sg=sg, xs=xs: e.activation(sg, xs, AF.Sign))
                b.op('v', lambda e, sq=sq, xs=xs: e.tensor_scalar_mul(
                    sq, xs, self.cols2[:, 32:33]))
                b.selfwait('v')
                b.op('v', lambda e, sq=sq: e.tensor_mul(sq, sq, sq))
                b.selfwait('v')
                b.op('v', lambda e, sq=sq: e.tensor_scalar(sq, sq, 1.0, None, ALU.is_ge))
                b.selfwait('v')
                b.wait('v', 'es', sv)
                last_v = b.op('v', lambda e, wd=wd, sq=sq, sg=sg: e.tensor_mul(wd, sq, sg))
                last_s = sv
            b.wait('y', 'ev', last_v)

    def emit_l0_ffn(self):
        b, d = self.b, self.dram
        b2 = b.dma_group('y', [(self.cols[:, 0:NKC], d["L0_ffn_lng"], {}),
                               (self.cols[:, 8:16], d["L0_ffn_lnb"], {}),
                               (self.cols[:, 16:48], d["L0_ffn_fc1_bias"], {}),
                               (self.cols2[:, 0:8], d["L0_ffn_fc2_bias"], {})])
        b.wait_t('s', b2)
        b.wait_t('v', b2)
        self.emit_ln(gamma_col=lambda kc: self.cols[:, kc:kc+1],
                     beta_col=lambda kc: self.cols[:, 8+kc:9+kc])
        self._dequant("L0_ffn_fc1_q", "L0_ffn_fc1_scales", D, 4096, wdst_off=0)
        b.barrier()
        w1 = self.abf(0, NKC * 4096)
        cons1 = self.mk_spill(d["spill_a"], AF.Silu,
                              bias_at=lambda oc: self.cols[:, 16+oc:17+oc])
        self.gemm_rw(lambda kc, oc: w1[:, kc*4096 + oc*128: kc*4096 + oc*128 + 128],
                     D, 4096, cons1, in_tile=self.nrm_tile, name="l0f1")
        b.barrier()
        self._dequant("L0_ffn_fc2_q", "L0_ffn_fc2_scales", 4096, D, wdst_off=0)
        b.barrier()
        w2 = self.abf(0, 32 * 1024)
        b.op('v', lambda e: e.tensor_scalar_mul(self.cols2[:, 8:16], self.cols2[:, 0:8], 0.5))
        b.selfwait('v')
        consd = self.mk_delta(bias_at=lambda oc: self.cols2[:, 8+oc:9+oc], scale=0.5)
        self.gemm_rw(lambda kc, oc: w2[:, kc*1024 + oc*128: kc*1024 + oc*128 + 128],
                     4096, D, consd, in_stream=d["spill_a"], ibuf='nrm', name="l0f2")
        b.barrier()

    def _dequant(self, qname, sname, Kdim, Ndim, wdst_off):
        b, d = self.b, self.dram
        b.barrier()
        nkc = Kdim // 128
        Q, S = d[qname], d[sname]
        xt = self.xtile
        wdst = self.abf(wdst_off, nkc * Ndim)
        nblk = Ndim // TW
        sttv = []
        for kc in range(nkc):
            if kc > 0:
                b.wait('g', 'ev', sttv[-1])
            lv = b.dma('g', xt[:, 0:Ndim], Q[kc*128:(kc+1)*128, 0:Ndim])
            sv = b.dma('y', self.rows2[0:1, 0:Ndim], S[kc:kc+1, 0:Ndim])
            b.wait_t('t', sv)
            for nb in range(nblk):
                gi = kc * nblk + nb
                if gi >= 2:
                    b.wait('t', 'ev', sttv[gi - 2])
                mv = b.op('t', lambda e, nb=nb: e.matmul(
                    self.ps[4 + nb % 2][:, 0:TW], self.ones_f[0:1, :],
                    self.rows2[0:1, nb*TW:(nb+1)*TW], start=True, stop=True))
                b.wait('v', 'et', mv)
                if nb == 0:
                    b.wait_t('v', lv)
                sttv.append(b.op('v', lambda e, kc=kc, nb=nb: e.scalar_tensor_tensor(
                    wdst[:, kc*Ndim + nb*TW: kc*Ndim + (nb+1)*TW],
                    xt[:, nb*TW:(nb+1)*TW], -128.0, self.ps[4 + nb % 2][:, 0:TW],
                    ALU.add, ALU.mult)))
            b.wait('y', 'et', b.cnt['et'])


    def emit_l0_attn(self):
        b, d = self.b, self.dram
        self.emit_ln()
        ldd = b.dma_group('y', [(self.rows[0:1, 0:8], d["L0_attn_decay"], {}),
                                (self.cols[:, 60:61], d["iota_col"], {}),
                                (self.scr[:, 0:128], d["exp_tab"], {}),
                                (self.masks[:, 2048:2176], d["rot_tab"], {})])
        b.wait_t('s', ldd)
        b.op('s', lambda e: e.activation(self.rows[0:1, 8:16], self.rows[0:1, 0:8], AF.Ln))
        b.selfwait('s')
        sv = b.op('s', lambda e: e.activation(self.rows[0:1, 16:24], self.rows[0:1, 8:16],
                                              AF.Exp, scale=128.0))
        b.wait('t', 'es', sv)
        for h in range(8):
            b.op('t', lambda e, h=h: e.matmul(self.ps[6][:, h:h+1], self.ones_f[0:1, :],
                                              self.rows[0:1, 8+h:9+h], start=True, stop=True))
        for h in range(8):
            mv = b.op('t', lambda e, h=h: e.matmul(self.ps[6][:, 8+h:9+h], self.ones_f[0:1, :],
                                                   self.rows[0:1, 16+h:17+h], start=True, stop=True))
        b.wait('s', 'et', mv)
        for h in range(8):
            b.op('s', lambda e, h=h: e.activation(self.cols2[:, h:h+1],
                                                  self.ps[6][:, h:h+1], AF.Copy))
            b.op('s', lambda e, h=h: e.activation(self.cols2[:, 32+h:33+h],
                                                  self.ps[6][:, 8+h:9+h], AF.Copy))
        b.selfwait('s')
        for h in range(8):
            b.op('s', lambda e, h=h: e.activation(self.cols2[:, 8+h:9+h],
                                                  self.cols2[:, h:h+1], AF.Copy, scale=-1.0))
            b.op('s', lambda e, h=h: e.activation(self.cols2[:, 16+h:17+h],
                                                  self.cols2[:, h:h+1], AF.Copy, scale=128.0))
        b.selfwait('s')
        for h in range(8):
            b.op('s', lambda e, h=h: e.activation(self.masks[:, h*128:(h+1)*128],
                                                  self.scr[:, 0:128], AF.Exp,
                                                  scale=self.cols2[:, h:h+1]))
        for h in range(8):
            sv = b.op('s', lambda e, h=h: e.activation(self.cols2[:, 24+h:25+h],
                                                       self.cols[:, 60:61], AF.Exp,
                                                       scale=self.cols2[:, 8+h:9+h],
                                                       bias=self.cols2[:, 16+h:17+h]))
        b.wait('t', 'es', b.cnt['es'])
        for h in range(8):
            mv = b.op('t', lambda e, h=h: e.matmul(self.ps[6][:, 128:256], self.ones_b[0:1, :],
                                                   self.masks[0:1, h*128:(h+1)*128],
                                                   start=True, stop=True))
            b.wait('v', 'et', mv)
            vv = b.op('v', lambda e, h=h: e.tensor_copy(
                self.masks[:, 1024 + h*128: 1024 + (h+1)*128], self.ps[6][:, 128:256]))
            b.wait('t', 'ev', vv)
        b.barrier()
        wq = self.abf(0, 2048)
        wk = self.abf(4*1024, 2048)
        wv = self.abf(8*1024, 4096)
        wg = self.abf(16*1024, 4096)
        qs = self.abf(24*1024, 1024)
        ks = self.abf(26*1024, 1024)
        rawb = self.abf(28*1024, 512)
        vs = self.abf(29*1024, 2048)
        gs = self.abf(33*1024, 2048)
        tabs = self.af32(37*1024, 4096)
        sbf = self.abf(53*1024, 256)
        ATb = self.abf(53*1024 + 512, 128)
        ktb = self.abf(53*1024 + 768, 128)
        kdb = self.abf(54*1024, 128)
        qgb = self.abf(54*1024 + 256, 128)
        nrb = self.abf(54*1024 + 512, 256)
        pinb = self.abf(55*1024, 2048)
        st = self.af32(60*1024, 512)
        Wqkv, Wg, Wp = d["L0_attn_W_qkv"], d["L0_attn_W_g"], d["L0_attn_W_proj"]
        TABN = ("xpos_cq", "xpos_sq", "xpos_ck", "xpos_sk")
        prev_stok = None
        for pr in range(getattr(self, 'l0_pair_limit', None) or 4):
            b.wait('g', 'et', b.cnt['et'])
            wl1 = b.dma_group('g', [(wq[:, kc*256:(kc+1)*256],
                                     Wqkv[kc*128:(kc+1)*128, pr*256: (pr+1)*256], {})
                                    for kc in range(NKC)] +
                                   [(wk[:, kc*256:(kc+1)*256],
                                     Wqkv[kc*128:(kc+1)*128, 1024 + pr*256: 1024 + (pr+1)*256], {})
                                    for kc in range(NKC)])
            wl2 = b.dma_group('g', [(wv[:, kc*512:(kc+1)*512],
                                     Wqkv[kc*128:(kc+1)*128, 2048 + pr*512: 2048 + (pr+1)*512], {})
                                    for kc in range(NKC)])
            wl3 = b.dma_group('g', [(wg[:, kc*512:(kc+1)*512],
                                     Wg[kc*128:(kc+1)*128, pr*512:(pr+1)*512], {})
                                    for kc in range(NKC)])
            b.selfwait('v')
            b.wait('v', 'et', b.cnt['et'])
            b.op('v', lambda e, st=st: e.memset(st[:], 0.0))
            for tt in range(getattr(self, 'l0_tt_limit', None) or NTT):
                t0 = tt * TW
                b.wait('y', 'ev', b.cnt['ev'])
                tl = b.dma_group('y', [
                    (tabs[:, (ti*2+hc)*512:(ti*2+hc+1)*512],
                     d[TABN[ti]][pr*256 + hc*128: pr*256 + (hc+1)*128, t0:t0+TW], {})
                    for ti in range(4) for hc in range(2)])
                if tt == 0:
                    b.wait_t('t', wl1)
                for which in range(2):
                    wsb = wq if which == 0 else wk
                    dstq = qs if which == 0 else ks
                    ci = 0 if which == 0 else 2
                    for hc in range(2):
                        b.wait('t', 'ev', b.cnt['ev'])
                        b.wait('t', 'es', b.cnt['es'])
                        for kc in range(NKC):
                            b.op('t', lambda e, wsb=wsb, kc=kc, hc=hc, tt=tt: e.matmul(
                                self.ps[0][:, 0:TW],
                                wsb[:, kc*256 + hc*128: kc*256 + (hc+1)*128],
                                self.nrm_tile(kc, tt), start=(kc == 0), stop=(kc == NKC-1)))
                        mv = b.cnt['et']
                        b.wait('s', 'et', mv)
                        sv = b.op('s', lambda e: e.activation(rawb[:], self.ps[0][:, 0:TW],
                                                              AF.Copy))
                        b.wait('t', 'es', sv)
                        mr = b.op('t', lambda e: e.matmul(self.ps[1][:, 0:TW],
                                                          self.masks[:, 2048:2176], rawb[:],
                                                          start=True, stop=True))
                        b.wait('v', 'et', mr)
                        b.wait_t('v', tl)
                        b.op('v', lambda e, ci=ci, hc=hc: e.tensor_mul(
                            self.scr[:, 0:TW], rawb[:], tabs[:, (ci*2+hc)*512:(ci*2+hc+1)*512]))
                        b.op('v', lambda e, ci=ci, hc=hc: e.tensor_mul(
                            self.scr[:, TW:2*TW], self.ps[1][:, 0:TW],
                            tabs[:, ((ci+1)*2+hc)*512:((ci+1)*2+hc+1)*512]))
                        b.selfwait('v')
                        b.op('v', lambda e, dstq=dstq, hc=hc: e.tensor_add(
                            dstq[:, hc*TW:(hc+1)*TW], self.scr[:, 0:TW], self.scr[:, TW:2*TW]))
                if tt == 0:
                    b.wait_t('t', wl2)
                    b.wait_t('t', wl3)
                for which in range(2):
                    wsb = wv if which == 0 else wg
                    dstv = vs if which == 0 else gs
                    for tc in range(4):
                        psv = self.ps[2 + tc % 2]
                        b.wait('t', 'ev', b.cnt['ev'])
                        b.wait('t', 'es', b.cnt['es'])
                        for kc in range(NKC):
                            b.op('t', lambda e, psv=psv, wsb=wsb, kc=kc, tc=tc, t0=t0: e.matmul(
                                psv[:, 0:TW],
                                self.nrm[:, kc*T + t0 + tc*128: kc*T + t0 + (tc+1)*128],
                                wsb[:, kc*512:(kc+1)*512], start=(kc == 0), stop=(kc == NKC-1)))
                        mv = b.cnt['et']
                        if which == 0:
                            b.wait('v', 'et', mv)
                            b.op('v', lambda e, psv=psv, dstv=dstv, tc=tc: e.tensor_copy(
                                dstv[:, tc*512:(tc+1)*512], psv[:, 0:TW]))
                        else:
                            b.wait('s', 'et', mv)
                            b.op('s', lambda e, psv=psv, dstv=dstv, tc=tc: e.activation(
                                dstv[:, tc*512:(tc+1)*512], psv[:, 0:TW], AF.Silu))
                b.wait('v', 'es', b.cnt['es'])
                if prev_stok is not None:
                    b.wait_t('v', prev_stok)
                for hc in range(2):
                    h = pr * 2 + hc
                    for c in range(4):
                        b.selfwait('v')
                        b.wait('v', 'et', b.cnt['et'])
                        scp = b.op('v', lambda e, hc=hc, st=st: e.tensor_copy(
                            sbf[:], st[:, hc*256:(hc+1)*256]))
                        b.wait('t', 'ev', scp)
                        mv = b.op('t', lambda e, hc=hc, c=c: e.matmul(
                            self.ps[0][:, 0:128], ks[:, hc*TW + c*128: hc*TW + (c+1)*128],
                            qs[:, hc*TW + c*128: hc*TW + (c+1)*128], start=True, stop=True))
                        b.wait('v', 'et', mv)
                        b.op('v', lambda e, h=h: e.tensor_mul(
                            ATb[:], self.ps[0][:, 0:128], self.masks[:, h*128:(h+1)*128]))
                        qv = b.op('v', lambda e, h=h, hc=hc, c=c: e.tensor_mul(
                            qgb[:], qs[:, hc*TW + c*128: hc*TW + (c+1)*128],
                            self.masks[:, 1024 + h*128: 1024 + (h+1)*128]))
                        b.wait('t', 'ev', qv)
                        b.wait('t', 'es', b.cnt['es'])
                        b.op('t', lambda e, c=c, hc=hc: e.matmul(
                            self.ps[1][:, 0:256], ATb[:],
                            vs[:, c*512 + hc*256: c*512 + hc*256 + 256],
                            start=True, stop=False))
                        mo = b.op('t', lambda e: e.matmul(
                            self.ps[1][:, 0:256], qgb[:], sbf[:], start=False, stop=True))
                        tv = b.op('t', lambda e, hc=hc, c=c: e.transpose(
                            self.psb[:, 0:128], ks[:, hc*TW + c*128: hc*TW + (c+1)*128],
                            self.ident[:]))
                        b.wait('v', 'et', tv)
                        b.op('v', lambda e: e.tensor_copy(ktb[:], self.psb[:, 0:128]))
                        b.selfwait('v')
                        k2 = b.op('v', lambda e, h=h: e.tensor_scalar_mul(
                            kdb[:], ktb[:], self.cols2[:, 24+h:25+h]))
                        b.wait('t', 'ev', k2)
                        mi = b.op('t', lambda e, c=c, hc=hc: e.matmul(
                            self.ps[2][:, 0:256], kdb[:],
                            vs[:, c*512 + hc*256: c*512 + hc*256 + 256],
                            start=True, stop=True))
                        b.wait('v', 'et', mi)
                        b.op('v', lambda e, h=h, hc=hc, st=st: e.scalar_tensor_tensor(
                            st[:, hc*256:(hc+1)*256], st[:, hc*256:(hc+1)*256],
                            self.cols2[:, 32+h:33+h], self.ps[2][:, 0:256],
                            ALU.mult, ALU.add))
                        b.wait('s', 'et', mo)
                        sq = b.op('s', lambda e: e.activation(self.scr2[:, 0:256],
                                                              self.ps[1][:, 0:256], AF.Square))
                        b.wait('v', 'et', mo)
                        b.op('v', lambda e: e.reduce_sum(self.cols[:, 61:62],
                                                         self.ps[1][:, 0:256], axis=AX.X))
                        b.wait('v', 'es', sq)
                        b.op('v', lambda e: e.reduce_sum(self.cols[:, 62:63],
                                                         self.scr2[:, 0:256], axis=AX.X))
                        b.selfwait('v')
                        b.op('v', lambda e: e.tensor_scalar_mul(self.cols[:, 61:62],
                                                                self.cols[:, 61:62], 1.0/256))
                        b.op('v', lambda e: e.tensor_scalar_mul(self.cols[:, 62:63],
                                                                self.cols[:, 62:63], 1.0/256))
                        b.selfwait('v')
                        b.op('v', lambda e: e.tensor_mul(self.cols[:, 63:64],
                                                         self.cols[:, 61:62], self.cols[:, 61:62]))
                        b.selfwait('v')
                        b.op('v', lambda e: e.tensor_scalar(self.cols[:, 63:64],
                                                            self.cols[:, 63:64], -1.0, 1e-5,
                                                            ALU.mult, ALU.add))
                        b.selfwait('v')
                        vv = b.op('v', lambda e: e.tensor_add(self.cols[:, 62:63],
                                                              self.cols[:, 62:63],
                                                              self.cols[:, 63:64]))
                        b.wait('s', 'ev', vv)
                        sv = b.op('s', lambda e: e.activation(self.cols[:, 63:64],
                                                              self.cols[:, 62:63], AF.Sqrt))
                        b.wait('v', 'es', sv)
                        b.op('v', lambda e: e.reciprocal(self.cols[:, 62:63],
                                                         self.cols[:, 63:64]))
                        b.selfwait('v')
                        b.op('v', lambda e: e.tensor_mul(self.cols[:, 63:64],
                                                         self.cols[:, 61:62], self.cols[:, 62:63]))
                        b.selfwait('v')
                        b.op('v', lambda e: e.tensor_scalar_mul(self.cols[:, 63:64],
                                                                self.cols[:, 63:64], -1.0))
                        b.selfwait('v')
                        b.op('v', lambda e: e.tensor_scalar(
                            nrb[:], self.ps[1][:, 0:256], self.cols[:, 62:63],
                            self.cols[:, 63:64], ALU.mult, ALU.add))
                        b.selfwait('v')
                        pv2 = b.op('v', lambda e, c=c, hc=hc: e.tensor_mul(
                            nrb[:], nrb[:], gs[:, c*512 + hc*256: c*512 + hc*256 + 256]))
                        b.wait('t', 'ev', pv2)
                        for dh in range(2):
                            b.op('t', lambda e, dh=dh: e.transpose(
                                self.psb[:, 128 + dh*128: 256 + dh*128],
                                nrb[:, dh*128:(dh+1)*128], self.ident[:]))
                        mt = b.cnt['et']
                        b.wait('v', 'et', mt)
                        for dh in range(2):
                            vcl = hc * 2 + dh
                            b.op('v', lambda e, vcl=vcl, c=c, dh=dh: e.tensor_copy(
                                pinb[:, vcl*512 + c*128: vcl*512 + (c+1)*128],
                                self.psb[:, 128 + dh*128: 256 + dh*128]))
                spv = b.cnt['ev']
                b.wait('y', 'ev', spv)
                prev_stok = b.dma_group('y', [
                    (d["spill_a"][(pr*4+v)*128:(pr*4+v+1)*128, t0:t0+TW],
                     pinb[:, v*512:(v+1)*512], {}) for v in range(4)])
        b.barrier()
        if getattr(self, 'l0_no_proj', False):
            return
        wp = self.abf(0, 16*1024)
        for half in range(2):
            wlp = b.dma_group('g', [(wp[:, (half*8+k2)*1024:(half*8+k2+1)*1024],
                                     Wp[(half*8+k2)*128:(half*8+k2+1)*128, :], {})
                                    for k2 in range(8)])
            b.wait_t('t', wlp)
        consd = self.mk_delta()
        self.gemm_rw(lambda kc, oc: wp[:, kc*1024 + oc*128: kc*1024 + (oc+1)*128],
                     2048, D, consd, in_stream=d["spill_a"], ibuf='arena',
                     ibuf_off=32*1024, name="l0proj")
        b.barrier()

    def _attn_bound(self, qp, kp):
        b = self.b
        b.wait('t', 'ev', b.cnt['ev'])
        b.wait('s', 'ev', b.cnt['ev'])
        units = [(si, tt) for si in range(2) for tt in range(NTT)]
        mmv = {}
        rmv = {}
        for i, (si, tt) in enumerate(units):
            src = (qp, kp)[si]
            slot = (i % 2) * TW
            bb = 4 if i % 2 == 0 else 0
            if i >= 2:
                b.wait('s', 'et', mmv[i - 2])
            sv = b.op('s', lambda e, src=src, tt=tt, slot=slot: e.activation(
                self.scr[:, slot:slot+TW], src[:, tt*TW:(tt+1)*TW], AF.Square))
            b.wait('t', 'es', sv)
            if i >= 2:
                b.wait('t', 'ev', rmv[i - 2])
            b.op('t', lambda e, slot=slot, bb=bb: e.matmul(
                self.ps[bb][0:1, 0:TW], self.ones_c[0:64, 0:1],
                self.scr[0:64, slot:slot+TW], start=True, stop=True,
                tile_position=(0, 0)))
            mmv[i] = b.op('t', lambda e, slot=slot, bb=bb: e.matmul(
                self.ps[bb + 1][0:1, 0:TW], self.ones_c[64:128, 0:1],
                self.scr[64:128, slot:slot+TW], start=True, stop=True,
                tile_position=(64, 0)))
            b.wait('v', 'et', mmv[i])
            for h2 in range(2):
                dst = 32 + (h2 * 2 + si) * 8 + tt
                rmv[i] = b.op('v', lambda e, h2=h2, dst=dst, bb=bb: e.reduce_max(
                    self.rows[0:1, dst:dst+1], self.ps[bb + h2][0:1, 0:TW], axis=AX.X))
        b.selfwait('v')
        negb = {}
        for h2 in range(2):
            b.op('v', lambda e, h2=h2: e.reduce_max(
                self.rows[0:1, 0:1], self.rows[0:1, 32 + h2*16: 40 + h2*16], axis=AX.X))
            b.selfwait('v')
            b.op('v', lambda e, h2=h2: e.reduce_max(
                self.rows[0:1, 1:2], self.rows[0:1, 40 + h2*16: 48 + h2*16], axis=AX.X))
            b.selfwait('v')
            vv = b.op('v', lambda e: e.tensor_mul(self.rows[0:1, 2:3],
                                                  self.rows[0:1, 0:1], self.rows[0:1, 1:2]))
            b.wait('s', 'ev', vv)
            sv = b.op('s', lambda e: e.activation(self.rows[0:1, 3:4],
                                                  self.rows[0:1, 2:3], AF.Sqrt))
            b.wait('v', 'es', sv)
            vv = b.op('v', lambda e, h2=h2: e.tensor_scalar_mul(
                self.scr2[0:1, h2:h2+1], self.rows[0:1, 3:4], -1.0))
            b.wait('t', 'ev', vv)
            mv = b.op('t', lambda e, h2=h2: e.matmul(
                self.ps[4][:, 8:9], self.bcrf[0:64, 0:128], self.scr2[0:64, h2:h2+1],
                start=True, stop=True, tile_position=(0, 0)))
            b.wait('s', 'et', mv)
            negb[h2] = b.op('s', lambda e, h2=h2: e.activation(
                self.cols2[:, 44 + h2:45 + h2], self.ps[4][:, 8:9], AF.Copy))
            b.selfwait('s')
        b.wait('v', 'es', negb[1])
        b.wait('t', 'es', negb[1])
        b.wait('t', 'ev', b.cnt['ev'])
        b.wait('t', 'et', b.cnt['et'])

    def _attn_flash(self, qp, kp, vp, aop, pbuf, njc_of):
        b = self.b
        PIPE = 3
        SB = (self.ps[0], self.ps[1], self.ps[4], self.ps[5])
        st = self._flash_state
        for qt in range(NTT):
            njc = njc_of(qt)
            smm = {}
            expv = {}
            mskv = {}
            pvet = {}

            def S_(jc, qt=qt):
                if jc >= 2:
                    b.wait('t', 'es', expv[(jc - 2, 1)])
                else:
                    b.wait('t', 'es', st['last_exp'])
                for h2 in range(2):
                    b.op('t', lambda e, jc=jc, h2=h2, qt=qt: e.matmul(
                        SB[(2*jc + h2) % 4][:, 0:TW],
                        kp[64*h2: 64*h2 + 64, jc*128:(jc+1)*128],
                        qp[64*h2: 64*h2 + 64, qt*TW:(qt+1)*TW],
                        start=True, stop=True, tile_position=(64*h2, 0)))
                    smm[(jc, h2)] = b.cnt['et']

            def E_(jc, qt=qt):
                for h2 in range(2):
                    b.wait('s', 'et', smm[(jc, h2)])
                    b.wait('s', 'et', pvet[(jc - 4, h2)] if jc >= 4 else st['last_pv'])
                    expv[(jc, h2)] = b.op('s', lambda e, jc=jc, h2=h2: e.activation(
                        pbuf[:, ((2*jc + h2) % 8)*TW:((2*jc + h2) % 8 + 1)*TW],
                        SB[(2*jc + h2) % 4][:, 0:TW],
                        AF.Exp, bias=self.cols2[:, 44 + h2:45 + h2]))
                    rel = jc - 4 * qt
                    if rel >= 0:
                        b.wait('v', 'es', expv[(jc, h2)])
                        mskv[(jc, h2)] = b.op('v', lambda e, jc=jc, h2=h2, rel=rel: e.tensor_mul(
                            pbuf[:, ((2*jc + h2) % 8)*TW:((2*jc + h2) % 8 + 1)*TW],
                            pbuf[:, ((2*jc + h2) % 8)*TW:((2*jc + h2) % 8 + 1)*TW],
                            self.masks[:, rel*TW:(rel+1)*TW]))

            def PV_(jc, qt=qt, njc=njc):
                for h2 in range(2):
                    psA = self.ps[2 + h2]
                    if (jc, h2) in mskv:
                        b.wait('t', 'ev', mskv[(jc, h2)])
                    else:
                        b.wait('t', 'es', expv[(jc, h2)])
                    if jc == 0:
                        b.wait('t', 'ev', st['tail_c'][h2])
                    b.op('t', lambda e, jc=jc, h2=h2, njc=njc, psA=psA: e.matmul(
                        psA[0:65, 0:TW], vp[:, jc*130 + 65*h2: jc*130 + 65*h2 + 65],
                        pbuf[:, ((2*jc + h2) % 8)*TW:((2*jc + h2) % 8 + 1)*TW],
                        start=(jc == 0), stop=(jc == njc - 1)))
                    pvet[(jc, h2)] = b.cnt['et']

            for jc in range(min(PIPE, njc)):
                S_(jc)
                E_(jc)
            for jc in range(njc):
                if jc + PIPE < njc:
                    S_(jc + PIPE)
                    E_(jc + PIPE)
                PV_(jc)
            st['last_exp'] = expv[(njc - 1, 1)]
            st['last_pv'] = pvet[(njc - 1, 1)]
            prev_mb = 0
            for h2 in range(2):
                psA = self.ps[2 + h2]
                mv = max(pvet[(njc - 1, h2)], prev_mb)
                b.wait('v', 'et', mv)
                rv = b.op('v', lambda e, psA=psA: e.reciprocal(
                    self.scr2[0:1, 8:520], psA[64:65, 0:TW]))
                b.wait('t', 'ev', rv)
                mb = b.op('t', lambda e: e.matmul(
                    self.ps[6][0:64, 0:TW], self.bcrf[0:64, 0:64],
                    self.scr2[0:64, 8:520], start=True, stop=True,
                    tile_position=(0, 0)))
                prev_mb = mb
                slot = h2 * TW
                cv = b.op('v', lambda e, psA=psA, slot=slot: e.tensor_copy(
                    self.scr[0:64, slot:slot+TW], psA[0:64, 0:TW]))
                st['tail_c'][h2] = cv
                b.wait('v', 'et', mb)
                ov = b.op('v', lambda e, h2=h2, qt=qt, slot=slot: e.tensor_mul(
                    aop[64*h2:64*h2 + 64, qt*TW:(qt+1)*TW],
                    self.scr[0:64, slot:slot+TW], self.ps[6][0:64, 0:TW]))
                st['ov'].append(ov)
        b.wait('t', 'et', b.cnt['et'])

    def emit_l1_attn(self):
        b, d = self.b, self.dram
        ldc = b.dma_group('y', [(self.cols[:, 0:8], d["L1_attn_bo"], {}),
                                (self.cols[:, 8:16], d["L1_attn_b1"], {}),
                                (self.cols[:, 16:24], d["L1_attn_b2"], {}),
                                (self.cols[:, 24:48], d["L1_attn_bg"], {})])
        ldm = b.dma('g', self.masks[:, 0:2560], d["l1_masks"])
        self.emit_ln()
        b.wait_t('s', ldc)
        b.wait_t('v', ldc)
        cons1 = self.mk_spill(d["spill_a"][0:1024, :], AF.Silu,
                              bias_at=lambda oc: self.cols[:, 8+oc:9+oc])
        self.gemm_sw(d["L1_attn_W1t"], D, D, cons1, gw=512, wbuf_off=0,
                     in_tile=self.nrm_tile, name="l1aA")
        b.barrier()
        st2 = {"n": 0}
        def cons2(oc, tt, ps, tok0):
            n = st2["n"]; st2["n"] += 1
            b.wait('v', 'et', b.cnt['et'])
            v = b.op('v', lambda e, oc=oc, tt=tt, ps=ps: e.tensor_scalar(
                self.nrm[:, oc*T + tt*TW: oc*T + (tt+1)*TW], ps[:, 0:TW],
                self.cols[:, 16+oc:17+oc], None, ALU.add))
            return ('ev', v)
        self.gemm_sw(d["L1_attn_W2t"], D, D, cons2, gw=512, wbuf_off=0,
                     in_stream=d["spill_a"][0:1024, :], ibuf='arena',
                     ibuf_off=16 * 1024, name="l1aB")
        b.barrier()
        qp = self.abf(0, T)
        kp = self.abf(8 * 1024, T)
        vfm = self.abf(16 * 1024, T)
        vp = self.abf(24 * 1024, 32 * 130)
        aop = self.abf(33 * 1024, T)
        wqkv = self.abf(41 * 1024, 3 * 1024)
        pbuf = self.abf(47 * 1024, 8 * TW)
        b.op('v', lambda e, vp=vp: e.memset(
            vp[:].rearrange("p (a c) -> p a c", c=65)[:, :, 64:65], 1.0))
        b.op('v', lambda e: e.memset(self.scr2[:], 0.0))
        beta = float(64.0 ** -0.5)
        self._flash_state = {'last_exp': 0, 'last_pv': 0,
                             'tail_r': {0: 0, 1: 0}, 'tail_c': {0: 0, 1: 0},
                             'ov': []}
        proj_done = 0
        stok = None
        for p in range(getattr(self, 'l1_pair_limit', None) or 8):
            b.wait('g', 'et', proj_done)
            wl = b.dma_group('g', [
                (wqkv[:, pi*1024 + kc*128: pi*1024 + (kc+1)*128],
                 d["L1_attn_Wgt"][kc*128:(kc+1)*128,
                                  pi*1024 + p*128: pi*1024 + (p+1)*128], {})
                for pi in range(3) for kc in range(NKC)])
            b.wait_t('t', wl)
            pv = {}
            for pi in range(3):
                dstt = (qp, kp, vfm)[pi]
                bgc = 24 + pi * 8 + p
                for tt in range(NTT):
                    psq = self.ps[(pi * NTT + tt) % 2]
                    if pi * NTT + tt >= 2:
                        b.wait('t', 'ev', pv[(pi * NTT + tt - 2)])
                    for kc in range(NKC):
                        b.op('t', lambda e, psq=psq, pi=pi, kc=kc, tt=tt: e.matmul(
                            psq[:, 0:TW],
                            wqkv[:, pi*1024 + kc*128: pi*1024 + (kc+1)*128],
                            self.nrm_tile(kc, tt), start=(kc == 0), stop=(kc == NKC - 1)))
                    mv = b.cnt['et']
                    b.wait('v', 'et', mv)
                    sc = beta if pi == 0 else 1.0
                    if pi == 2:
                        pv[pi * NTT + tt] = b.op('v', lambda e, psq=psq, dstt=dstt, tt=tt, bgc=bgc: e.tensor_scalar(
                            dstt[:, tt*TW:(tt+1)*TW], psq[:, 0:TW],
                            self.cols[:, bgc:bgc+1], None, ALU.add))
                    else:
                        pv[pi * NTT + tt] = b.op('v', lambda e, psq=psq, dstt=dstt, tt=tt, bgc=bgc, sc=sc: e.tensor_scalar(
                            dstt[:, tt*TW:(tt+1)*TW], psq[:, 0:TW],
                            self.cols[:, bgc:bgc+1], sc, ALU.add, ALU.mult))
            proj_done = b.cnt['et']
            cpv = []
            b.wait('t', 'ev', pv[3 * NTT - 1])
            for jc in range(32):
                if cpv:
                    b.wait('t', 'ev', cpv[-1])
                b.op('t', lambda e, jc=jc, vfm=vfm: e.transpose(
                    self.psb[:, 0:128], vfm[:, jc*128:(jc+1)*128], self.ident[:]))
                mv = b.cnt['et']
                b.wait('v', 'et', mv)
                cpv.append(b.op('v', lambda e, jc=jc, vp=vp: e.tensor_copy(
                    vp[:, jc*130: jc*130 + 64], self.psb[:, 0:64])))
                cpv.append(b.op('v', lambda e, jc=jc, vp=vp: e.tensor_copy(
                    vp[:, jc*130 + 65: jc*130 + 129], self.psb[:, 64:128])))
            self._attn_bound(qp, kp)
            if stok is not None:
                b.wait_t('v', stok)
            self._attn_flash(qp, kp, vp, aop, pbuf, lambda qt: min(4 * qt + 5, 32))
            b.wait('y', 'ev', b.cnt['ev'])
            stok = b.dma('y', d["spill_b"][p*128:(p+1)*128, :], aop[:, 0:T])
        b.barrier()
        consd = self.mk_delta()
        self.gemm_sw(d["L1_attn_Wot"], D, D, consd, gw=512, wbuf_off=0,
                     in_stream=d["spill_b"][0:1024, :], ibuf='arena',
                     ibuf_off=16 * 1024, name="l1o")
        b.barrier()
        bcol = self.cols[:, 0:8]
        for oc in range(8):
            vv = b.op('v', lambda e, oc=oc, bcol=bcol: e.tensor_scalar(
                self.evac[:, (oc % 2)*TW:(oc % 2)*TW + TW], self.ab[:, 0:TW],
                0.0, bcol[:, oc:oc+1], ALU.mult, ALU.add))
            b.wait('g', 'ev', vv)
            dv = None
            for tt in range(NTT):
                dv = b.dma('g', d["x_work"][oc*128:(oc+1)*128, tt*TW:(tt+1)*TW],
                           self.evac[:, (oc % 2)*TW:(oc % 2)*TW + TW], accum_op=ALU.add)
            b.wait_t('v', dv)
        b.barrier()

    def emit_l2_attn(self):
        b, d = self.b, self.dram
        G = 256
        ldc = b.dma_group('y', [
            (self.cols[:, 0:16], d["L2_attn_bg"], {}),
            (self.cols[:, 16:17], d["L2_attn_binc"], {}),
            (self.cols[:, 17:25], d["L2_attn_bout"], {}),
            (self.cols2[:, 0:8], d["L2_attn_affs"], {}),
            (self.cols[:, 25:41], d["L2_attn_bv"], {}),
        ])
        ldm = b.dma('g', self.masks[:, 0:512], d["l2_masks"])
        ldw = b.dma_group('g', [(self.masks[:, 512 + kc*128: 512 + (kc+1)*128],
                                 d["L2_attn_Win"][kc*128:(kc+1)*128, :], {})
                                for kc in range(NKC)])
        b.wait_t('v', ldc)
        b.op('v', lambda e: e.tensor_scalar_mul(self.cols2[:, 8:10], self.cols2[:, 0:2],
                                                float(2048.0 ** -0.5)))
        st = self.af32(54 * 1024, 2048)
        b.op('v', lambda e: e.memset(st[:], 0.0))
        b.selfwait('v')
        self.emit_ln()
        consV = self.mk_spill(d["spill_a"][0:2048, :], AF.Silu,
                              bias_at=lambda oc: self.cols[:, 25+oc:26+oc])
        self.gemm_sw(d["L2_attn_Wv"], D, 2048, consV, gw=512, wbuf_off=0,
                     in_tile=self.nrm_tile, name="l2av")
        b.barrier()
        consG = self.mk_spill(d["spill_b"][0:2048, :], AF.Silu,
                              bias_at=lambda oc: self.cols[:, oc:oc+1])
        self.gemm_sw(d["L2_attn_Wg"], D, 2048, consG, gw=512, wbuf_off=0,
                     in_tile=self.nrm_tile, name="l2ag")
        b.barrier()
        AR_VF, AR_GT = 0, 16 * 1024
        AR_VT, AR_XQ = 32 * 1024, 40 * 1024
        AR_EB = 48 * 1024
        vf = self.abf(AR_VF, 16 * TW)
        gt = self.abf(AR_GT, 16 * TW)
        vt = self.nrm[:, 0:4096] if getattr(self, 'l2_vt_normal', False) else self.abf(AR_VT, 4096)
        x_ = self.abf(AR_XQ, TW)
        qq = self.abf(AR_XQ + 1024, TW)
        qk = self.abf(AR_XQ + 2048, TW)
        lq = self.abf(AR_XQ + 3072, TW)
        lk = self.abf(AR_XQ + 4096, TW)
        Pb = self.abf(AR_XQ + 5120, TW)
        P2 = self.abf(AR_XQ + 6144, TW)
        sbf = self.abf(AR_XQ + 7168, 256)
        lkT = self.abf(AR_EB + 3 * 1024, 256)
        oe = self.abf(AR_EB, 6 * 256)
        Win = self.masks.ap() if False else None
        win_sb = self.masks[:, 512:1536]
        spill_o = d["spill_a"][2048:4096, :]
        spill_v = d["spill_a"][0:2048, :]
        spill_g = d["spill_b"][0:2048, :]
        b.wait_t('t', ldw)
        ost = {"n": 0, "dma": [], "ev": []}
        self._l2_last_rl = None
        ntt_run = getattr(self, 'l2_tt_limit', None) or NTT
        for tt in range(ntt_run):
            t0 = tt * TW
            if tt > 0:
                b.wait('y', 'et', b.cnt['et'])
                b.wait('y', 'ev', b.cnt['ev'])
            lvf = b.dma_group('y', [(vf[:, ec*TW:(ec+1)*TW],
                                     spill_v[ec*128:(ec+1)*128, t0:t0+TW], {})
                                    for ec in range(16)])
            lgt = b.dma_group('y', [(gt[:, ec*TW:(ec+1)*TW],
                                     spill_g[ec*128:(ec+1)*128, t0:t0+TW], {})
                                    for ec in range(16)])
            for kc in range(NKC):
                b.op('t', lambda e, kc=kc, tt=tt: e.matmul(
                    self.ps[4][:, 0:TW], win_sb[:, kc*128:(kc+1)*128],
                    self.nrm_tile(kc, tt), start=(kc == 0), stop=(kc == NKC - 1)))
            mv = b.cnt['et']
            b.wait('s', 'et', mv)
            sv = b.op('s', lambda e: e.activation(x_[:], self.ps[4][:, 0:TW], AF.Silu,
                                                  bias=self.cols[:, 16:17]))
            b.wait('v', 'es', sv)
            b.op('v', lambda e: e.tensor_scalar(qq[:], x_[:], self.cols2[:, 8:9],
                                                self.cols2[:, 9:10], ALU.mult, ALU.add))
            b.op('v', lambda e: e.tensor_scalar(qk[:], x_[:], self.cols2[:, 2:3],
                                                self.cols2[:, 3:4], ALU.mult, ALU.add))
            b.op('v', lambda e: e.tensor_scalar(lq[:], x_[:], self.cols2[:, 4:5],
                                                self.cols2[:, 5:6], ALU.mult, ALU.add))
            aff = b.op('v', lambda e: e.tensor_scalar(lk[:], x_[:], self.cols2[:, 6:7],
                                                      self.cols2[:, 7:8], ALU.mult, ALU.add))
            b.wait_t('t', lvf)
            for g in range(getattr(self, 'l2_g_limit', None) or 2):
                n0 = g * G
                cpv = []
                for tc in range(2):
                    for ec8 in range(2):
                        if cpv:
                            b.wait('t', 'ev', cpv[-1])
                        for e8 in range(8):
                            ec = ec8 * 8 + e8
                            b.op('t', lambda e, ec=ec, tc=tc, g=g, e8=e8: e.transpose(
                                self.psb[:, e8*128:(e8+1)*128],
                                vf[:, ec*TW + g*G + tc*128: ec*TW + g*G + tc*128 + 128],
                                self.ident[:]))
                        mv = b.cnt['et']
                        b.wait('v', 'et', mv)
                        cpv.append(b.op('v', lambda e, tc=tc, ec8=ec8: e.tensor_copy(
                            vt[:, tc*2048 + ec8*1024: tc*2048 + (ec8+1)*1024],
                            self.psb[:, 0:1024])))
                b.wait('t', 'ev', aff)
                if self._l2_last_rl is not None:
                    b.wait('t', 'es', self._l2_last_rl)
                sep = getattr(self, 'l2_sep_st', False)
                for tc in range(2):
                    bank = self.ps[5 + tc] if sep else self.ps[5]
                    col = 0 if sep else tc * G
                    b.op('t', lambda e, tc=tc, g=g, bank=bank, col=col, n0=n0: e.matmul(
                        bank[:, col:col+G], qk[:, n0+tc*128: n0+tc*128+128],
                        qq[:, n0:n0+G], start=True, stop=True))
                mv = b.cnt['et']
                b.wait('s', 'et', mv)
                rl = []
                for tc in range(2):
                    bank = self.ps[5 + tc] if sep else self.ps[5]
                    col = 0 if sep else tc * G
                    rl.append(b.op('s', lambda e, tc=tc, bank=bank, col=col: e.activation(
                        Pb[:, tc*G:(tc+1)*G], bank[:, col:col+G], AF.Relu)))
                self._l2_last_rl = rl[-1]
                b.wait('v', 'es', rl[-1])
                for tc in range(2):
                    b.op('v', lambda e, tc=tc: e.tensor_mul(
                        P2[:, tc*G:(tc+1)*G], Pb[:, tc*G:(tc+1)*G], Pb[:, tc*G:(tc+1)*G]))
                b.selfwait('v')
                for tc in range(2):
                    pm = b.op('v', lambda e, tc=tc: e.tensor_mul(
                        Pb[:, tc*G:(tc+1)*G], P2[:, tc*G:(tc+1)*G],
                        self.masks[:, tc*G:(tc+1)*G]))
                if getattr(self, 'l2_pb_normal', False):
                    b.selfwait('v')
                    pm = b.op('v', lambda e: e.tensor_copy(self.masks[:, 1536:2048], Pb[:, 0:512]))
                b.wait('t', 'ev', pm)
                b.wait_t('t', lgt)
                esmm = []
                for es in range(getattr(self, 'l2_es_limit', None) or 16):
                    nb_rot = getattr(self, 'l2_nbank', 4)
                    psx = self.ps[es % nb_rot]
                    if es >= 2:
                        b.wait('v', 'et', esmm[es - 2])
                    scp = b.op('v', lambda e, es=es: e.tensor_copy(
                        sbf[:, (es % 2)*128:(es % 2)*128 + 128],
                        st[:, es*128:(es+1)*128]))
                    if len(ost["esv"]) >= 4 if "esv" in ost else False:
                        pass
                    if len(ost.get("esv", [])) >= 4:
                        b.wait('t', 'es', ost["esv"][-4])
                    b.wait('t', 'ev', scp)
                    novl = getattr(self, 'l2_no_vl', False)
                    pbsrc = self.masks[:, 1536:2048] if getattr(self, 'l2_pb_normal', False) else Pb
                    for tc in range(2):
                        b.op('t', lambda e, es=es, tc=tc, novl=novl, pbsrc=pbsrc, psx=psx: e.matmul(
                            psx[:, 0:G], vt[:, tc*2048 + es*128:(tc*2048) + (es+1)*128],
                            pbsrc[:, tc*G:(tc+1)*G], start=(tc == 0),
                            stop=(novl and tc == 1)))
                    if getattr(self, 'l2_no_vl', False):
                        pass
                    else:
                        b.op('t', lambda e, es=es, g=g, psx=psx, n0=n0: e.matmul(
                            psx[:, 0:G], sbf[:, (es % 2)*128:(es % 2)*128 + 128],
                            lq[:, n0:n0+G], start=False, stop=True))
                    if getattr(self, 'l2_drain', False):
                        mv = b.op('t', lambda e: e.drain())
                    else:
                        mv = b.cnt['et']
                    esmm.append(mv)
                    n = ost["n"]; ost["n"] += 1
                    if n >= 6:
                        b.wait_t('v', ost["dma"][n - 6])
                    b.wait('s', 'et', mv)
                    if len(ost["ev"]) >= 2:
                        b.wait('s', 'ev', ost["ev"][-2])
                    sv = b.op('s', lambda e, n=n, psx=psx: e.activation(
                        self.scr[:, (n % 2)*TW:(n % 2)*TW + G], psx[:, 0:G], AF.Copy))
                    b.wait('v', 'es', sv)
                    ov = b.op('v', lambda e, es=es, n=n, n0=n0: e.tensor_mul(
                        oe[:, (n % 6)*G:(n % 6)*G + G],
                        self.scr[:, (n % 2)*TW:(n % 2)*TW + G],
                        gt[:, es*TW + n0: es*TW + n0 + G]))
                    ost["ev"].append(max(ov, sv) if False else ov)
                    ost.setdefault("esv", []).append(sv)
                    b.wait('y', 'ev', ov)
                    ost["dma"].append(b.dma('y', spill_o[es*128:(es+1)*128, t0+n0:t0+n0+G],
                                            oe[:, (n % 6)*G:(n % 6)*G + G]))
                tv = []
                for tc in range(2):
                    b.op('t', lambda e, tc=tc, g=g, n0=n0: e.transpose(
                        self.psb[:, tc*128:(tc+1)*128],
                        lk[:, n0+tc*128: n0+tc*128+128], self.ident[:]))
                mv = b.cnt['et']
                b.wait('v', 'et', mv)
                lkc = b.op('v', lambda e: e.tensor_copy(lkT[:, 0:256], self.psb[:, 0:256]))
                b.wait('t', 'ev', lkc)
                upd = []
                for et4 in range(4):
                    for tc in range(2):
                        b.op('t', lambda e, et4=et4, tc=tc: e.matmul(
                            self.ps[4][:, 0:TW], lkT[:, tc*128:(tc+1)*128],
                            vt[:, tc*2048 + et4*TW: tc*2048 + (et4+1)*TW],
                            start=(tc == 0), stop=(tc == 1)))
                    mv = b.cnt['et']
                    b.wait('v', 'et', mv)
                    uv = b.op('v', lambda e, et4=et4: e.tensor_add(
                        st[:, et4*TW:(et4+1)*TW], st[:, et4*TW:(et4+1)*TW],
                        self.ps[4][:, 0:TW]))
                    upd.append(uv)
                    if et4 < 3:
                        b.wait('t', 'ev', uv)
                b.wait('t', 'ev', upd[-1])
                b.selfwait('v')
        b.barrier()
        if getattr(self, 'l2_no_pass4', False):
            return
        consd = self.mk_delta(bias_at=lambda oc: self.cols[:, 17+oc:18+oc])
        self.gemm_sw(d["L2_attn_Wout"], 2048, D, consd, gw=256, wbuf_off=0,
                     in_stream=spill_o, ibuf='nrm', name="l2ao")
        b.barrier()

    def emit_l3_attn(self):
        b, d = self.b, self.dram
        ldc = b.dma_group('y', [(self.cols[:, 0:8], d["L3_attn_bo"], {})])
        ldm = b.dma('g', self.masks[:, 0:2048], d["l3_masks"])
        self.emit_ln()
        qp = self.abf(0, T)
        kp = self.abf(8 * 1024, T)
        vfm = self.abf(16 * 1024, T)
        vp = self.abf(24 * 1024, 32 * 130)
        aop = self.abf(33 * 1024, T)
        wqkv = self.abf(41 * 1024, 3 * 1024)
        pbuf = self.abf(47 * 1024, 8 * TW)
        b.op('v', lambda e, vp=vp: e.memset(
            vp[:].rearrange("p (a c) -> p a c", c=65)[:, :, 64:65], 1.0))
        b.op('v', lambda e: e.memset(self.scr2[:], 0.0))
        beta = float(64.0 ** -0.5)
        self._flash_state = {'last_exp': 0, 'last_pv': 0,
                             'tail_r': {0: 0, 1: 0}, 'tail_c': {0: 0, 1: 0},
                             'ov': []}
        WN = {"q": "L3_attn_Wq", "k": "L3_attn_Wk", "v": "L3_attn_Wv"}
        proj_done = 0
        stok = None
        for p in range(getattr(self, 'l3_pair_limit', None) or 8):
            b.wait('g', 'et', proj_done)
            wl = b.dma_group('g', [
                (wqkv[:, pi*1024 + kc*128: pi*1024 + (kc+1)*128],
                 d[WN[pn]][kc*128:(kc+1)*128, p*128:(p+1)*128], {})
                for pi, pn in enumerate(("q", "k", "v")) for kc in range(NKC)])
            b.wait_t('t', wl)
            pv = {}
            for pi, pn in enumerate(("q", "k", "v")):
                dstt = (qp, kp, vfm)[pi]
                for tt in range(NTT):
                    psq = self.ps[(pi * NTT + tt) % 2]
                    if pi * NTT + tt >= 2:
                        b.wait('t', 'es', pv[(pi * NTT + tt - 2)])
                    for kc in range(NKC):
                        b.op('t', lambda e, psq=psq, pi=pi, kc=kc, tt=tt: e.matmul(
                            psq[:, 0:TW],
                            wqkv[:, pi*1024 + kc*128: pi*1024 + (kc+1)*128],
                            self.nrm_tile(kc, tt), start=(kc == 0), stop=(kc == NKC - 1)))
                    mv = b.cnt['et']
                    b.wait('s', 'et', mv)
                    sc = {}
                    if pn == "q":
                        sc['scale'] = beta
                    pv[pi * NTT + tt] = b.op('s', lambda e, psq=psq, dstt=dstt, tt=tt, sc=sc: e.activation(
                        dstt[:, tt*TW:(tt+1)*TW], psq[:, 0:TW], AF.Copy, **sc))
            proj_done = b.cnt['et']
            if getattr(self, 'l3_stage', 9) < 2:
                continue
            cpv = []
            b.wait('t', 'es', pv[3 * NTT - 1])
            for jc in range(32):
                if cpv:
                    b.wait('t', 'ev', cpv[-1])
                b.op('t', lambda e, jc=jc, vfm=vfm: e.transpose(
                    self.psb[:, 0:128], vfm[:, jc*128:(jc+1)*128], self.ident[:]))
                mv = b.cnt['et']
                b.wait('v', 'et', mv)
                cpv.append(b.op('v', lambda e, jc=jc, vp=vp: e.tensor_copy(
                    vp[:, jc*130: jc*130 + 64], self.psb[:, 0:64])))
                cpv.append(b.op('v', lambda e, jc=jc, vp=vp: e.tensor_copy(
                    vp[:, jc*130 + 65: jc*130 + 129], self.psb[:, 64:128])))
            if getattr(self, 'l3_stage', 9) < 3:
                continue
            self._attn_bound(qp, kp)
            if getattr(self, 'l3_stage', 9) < 4:
                continue
            if stok is not None:
                b.wait_t('v', stok)
            self._attn_flash(qp, kp, vp, aop, pbuf, lambda qt: 4 * qt + 4)
            b.wait('y', 'ev', b.cnt['ev'])
            stok = b.dma('y', d["spill_b"][p*128:(p+1)*128, :], aop[:, 0:T])
        b.barrier()
        if getattr(self, 'l3_stage', 9) < 5:
            return
        consd = self.mk_delta()
        self.gemm_sw(d["L3_attn_Wo"], D, D, consd, gw=512, wbuf_off=0,
                     in_stream=d["spill_b"][0:1024, :], ibuf='arena',
                     ibuf_off=16 * 1024, name="l3o")
        b.barrier()
        bcol = self.cols[:, 0:8]
        for oc in range(8):
            vv = b.op('v', lambda e, oc=oc, bcol=bcol: e.tensor_scalar(
                self.evac[:, (oc % 2)*TW:(oc % 2)*TW + TW], self.ab[:, 0:TW],
                0.0, bcol[:, oc:oc+1], ALU.mult, ALU.add))
            b.wait('g', 'ev', vv)
            dv = None
            for tt in range(NTT):
                dv = b.dma('g', d["x_work"][oc*128:(oc+1)*128, tt*TW:(tt+1)*TW],
                           self.evac[:, (oc % 2)*TW:(oc % 2)*TW + TW], accum_op=ALU.add)
            b.wait_t('v', dv)
        b.barrier()

    def emit_l2_ffn(self):
        b, d = self.b, self.dram
        self.emit_ln()
        lv = b.dma('y', self.rows2[0:1, 8:12], d["L2_ffn_sc"])
        b.wait_t('v', lv)
        b.op('v', lambda e: e.tensor_scalar_mul(self.rows2[0:1, 12:13], self.rows2[0:1, 8:9], 4.0))
        b.op('v', lambda e: e.tensor_scalar_mul(self.rows2[0:1, 13:14], self.rows2[0:1, 9:10], 2.0))
        vv = b.op('v', lambda e: e.tensor_copy(self.rows2[0:1, 14:15], self.rows2[0:1, 10:11]))
        b.wait('t', 'ev', vv)
        for i in range(3):
            mv = b.op('t', lambda e, i=i: e.matmul(self.ps[6][:, 8+i:9+i], self.ones_f[0:1, :],
                                                   self.rows2[0:1, 12+i:13+i], start=True, stop=True))
        b.wait('v', 'et', mv)
        cv = None
        for i in range(3):
            cv = b.op('v', lambda e, i=i: e.tensor_copy(self.cols2[:, 40+i:41+i],
                                                        self.ps[6][:, 8+i:9+i]))
        b.wait('s', 'ev', cv)
        b.wait('t', 'ev', cv)
        b.barrier()
        wdst = self.abf(0, NKC * 4096)
        K0 = d["L2_ffn_k0"]
        subv = []
        for kc in range(NKC):
            for nb in range(8):
                half = (nb % 2) * TW
                gi = kc * 8 + nb
                if gi >= 2:
                    b.wait('g', 'ev', subv[gi - 2])
                l2 = b.dma_group('g', [
                    (self.scr[:, half:half+TW],
                     K0[kc*128:(kc+1)*128, nb*TW:(nb+1)*TW], {}),
                    (self.scr2[:, half:half+TW],
                     K0[1024 + kc*128: 1024 + (kc+1)*128, nb*TW:(nb+1)*TW], {})])
                b.wait_t('v', l2)
                subv.append(b.op('v', lambda e, kc=kc, nb=nb, half=half: e.tensor_sub(
                    wdst[:, kc*4096 + nb*TW: kc*4096 + (nb+1)*TW],
                    self.scr[:, half:half+TW], self.scr2[:, half:half+TW])))
        b.barrier()
        cons0 = self.mk_spill(d["spill_a"], AF.Relu,
                              scale_at=lambda oc: self.cols2[:, 40:41])
        self.gemm_rw(lambda kc, oc: wdst[:, kc*4096 + oc*128: kc*4096 + oc*128 + 128],
                     D, 4096, cons0, in_tile=self.nrm_tile, name="l2f0")
        b.barrier()
        def sgn_xform(og, buf, lv, wbuf, nkc, gw, ow):
            b.wait_t('v', lv)
            last = None
            for kc in range(nkc):
                w_ap = wbuf[:, buf*nkc*gw + kc*gw: buf*nkc*gw + kc*gw + ow]
                last = b.op('v', lambda e, w_ap=w_ap: e.tensor_scalar(
                    w_ap, w_ap, 2.0, -1.0, ALU.mult, ALU.add))
            return ('ev', last)
        cons1 = self.mk_spill(d["spill_b"], AF.Relu,
                              scale_at=lambda oc: self.cols2[:, 41:42])
        self.gemm_sw(d["L2_ffn_k1"], 4096, 4096, cons1, gw=256, wbuf_off=0,
                     in_stream=d["spill_a"], ibuf='nrm', w_xform=sgn_xform, name="l2f1")
        b.barrier()
        consd = self.mk_delta(scale_at=lambda oc: self.cols2[:, 42:43])
        self.gemm_sw(d["L2_ffn_k2"], 4096, D, consd, gw=256, wbuf_off=0,
                     in_stream=d["spill_b"], ibuf='nrm', w_xform=sgn_xform, name="l2f2")
        b.barrier()

    def emit_l1_ffn(self):
        b, d = self.b, self.dram
        lv = b.dma_group('y', [(self.cols[:, 0:32], d["L1_ffn_w1_s"], {}),
                               (self.cols[:, 32:64], d["L1_ffn_w2_s"], {}),
                               (self.cols2[:, 0:32], d["L1_ffn_h1_s"], {}),
                               (self.cols2[:, 32:64], d["L1_ffn_h2_s"], {})])
        b.wait_t('s', lv)
        b.wait_t('v', lv)
        self.emit_ln()
        consA = self.mk_spill(d["spill_b"], AF.Silu,
                              scale_at=lambda oc: self.cols[:, oc:oc+1])
        self.gemm_sw(d["L1_ffn_w1t"], D, 4096, consA, gw=512, wbuf_off=0,
                     in_tile=self.nrm_tile, name="l1fA")
        b.barrier()
        if getattr(self, 'l1f_stage', 9) < 2:
            return
        tbuf = self.abf(32 * 1024, 4 * TW)
        ebuf = self.abf(40 * 1024, 6 * TW)
        glist = [(og*4 + oc, tt*TW) for og in range(8) for tt in range(8)
                 for oc in range(4)]
        stateB = {"n": 0, "dma": [], "tl": [], "ld": []}

        def issue_t1(n):
            ocg, tok0 = glist[n]
            if n >= 4:
                b.wait('y', 'ev', stateB["tl"][n - 4])
            stateB["ld"].append(b.dma('y', tbuf[:, (n % 4)*TW:(n % 4)*TW + TW],
                                      d["spill_b"][ocg*128:(ocg+1)*128, tok0:tok0+TW]))
        issue_t1(0)
        issue_t1(1)

        def consB(oc, tt, ps, tok0):
            n = stateB["n"]; stateB["n"] += 1
            if n + 2 < len(glist):
                issue_t1(n + 2)
            tb = tbuf[:, (n % 4)*TW:(n % 4)*TW + TW]
            dst = ebuf[:, (n % 6)*TW:(n % 6)*TW + TW]
            if n >= 6:
                b.wait_t('v', stateB["dma"][n - 6])
            b.wait('v', 'et', b.cnt['et'])
            b.wait_t('v', stateB["ld"][n])
            v = b.op('v', lambda e, dst=dst, ps=ps, oc=oc, tb=tb: e.scalar_tensor_tensor(
                dst, ps[:, 0:TW], self.cols[:, 32+oc:33+oc], tb, ALU.mult, ALU.mult))
            stateB["tl"].append(v)
            b.wait('y', 'ev', v)
            dv = b.dma('y', d["spill_a"][oc*128:(oc+1)*128, tok0:tok0+TW], dst)
            stateB["dma"].append(dv)
            return ('ev', v)
        self.gemm_sw(d["L1_ffn_w2t"], D, 4096, consB, gw=512, wbuf_off=0,
                     in_tile=self.nrm_tile, name="l1fB")
        b.barrier()
        if getattr(self, 'l1f_stage', 9) < 3:
            return
        consC = self.mk_spill(d["spill_b"], AF.Gelu,
                              scale_at=lambda oc: self.cols2[:, oc:oc+1])
        self.gemm_sw(d["L1_ffn_h1t"], 4096, 4096, consC, gw=512, wbuf_off=0,
                     in_stream=d["spill_a"], ibuf='nrm', name="l1fC")
        b.barrier()
        if getattr(self, 'l1f_stage', 9) < 4:
            return
        consD = self.mk_spill(d["spill_a"], AF.Gelu,
                              scale_at=lambda oc: self.cols2[:, 32+oc:33+oc])
        self.gemm_sw(d["L1_ffn_h2t"], 4096, 4096, consD, gw=512, wbuf_off=0,
                     in_stream=d["spill_b"], ibuf='nrm', name="l1fD")
        b.barrier()
        if getattr(self, 'l1f_stage', 9) < 5:
            return
        lv = b.dma('y', self.cols[:, 0:8], d["L1_ffn_w3_s"])
        b.wait_t('v', lv)
        consE = self.mk_delta(scale_at=lambda oc: self.cols[:, oc:oc+1])
        self.gemm_sw(d["L1_ffn_w3t"], 4096, D, consE, gw=512, wbuf_off=0,
                     in_stream=d["spill_a"], ibuf='nrm', name="l1fE")
        b.barrier()


def host_prep(d, core, layers, x_start):
    ins = {}
    ins["x_in"] = np.ascontiguousarray(x_start[core].T).astype(np.float32)
    ins["ones_tab"] = np.ones((1, 128), np.float32)
    ins["ident_tab"] = np.eye(128, dtype=np.float32).astype(BF)
    if "l0_attn" in layers:
        ins["L0_attn_W_qkv"] = d["L0_attn_W_qkv"].astype(np.float32)
        ins["L0_attn_W_g"] = d["L0_attn_W_g"].astype(np.float32)
        ins["L0_attn_W_proj"] = d["L0_attn_W_proj"].astype(np.float32)
        ins["L0_attn_decay"] = d["L0_attn_decay"].reshape(1, 8).astype(np.float32)
        QK = 1024
        pos = np.arange(T, dtype=np.float32)
        base_scale = ((np.arange(0, QK, 2, dtype=np.float32) + 0.4 * QK) / (1.4 * QK))
        scale = base_scale[None, :] ** (pos[:, None] / np.float32(512.0))
        inv_freq = (1.0 / 10000.0 ** (np.arange(QK // 2, dtype=np.float32) / (QK // 2))
                    ).astype(np.float32)
        sinusoid = pos[:, None] * inv_freq[None, :]
        sn, cs = np.sin(sinusoid), np.cos(sinusoid)
        beta = np.float32(128.0 ** -0.5)
        rep = lambda a: np.ascontiguousarray(np.repeat(a, 2, axis=1).T.astype(np.float32))
        ins["xpos_cq"] = rep(cs * scale) * beta
        ins["xpos_sq"] = rep(sn * scale) * beta
        ins["xpos_ck"] = rep(cs / scale)
        ins["xpos_sk"] = rep(sn / scale)
        R = np.zeros((128, 128), np.float32)
        ii = np.arange(0, 128, 2)
        R[ii + 1, ii] = -1.0
        R[ii, ii + 1] = 1.0
        ins["rot_tab"] = R.astype(BF)
        jj = np.arange(128, dtype=np.float32)
        E = jj[None, :] - jj[:, None]
        ins["exp_tab"] = np.where(E >= 0, E, 1e9).astype(np.float32)
        ins["iota_col"] = jj.reshape(128, 1).copy()
    if "l0_ffn" in layers:
        ins["L0_ffn_lng"] = np.ascontiguousarray(d["L0_ffn_ln_gamma"].reshape(NKC, 128).T)
        ins["L0_ffn_lnb"] = np.ascontiguousarray(d["L0_ffn_ln_beta"].reshape(NKC, 128).T)
        ins["L0_ffn_fc1_q"] = d["L0_ffn_fc1_q"]
        ins["L0_ffn_fc1_scales"] = d["L0_ffn_fc1_scales"]
        ins["L0_ffn_fc1_bias"] = np.ascontiguousarray(d["L0_ffn_fc1_bias"].reshape(32, 128).T)
        ins["L0_ffn_fc2_q"] = d["L0_ffn_fc2_q"]
        ins["L0_ffn_fc2_scales"] = d["L0_ffn_fc2_scales"]
        ins["L0_ffn_fc2_bias"] = np.ascontiguousarray(d["L0_ffn_fc2_bias"].reshape(8, 128).T)
    if "l1_attn" in layers:
        ins["L1_attn_W1t"] = np.ascontiguousarray(d["L1_attn_W1"].T)
        ins["L1_attn_b1"] = d["L1_attn_b1"].reshape(8, 128).T.copy()
        ins["L1_attn_W2t"] = np.ascontiguousarray(d["L1_attn_W2"].T)
        ins["L1_attn_b2"] = d["L1_attn_b2"].reshape(8, 128).T.copy()
        ins["L1_attn_Wgt"] = np.ascontiguousarray(d["L1_attn_Wg"].T)
        ins["L1_attn_bg"] = d["L1_attn_bg"].reshape(24, 128).T.copy()
        ins["L1_attn_Wot"] = np.ascontiguousarray(d["L1_attn_Wo"].T)
        ins["L1_attn_bo"] = d["L1_attn_bo"].reshape(8, 128).T.copy()
        TWl = 512
        mm = np.zeros((128, 5 * TWl), np.float32)
        for k in range(5):
            r = np.arange(128)[:, None] + 128 * k
            c = np.arange(TWl)[None, :]
            mm[:, k*TWl:(k+1)*TWl] = (r <= c + 63)
        ins["l1_masks"] = mm.astype(BF)
    if "l2_attn" in layers:
        ins["L2_attn_Wv"] = d["L2_attn_Wv"]
        ins["L2_attn_bv"] = np.ascontiguousarray(d["L2_attn_bv"].reshape(16, 128).T)
        ins["L2_attn_Wg"] = d["L2_attn_Wg"]
        ins["L2_attn_bg"] = np.ascontiguousarray(d["L2_attn_bg"].reshape(16, 128).T)
        ins["L2_attn_Win"] = d["L2_attn_Win"]
        ins["L2_attn_binc"] = d["L2_attn_bin_"].reshape(1, 128).T.copy()
        ins["L2_attn_Wout"] = d["L2_attn_Wout"]
        ins["L2_attn_bout"] = np.ascontiguousarray(d["L2_attn_bout"].reshape(8, 128).T)
        ins["L2_attn_affs"] = np.stack([d["L2_attn_" + nm] for nm in
                                        ("g_qq", "b_qq", "g_qk", "b_qk",
                                         "g_lq", "b_lq", "g_lk", "b_lk")], axis=1).astype(np.float32)
        G = 256
        m = np.zeros((128, 512), np.float32)
        n = np.arange(G)
        m[:, 0:256] = (np.arange(128)[:, None] <= n[None, :])
        m[:, 256:512] = ((128 + np.arange(128))[:, None] <= n[None, :])
        ins["l2_masks"] = m.astype(BF)
    if "l2_ffn" in layers:
        ins["L2_ffn_k0"] = d["L2_ffn_k0"].astype(np.uint8)
        ins["L2_ffn_k1"] = d["L2_ffn_k1"].astype(np.uint8)
        ins["L2_ffn_k2"] = d["L2_ffn_k2"].astype(np.uint8)
        ins["L2_ffn_sc"] = np.array([[d["L2_ffn_s0"], d["L2_ffn_s1"], d["L2_ffn_s2"], 0.0]],
                                    np.float32)
    if "l1_ffn" in layers:
        for nm, tr in (("w1", "L1_ffn_w1_w"), ("w2", "L1_ffn_w2_w"),
                       ("h1", "L1_ffn_h1_w"), ("h2", "L1_ffn_h2_w"),
                       ("w3", "L1_ffn_w3_w")):
            ins[f"L1_ffn_{nm}t"] = np.ascontiguousarray(d[tr].T)
        for nm in ("w1_s", "w2_s", "h1_s", "h2_s", "w3_s"):
            v = d[f"L1_ffn_{nm}"]
            ins[f"L1_ffn_{nm}"] = np.ascontiguousarray(v.reshape(-1, 128).T)
    if "l3_attn" in layers:
        for nm in ("Wq", "Wk", "Wv", "Wo"):
            ins[f"L3_attn_{nm}"] = d[f"L3_attn_{nm}"]
        ins["L3_attn_bo"] = d["L3_attn_bo"].reshape(8, 128).T.copy()
        TWl = 512
        mm = np.zeros((128, 4 * TWl), np.float32)
        for k in range(4):
            r = np.arange(128)[:, None] + 128 * k
            c = np.arange(TWl)[None, :]
            mm[:, k*TWl:(k+1)*TWl] = (r <= c)
        ins["l3_masks"] = mm.astype(BF)
    if "l3_ffn" in layers:
        ins["L3_ffn_W1t"] = np.ascontiguousarray(d["L3_ffn_W1"].T)
        ins["L3_ffn_b1"] = np.ascontiguousarray(d["L3_ffn_b1"].reshape(32, 128).T)
        ins["L3_ffn_W2t"] = np.ascontiguousarray(d["L3_ffn_W2"].T)
        ins["L3_ffn_b2"] = np.ascontiguousarray(d["L3_ffn_b2"].reshape(8, 128).T)
    return ins


def build(layers, dbg=False, dbg_hook=None):
    nc = bass.Bass("TRN2", target_bir_lowering=False)
    with ExitStack() as stack:
        mk = MK(nc, stack, layers, dbg=dbg)
        mk.emit_init()
        for lay in layers:
            getattr(mk, "emit_" + lay)()
            if dbg_hook is not None:
                dbg_hook(mk, lay)
        mk.emit_finish()
        mk.b.finalize_into_block()
    return nc, mk


def dump_nrm(mk, lay=None):
    b, d = mk.b, mk.dram
    b.barrier()
    tok = b.dma_group('y', [(d["dbg_nrm"][kc*128:(kc+1)*128, :],
                             mk.nrm[:, kc*T:(kc+1)*T], {}) for kc in range(NKC)])
    b.wait_t('y', tok)
    b.barrier()


def dump_arena(mk):
    b, d = mk.b, mk.dram
    b.barrier()
    tok = b.dma_group('y', [(d["dbg_w"], mk.ar_bf[:, 0:32768], {})])
    b.wait_t('y', tok)
    b.barrier()


LAYERS = ["l0_attn", "l0_ffn", "l1_attn", "l1_ffn",
          "l2_attn", "l2_ffn", "l3_attn", "l3_ffn"]
_CACHE = {}


def _get_program():
    if "nc" not in _CACHE:
        nc, mkk = build(LAYERS)
        _CACHE["nc"] = nc
    return _CACHE["nc"]


def kernel(**inputs):
    nc = _get_program()
    x = np.asarray(inputs["L0_attn_hidden_states"], np.float32)
    ins_list = [host_prep(inputs, core, LAYERS, x) for core in range(8)]
    res = run_bass_kernel_spmd(nc, ins_list, core_ids=list(range(8)))
    out = np.empty((8, T, D), np.float32)
    for core in range(8):
        out[core] = np.asarray(res.results[core]["x_out"], np.float32).T
    return out


# version 04
# speedup vs baseline: 1.7695x; 1.0266x over previous
import math
from contextlib import ExitStack

import numpy as np
import ml_dtypes

import concourse.bass as bass
import concourse.mybir as mybir
from concourse.bass_utils import run_bass_kernel_spmd

BF = ml_dtypes.bfloat16
F32 = mybir.dt.float32
BF16 = mybir.dt.bfloat16
I32 = mybir.dt.int32
U8 = mybir.dt.uint8
AF = mybir.ActivationFunctionType
ALU = mybir.AluOpType
AX = mybir.AxisListType

T = 4096
D = 1024
NKC = D // 128
TW = 512
NTT = T // TW

ENGS = ('t', 'v', 's', 'g', 'y')
SEM_OF = {'t': 'et', 'v': 'ev', 's': 'es'}
NLANE = 4


class Bld:

    def __init__(self, nc, stack):
        self.nc = nc
        self.q = {e: [] for e in ENGS}
        self.semh = {}
        self.cnt = {}
        for name in SEM_OF.values():
            self.semh[name] = stack.enter_context(nc.semaphore(name))
            self.cnt[name] = 0
        for qn in ('g', 'y'):
            for i in range(NLANE):
                nm = f"{qn}{i}"
                self.semh[nm] = stack.enter_context(nc.semaphore(nm))
                self.cnt[nm] = 0
        self.rot = {'g': 0, 'y': 0}
        self.lane_prev = {f"{qn}{i}": 0 for qn in ('g', 'y') for i in range(NLANE)}

    def op(self, eng, fn, n=1):
        assert eng in ('t', 'v', 's')
        sem = SEM_OF[eng]
        self.cnt[sem] += n
        h = self.semh[sem]
        def f(e, fn=fn, h=h, n=n):
            fn(e).then_inc(h, n)
        self.q[eng].append(f)
        return self.cnt[sem]

    def _lane(self, qn):
        i = self.rot[qn] % NLANE
        self.rot[qn] += 1
        return f"{qn}{i}"

    def dma(self, qn, dst, src, **kw):
        return self.dma_group(qn, [(dst, src, kw)])

    def dma_group(self, qn, items):
        assert qn in ('g', 'y')
        lane = self._lane(qn)
        if self.lane_prev[lane] > 0:
            self.wait(qn, lane, self.lane_prev[lane])
        h = self.semh[lane]
        for dst, src, kw in items:
            self.cnt[lane] += 16
            def f(e, dst=dst, src=src, kw=kw, h=h):
                e.dma_start(dst, src, **kw).then_inc(h, 16)
            self.q[qn].append(f)
        self.lane_prev[lane] = self.cnt[lane]
        return (lane, self.cnt[lane])

    def wait(self, eng, sem, v):
        assert v <= self.cnt[sem], f"wait {eng} {sem} {v} > {self.cnt[sem]}"
        if v <= 0:
            return
        h = self.semh[sem]
        self.q[eng].append(lambda e, h=h, v=v: e.wait_ge(h, v))

    def wait_t(self, eng, tok):
        if tok is not None:
            self.wait(eng, tok[0], tok[1])

    def selfwait(self, eng):
        sem = SEM_OF[eng]
        self.wait(eng, sem, self.cnt[sem])

    def barrier(self):
        snap = dict(self.cnt)
        own = {'t': 'et', 'v': 'ev', 's': 'es'}
        for eng in ENGS:
            for sem, v in snap.items():
                if own.get(eng) != sem:
                    self.wait(eng, sem, v)

    def finalize_into_block(self):
        nc = self.nc
        with nc.Block() as blk:
            @blk.tensor
            def _(e):
                for f in self.q['t']: f(e)
            @blk.vector
            def _(e):
                for f in self.q['v']: f(e)
            @blk.scalar
            def _(e):
                for f in self.q['s']: f(e)
            @blk.gpsimd
            def _(e):
                for f in self.q['g']: f(e)
            @blk.sync
            def _(e):
                for f in self.q['y']: f(e)


class MK:
    def __init__(self, nc, stack, layers, dbg=False):
        self.nc = nc
        self.b = Bld(nc, stack)
        self.layers = layers
        self.dbg = dbg
        self.dram = {}
        self._declare_dram()
        self._alloc_sbuf(stack)
        self.ps = [stack.enter_context(nc.psum_tensor(f"ps{i}", [128, 512], F32))
                   for i in range(7)]
        self.psb = stack.enter_context(nc.psum_tensor("psb", [128, 1024], BF16))

    def dt(self, name, shape, dtype, kind="ExternalInput"):
        t = self.nc.dram_tensor(name, list(shape), dtype, kind=kind)
        self.dram[name] = t.ap()
        return self.dram[name]

    def _declare_dram(self):
        dt = self.dt
        L = self.layers
        dt("x_in", [D, T], F32)
        dt("x_out", [D, T], F32, kind="ExternalOutput")
        dt("x_work", [D, T], F32, kind="Internal")
        spk = "ExternalOutput" if self.dbg else "Internal"
        dt("spill_a", [4096, T], BF16, kind=spk)
        dt("spill_b", [4096, T], BF16, kind=spk)
        if self.dbg:
            dt("dbg_nrm", [D, T], BF16, kind="ExternalOutput")
            dt("dbg_w", [128, 32768], BF16, kind="ExternalOutput")
        dt("ones_tab", [1, 128], F32)
        dt("ident_tab", [128, 128], BF16)
        if "l0_attn" in L:
            dt("L0_attn_W_qkv", [D, 4096], F32)
            dt("L0_attn_W_g", [D, 2048], F32)
            dt("L0_attn_W_proj", [2048, D], F32)
            dt("L0_attn_decay", [1, 8], F32)
            for nm in ("xpos_cq", "xpos_sq", "xpos_ck", "xpos_sk"):
                dt(nm, [D, T], F32)
            dt("rot_tab", [128, 128], BF16)
            dt("exp_tab", [128, 128], F32)
            dt("iota_col", [128, 1], F32)
        if "l0_ffn" in L:
            dt("L0_ffn_lng", [128, NKC], F32)
            dt("L0_ffn_lnb", [128, NKC], F32)
            dt("L0_ffn_fc1_q", [D, 4096], I32)
            dt("L0_ffn_fc1_scales", [8, 4096], F32)
            dt("L0_ffn_fc1_bias", [128, 32], F32)
            dt("L0_ffn_fc2_q", [4096, D], I32)
            dt("L0_ffn_fc2_scales", [32, D], F32)
            dt("L0_ffn_fc2_bias", [128, 8], F32)
        if "l1_attn" in L:
            dt("L1_attn_W1t", [D, D], F32)
            dt("L1_attn_b1", [128, 8], F32)
            dt("L1_attn_W2t", [D, D], F32)
            dt("L1_attn_b2", [128, 8], F32)
            dt("L1_attn_Wgt", [D, 3072], F32)
            dt("L1_attn_bg", [128, 24], F32)
            dt("L1_attn_Wot", [D, D], F32)
            dt("L1_attn_bo", [128, 8], F32)
            dt("l1_masks", [128, 5 * TW], BF16)
        if "l1_ffn" in L:
            dt("L1_ffn_w1t", [D, 4096], F32)
            dt("L1_ffn_w1_s", [128, 32], F32)
            dt("L1_ffn_w2t", [D, 4096], F32)
            dt("L1_ffn_w2_s", [128, 32], F32)
            dt("L1_ffn_h1t", [4096, 4096], F32)
            dt("L1_ffn_h1_s", [128, 32], F32)
            dt("L1_ffn_h2t", [4096, 4096], F32)
            dt("L1_ffn_h2_s", [128, 32], F32)
            dt("L1_ffn_w3t", [4096, D], F32)
            dt("L1_ffn_w3_s", [128, 8], F32)
        if "l2_attn" in L:
            dt("L2_attn_Wv", [D, 2048], F32)
            dt("L2_attn_bv", [128, 16], F32)
            dt("L2_attn_Wg", [D, 2048], F32)
            dt("L2_attn_bg", [128, 16], F32)
            dt("L2_attn_Win", [D, 128], F32)
            dt("L2_attn_binc", [128, 1], F32)
            dt("L2_attn_Wout", [2048, D], F32)
            dt("L2_attn_bout", [128, 8], F32)
            dt("L2_attn_affs", [128, 8], F32)
            dt("l2_masks", [128, 2 * 256], BF16)
        if "l2_ffn" in L:
            dt("L2_ffn_k0", [2048, 4096], U8)
            dt("L2_ffn_k1", [4096, 4096], U8)
            dt("L2_ffn_k2", [4096, D], U8)
            dt("L2_ffn_sc", [1, 4], F32)
        if "l3_attn" in L:
            dt("L3_attn_Wq", [D, D], F32)
            dt("L3_attn_Wk", [D, D], F32)
            dt("L3_attn_Wv", [D, D], F32)
            dt("L3_attn_Wo", [D, D], F32)
            dt("L3_attn_bo", [128, 8], F32)
            dt("l3_masks", [128, 4 * TW], BF16)
        if "l3_ffn" in L:
            dt("L3_ffn_W1t", [D, 4096], F32)
            dt("L3_ffn_b1", [128, 32], F32)
            dt("L3_ffn_W2t", [4096, D], F32)
            dt("L3_ffn_b2", [128, 8], F32)

    def _alloc_sbuf(self, stack):
        nc = self.nc
        def sb(name, shape, dtype):
            t = stack.enter_context(nc.sbuf_tensor(name, list(shape), dtype))
            setattr(self, name, t)
            return t
        sb("nrm", [128, NKC * T], BF16)
        sb("xtile", [128, NKC * TW], F32)
        sb("evac", [128, 6 * TW], F32)
        sb("rows", [1, 2048], F32)
        sb("rows2", [1, 4096], F32)
        sb("ab", [128, 2 * TW], F32)
        sb("cols", [128, 64], F32)
        sb("cols2", [128, 64], F32)
        sb("ones_f", [1, 128], F32)
        sb("ones_c", [128, 1], F32)
        sb("ones_b", [1, 128], BF16)
        sb("ident", [128, 128], BF16)
        sb("bcrf", [128, 128], F32)
        sb("masks", [128, 2688], BF16)
        sb("scr", [128, 2 * TW], F32)
        sb("scr2", [128, 2 * TW], F32)
        self.ARENA = 72 * 1024
        sb("arena8", [128, self.ARENA], U8)
        self.ar_bf = self.arena8.bitcast(BF16)
        self.ar_f32 = self.arena8.bitcast(F32)
        rem = nc.sbuf_bytes_remaining
        assert rem >= 0, f"SBUF over budget by {-rem}"
        self.sbuf_remaining = rem

    def abf(self, off_b, width):
        assert off_b % 2 == 0 and off_b + 2 * width <= self.ARENA, (off_b, width)
        return self.ar_bf[:, off_b // 2: off_b // 2 + width]

    def af32(self, off_b, width):
        assert off_b % 4 == 0 and off_b + 4 * width <= self.ARENA, (off_b, width)
        return self.ar_f32[:, off_b // 4: off_b // 4 + width]

    def emit_init(self):
        b, d = self.b, self.dram
        b.dma('y', self.ones_f[:], d["ones_tab"])
        b.dma('y', self.ident[:], d["ident_tab"])
        b.op('v', lambda e: e.memset(self.ones_b[:], 1.0))
        b.op('v', lambda e: e.memset(self.bcrf[:], 0.0))
        b.op('v', lambda e: e.memset(self.bcrf[0:1, :], 1.0))
        b.op('v', lambda e: e.memset(self.ones_c[:], 1.0))
        b.dma('y', d["x_work"], d["x_in"])
        b.barrier()

    def emit_finish(self):
        b, d = self.b, self.dram
        b.barrier()
        done = b.dma('y', d["x_out"], d["x_work"])
        b.wait_t('y', done)
        b.barrier()

    def emit_ln(self, gamma_col=None, beta_col=None, eps=1e-5):
        b, d = self.b, self.dram
        x_src = d["x_work"]
        nrm, xtile, R, ab, scr = self.nrm, self.xtile, self.rows, self.ab, self.scr
        psS, psQ, psB = self.ps[4], self.ps[5], self.ps[6]
        for tt in range(NTT):
            t0 = tt * TW
            ldv = b.dma_group('y', [(xtile[:, kc*TW:(kc+1)*TW],
                                     x_src[kc*128:(kc+1)*128, t0:t0+TW], {})
                                    for kc in range(NKC)])
            b.wait_t('t', ldv)
            for kc in range(NKC):
                b.op('t', lambda e, kc=kc: e.matmul(
                    psS[0:1, 0:TW], self.ones_c[:, 0:1], xtile[:, kc*TW:(kc+1)*TW],
                    start=(kc == 0), stop=(kc == NKC - 1)))
            et_s1 = b.cnt['et']
            b.wait_t('s', ldv)
            sqv = []
            mm2 = []
            for kc in range(NKC):
                half = (kc % 2) * TW
                if kc >= 2:
                    b.wait('s', 'et', mm2[kc - 2])
                sqv.append(b.op('s', lambda e, kc=kc, half=half: e.activation(
                    scr[:, half:half+TW], xtile[:, kc*TW:(kc+1)*TW], AF.Square)))
                b.wait('t', 'es', sqv[kc])
                mm2.append(b.op('t', lambda e, kc=kc, half=half: e.matmul(
                    psQ[0:1, 0:TW], self.ones_c[:, 0:1], scr[:, half:half+TW],
                    start=(kc == 0), stop=(kc == NKC - 1))))
            mmv = b.cnt['et']
            b.wait('v', 'et', mmv)
            b.op('v', lambda e: e.tensor_scalar_mul(R[0:1, 0:TW], psS[0:1, 0:TW], 1.0 / D))
            b.op('v', lambda e: e.tensor_scalar_mul(R[0:1, TW:2*TW], psQ[0:1, 0:TW], 1.0 / D))
            b.selfwait('v')
            b.op('v', lambda e: e.tensor_mul(R[0:1, 2*TW:3*TW], R[0:1, 0:TW], R[0:1, 0:TW]))
            b.selfwait('v')
            b.op('v', lambda e: e.tensor_scalar(R[0:1, 3*TW:4*TW], R[0:1, 2*TW:3*TW],
                                                -1.0, eps, ALU.mult, ALU.add))
            b.selfwait('v')
            vv = b.op('v', lambda e: e.tensor_add(R[0:1, 3*TW:4*TW], R[0:1, 3*TW:4*TW],
                                                  R[0:1, TW:2*TW]))
            b.wait('s', 'ev', vv)
            sv = b.op('s', lambda e: e.activation(R[0:1, 2*TW:3*TW], R[0:1, 3*TW:4*TW], AF.Sqrt))
            b.wait('v', 'es', sv)
            b.op('v', lambda e: e.reciprocal(R[0:1, 3*TW:4*TW], R[0:1, 2*TW:3*TW]))
            vv = b.op('v', lambda e: e.tensor_scalar_mul(R[0:1, 2*TW:3*TW],
                                                         R[0:1, 0:TW], -1.0))
            b.wait('t', 'ev', vv)
            b.op('t', lambda e: e.matmul(psB[:, 0:TW], self.ones_f[0:1, :],
                                         R[0:1, 3*TW:4*TW], start=True, stop=True))
            mmv = b.op('t', lambda e: e.matmul(psS[:, 0:TW], self.ones_f[0:1, :],
                                               R[0:1, 2*TW:3*TW], start=True, stop=True))
            b.wait('v', 'et', mmv)
            b.op('v', lambda e: e.tensor_copy(ab[:, 0:TW], psB[:, 0:TW]))
            b.op('v', lambda e: e.tensor_copy(ab[:, TW:2*TW], psS[:, 0:TW]))
            b.selfwait('v')
            for kc in range(NKC):
                half = (kc % 2) * TW
                dst = nrm[:, kc*T + t0: kc*T + t0 + TW]
                b.op('v', lambda e, kc=kc, half=half: e.tensor_add(
                    scr[:, half:half+TW], xtile[:, kc*TW:(kc+1)*TW], ab[:, TW:2*TW]))
                b.selfwait('v')
                b.op('v', lambda e, half=half, dst=dst: e.tensor_mul(
                    dst, scr[:, half:half+TW], ab[:, 0:TW]))
                if gamma_col is not None:
                    gk = gamma_col(kc); bk = beta_col(kc)
                    b.selfwait('v')
                    b.op('v', lambda e, dst=dst, gk=gk, bk=bk: e.tensor_scalar(
                        dst, dst, gk, bk, ALU.mult, ALU.add))
            b.wait('y', 'ev', b.cnt['ev'])
            b.wait('y', 'es', b.cnt['es'])
            b.wait('y', 'et', b.cnt['et'])
        b.barrier()

    def gemm_sw(self, W_dram, Kdim, Ndim, consume, *, in_tile=None, in_stream=None,
                gw=512, toks=T, wbuf_off=0, ibuf='arena', ibuf_off=None, name="",
                w_xform=None):
        b = self.b
        nkc = Kdim // 128
        nog = (Ndim + gw - 1) // gw
        ntt = toks // TW
        wbuf = self.abf(wbuf_off, 2 * nkc * gw)
        if in_stream is not None:
            if ibuf == 'nrm':
                assert 2 * nkc * TW <= NKC * T
                ibuft = self.nrm[:, 0: 2 * nkc * TW]
            else:
                ibuft = self.abf(ibuf_off, 2 * nkc * TW)
        PS = self.ps[0:4]
        group_after = {}
        tile_after = []
        cons = []
        ngroups = 0
        git = 0
        wl = {}
        ilvs = {}

        def issue_tile(g):
            half = g % 2
            if g >= 2:
                b.wait('y', 'et', tile_after[g - 2])
            assert nkc % 4 == 0
            tok0 = (g % ntt) * TW
            ilvs[g] = b.dma_group('y', [
                (ibuft[:, half*nkc*TW + g4*4*TW: half*nkc*TW + (g4+1)*4*TW]
                 .rearrange("p (k c) -> p k c", k=4),
                 in_stream[g4*512:(g4+1)*512, tok0:tok0+TW]
                 .rearrange("(k p) c -> p k c", p=128), {})
                for g4 in range(nkc // 4)])
        for og in range(nog):
            buf = og % 2
            ow = min(gw, Ndim - og * gw)
            noc = (ow + 127) // 128
            if og >= 2:
                b.wait('g', 'et', group_after[og - 2])
            lv = b.dma_group('g', [(wbuf[:, buf*nkc*gw + kc*gw: buf*nkc*gw + kc*gw + ow],
                                    W_dram[kc*128:(kc+1)*128, og*gw:og*gw + ow], {})
                                   for kc in range(nkc)])
            if w_xform is not None:
                wl[og] = w_xform(og, buf, lv, wbuf, nkc, gw, ow)
            else:
                wl[og] = lv
            for tt in range(ntt):
                tok0 = tt * TW
                if in_stream is not None:
                    half = git % 2
                    if git == 0:
                        issue_tile(0)
                    if git + 1 < nog * ntt:
                        issue_tile(git + 1)
                for oc in range(noc):
                    ocg = og * (gw // 128) + oc
                    ps = PS[ngroups % 4]
                    if ngroups >= 4:
                        sem, val = cons[ngroups - 4]
                        b.wait('t', sem, val)
                    if oc == 0 and tt == 0:
                        b.wait_t('t', wl[og])
                    if in_stream is not None and oc == 0:
                        b.wait_t('t', ilvs[git])
                    for kc in range(nkc):
                        if in_stream is not None:
                            rhs = ibuft[:, half*nkc*TW + kc*TW: half*nkc*TW + (kc+1)*TW]
                        else:
                            rhs = in_tile(kc, tt)
                        b.op('t', lambda e, ps=ps, buf=buf, kc=kc, oc=oc, ow=ow, rhs=rhs: e.matmul(
                            ps[:, 0:TW],
                            wbuf[:, buf*nkc*gw + kc*gw + oc*128: buf*nkc*gw + kc*gw + min(oc*128+128, ow)],
                            rhs, start=(kc == 0), stop=(kc == nkc - 1)))
                    group_after[og] = b.cnt['et']
                    cons.append(consume(ocg, tt, ps, tok0))
                    ngroups += 1
                tile_after.append(b.cnt['et'])
                git += 1
        return cons

    def gemm_rw(self, w_sb_fn, Kdim, Ndim, consume, *, in_tile=None, in_stream=None,
                toks=T, ibuf='nrm', ibuf_off=None, name=""):
        b = self.b
        nkc = Kdim // 128
        noc = (Ndim + 127) // 128
        ntt = toks // TW
        if in_stream is not None:
            if ibuf == 'nrm':
                assert 2 * nkc * TW <= NKC * T
                ibuft = self.nrm[:, 0: 2 * nkc * TW]
            else:
                ibuft = self.abf(ibuf_off, 2 * nkc * TW)
        PS = self.ps[0:4]
        cons = []
        tile_after = []
        ngroups = 0
        ilvs = {}

        def issue_tile(g):
            half = g % 2
            if g >= 2:
                b.wait('y', 'et', tile_after[g - 2])
            assert nkc % 4 == 0
            tok0 = g * TW
            ilvs[g] = b.dma_group('y', [
                (ibuft[:, half*nkc*TW + g4*4*TW: half*nkc*TW + (g4+1)*4*TW]
                 .rearrange("p (k c) -> p k c", k=4),
                 in_stream[g4*512:(g4+1)*512, tok0:tok0+TW]
                 .rearrange("(k p) c -> p k c", p=128), {})
                for g4 in range(nkc // 4)])
        for tt in range(ntt):
            tok0 = tt * TW
            if in_stream is not None:
                half = tt % 2
                if tt == 0:
                    issue_tile(0)
                if tt + 1 < ntt:
                    issue_tile(tt + 1)
            for oc in range(noc):
                ps = PS[ngroups % 4]
                if ngroups >= 4:
                    sem, val = cons[ngroups - 4]
                    b.wait('t', sem, val)
                if in_stream is not None and oc == 0:
                    b.wait_t('t', ilvs[tt])
                for kc in range(nkc):
                    if in_stream is not None:
                        rhs = ibuft[:, half*nkc*TW + kc*TW: half*nkc*TW + (kc+1)*TW]
                    else:
                        rhs = in_tile(kc, tt)
                    b.op('t', lambda e, ps=ps, kc=kc, oc=oc, rhs=rhs: e.matmul(
                        ps[:, 0:TW], w_sb_fn(kc, oc), rhs,
                        start=(kc == 0), stop=(kc == nkc - 1)))
                cons.append(consume(oc, tt, ps, tok0))
                ngroups += 1
            tile_after.append(b.cnt['et'])
        return cons

    def nrm_tile(self, kc, tt):
        return self.nrm[:, kc*T + tt*TW: kc*T + tt*TW + TW]

    def mk_delta(self, bias_at=None, scale=None, scale_at=None):
        b, d = self.b, self.dram
        evac = self.evac
        state = {"n": 0, "dma": []}
        def consume(oc, tt, ps, tok0):
            n = state["n"]; state["n"] += 1
            dst = evac[:, (n % 6) * TW:(n % 6) * TW + TW]
            if n >= 6:
                b.wait_t('v', state["dma"][n - 6])
            b.wait('v', 'et', b.cnt['et'])
            if scale_at is not None:
                v = b.op('v', lambda e, dst=dst, ps=ps, oc=oc: e.tensor_scalar(
                    dst, ps[:, 0:TW], scale_at(oc), None, ALU.mult))
            elif bias_at is not None and scale is not None:
                v = b.op('v', lambda e, dst=dst, ps=ps, oc=oc: e.tensor_scalar(
                    dst, ps[:, 0:TW], scale, bias_at(oc), ALU.mult, ALU.add))
            elif bias_at is not None:
                v = b.op('v', lambda e, dst=dst, ps=ps, oc=oc: e.tensor_scalar(
                    dst, ps[:, 0:TW], bias_at(oc), None, ALU.add))
            elif scale is not None:
                v = b.op('v', lambda e, dst=dst, ps=ps: e.tensor_scalar(
                    dst, ps[:, 0:TW], scale, None, ALU.mult))
            else:
                v = b.op('v', lambda e, dst=dst, ps=ps: e.tensor_copy(dst, ps[:, 0:TW]))
            b.wait('g', 'ev', v)
            dv = b.dma('g', d["x_work"][oc*128:(oc+1)*128, tok0:tok0+TW], dst,
                       accum_op=ALU.add)
            state["dma"].append(dv)
            return ('ev', v)
        return consume

    def mk_spill(self, spill, act, bias_at=None, scale_at=None, scale=None,
                 ebuf_off=64 * 1024):
        b = self.b
        state = {"n": 0, "dma": []}
        ebuf = self.abf(ebuf_off, 8 * TW)
        def consume(oc, tt, ps, tok0):
            n = state["n"]; state["n"] += 1
            dst = ebuf[:, (n % 8) * TW:(n % 8) * TW + TW]
            if n >= 8:
                b.wait_t('s', state["dma"][n - 8])
            b.wait('s', 'et', b.cnt['et'])
            kw = {}
            if bias_at is not None:
                kw['bias'] = bias_at(oc)
            if scale_at is not None:
                kw['scale'] = scale_at(oc)
            elif scale is not None:
                kw['scale'] = scale
            v = b.op('s', lambda e, dst=dst, ps=ps, kw=kw: e.activation(
                dst, ps[:, 0:TW], act, **kw))
            b.wait('y', 'es', v)
            dv = b.dma('y', spill[oc*128:(oc+1)*128, tok0:tok0+TW], dst)
            state["dma"].append(dv)
            return ('es', v)
        return consume

    def emit_l3_ffn(self):
        b, d = self.b, self.dram
        ldb = b.dma_group('y', [(self.cols[:, 0:32], d["L3_ffn_b1"], {}),
                                (self.cols2[:, 0:8], d["L3_ffn_b2"], {})])
        b.wait_t('s', ldb)
        b.wait_t('v', ldb)
        self.emit_ln()
        self._ternarize("L3_ffn_W1t", D, 4096, wdst_off=0)
        b.barrier()
        w1 = self.abf(0, NKC * 4096)
        cons1 = self.mk_spill(d["spill_a"], AF.Gelu,
                              bias_at=lambda oc: self.cols[:, oc:oc+1])
        self.gemm_rw(lambda kc, oc: w1[:, kc*4096 + oc*128: kc*4096 + oc*128 + 128],
                     D, 4096, cons1, in_tile=self.nrm_tile, name="l3f1")
        b.barrier()
        self._ternarize("L3_ffn_W2t", 4096, D, wdst_off=0)
        b.barrier()
        w2 = self.abf(0, 32 * 1024)
        consd = self.mk_delta(bias_at=lambda oc: self.cols2[:, oc:oc+1])
        self.gemm_rw(lambda kc, oc: w2[:, kc*1024 + oc*128: kc*1024 + oc*128 + 128],
                     4096, D, consd, in_stream=d["spill_a"], ibuf='nrm', name="l3f2")
        b.barrier()

    def _ternarize(self, wname, Kdim, Ndim, wdst_off):
        b, d = self.b, self.dram
        b.barrier()
        nkc = Kdim // 128
        W = d[wname]
        xt, scr, scr2 = self.xtile, self.scr, self.scr2
        for kc in range(nkc):
            lv = b.dma('y', xt[:, 0:Ndim], W[kc*128:(kc+1)*128, 0:Ndim])
            b.wait_t('v', lv)
            vv = b.op('v', lambda e, kc=kc: e.reduce_sum(
                scr2[:, kc:kc+1], xt[:, 0:Ndim], axis=AX.X, apply_absolute_value=True))
            b.wait('y', 'ev', vv)
        b.selfwait('v')
        vv = b.op('v', lambda e, nkc=nkc: e.reduce_sum(scr2[:, 40:41], scr2[:, 0:nkc],
                                                       axis=AX.X))
        b.wait('t', 'ev', vv)
        mv = b.op('t', lambda e: e.matmul(self.ps[6][0:1, 0:1], self.ones_c[:, 0:1],
                                          scr2[:, 40:41], start=True, stop=True))
        b.wait('v', 'et', mv)
        b.op('v', lambda e: e.tensor_scalar(self.rows2[0:1, 0:1], self.ps[6][0:1, 0:1],
                                            1.0 / (Kdim * Ndim), 1e-8, ALU.mult, ALU.max))
        b.selfwait('v')
        b.op('v', lambda e: e.reciprocal(self.rows2[0:1, 1:2], self.rows2[0:1, 0:1]))
        b.selfwait('v')
        vv = b.op('v', lambda e: e.tensor_scalar_mul(self.rows2[0:1, 1:2],
                                                     self.rows2[0:1, 1:2], 2.0))
        b.wait('t', 'ev', vv)
        mv = b.op('t', lambda e: e.matmul(self.ps[6][:, 4:5], self.ones_f[0:1, :],
                                          self.rows2[0:1, 1:2], start=True, stop=True))
        b.wait('v', 'et', mv)
        rsc = b.op('v', lambda e: e.tensor_copy(self.cols2[:, 32:33], self.ps[6][:, 4:5]))
        wdst = self.abf(wdst_off, nkc * Ndim)
        nblk = (Ndim + TW - 1) // TW
        for kc in range(nkc):
            lv = b.dma('y', xt[:, 0:Ndim], W[kc*128:(kc+1)*128, 0:Ndim])
            b.wait_t('s', lv)
            b.wait_t('v', lv)
            if kc == 0:
                b.wait('v', 'ev', rsc)
            last_s = last_v = None
            for nb in range(nblk):
                bw = min(TW, Ndim - nb * TW)
                sg = scr[:, (nb % 2)*TW: (nb % 2)*TW + bw]
                sq = scr2[:, (nb % 2)*TW: (nb % 2)*TW + bw]
                xs = xt[:, nb*TW: nb*TW + bw]
                wd = wdst[:, kc*Ndim + nb*TW: kc*Ndim + nb*TW + bw]
                if nb >= 2:
                    b.wait('s', 'ev', last_v)
                sv = b.op('s', lambda e, sg=sg, xs=xs: e.activation(sg, xs, AF.Sign))
                b.op('v', lambda e, sq=sq, xs=xs: e.tensor_scalar_mul(
                    sq, xs, self.cols2[:, 32:33]))
                b.selfwait('v')
                b.op('v', lambda e, sq=sq: e.tensor_mul(sq, sq, sq))
                b.selfwait('v')
                b.op('v', lambda e, sq=sq: e.tensor_scalar(sq, sq, 1.0, None, ALU.is_ge))
                b.selfwait('v')
                b.wait('v', 'es', sv)
                last_v = b.op('v', lambda e, wd=wd, sq=sq, sg=sg: e.tensor_mul(wd, sq, sg))
                last_s = sv
            b.wait('y', 'ev', last_v)

    def emit_l0_ffn(self):
        b, d = self.b, self.dram
        b2 = b.dma_group('y', [(self.cols[:, 0:NKC], d["L0_ffn_lng"], {}),
                               (self.cols[:, 8:16], d["L0_ffn_lnb"], {}),
                               (self.cols[:, 16:48], d["L0_ffn_fc1_bias"], {}),
                               (self.cols2[:, 0:8], d["L0_ffn_fc2_bias"], {})])
        b.wait_t('s', b2)
        b.wait_t('v', b2)
        self.emit_ln(gamma_col=lambda kc: self.cols[:, kc:kc+1],
                     beta_col=lambda kc: self.cols[:, 8+kc:9+kc])
        self._dequant("L0_ffn_fc1_q", "L0_ffn_fc1_scales", D, 4096, wdst_off=0)
        b.barrier()
        w1 = self.abf(0, NKC * 4096)
        cons1 = self.mk_spill(d["spill_a"], AF.Silu,
                              bias_at=lambda oc: self.cols[:, 16+oc:17+oc])
        self.gemm_rw(lambda kc, oc: w1[:, kc*4096 + oc*128: kc*4096 + oc*128 + 128],
                     D, 4096, cons1, in_tile=self.nrm_tile, name="l0f1")
        b.barrier()
        self._dequant("L0_ffn_fc2_q", "L0_ffn_fc2_scales", 4096, D, wdst_off=0)
        b.barrier()
        w2 = self.abf(0, 32 * 1024)
        b.op('v', lambda e: e.tensor_scalar_mul(self.cols2[:, 8:16], self.cols2[:, 0:8], 0.5))
        b.selfwait('v')
        consd = self.mk_delta(bias_at=lambda oc: self.cols2[:, 8+oc:9+oc], scale=0.5)
        self.gemm_rw(lambda kc, oc: w2[:, kc*1024 + oc*128: kc*1024 + oc*128 + 128],
                     4096, D, consd, in_stream=d["spill_a"], ibuf='nrm', name="l0f2")
        b.barrier()

    def _dequant(self, qname, sname, Kdim, Ndim, wdst_off):
        b, d = self.b, self.dram
        b.barrier()
        nkc = Kdim // 128
        Q, S = d[qname], d[sname]
        xt = self.xtile
        wdst = self.abf(wdst_off, nkc * Ndim)
        nblk = Ndim // TW
        sttv = []
        for kc in range(nkc):
            if kc > 0:
                b.wait('g', 'ev', sttv[-1])
            lv = b.dma('g', xt[:, 0:Ndim], Q[kc*128:(kc+1)*128, 0:Ndim])
            sv = b.dma('y', self.rows2[0:1, 0:Ndim], S[kc:kc+1, 0:Ndim])
            b.wait_t('t', sv)
            for nb in range(nblk):
                gi = kc * nblk + nb
                if gi >= 2:
                    b.wait('t', 'ev', sttv[gi - 2])
                mv = b.op('t', lambda e, nb=nb: e.matmul(
                    self.ps[4 + nb % 2][:, 0:TW], self.ones_f[0:1, :],
                    self.rows2[0:1, nb*TW:(nb+1)*TW], start=True, stop=True))
                b.wait('v', 'et', mv)
                if nb == 0:
                    b.wait_t('v', lv)
                sttv.append(b.op('v', lambda e, kc=kc, nb=nb: e.scalar_tensor_tensor(
                    wdst[:, kc*Ndim + nb*TW: kc*Ndim + (nb+1)*TW],
                    xt[:, nb*TW:(nb+1)*TW], -128.0, self.ps[4 + nb % 2][:, 0:TW],
                    ALU.add, ALU.mult)))
            b.wait('y', 'et', b.cnt['et'])


    def emit_l0_attn(self):
        b, d = self.b, self.dram
        self.emit_ln()
        ldd = b.dma_group('y', [(self.rows[0:1, 0:8], d["L0_attn_decay"], {}),
                                (self.cols[:, 60:61], d["iota_col"], {}),
                                (self.scr[:, 0:128], d["exp_tab"], {}),
                                (self.masks[:, 2048:2176], d["rot_tab"], {})])
        b.wait_t('s', ldd)
        b.op('s', lambda e: e.activation(self.rows[0:1, 8:16], self.rows[0:1, 0:8], AF.Ln))
        b.selfwait('s')
        sv = b.op('s', lambda e: e.activation(self.rows[0:1, 16:24], self.rows[0:1, 8:16],
                                              AF.Exp, scale=128.0))
        b.wait('t', 'es', sv)
        for h in range(8):
            b.op('t', lambda e, h=h: e.matmul(self.ps[6][:, h:h+1], self.ones_f[0:1, :],
                                              self.rows[0:1, 8+h:9+h], start=True, stop=True))
        for h in range(8):
            mv = b.op('t', lambda e, h=h: e.matmul(self.ps[6][:, 8+h:9+h], self.ones_f[0:1, :],
                                                   self.rows[0:1, 16+h:17+h], start=True, stop=True))
        b.wait('s', 'et', mv)
        for h in range(8):
            b.op('s', lambda e, h=h: e.activation(self.cols2[:, h:h+1],
                                                  self.ps[6][:, h:h+1], AF.Copy))
            b.op('s', lambda e, h=h: e.activation(self.cols2[:, 32+h:33+h],
                                                  self.ps[6][:, 8+h:9+h], AF.Copy))
        b.selfwait('s')
        for h in range(8):
            b.op('s', lambda e, h=h: e.activation(self.cols2[:, 8+h:9+h],
                                                  self.cols2[:, h:h+1], AF.Copy, scale=-1.0))
            b.op('s', lambda e, h=h: e.activation(self.cols2[:, 16+h:17+h],
                                                  self.cols2[:, h:h+1], AF.Copy, scale=128.0))
        b.selfwait('s')
        for h in range(8):
            b.op('s', lambda e, h=h: e.activation(self.masks[:, h*128:(h+1)*128],
                                                  self.scr[:, 0:128], AF.Exp,
                                                  scale=self.cols2[:, h:h+1]))
        for h in range(8):
            sv = b.op('s', lambda e, h=h: e.activation(self.cols2[:, 24+h:25+h],
                                                       self.cols[:, 60:61], AF.Exp,
                                                       scale=self.cols2[:, 8+h:9+h],
                                                       bias=self.cols2[:, 16+h:17+h]))
        b.op('s', lambda e: e.activation(self.cols2[:, 52:53], self.ones_c[:, 0:1],
                                         AF.Copy, scale=1e-5))
        b.wait('t', 'es', b.cnt['es'])
        for h in range(8):
            mv = b.op('t', lambda e, h=h: e.matmul(self.ps[6][:, 128:256], self.ones_b[0:1, :],
                                                   self.masks[0:1, h*128:(h+1)*128],
                                                   start=True, stop=True))
            b.wait('v', 'et', mv)
            vv = b.op('v', lambda e, h=h: e.tensor_copy(
                self.masks[:, 1024 + h*128: 1024 + (h+1)*128], self.ps[6][:, 128:256]))
            b.wait('t', 'ev', vv)
        b.barrier()
        wq = self.abf(0, 2048)
        wk = self.abf(4*1024, 2048)
        wv = self.abf(8*1024, 4096)
        wg = self.abf(16*1024, 4096)
        qs = self.abf(24*1024, 1024)
        ks = self.abf(26*1024, 1024)
        rawb = self.abf(28*1024, 1024)
        vs = self.abf(29*1024, 2048)
        gs = self.abf(33*1024, 2048)
        tabs = self.af32(37*1024, 4096)
        qgf = self.af32(53*1024, 256)
        ATb = self.abf(54*1024, 256)
        ktb = self.abf(54*1024 + 512, 256)
        kdb = self.abf(55*1024, 256)
        nrb = self.abf(55*1024 + 512, 512)
        pinb = self.abf(56*1024 + 512, 2048)
        st = self.af32(61*1024, 512)
        xpv = self.af32(63*1024, 1024)
        Wqkv, Wg, Wp = d["L0_attn_W_qkv"], d["L0_attn_W_g"], d["L0_attn_W_proj"]
        TABN = ("xpos_cq", "xpos_sq", "xpos_ck", "xpos_sk")
        self._l0_rel = {k: {0: 0, 1: 0} for k in
                        ('at', 'mmA', 'gn8a', 'stt', 'stt2', 'cps', 'inc',
                         'ex2', 'pint', 'pinc', 'qevac', 'qv1', 'qv2',
                         'vgev', 'vges')}
        self._l0_rel['retdone'] = 0
        self._l0_rel['tabdone'] = 0
        self._l0_rel['pin8l'] = 0
        prev_stok = None
        for pr in range(getattr(self, 'l0_pair_limit', None) or 4):
            b.wait('g', 'et', b.cnt['et'])
            wl1 = b.dma_group('g', [(wq[:, kc*256:(kc+1)*256],
                                     Wqkv[kc*128:(kc+1)*128, pr*256: (pr+1)*256], {})
                                    for kc in range(NKC)] +
                                   [(wk[:, kc*256:(kc+1)*256],
                                     Wqkv[kc*128:(kc+1)*128, 1024 + pr*256: 1024 + (pr+1)*256], {})
                                    for kc in range(NKC)])
            wl2 = b.dma_group('g', [(wv[:, kc*512:(kc+1)*512],
                                     Wqkv[kc*128:(kc+1)*128, 2048 + pr*512: 2048 + (pr+1)*512], {})
                                    for kc in range(NKC)])
            wl3 = b.dma_group('g', [(wg[:, kc*512:(kc+1)*512],
                                     Wg[kc*128:(kc+1)*128, pr*512:(pr+1)*512], {})
                                    for kc in range(NKC)])
            b.selfwait('v')
            b.wait('v', 'et', b.cnt['et'])
            msv = b.op('v', lambda e, st=st: e.memset(st[:], 0.0))
            self._l0_rel['stt'][0] = msv
            self._l0_rel['stt'][1] = msv
            rl0 = self._l0_rel
            for tt in range(getattr(self, 'l0_tt_limit', None) or NTT):
                t0 = tt * TW
                b.wait('y', 'ev', rl0['tabdone'])
                tl = b.dma_group('y', [
                    (tabs[:, (ti*2+hc)*512:(ti*2+hc+1)*512],
                     d[TABN[ti]][pr*256 + hc*128: pr*256 + (hc+1)*128, t0:t0+TW], {})
                    for ti in range(4) for hc in range(2)])
                if tt == 0:
                    b.wait_t('t', wl1)
                v3 = 0
                for which in range(2):
                    wsb = wq if which == 0 else wk
                    dstq = qs if which == 0 else ks
                    ci = 0 if which == 0 else 2
                    for hc in range(2):
                        pj = which * 2 + hc
                        pp = pj % 2
                        psr = self.ps[pp]
                        psrot = self.ps[2 + pp]
                        b.wait('t', 'es', rl0['qevac'][pp])
                        for kc in range(NKC):
                            b.op('t', lambda e, wsb=wsb, kc=kc, hc=hc, tt=tt, psr=psr: e.matmul(
                                psr[:, 0:TW],
                                wsb[:, kc*256 + hc*128: kc*256 + (hc+1)*128],
                                self.nrm_tile(kc, tt), start=(kc == 0), stop=(kc == NKC-1)))
                        mv = b.cnt['et']
                        b.wait('s', 'et', mv)
                        b.wait('s', 'ev', rl0['qv1'][pp])
                        sv = b.op('s', lambda e, psr=psr, pp=pp: e.activation(
                            rawb[:, pp*TW:(pp+1)*TW], psr[:, 0:TW], AF.Copy))
                        rl0['qevac'][pp] = sv
                        b.wait('t', 'es', sv)
                        b.wait('t', 'ev', rl0['qv2'][pp])
                        mr = b.op('t', lambda e, psrot=psrot, pp=pp: e.matmul(
                            psrot[:, 0:TW], self.masks[:, 2048:2176],
                            rawb[:, pp*TW:(pp+1)*TW], start=True, stop=True))
                        b.wait('v', 'et', mr)
                        b.wait_t('v', tl)
                        b.wait('v', 'et', rl0['retdone'])
                        rl0['qv1'][pp] = b.op('v', lambda e, ci=ci, hc=hc, pp=pp: e.tensor_mul(
                            self.scr[:, pp*TW:(pp+1)*TW], rawb[:, pp*TW:(pp+1)*TW],
                            tabs[:, (ci*2+hc)*512:(ci*2+hc+1)*512]))
                        rl0['qv2'][pp] = b.op('v', lambda e, ci=ci, hc=hc, pp=pp, psrot=psrot: e.tensor_mul(
                            xpv[:, pp*512:(pp+1)*512], psrot[:, 0:TW],
                            tabs[:, ((ci+1)*2+hc)*512:((ci+1)*2+hc+1)*512]))
                        b.selfwait('v')
                        v3 = b.op('v', lambda e, dstq=dstq, hc=hc, pp=pp: e.tensor_add(
                            dstq[:, hc*TW:(hc+1)*TW], self.scr[:, pp*TW:(pp+1)*TW],
                            xpv[:, pp*512:(pp+1)*512]))
                rl0['tabdone'] = v3
                if tt == 0:
                    b.wait_t('t', wl2)
                    b.wait_t('t', wl3)
                for which in range(2):
                    wsb = wv if which == 0 else wg
                    dstv = vs if which == 0 else gs
                    for tc in range(4):
                        u = which * 4 + tc
                        pu = u % 2
                        psv = self.ps[2 + pu]
                        if u < 2:
                            b.wait('t', 'ev', rl0['qv2'][pu])
                        b.wait('t', 'ev', rl0['vgev'][pu])
                        b.wait('t', 'es', rl0['vges'][pu])
                        for kc in range(NKC):
                            b.op('t', lambda e, psv=psv, wsb=wsb, kc=kc, tc=tc, t0=t0: e.matmul(
                                psv[:, 0:TW],
                                self.nrm[:, kc*T + t0 + tc*128: kc*T + t0 + (tc+1)*128],
                                wsb[:, kc*512:(kc+1)*512], start=(kc == 0), stop=(kc == NKC-1)))
                        mv = b.cnt['et']
                        if which == 0:
                            b.wait('v', 'et', mv)
                            b.wait('v', 'et', rl0['retdone'])
                            rl0['vgev'][pu] = b.op('v', lambda e, psv=psv, dstv=dstv, tc=tc: e.tensor_copy(
                                dstv[:, tc*512:(tc+1)*512], psv[:, 0:TW]))
                        else:
                            b.wait('s', 'et', mv)
                            b.wait('s', 'ev', rl0['pin8l'])
                            rl0['vges'][pu] = b.op('s', lambda e, psv=psv, dstv=dstv, tc=tc: e.activation(
                                dstv[:, tc*512:(tc+1)*512], psv[:, 0:TW], AF.Silu))
                b.wait('v', 'es', b.cnt['es'])
                if prev_stok is not None:
                    b.wait_t('v', prev_stok)
                b.wait('t', 'ev', b.cnt['ev'])
                b.wait('t', 'es', b.cnt['es'])
                rl = self._l0_rel
                for c in range(4):
                    for hc in range(2):
                        h = pr * 2 + hc
                        n = 2 * c + hc
                        p2 = n % 2
                        psS = self.ps[0 + 2 * p2]
                        psO = self.ps[1 + 2 * p2]
                        psI = self.ps[4 + p2]
                        b.wait('t', 'ev', rl['at'][p2])
                        mv = b.op('t', lambda e, hc=hc, c=c, psS=psS: e.matmul(
                            psS[:, 0:128], ks[:, hc*TW + c*128: hc*TW + (c+1)*128],
                            qs[:, hc*TW + c*128: hc*TW + (c+1)*128],
                            start=True, stop=True))
                        b.wait('v', 'et', mv)
                        b.wait('v', 'et', rl['mmA'][p2])
                        av = b.op('v', lambda e, h=h, p2=p2, psS=psS: e.tensor_mul(
                            ATb[:, p2*128:(p2+1)*128], psS[:, 0:128],
                            self.masks[:, h*128:(h+1)*128]))
                        rl['at'][p2] = av
                        qv = b.op('v', lambda e, h=h, hc=hc, c=c, p2=p2: e.tensor_mul(
                            qgf[:, p2*128:(p2+1)*128],
                            qs[:, hc*TW + c*128: hc*TW + (c+1)*128],
                            self.masks[:, 1024 + h*128: 1024 + (h+1)*128]))
                        b.wait('t', 'ev', qv)
                        b.wait('t', 'es', rl['gn8a'][p2])
                        b.wait('t', 'ev', rl['stt'][hc])
                        b.op('t', lambda e, c=c, hc=hc, p2=p2, psO=psO: e.matmul(
                            psO[:, 0:256], ATb[:, p2*128:(p2+1)*128],
                            vs[:, c*512 + hc*256: c*512 + hc*256 + 256],
                            start=True, stop=False))
                        mo = b.op('t', lambda e, hc=hc, p2=p2, psO=psO, st=st: e.matmul(
                            psO[:, 0:256], qgf[:, p2*128:(p2+1)*128],
                            st[:, hc*256:(hc+1)*256], start=False, stop=True))
                        rl['mmA'][p2] = mo
                        b.wait('t', 'es', rl['cps'][p2])
                        tv = b.op('t', lambda e, hc=hc, c=c, p2=p2: e.transpose(
                            self.psb[:, p2*128:(p2+1)*128],
                            ks[:, hc*TW + c*128: hc*TW + (c+1)*128], self.ident[:]))
                        b.wait('s', 'et', tv)
                        b.wait('s', 'et', rl['inc'][p2])
                        cps = b.op('s', lambda e, p2=p2: e.activation(
                            ktb[:, p2*128:(p2+1)*128], self.psb[:, p2*128:(p2+1)*128],
                            AF.Copy))
                        rl['cps'][p2] = cps
                        b.wait('v', 'es', cps)
                        kdv = b.op('v', lambda e, h=h, p2=p2: e.tensor_scalar_mul(
                            kdb[:, p2*128:(p2+1)*128], ktb[:, p2*128:(p2+1)*128],
                            self.cols2[:, 24+h:25+h]))
                        b.wait('t', 'ev', kdv)
                        b.wait('t', 'ev', rl['stt2'][p2])
                        mi = b.op('t', lambda e, c=c, hc=hc, p2=p2, psI=psI: e.matmul(
                            psI[:, 0:256], kdb[:, p2*128:(p2+1)*128],
                            vs[:, c*512 + hc*256: c*512 + hc*256 + 256],
                            start=True, stop=True))
                        rl['inc'][p2] = mi
                        b.wait('v', 'et', mi)
                        sv2 = b.op('v', lambda e, h=h, hc=hc, psI=psI, st=st: e.scalar_tensor_tensor(
                            st[:, hc*256:(hc+1)*256], st[:, hc*256:(hc+1)*256],
                            self.cols2[:, 32+h:33+h], psI[:, 0:256],
                            ALU.mult, ALU.add))
                        rl['stt'][hc] = sv2
                        rl['stt2'][p2] = sv2
                        cb = 56 + 4 * p2
                        c2b = 48 + 2 * p2
                        b.wait('v', 'et', mo)
                        muv = b.op('v', lambda e, cb=cb, psO=psO: e.reduce_sum(
                            self.cols[:, cb:cb+1], psO[:, 0:256], axis=AX.X))
                        b.wait('s', 'et', mo)
                        b.wait('s', 'ev', rl['ex2'][p2])
                        sq = b.op('s', lambda e, p2=p2, psO=psO: e.activation(
                            self.scr2[:, p2*256:(p2+1)*256], psO[:, 0:256], AF.Square))
                        b.wait('v', 'es', sq)
                        ex2 = b.op('v', lambda e, cb=cb, p2=p2: e.reduce_sum(
                            self.cols[:, cb+1:cb+2], self.scr2[:, p2*256:(p2+1)*256],
                            axis=AX.X))
                        rl['ex2'][p2] = ex2
                        b.wait('s', 'ev', ex2)
                        msq = b.op('s', lambda e, cb=cb, c2b=c2b: e.activation(
                            self.cols2[:, c2b:c2b+1], self.cols[:, cb:cb+1],
                            AF.Square, scale=1.0/256))
                        nbr = b.op('s', lambda e, cb=cb, c2b=c2b: e.activation(
                            self.cols2[:, c2b+1:c2b+2], self.cols[:, cb:cb+1],
                            AF.Copy, scale=-1.0/256))
                        b.wait('v', 'es', nbr)
                        varx = b.op('v', lambda e, cb=cb, c2b=c2b: e.scalar_tensor_tensor(
                            self.cols[:, cb+2:cb+3], self.cols[:, cb+1:cb+2],
                            1.0/256, self.cols2[:, c2b:c2b+1],
                            ALU.mult, ALU.subtract))
                        b.wait('s', 'ev', varx)
                        sdt = b.op('s', lambda e, cb=cb: e.activation(
                            self.cols[:, cb+3:cb+4], self.cols[:, cb+2:cb+3],
                            AF.Sqrt, bias=self.cols2[:, 52:53]))
                        b.wait('v', 'es', sdt)
                        rst = b.op('v', lambda e, cb=cb: e.reciprocal(
                            self.cols[:, cb+2:cb+3], self.cols[:, cb+3:cb+4]))
                        b.wait('s', 'ev', rst)
                        b.selfwait('s')
                        t8a = b.op('s', lambda e, p2=p2, psO=psO, c2b=c2b: e.activation(
                            self.scr2[:, 512 + p2*256: 512 + (p2+1)*256],
                            psO[:, 0:256], AF.Identity, bias=self.cols2[:, c2b+1:c2b+2]))
                        rl['gn8a'][p2] = t8a
                        b.wait('v', 'es', t8a)
                        b.wait('v', 'et', rl['pint'][p2])
                        pin8 = b.op('v', lambda e, p2=p2, cb=cb, c=c, hc=hc: e.scalar_tensor_tensor(
                            nrb[:, p2*256:(p2+1)*256],
                            self.scr2[:, 512 + p2*256: 512 + (p2+1)*256],
                            self.cols[:, cb+2:cb+3],
                            gs[:, c*512 + hc*256: c*512 + hc*256 + 256],
                            ALU.mult, ALU.mult))
                        b.wait('t', 'ev', pin8)
                        b.wait('t', 'ev', rl['pinc'][p2])
                        for dh in range(2):
                            b.op('t', lambda e, dh=dh, p2=p2: e.transpose(
                                self.psb[:, 256 + p2*256 + dh*128: 384 + p2*256 + dh*128],
                                nrb[:, p2*256 + dh*128: p2*256 + (dh+1)*128],
                                self.ident[:]))
                        mt = b.cnt['et']
                        rl['pint'][p2] = mt
                        b.wait('v', 'et', mt)
                        pcv = None
                        for dh in range(2):
                            vcl = hc * 2 + dh
                            pcv = b.op('v', lambda e, vcl=vcl, c=c, dh=dh, p2=p2: e.tensor_copy(
                                pinb[:, vcl*512 + c*128: vcl*512 + (c+1)*128],
                                self.psb[:, 256 + p2*256 + dh*128: 384 + p2*256 + dh*128]))
                        rl['pinc'][p2] = pcv
                rl0['retdone'] = b.cnt['et']
                rl0['pin8l'] = b.cnt['ev']
                spv = b.cnt['ev']
                b.wait('y', 'ev', spv)
                prev_stok = b.dma_group('y', [
                    (d["spill_a"][(pr*4+v)*128:(pr*4+v+1)*128, t0:t0+TW],
                     pinb[:, v*512:(v+1)*512], {}) for v in range(4)])
        b.barrier()
        if getattr(self, 'l0_no_proj', False):
            return
        wp = self.abf(0, 16*1024)
        for half in range(2):
            wlp = b.dma_group('g', [(wp[:, (half*8+k2)*1024:(half*8+k2+1)*1024],
                                     Wp[(half*8+k2)*128:(half*8+k2+1)*128, :], {})
                                    for k2 in range(8)])
            b.wait_t('t', wlp)
        consd = self.mk_delta()
        self.gemm_rw(lambda kc, oc: wp[:, kc*1024 + oc*128: kc*1024 + (oc+1)*128],
                     2048, D, consd, in_stream=d["spill_a"], ibuf='arena',
                     ibuf_off=32*1024, name="l0proj")
        b.barrier()

    def _attn_bound(self, qp, kp):
        b = self.b
        b.wait('t', 'ev', b.cnt['ev'])
        b.wait('s', 'ev', b.cnt['ev'])
        units = [(si, tt) for si in range(2) for tt in range(NTT)]
        mmv = {}
        rmv = {}
        for i, (si, tt) in enumerate(units):
            src = (qp, kp)[si]
            slot = (i % 2) * TW
            bb = 4 if i % 2 == 0 else 0
            if i >= 2:
                b.wait('s', 'et', mmv[i - 2])
            sv = b.op('s', lambda e, src=src, tt=tt, slot=slot: e.activation(
                self.scr[:, slot:slot+TW], src[:, tt*TW:(tt+1)*TW], AF.Square))
            b.wait('t', 'es', sv)
            if i >= 2:
                b.wait('t', 'ev', rmv[i - 2])
            b.op('t', lambda e, slot=slot, bb=bb: e.matmul(
                self.ps[bb][0:1, 0:TW], self.ones_c[0:64, 0:1],
                self.scr[0:64, slot:slot+TW], start=True, stop=True,
                tile_position=(0, 0)))
            mmv[i] = b.op('t', lambda e, slot=slot, bb=bb: e.matmul(
                self.ps[bb + 1][0:1, 0:TW], self.ones_c[64:128, 0:1],
                self.scr[64:128, slot:slot+TW], start=True, stop=True,
                tile_position=(64, 0)))
            b.wait('v', 'et', mmv[i])
            for h2 in range(2):
                dst = 32 + (h2 * 2 + si) * 8 + tt
                rmv[i] = b.op('v', lambda e, h2=h2, dst=dst, bb=bb: e.reduce_max(
                    self.rows[0:1, dst:dst+1], self.ps[bb + h2][0:1, 0:TW], axis=AX.X))
        b.selfwait('v')
        negb = {}
        for h2 in range(2):
            b.op('v', lambda e, h2=h2: e.reduce_max(
                self.rows[0:1, 0:1], self.rows[0:1, 32 + h2*16: 40 + h2*16], axis=AX.X))
            b.selfwait('v')
            b.op('v', lambda e, h2=h2: e.reduce_max(
                self.rows[0:1, 1:2], self.rows[0:1, 40 + h2*16: 48 + h2*16], axis=AX.X))
            b.selfwait('v')
            vv = b.op('v', lambda e: e.tensor_mul(self.rows[0:1, 2:3],
                                                  self.rows[0:1, 0:1], self.rows[0:1, 1:2]))
            b.wait('s', 'ev', vv)
            sv = b.op('s', lambda e: e.activation(self.rows[0:1, 3:4],
                                                  self.rows[0:1, 2:3], AF.Sqrt))
            b.wait('v', 'es', sv)
            vv = b.op('v', lambda e, h2=h2: e.tensor_scalar_mul(
                self.scr2[0:1, h2:h2+1], self.rows[0:1, 3:4], -1.0))
            b.wait('t', 'ev', vv)
            mv = b.op('t', lambda e, h2=h2: e.matmul(
                self.ps[4][:, 8:9], self.bcrf[0:64, 0:128], self.scr2[0:64, h2:h2+1],
                start=True, stop=True, tile_position=(0, 0)))
            b.wait('s', 'et', mv)
            negb[h2] = b.op('s', lambda e, h2=h2: e.activation(
                self.cols2[:, 44 + h2:45 + h2], self.ps[4][:, 8:9], AF.Copy))
            b.selfwait('s')
        b.wait('v', 'es', negb[1])
        b.wait('t', 'es', negb[1])
        b.wait('t', 'ev', b.cnt['ev'])
        b.wait('t', 'et', b.cnt['et'])

    def _attn_flash(self, qp, kp, vp, aop, pbuf, njc_of):
        b = self.b
        PIPE = 4
        NSL = 10
        SB = (self.ps[0], self.ps[1], self.ps[4], self.ps[5])
        st = self._flash_state
        for qt in range(NTT):
            njc = njc_of(qt)
            smm = {}
            expv = {}
            mskv = {}
            pvet = {}

            def S_(jc, qt=qt):
                if jc >= 2:
                    b.wait('t', 'es', expv[(jc - 2, 1)])
                else:
                    b.wait('t', 'es', st['last_exp'])
                for h2 in range(2):
                    b.op('t', lambda e, jc=jc, h2=h2, qt=qt: e.matmul(
                        SB[(2*jc + h2) % 4][:, 0:TW],
                        kp[64*h2: 64*h2 + 64, jc*128:(jc+1)*128],
                        qp[64*h2: 64*h2 + 64, qt*TW:(qt+1)*TW],
                        start=True, stop=True, tile_position=(64*h2, 0)))
                    smm[(jc, h2)] = b.cnt['et']

            def E_(jc, qt=qt):
                for h2 in range(2):
                    b.wait('s', 'et', smm[(jc, h2)])
                    b.wait('s', 'et', pvet[(jc - 5, h2)] if jc >= 5 else st['last_pv'])
                    expv[(jc, h2)] = b.op('s', lambda e, jc=jc, h2=h2: e.activation(
                        pbuf[:, ((2*jc + h2) % NSL)*TW:((2*jc + h2) % NSL + 1)*TW],
                        SB[(2*jc + h2) % 4][:, 0:TW],
                        AF.Exp, bias=self.cols2[:, 44 + h2:45 + h2]))
                    rel = jc - 4 * qt
                    if rel >= 0:
                        b.wait('v', 'es', expv[(jc, h2)])
                        mskv[(jc, h2)] = b.op('v', lambda e, jc=jc, h2=h2, rel=rel: e.tensor_mul(
                            pbuf[:, ((2*jc + h2) % NSL)*TW:((2*jc + h2) % NSL + 1)*TW],
                            pbuf[:, ((2*jc + h2) % NSL)*TW:((2*jc + h2) % NSL + 1)*TW],
                            self.masks[:, rel*TW:(rel+1)*TW]))

            def PV_(jc, qt=qt, njc=njc):
                for h2 in range(2):
                    psA = self.ps[2 + h2]
                    if (jc, h2) in mskv:
                        b.wait('t', 'ev', mskv[(jc, h2)])
                    else:
                        b.wait('t', 'es', expv[(jc, h2)])
                    if jc == 0:
                        b.wait('t', 'ev', st['tail_c'][h2])
                    b.op('t', lambda e, jc=jc, h2=h2, njc=njc, psA=psA: e.matmul(
                        psA[0:65, 0:TW], vp[:, jc*130 + 65*h2: jc*130 + 65*h2 + 65],
                        pbuf[:, ((2*jc + h2) % NSL)*TW:((2*jc + h2) % NSL + 1)*TW],
                        start=(jc == 0), stop=(jc == njc - 1)))
                    pvet[(jc, h2)] = b.cnt['et']

            for jc in range(min(PIPE, njc)):
                S_(jc)
                E_(jc)
            for jc in range(njc):
                if jc + PIPE < njc:
                    S_(jc + PIPE)
                    E_(jc + PIPE)
                PV_(jc)
            st['last_exp'] = expv[(njc - 1, 1)]
            st['last_pv'] = pvet[(njc - 1, 1)]
            prev_mb = 0
            for h2 in range(2):
                psA = self.ps[2 + h2]
                mv = max(pvet[(njc - 1, h2)], prev_mb)
                b.wait('v', 'et', mv)
                rv = b.op('v', lambda e, psA=psA: e.reciprocal(
                    self.scr2[0:1, 8:520], psA[64:65, 0:TW]))
                b.wait('t', 'ev', rv)
                mb = b.op('t', lambda e: e.matmul(
                    self.ps[6][0:64, 0:TW], self.bcrf[0:64, 0:64],
                    self.scr2[0:64, 8:520], start=True, stop=True,
                    tile_position=(0, 0)))
                prev_mb = mb
                slot = h2 * TW
                cv = b.op('v', lambda e, psA=psA, slot=slot: e.tensor_copy(
                    self.scr[0:64, slot:slot+TW], psA[0:64, 0:TW]))
                st['tail_c'][h2] = cv
                b.wait('v', 'et', mb)
                ov = b.op('v', lambda e, h2=h2, qt=qt, slot=slot: e.tensor_mul(
                    aop[64*h2:64*h2 + 64, qt*TW:(qt+1)*TW],
                    self.scr[0:64, slot:slot+TW], self.ps[6][0:64, 0:TW]))
                st['ov'].append(ov)
        b.wait('t', 'et', b.cnt['et'])

    def emit_l1_attn(self):
        b, d = self.b, self.dram
        ldc = b.dma_group('y', [(self.cols[:, 0:8], d["L1_attn_bo"], {}),
                                (self.cols[:, 8:16], d["L1_attn_b1"], {}),
                                (self.cols[:, 16:24], d["L1_attn_b2"], {}),
                                (self.cols[:, 24:48], d["L1_attn_bg"], {})])
        ldm = b.dma('g', self.masks[:, 0:2560], d["l1_masks"])
        self.emit_ln()
        b.wait_t('s', ldc)
        b.wait_t('v', ldc)
        cons1 = self.mk_spill(d["spill_a"][0:1024, :], AF.Silu,
                              bias_at=lambda oc: self.cols[:, 8+oc:9+oc])
        self.gemm_sw(d["L1_attn_W1t"], D, D, cons1, gw=512, wbuf_off=0,
                     in_tile=self.nrm_tile, name="l1aA")
        b.barrier()
        st2 = {"n": 0}
        def cons2(oc, tt, ps, tok0):
            n = st2["n"]; st2["n"] += 1
            b.wait('v', 'et', b.cnt['et'])
            v = b.op('v', lambda e, oc=oc, tt=tt, ps=ps: e.tensor_scalar(
                self.nrm[:, oc*T + tt*TW: oc*T + (tt+1)*TW], ps[:, 0:TW],
                self.cols[:, 16+oc:17+oc], None, ALU.add))
            return ('ev', v)
        self.gemm_sw(d["L1_attn_W2t"], D, D, cons2, gw=512, wbuf_off=0,
                     in_stream=d["spill_a"][0:1024, :], ibuf='arena',
                     ibuf_off=16 * 1024, name="l1aB")
        b.barrier()
        qp = self.abf(0, T)
        kp = self.abf(8 * 1024, T)
        vfm = self.abf(16 * 1024, T)
        vp = self.abf(24 * 1024, 32 * 130)
        aop = self.abf(33 * 1024, T)
        wqkv = self.abf(41 * 1024, 3 * 1024)
        pbuf = self.abf(47 * 1024, 10 * TW)
        b.op('v', lambda e, vp=vp: e.memset(
            vp[:].rearrange("p (a c) -> p a c", c=65)[:, :, 64:65], 1.0))
        b.op('v', lambda e: e.memset(self.scr2[:], 0.0))
        beta = float(64.0 ** -0.5)
        self._flash_state = {'last_exp': 0, 'last_pv': 0,
                             'tail_r': {0: 0, 1: 0}, 'tail_c': {0: 0, 1: 0},
                             'ov': []}
        proj_done = 0
        stok = None
        for p in range(getattr(self, 'l1_pair_limit', None) or 8):
            b.wait('g', 'et', proj_done)
            wl = b.dma_group('g', [
                (wqkv[:, pi*1024 + kc*128: pi*1024 + (kc+1)*128],
                 d["L1_attn_Wgt"][kc*128:(kc+1)*128,
                                  pi*1024 + p*128: pi*1024 + (p+1)*128], {})
                for pi in range(3) for kc in range(NKC)])
            b.wait_t('t', wl)
            pv = {}
            for pi in range(3):
                dstt = (qp, kp, vfm)[pi]
                bgc = 24 + pi * 8 + p
                for tt in range(NTT):
                    psq = self.ps[(pi * NTT + tt) % 2]
                    if pi * NTT + tt >= 2:
                        b.wait('t', 'ev', pv[(pi * NTT + tt - 2)])
                    for kc in range(NKC):
                        b.op('t', lambda e, psq=psq, pi=pi, kc=kc, tt=tt: e.matmul(
                            psq[:, 0:TW],
                            wqkv[:, pi*1024 + kc*128: pi*1024 + (kc+1)*128],
                            self.nrm_tile(kc, tt), start=(kc == 0), stop=(kc == NKC - 1)))
                    mv = b.cnt['et']
                    b.wait('v', 'et', mv)
                    sc = beta if pi == 0 else 1.0
                    if pi == 2:
                        pv[pi * NTT + tt] = b.op('v', lambda e, psq=psq, dstt=dstt, tt=tt, bgc=bgc: e.tensor_scalar(
                            dstt[:, tt*TW:(tt+1)*TW], psq[:, 0:TW],
                            self.cols[:, bgc:bgc+1], None, ALU.add))
                    else:
                        pv[pi * NTT + tt] = b.op('v', lambda e, psq=psq, dstt=dstt, tt=tt, bgc=bgc, sc=sc: e.tensor_scalar(
                            dstt[:, tt*TW:(tt+1)*TW], psq[:, 0:TW],
                            self.cols[:, bgc:bgc+1], sc, ALU.add, ALU.mult))
            proj_done = b.cnt['et']
            cpv = []
            b.wait('t', 'ev', pv[3 * NTT - 1])
            for jc in range(32):
                if cpv:
                    b.wait('t', 'ev', cpv[-1])
                b.op('t', lambda e, jc=jc, vfm=vfm: e.transpose(
                    self.psb[:, 0:128], vfm[:, jc*128:(jc+1)*128], self.ident[:]))
                mv = b.cnt['et']
                b.wait('v', 'et', mv)
                cpv.append(b.op('v', lambda e, jc=jc, vp=vp: e.tensor_copy(
                    vp[:, jc*130: jc*130 + 64], self.psb[:, 0:64])))
                cpv.append(b.op('v', lambda e, jc=jc, vp=vp: e.tensor_copy(
                    vp[:, jc*130 + 65: jc*130 + 129], self.psb[:, 64:128])))
            self._attn_bound(qp, kp)
            if stok is not None:
                b.wait_t('v', stok)
            self._attn_flash(qp, kp, vp, aop, pbuf, lambda qt: min(4 * qt + 5, 32))
            b.wait('y', 'ev', b.cnt['ev'])
            stok = b.dma('y', d["spill_b"][p*128:(p+1)*128, :], aop[:, 0:T])
        b.barrier()
        consd = self.mk_delta()
        self.gemm_sw(d["L1_attn_Wot"], D, D, consd, gw=512, wbuf_off=0,
                     in_stream=d["spill_b"][0:1024, :], ibuf='arena',
                     ibuf_off=16 * 1024, name="l1o")
        b.barrier()
        bcol = self.cols[:, 0:8]
        for oc in range(8):
            vv = b.op('v', lambda e, oc=oc, bcol=bcol: e.tensor_scalar(
                self.evac[:, (oc % 2)*TW:(oc % 2)*TW + TW], self.ab[:, 0:TW],
                0.0, bcol[:, oc:oc+1], ALU.mult, ALU.add))
            b.wait('g', 'ev', vv)
            dv = None
            for tt in range(NTT):
                dv = b.dma('g', d["x_work"][oc*128:(oc+1)*128, tt*TW:(tt+1)*TW],
                           self.evac[:, (oc % 2)*TW:(oc % 2)*TW + TW], accum_op=ALU.add)
            b.wait_t('v', dv)
        b.barrier()

    def emit_l2_attn(self):
        b, d = self.b, self.dram
        G = 256
        ldc = b.dma_group('y', [
            (self.cols[:, 0:16], d["L2_attn_bg"], {}),
            (self.cols[:, 16:17], d["L2_attn_binc"], {}),
            (self.cols[:, 17:25], d["L2_attn_bout"], {}),
            (self.cols2[:, 0:8], d["L2_attn_affs"], {}),
            (self.cols[:, 25:41], d["L2_attn_bv"], {}),
        ])
        ldm = b.dma('g', self.masks[:, 0:512], d["l2_masks"])
        ldw = b.dma_group('g', [(self.masks[:, 512 + kc*128: 512 + (kc+1)*128],
                                 d["L2_attn_Win"][kc*128:(kc+1)*128, :], {})
                                for kc in range(NKC)])
        b.wait_t('v', ldc)
        b.op('v', lambda e: e.tensor_scalar_mul(self.cols2[:, 8:10], self.cols2[:, 0:2],
                                                float(2048.0 ** -0.5)))
        st = self.af32(54 * 1024, 2048)
        b.op('v', lambda e: e.memset(st[:], 0.0))
        b.selfwait('v')
        self.emit_ln()
        consV = self.mk_spill(d["spill_a"][0:2048, :], AF.Silu,
                              bias_at=lambda oc: self.cols[:, 25+oc:26+oc])
        self.gemm_sw(d["L2_attn_Wv"], D, 2048, consV, gw=512, wbuf_off=0,
                     in_tile=self.nrm_tile, name="l2av")
        b.barrier()
        consG = self.mk_spill(d["spill_b"][0:2048, :], AF.Silu,
                              bias_at=lambda oc: self.cols[:, oc:oc+1])
        self.gemm_sw(d["L2_attn_Wg"], D, 2048, consG, gw=512, wbuf_off=0,
                     in_tile=self.nrm_tile, name="l2ag")
        b.barrier()
        AR_VF, AR_GT = 0, 16 * 1024
        AR_VT, AR_XQ = 32 * 1024, 40 * 1024
        AR_EB = 48 * 1024
        vf = self.abf(AR_VF, 16 * TW)
        gt = self.abf(AR_GT, 16 * TW)
        vt = self.nrm[:, 0:4096] if getattr(self, 'l2_vt_normal', False) else self.abf(AR_VT, 4096)
        x_ = self.abf(AR_XQ, TW)
        qq = self.abf(AR_XQ + 1024, TW)
        qk = self.abf(AR_XQ + 2048, TW)
        lq = self.abf(AR_XQ + 3072, TW)
        lk = self.abf(AR_XQ + 4096, TW)
        Pb = self.abf(AR_XQ + 5120, TW)
        P2 = self.abf(AR_XQ + 6144, TW)
        sbf = self.abf(AR_XQ + 7168, 256)
        lkT = self.abf(AR_EB + 3 * 1024, 256)
        oe = self.abf(AR_EB, 6 * 256)
        Win = self.masks.ap() if False else None
        win_sb = self.masks[:, 512:1536]
        spill_o = d["spill_a"][2048:4096, :]
        spill_v = d["spill_a"][0:2048, :]
        spill_g = d["spill_b"][0:2048, :]
        b.wait_t('t', ldw)
        ost = {"n": 0, "dma": [], "ev": []}
        self._l2_last_rl = None
        ntt_run = getattr(self, 'l2_tt_limit', None) or NTT
        for tt in range(ntt_run):
            t0 = tt * TW
            if tt > 0:
                b.wait('y', 'et', b.cnt['et'])
                b.wait('y', 'ev', b.cnt['ev'])
            lvf = b.dma_group('y', [(vf[:, ec*TW:(ec+1)*TW],
                                     spill_v[ec*128:(ec+1)*128, t0:t0+TW], {})
                                    for ec in range(16)])
            lgt = b.dma_group('y', [(gt[:, ec*TW:(ec+1)*TW],
                                     spill_g[ec*128:(ec+1)*128, t0:t0+TW], {})
                                    for ec in range(16)])
            for kc in range(NKC):
                b.op('t', lambda e, kc=kc, tt=tt: e.matmul(
                    self.ps[4][:, 0:TW], win_sb[:, kc*128:(kc+1)*128],
                    self.nrm_tile(kc, tt), start=(kc == 0), stop=(kc == NKC - 1)))
            mv = b.cnt['et']
            b.wait('s', 'et', mv)
            sv = b.op('s', lambda e: e.activation(x_[:], self.ps[4][:, 0:TW], AF.Silu,
                                                  bias=self.cols[:, 16:17]))
            b.wait('v', 'es', sv)
            b.op('v', lambda e: e.tensor_scalar(qq[:], x_[:], self.cols2[:, 8:9],
                                                self.cols2[:, 9:10], ALU.mult, ALU.add))
            b.op('v', lambda e: e.tensor_scalar(qk[:], x_[:], self.cols2[:, 2:3],
                                                self.cols2[:, 3:4], ALU.mult, ALU.add))
            b.op('v', lambda e: e.tensor_scalar(lq[:], x_[:], self.cols2[:, 4:5],
                                                self.cols2[:, 5:6], ALU.mult, ALU.add))
            aff = b.op('v', lambda e: e.tensor_scalar(lk[:], x_[:], self.cols2[:, 6:7],
                                                      self.cols2[:, 7:8], ALU.mult, ALU.add))
            b.wait_t('t', lvf)
            for g in range(getattr(self, 'l2_g_limit', None) or 2):
                n0 = g * G
                cpv = []
                for tc in range(2):
                    for ec8 in range(2):
                        if cpv:
                            b.wait('t', 'ev', cpv[-1])
                        for e8 in range(8):
                            ec = ec8 * 8 + e8
                            b.op('t', lambda e, ec=ec, tc=tc, g=g, e8=e8: e.transpose(
                                self.psb[:, e8*128:(e8+1)*128],
                                vf[:, ec*TW + g*G + tc*128: ec*TW + g*G + tc*128 + 128],
                                self.ident[:]))
                        mv = b.cnt['et']
                        b.wait('v', 'et', mv)
                        cpv.append(b.op('v', lambda e, tc=tc, ec8=ec8: e.tensor_copy(
                            vt[:, tc*2048 + ec8*1024: tc*2048 + (ec8+1)*1024],
                            self.psb[:, 0:1024])))
                b.wait('t', 'ev', aff)
                if self._l2_last_rl is not None:
                    b.wait('t', 'es', self._l2_last_rl)
                sep = getattr(self, 'l2_sep_st', False)
                for tc in range(2):
                    bank = self.ps[5 + tc] if sep else self.ps[5]
                    col = 0 if sep else tc * G
                    b.op('t', lambda e, tc=tc, g=g, bank=bank, col=col, n0=n0: e.matmul(
                        bank[:, col:col+G], qk[:, n0+tc*128: n0+tc*128+128],
                        qq[:, n0:n0+G], start=True, stop=True))
                mv = b.cnt['et']
                b.wait('s', 'et', mv)
                rl = []
                for tc in range(2):
                    bank = self.ps[5 + tc] if sep else self.ps[5]
                    col = 0 if sep else tc * G
                    rl.append(b.op('s', lambda e, tc=tc, bank=bank, col=col: e.activation(
                        Pb[:, tc*G:(tc+1)*G], bank[:, col:col+G], AF.Relu)))
                self._l2_last_rl = rl[-1]
                b.wait('v', 'es', rl[-1])
                for tc in range(2):
                    b.op('v', lambda e, tc=tc: e.tensor_mul(
                        P2[:, tc*G:(tc+1)*G], Pb[:, tc*G:(tc+1)*G], Pb[:, tc*G:(tc+1)*G]))
                b.selfwait('v')
                for tc in range(2):
                    pm = b.op('v', lambda e, tc=tc: e.tensor_mul(
                        Pb[:, tc*G:(tc+1)*G], P2[:, tc*G:(tc+1)*G],
                        self.masks[:, tc*G:(tc+1)*G]))
                if getattr(self, 'l2_pb_normal', False):
                    b.selfwait('v')
                    pm = b.op('v', lambda e: e.tensor_copy(self.masks[:, 1536:2048], Pb[:, 0:512]))
                b.wait('t', 'ev', pm)
                b.wait_t('t', lgt)
                esmm = []
                for es in range(getattr(self, 'l2_es_limit', None) or 16):
                    nb_rot = getattr(self, 'l2_nbank', 4)
                    psx = self.ps[es % nb_rot]
                    if es >= 2:
                        b.wait('v', 'et', esmm[es - 2])
                    scp = b.op('v', lambda e, es=es: e.tensor_copy(
                        sbf[:, (es % 2)*128:(es % 2)*128 + 128],
                        st[:, es*128:(es+1)*128]))
                    if len(ost["esv"]) >= 4 if "esv" in ost else False:
                        pass
                    if len(ost.get("esv", [])) >= 4:
                        b.wait('t', 'es', ost["esv"][-4])
                    b.wait('t', 'ev', scp)
                    novl = getattr(self, 'l2_no_vl', False)
                    pbsrc = self.masks[:, 1536:2048] if getattr(self, 'l2_pb_normal', False) else Pb
                    for tc in range(2):
                        b.op('t', lambda e, es=es, tc=tc, novl=novl, pbsrc=pbsrc, psx=psx: e.matmul(
                            psx[:, 0:G], vt[:, tc*2048 + es*128:(tc*2048) + (es+1)*128],
                            pbsrc[:, tc*G:(tc+1)*G], start=(tc == 0),
                            stop=(novl and tc == 1)))
                    if getattr(self, 'l2_no_vl', False):
                        pass
                    else:
                        b.op('t', lambda e, es=es, g=g, psx=psx, n0=n0: e.matmul(
                            psx[:, 0:G], sbf[:, (es % 2)*128:(es % 2)*128 + 128],
                            lq[:, n0:n0+G], start=False, stop=True))
                    if getattr(self, 'l2_drain', False):
                        mv = b.op('t', lambda e: e.drain())
                    else:
                        mv = b.cnt['et']
                    esmm.append(mv)
                    n = ost["n"]; ost["n"] += 1
                    if n >= 6:
                        b.wait_t('v', ost["dma"][n - 6])
                    b.wait('s', 'et', mv)
                    if len(ost["ev"]) >= 2:
                        b.wait('s', 'ev', ost["ev"][-2])
                    sv = b.op('s', lambda e, n=n, psx=psx: e.activation(
                        self.scr[:, (n % 2)*TW:(n % 2)*TW + G], psx[:, 0:G], AF.Copy))
                    b.wait('v', 'es', sv)
                    ov = b.op('v', lambda e, es=es, n=n, n0=n0: e.tensor_mul(
                        oe[:, (n % 6)*G:(n % 6)*G + G],
                        self.scr[:, (n % 2)*TW:(n % 2)*TW + G],
                        gt[:, es*TW + n0: es*TW + n0 + G]))
                    ost["ev"].append(max(ov, sv) if False else ov)
                    ost.setdefault("esv", []).append(sv)
                    b.wait('y', 'ev', ov)
                    ost["dma"].append(b.dma('y', spill_o[es*128:(es+1)*128, t0+n0:t0+n0+G],
                                            oe[:, (n % 6)*G:(n % 6)*G + G]))
                tv = []
                for tc in range(2):
                    b.op('t', lambda e, tc=tc, g=g, n0=n0: e.transpose(
                        self.psb[:, tc*128:(tc+1)*128],
                        lk[:, n0+tc*128: n0+tc*128+128], self.ident[:]))
                mv = b.cnt['et']
                b.wait('v', 'et', mv)
                lkc = b.op('v', lambda e: e.tensor_copy(lkT[:, 0:256], self.psb[:, 0:256]))
                b.wait('t', 'ev', lkc)
                upd = []
                for et4 in range(4):
                    for tc in range(2):
                        b.op('t', lambda e, et4=et4, tc=tc: e.matmul(
                            self.ps[4][:, 0:TW], lkT[:, tc*128:(tc+1)*128],
                            vt[:, tc*2048 + et4*TW: tc*2048 + (et4+1)*TW],
                            start=(tc == 0), stop=(tc == 1)))
                    mv = b.cnt['et']
                    b.wait('v', 'et', mv)
                    uv = b.op('v', lambda e, et4=et4: e.tensor_add(
                        st[:, et4*TW:(et4+1)*TW], st[:, et4*TW:(et4+1)*TW],
                        self.ps[4][:, 0:TW]))
                    upd.append(uv)
                    if et4 < 3:
                        b.wait('t', 'ev', uv)
                b.wait('t', 'ev', upd[-1])
                b.selfwait('v')
        b.barrier()
        if getattr(self, 'l2_no_pass4', False):
            return
        consd = self.mk_delta(bias_at=lambda oc: self.cols[:, 17+oc:18+oc])
        self.gemm_sw(d["L2_attn_Wout"], 2048, D, consd, gw=256, wbuf_off=0,
                     in_stream=spill_o, ibuf='nrm', name="l2ao")
        b.barrier()

    def emit_l3_attn(self):
        b, d = self.b, self.dram
        ldc = b.dma_group('y', [(self.cols[:, 0:8], d["L3_attn_bo"], {})])
        ldm = b.dma('g', self.masks[:, 0:2048], d["l3_masks"])
        self.emit_ln()
        qp = self.abf(0, T)
        kp = self.abf(8 * 1024, T)
        vfm = self.abf(16 * 1024, T)
        vp = self.abf(24 * 1024, 32 * 130)
        aop = self.abf(33 * 1024, T)
        wqkv = self.abf(41 * 1024, 3 * 1024)
        pbuf = self.abf(47 * 1024, 10 * TW)
        b.op('v', lambda e, vp=vp: e.memset(
            vp[:].rearrange("p (a c) -> p a c", c=65)[:, :, 64:65], 1.0))
        b.op('v', lambda e: e.memset(self.scr2[:], 0.0))
        beta = float(64.0 ** -0.5)
        self._flash_state = {'last_exp': 0, 'last_pv': 0,
                             'tail_r': {0: 0, 1: 0}, 'tail_c': {0: 0, 1: 0},
                             'ov': []}
        WN = {"q": "L3_attn_Wq", "k": "L3_attn_Wk", "v": "L3_attn_Wv"}
        proj_done = 0
        stok = None
        for p in range(getattr(self, 'l3_pair_limit', None) or 8):
            b.wait('g', 'et', proj_done)
            wl = b.dma_group('g', [
                (wqkv[:, pi*1024 + kc*128: pi*1024 + (kc+1)*128],
                 d[WN[pn]][kc*128:(kc+1)*128, p*128:(p+1)*128], {})
                for pi, pn in enumerate(("q", "k", "v")) for kc in range(NKC)])
            b.wait_t('t', wl)
            pv = {}
            for pi, pn in enumerate(("q", "k", "v")):
                dstt = (qp, kp, vfm)[pi]
                for tt in range(NTT):
                    psq = self.ps[(pi * NTT + tt) % 2]
                    if pi * NTT + tt >= 2:
                        b.wait('t', 'es', pv[(pi * NTT + tt - 2)])
                    for kc in range(NKC):
                        b.op('t', lambda e, psq=psq, pi=pi, kc=kc, tt=tt: e.matmul(
                            psq[:, 0:TW],
                            wqkv[:, pi*1024 + kc*128: pi*1024 + (kc+1)*128],
                            self.nrm_tile(kc, tt), start=(kc == 0), stop=(kc == NKC - 1)))
                    mv = b.cnt['et']
                    b.wait('s', 'et', mv)
                    sc = {}
                    if pn == "q":
                        sc['scale'] = beta
                    pv[pi * NTT + tt] = b.op('s', lambda e, psq=psq, dstt=dstt, tt=tt, sc=sc: e.activation(
                        dstt[:, tt*TW:(tt+1)*TW], psq[:, 0:TW], AF.Copy, **sc))
            proj_done = b.cnt['et']
            if getattr(self, 'l3_stage', 9) < 2:
                continue
            cpv = []
            b.wait('t', 'es', pv[3 * NTT - 1])
            for jc in range(32):
                if cpv:
                    b.wait('t', 'ev', cpv[-1])
                b.op('t', lambda e, jc=jc, vfm=vfm: e.transpose(
                    self.psb[:, 0:128], vfm[:, jc*128:(jc+1)*128], self.ident[:]))
                mv = b.cnt['et']
                b.wait('v', 'et', mv)
                cpv.append(b.op('v', lambda e, jc=jc, vp=vp: e.tensor_copy(
                    vp[:, jc*130: jc*130 + 64], self.psb[:, 0:64])))
                cpv.append(b.op('v', lambda e, jc=jc, vp=vp: e.tensor_copy(
                    vp[:, jc*130 + 65: jc*130 + 129], self.psb[:, 64:128])))
            if getattr(self, 'l3_stage', 9) < 3:
                continue
            self._attn_bound(qp, kp)
            if getattr(self, 'l3_stage', 9) < 4:
                continue
            if stok is not None:
                b.wait_t('v', stok)
            self._attn_flash(qp, kp, vp, aop, pbuf, lambda qt: 4 * qt + 4)
            b.wait('y', 'ev', b.cnt['ev'])
            stok = b.dma('y', d["spill_b"][p*128:(p+1)*128, :], aop[:, 0:T])
        b.barrier()
        if getattr(self, 'l3_stage', 9) < 5:
            return
        consd = self.mk_delta()
        self.gemm_sw(d["L3_attn_Wo"], D, D, consd, gw=512, wbuf_off=0,
                     in_stream=d["spill_b"][0:1024, :], ibuf='arena',
                     ibuf_off=16 * 1024, name="l3o")
        b.barrier()
        bcol = self.cols[:, 0:8]
        for oc in range(8):
            vv = b.op('v', lambda e, oc=oc, bcol=bcol: e.tensor_scalar(
                self.evac[:, (oc % 2)*TW:(oc % 2)*TW + TW], self.ab[:, 0:TW],
                0.0, bcol[:, oc:oc+1], ALU.mult, ALU.add))
            b.wait('g', 'ev', vv)
            dv = None
            for tt in range(NTT):
                dv = b.dma('g', d["x_work"][oc*128:(oc+1)*128, tt*TW:(tt+1)*TW],
                           self.evac[:, (oc % 2)*TW:(oc % 2)*TW + TW], accum_op=ALU.add)
            b.wait_t('v', dv)
        b.barrier()

    def emit_l2_ffn(self):
        b, d = self.b, self.dram
        self.emit_ln()
        lv = b.dma('y', self.rows2[0:1, 8:12], d["L2_ffn_sc"])
        b.wait_t('v', lv)
        b.op('v', lambda e: e.tensor_scalar_mul(self.rows2[0:1, 12:13], self.rows2[0:1, 8:9], 4.0))
        b.op('v', lambda e: e.tensor_scalar_mul(self.rows2[0:1, 13:14], self.rows2[0:1, 9:10], 2.0))
        vv = b.op('v', lambda e: e.tensor_copy(self.rows2[0:1, 14:15], self.rows2[0:1, 10:11]))
        b.wait('t', 'ev', vv)
        for i in range(3):
            mv = b.op('t', lambda e, i=i: e.matmul(self.ps[6][:, 8+i:9+i], self.ones_f[0:1, :],
                                                   self.rows2[0:1, 12+i:13+i], start=True, stop=True))
        b.wait('v', 'et', mv)
        cv = None
        for i in range(3):
            cv = b.op('v', lambda e, i=i: e.tensor_copy(self.cols2[:, 40+i:41+i],
                                                        self.ps[6][:, 8+i:9+i]))
        b.wait('s', 'ev', cv)
        b.wait('t', 'ev', cv)
        b.barrier()
        wdst = self.abf(0, NKC * 4096)
        K0 = d["L2_ffn_k0"]
        subv = []
        for kc in range(NKC):
            for nb in range(8):
                half = (nb % 2) * TW
                gi = kc * 8 + nb
                if gi >= 2:
                    b.wait('g', 'ev', subv[gi - 2])
                l2 = b.dma_group('g', [
                    (self.scr[:, half:half+TW],
                     K0[kc*128:(kc+1)*128, nb*TW:(nb+1)*TW], {}),
                    (self.scr2[:, half:half+TW],
                     K0[1024 + kc*128: 1024 + (kc+1)*128, nb*TW:(nb+1)*TW], {})])
                b.wait_t('v', l2)
                subv.append(b.op('v', lambda e, kc=kc, nb=nb, half=half: e.tensor_sub(
                    wdst[:, kc*4096 + nb*TW: kc*4096 + (nb+1)*TW],
                    self.scr[:, half:half+TW], self.scr2[:, half:half+TW])))
        b.barrier()
        cons0 = self.mk_spill(d["spill_a"], AF.Relu,
                              scale_at=lambda oc: self.cols2[:, 40:41])
        self.gemm_rw(lambda kc, oc: wdst[:, kc*4096 + oc*128: kc*4096 + oc*128 + 128],
                     D, 4096, cons0, in_tile=self.nrm_tile, name="l2f0")
        b.barrier()
        def sgn_xform(og, buf, lv, wbuf, nkc, gw, ow):
            b.wait_t('v', lv)
            last = None
            for kc in range(nkc):
                w_ap = wbuf[:, buf*nkc*gw + kc*gw: buf*nkc*gw + kc*gw + ow]
                last = b.op('v', lambda e, w_ap=w_ap: e.tensor_scalar(
                    w_ap, w_ap, 2.0, -1.0, ALU.mult, ALU.add))
            return ('ev', last)
        cons1 = self.mk_spill(d["spill_b"], AF.Relu,
                              scale_at=lambda oc: self.cols2[:, 41:42])
        self.gemm_sw(d["L2_ffn_k1"], 4096, 4096, cons1, gw=256, wbuf_off=0,
                     in_stream=d["spill_a"], ibuf='nrm', w_xform=sgn_xform, name="l2f1")
        b.barrier()
        consd = self.mk_delta(scale_at=lambda oc: self.cols2[:, 42:43])
        self.gemm_sw(d["L2_ffn_k2"], 4096, D, consd, gw=256, wbuf_off=0,
                     in_stream=d["spill_b"], ibuf='nrm', w_xform=sgn_xform, name="l2f2")
        b.barrier()

    def emit_l1_ffn(self):
        b, d = self.b, self.dram
        lv = b.dma_group('y', [(self.cols[:, 0:32], d["L1_ffn_w1_s"], {}),
                               (self.cols[:, 32:64], d["L1_ffn_w2_s"], {}),
                               (self.cols2[:, 0:32], d["L1_ffn_h1_s"], {}),
                               (self.cols2[:, 32:64], d["L1_ffn_h2_s"], {})])
        b.wait_t('s', lv)
        b.wait_t('v', lv)
        self.emit_ln()
        consA = self.mk_spill(d["spill_b"], AF.Silu,
                              scale_at=lambda oc: self.cols[:, oc:oc+1])
        self.gemm_sw(d["L1_ffn_w1t"], D, 4096, consA, gw=512, wbuf_off=0,
                     in_tile=self.nrm_tile, name="l1fA")
        b.barrier()
        if getattr(self, 'l1f_stage', 9) < 2:
            return
        tbuf = self.abf(32 * 1024, 4 * TW)
        ebuf = self.abf(40 * 1024, 6 * TW)
        glist = [(og*4 + oc, tt*TW) for og in range(8) for tt in range(8)
                 for oc in range(4)]
        stateB = {"n": 0, "dma": [], "tl": [], "ld": []}

        def issue_t1(n):
            ocg, tok0 = glist[n]
            if n >= 4:
                b.wait('y', 'ev', stateB["tl"][n - 4])
            stateB["ld"].append(b.dma('y', tbuf[:, (n % 4)*TW:(n % 4)*TW + TW],
                                      d["spill_b"][ocg*128:(ocg+1)*128, tok0:tok0+TW]))
        issue_t1(0)
        issue_t1(1)

        def consB(oc, tt, ps, tok0):
            n = stateB["n"]; stateB["n"] += 1
            if n + 2 < len(glist):
                issue_t1(n + 2)
            tb = tbuf[:, (n % 4)*TW:(n % 4)*TW + TW]
            dst = ebuf[:, (n % 6)*TW:(n % 6)*TW + TW]
            if n >= 6:
                b.wait_t('v', stateB["dma"][n - 6])
            b.wait('v', 'et', b.cnt['et'])
            b.wait_t('v', stateB["ld"][n])
            v = b.op('v', lambda e, dst=dst, ps=ps, oc=oc, tb=tb: e.scalar_tensor_tensor(
                dst, ps[:, 0:TW], self.cols[:, 32+oc:33+oc], tb, ALU.mult, ALU.mult))
            stateB["tl"].append(v)
            b.wait('y', 'ev', v)
            dv = b.dma('y', d["spill_a"][oc*128:(oc+1)*128, tok0:tok0+TW], dst)
            stateB["dma"].append(dv)
            return ('ev', v)
        self.gemm_sw(d["L1_ffn_w2t"], D, 4096, consB, gw=512, wbuf_off=0,
                     in_tile=self.nrm_tile, name="l1fB")
        b.barrier()
        if getattr(self, 'l1f_stage', 9) < 3:
            return
        consC = self.mk_spill(d["spill_b"], AF.Gelu,
                              scale_at=lambda oc: self.cols2[:, oc:oc+1])
        self.gemm_sw(d["L1_ffn_h1t"], 4096, 4096, consC, gw=512, wbuf_off=0,
                     in_stream=d["spill_a"], ibuf='nrm', name="l1fC")
        b.barrier()
        if getattr(self, 'l1f_stage', 9) < 4:
            return
        consD = self.mk_spill(d["spill_a"], AF.Gelu,
                              scale_at=lambda oc: self.cols2[:, 32+oc:33+oc])
        self.gemm_sw(d["L1_ffn_h2t"], 4096, 4096, consD, gw=512, wbuf_off=0,
                     in_stream=d["spill_b"], ibuf='nrm', name="l1fD")
        b.barrier()
        if getattr(self, 'l1f_stage', 9) < 5:
            return
        lv = b.dma('y', self.cols[:, 0:8], d["L1_ffn_w3_s"])
        b.wait_t('v', lv)
        consE = self.mk_delta(scale_at=lambda oc: self.cols[:, oc:oc+1])
        self.gemm_sw(d["L1_ffn_w3t"], 4096, D, consE, gw=512, wbuf_off=0,
                     in_stream=d["spill_a"], ibuf='nrm', name="l1fE")
        b.barrier()


def host_prep(d, core, layers, x_start):
    ins = {}
    ins["x_in"] = np.ascontiguousarray(x_start[core].T).astype(np.float32)
    ins["ones_tab"] = np.ones((1, 128), np.float32)
    ins["ident_tab"] = np.eye(128, dtype=np.float32).astype(BF)
    if "l0_attn" in layers:
        ins["L0_attn_W_qkv"] = d["L0_attn_W_qkv"].astype(np.float32)
        ins["L0_attn_W_g"] = d["L0_attn_W_g"].astype(np.float32)
        ins["L0_attn_W_proj"] = d["L0_attn_W_proj"].astype(np.float32)
        ins["L0_attn_decay"] = d["L0_attn_decay"].reshape(1, 8).astype(np.float32)
        QK = 1024
        pos = np.arange(T, dtype=np.float32)
        base_scale = ((np.arange(0, QK, 2, dtype=np.float32) + 0.4 * QK) / (1.4 * QK))
        scale = base_scale[None, :] ** (pos[:, None] / np.float32(512.0))
        inv_freq = (1.0 / 10000.0 ** (np.arange(QK // 2, dtype=np.float32) / (QK // 2))
                    ).astype(np.float32)
        sinusoid = pos[:, None] * inv_freq[None, :]
        sn, cs = np.sin(sinusoid), np.cos(sinusoid)
        beta = np.float32(128.0 ** -0.5)
        rep = lambda a: np.ascontiguousarray(np.repeat(a, 2, axis=1).T.astype(np.float32))
        ins["xpos_cq"] = rep(cs * scale) * beta
        ins["xpos_sq"] = rep(sn * scale) * beta
        ins["xpos_ck"] = rep(cs / scale)
        ins["xpos_sk"] = rep(sn / scale)
        R = np.zeros((128, 128), np.float32)
        ii = np.arange(0, 128, 2)
        R[ii + 1, ii] = -1.0
        R[ii, ii + 1] = 1.0
        ins["rot_tab"] = R.astype(BF)
        jj = np.arange(128, dtype=np.float32)
        E = jj[None, :] - jj[:, None]
        ins["exp_tab"] = np.where(E >= 0, E, 1e9).astype(np.float32)
        ins["iota_col"] = jj.reshape(128, 1).copy()
    if "l0_ffn" in layers:
        ins["L0_ffn_lng"] = np.ascontiguousarray(d["L0_ffn_ln_gamma"].reshape(NKC, 128).T)
        ins["L0_ffn_lnb"] = np.ascontiguousarray(d["L0_ffn_ln_beta"].reshape(NKC, 128).T)
        ins["L0_ffn_fc1_q"] = d["L0_ffn_fc1_q"]
        ins["L0_ffn_fc1_scales"] = d["L0_ffn_fc1_scales"]
        ins["L0_ffn_fc1_bias"] = np.ascontiguousarray(d["L0_ffn_fc1_bias"].reshape(32, 128).T)
        ins["L0_ffn_fc2_q"] = d["L0_ffn_fc2_q"]
        ins["L0_ffn_fc2_scales"] = d["L0_ffn_fc2_scales"]
        ins["L0_ffn_fc2_bias"] = np.ascontiguousarray(d["L0_ffn_fc2_bias"].reshape(8, 128).T)
    if "l1_attn" in layers:
        ins["L1_attn_W1t"] = np.ascontiguousarray(d["L1_attn_W1"].T)
        ins["L1_attn_b1"] = d["L1_attn_b1"].reshape(8, 128).T.copy()
        ins["L1_attn_W2t"] = np.ascontiguousarray(d["L1_attn_W2"].T)
        ins["L1_attn_b2"] = d["L1_attn_b2"].reshape(8, 128).T.copy()
        ins["L1_attn_Wgt"] = np.ascontiguousarray(d["L1_attn_Wg"].T)
        ins["L1_attn_bg"] = d["L1_attn_bg"].reshape(24, 128).T.copy()
        ins["L1_attn_Wot"] = np.ascontiguousarray(d["L1_attn_Wo"].T)
        ins["L1_attn_bo"] = d["L1_attn_bo"].reshape(8, 128).T.copy()
        TWl = 512
        mm = np.zeros((128, 5 * TWl), np.float32)
        for k in range(5):
            r = np.arange(128)[:, None] + 128 * k
            c = np.arange(TWl)[None, :]
            mm[:, k*TWl:(k+1)*TWl] = (r <= c + 63)
        ins["l1_masks"] = mm.astype(BF)
    if "l2_attn" in layers:
        ins["L2_attn_Wv"] = d["L2_attn_Wv"]
        ins["L2_attn_bv"] = np.ascontiguousarray(d["L2_attn_bv"].reshape(16, 128).T)
        ins["L2_attn_Wg"] = d["L2_attn_Wg"]
        ins["L2_attn_bg"] = np.ascontiguousarray(d["L2_attn_bg"].reshape(16, 128).T)
        ins["L2_attn_Win"] = d["L2_attn_Win"]
        ins["L2_attn_binc"] = d["L2_attn_bin_"].reshape(1, 128).T.copy()
        ins["L2_attn_Wout"] = d["L2_attn_Wout"]
        ins["L2_attn_bout"] = np.ascontiguousarray(d["L2_attn_bout"].reshape(8, 128).T)
        ins["L2_attn_affs"] = np.stack([d["L2_attn_" + nm] for nm in
                                        ("g_qq", "b_qq", "g_qk", "b_qk",
                                         "g_lq", "b_lq", "g_lk", "b_lk")], axis=1).astype(np.float32)
        G = 256
        m = np.zeros((128, 512), np.float32)
        n = np.arange(G)
        m[:, 0:256] = (np.arange(128)[:, None] <= n[None, :])
        m[:, 256:512] = ((128 + np.arange(128))[:, None] <= n[None, :])
        ins["l2_masks"] = m.astype(BF)
    if "l2_ffn" in layers:
        ins["L2_ffn_k0"] = d["L2_ffn_k0"].astype(np.uint8)
        ins["L2_ffn_k1"] = d["L2_ffn_k1"].astype(np.uint8)
        ins["L2_ffn_k2"] = d["L2_ffn_k2"].astype(np.uint8)
        ins["L2_ffn_sc"] = np.array([[d["L2_ffn_s0"], d["L2_ffn_s1"], d["L2_ffn_s2"], 0.0]],
                                    np.float32)
    if "l1_ffn" in layers:
        for nm, tr in (("w1", "L1_ffn_w1_w"), ("w2", "L1_ffn_w2_w"),
                       ("h1", "L1_ffn_h1_w"), ("h2", "L1_ffn_h2_w"),
                       ("w3", "L1_ffn_w3_w")):
            ins[f"L1_ffn_{nm}t"] = np.ascontiguousarray(d[tr].T)
        for nm in ("w1_s", "w2_s", "h1_s", "h2_s", "w3_s"):
            v = d[f"L1_ffn_{nm}"]
            ins[f"L1_ffn_{nm}"] = np.ascontiguousarray(v.reshape(-1, 128).T)
    if "l3_attn" in layers:
        for nm in ("Wq", "Wk", "Wv", "Wo"):
            ins[f"L3_attn_{nm}"] = d[f"L3_attn_{nm}"]
        ins["L3_attn_bo"] = d["L3_attn_bo"].reshape(8, 128).T.copy()
        TWl = 512
        mm = np.zeros((128, 4 * TWl), np.float32)
        for k in range(4):
            r = np.arange(128)[:, None] + 128 * k
            c = np.arange(TWl)[None, :]
            mm[:, k*TWl:(k+1)*TWl] = (r <= c)
        ins["l3_masks"] = mm.astype(BF)
    if "l3_ffn" in layers:
        ins["L3_ffn_W1t"] = np.ascontiguousarray(d["L3_ffn_W1"].T)
        ins["L3_ffn_b1"] = np.ascontiguousarray(d["L3_ffn_b1"].reshape(32, 128).T)
        ins["L3_ffn_W2t"] = np.ascontiguousarray(d["L3_ffn_W2"].T)
        ins["L3_ffn_b2"] = np.ascontiguousarray(d["L3_ffn_b2"].reshape(8, 128).T)
    return ins


def build(layers, dbg=False, dbg_hook=None):
    nc = bass.Bass("TRN2", target_bir_lowering=False)
    with ExitStack() as stack:
        mk = MK(nc, stack, layers, dbg=dbg)
        mk.emit_init()
        for lay in layers:
            getattr(mk, "emit_" + lay)()
            if dbg_hook is not None:
                dbg_hook(mk, lay)
        mk.emit_finish()
        mk.b.finalize_into_block()
    return nc, mk


def dump_nrm(mk, lay=None):
    b, d = mk.b, mk.dram
    b.barrier()
    tok = b.dma_group('y', [(d["dbg_nrm"][kc*128:(kc+1)*128, :],
                             mk.nrm[:, kc*T:(kc+1)*T], {}) for kc in range(NKC)])
    b.wait_t('y', tok)
    b.barrier()


def dump_arena(mk):
    b, d = mk.b, mk.dram
    b.barrier()
    tok = b.dma_group('y', [(d["dbg_w"], mk.ar_bf[:, 0:32768], {})])
    b.wait_t('y', tok)
    b.barrier()

LAYERS = ["l0_attn", "l0_ffn", "l1_attn", "l1_ffn",
          "l2_attn", "l2_ffn", "l3_attn", "l3_ffn"]
_CACHE = {}


def _get_program():
    if "nc" not in _CACHE:
        nc, mkk = build(LAYERS)
        _CACHE["nc"] = nc
    return _CACHE["nc"]


def kernel(**inputs):
    nc = _get_program()
    x = np.asarray(inputs["L0_attn_hidden_states"], np.float32)
    ins_list = [host_prep(inputs, core, LAYERS, x) for core in range(8)]
    res = run_bass_kernel_spmd(nc, ins_list, core_ids=list(range(8)))
    out = np.empty((8, T, D), np.float32)
    for core in range(8):
        out[core] = np.asarray(res.results[core]["x_out"], np.float32).T
    return out
